# version 01
# baseline (speedup 1.0000x reference)
import numpy as np
import ml_dtypes
import concourse.bass as bass
import concourse.mybir as mybir
from concourse.alu_op_type import AluOpType as OP
from concourse.bass_utils import run_bass_kernel_spmd

F32 = mybir.dt.float32
F32R = mybir.dt.float32r
BF16 = mybir.dt.bfloat16
I32 = mybir.dt.int32
AF = mybir.ActivationFunctionType

T, C, I = 4096, 1024, 4096
CH = C // 128
ICH = I // 128
SBA = 512
NSB = T // SBA
NTIL = SBA // 128
NCORE = 8

ENGS = ("sync", "gpsimd", "vector", "scalar", "tensor")

_TAPS = []


def _build(taps=(), detect_races=False):
    nc = bass.Bass("TRN2", target_bir_lowering=False, detect_race_conditions=detect_races)
    dr = lambda n, sh, dt: nc.dram_tensor(n, sh, dt, kind="ExternalInput").ap()

    x_in = dr("x", [T, C], F32)
    wk_in = dr("wk", [C, C], F32)
    wv_in = dr("wv", [C, C], F32)
    wr_in = dr("wr", [C, C], F32)
    wo_in = dr("wo", [C, C], F32)
    wr2_in = dr("wr2", [C, C], F32)
    wk2_in = dr("wk2", [C, I], F32)
    wv2_in = dr("wv2", [I, C], F32)
    w1_in = dr("w1", [C, I], F32)
    w2_in = dr("w2", [I, C], F32)
    vecs = {n: dr(n, [1, C], F32) for n in
            ("ln1", "ln2", "tmk", "tmv", "tmr", "tmk2", "tmr2", "b2")}
    b1_in = dr("b1", [1, I], F32)
    tdb_in = dr("tdb", [128, C], F32)
    ub_in = dr("ub", [128, C], F32)
    ltc_in = dr("ltc", [128, 128], F32R)
    ones_in = dr("ones1", [1, 128], F32R)
    idnf_in = dr("idnf", [128, 128], F32)
    onesc_in = dr("onesc", [128, 1], BF16)
    onesr_in = dr("onesr", [128, 1], F32R)
    cbias_in = dr("cbias", [128, 8], F32)

    out_t = nc.dram_tensor("out", [T, C], F32, kind="ExternalOutput").ap()

    from contextlib import ExitStack
    ctx = ExitStack()

    _off = [16512]
    handles = {}

    def place(name, shape, dtype, at=None):
        nbytes = int(np.prod(shape[1:])) * mybir.dt.size(dtype)
        if at is None:
            at = _off[0]
            _off[0] = (at + nbytes + 31) & ~31
        h = nc.alloc_sbuf_tensor_at(name, shape, dtype, offset=at)
        handles[name] = (h, at, nbytes)
        return h

    LTC = place("LTC", [128, 128], F32R)
    ONES = place("ONES", [1, 128], F32R)
    IDNF = place("IDNF", [128, 128], F32)
    ONESC = place("ONESC", [128, 1], BF16)
    ONESR = place("ONESR", [128, 1], F32R)
    WCL = place("WCL", [128, C], F32)
    UB = place("UB", [128, C], F32)
    EMIN = place("EMIN", [128, C], F32)
    ESH = place("ESH", [128, C], F32)
    NIOT = place("NIOT", [128, 1], F32)
    IOTM1 = place("IOTM1", [128, 1], F32)
    EW2R = place("EW2R", [1, C], F32)
    CARRY = place("CARRY", [1, 2 * C], F32R)
    PC = {}
    for nme in ("ln1", "ln2", "tmk", "tmv", "tmr", "tmk2", "tmr2"):
        PC[nme] = place("PC_" + nme, [128, CH], F32)
    B1S = place("B1S", [128, ICH], F32)
    B2S = place("B2S", [128, CH], F32)
    SUMS = place("SUMS", [128, 32], F32)
    SQS = place("SQS", [128, 32], F32)
    MC = place("MC", [128, 32], F32)
    RSTD = place("RSTD", [128, 32], F32)
    RSTD1 = place("RSTD1", [128, 32], F32)
    TCA = place("TCA", [128, 4], F32)
    TCB = place("TCB", [128, 4], F32)
    RSTD1B = place("RSTD1B", [128, SBA], BF16)
    RSTD2B = place("RSTD2B", [128, SBA], BF16)
    XNFM = place("XNFM", [128, CH * SBA], BF16)
    X1BF = place("X1BF", [128, CH * SBA], BF16)
    FINALW = place("FINALW", [128, CH * SBA], F32)
    HFM = place("HFM", [128, CH * (SBA + 1)], BF16)
    DBUF = place("DBUF", [128, CH * SBA], BF16)
    RWKVFM = place("RWKVFM", [128, CH * SBA], BF16)
    H2FM = place("H2FM", [128, CH * (SBA + 1)], BF16)
    MIXT = place("MIXT", [128, 2 * 256], BF16)
    XR2J = place("XR2J", [128, SBA], BF16)
    GBUF = place("GBUF", [128, SBA], BF16)
    K2STG = place("K2STG", [128, ICH * SBA], BF16)
    CBIA = place("CBIA", [128, 8], F32)
    XT = place("XT", [128, C], F32)
    OUTSB = place("OUTSB", [128, C], F32)

    ovl0 = _off[0]

    def placeA(name, shape, dtype):
        return place(name, shape, dtype)

    KTM = placeA("KTM", [128, 2 * C], F32)
    VTM = placeA("VTM", [128, 2 * C], F32)
    ERT = placeA("ERT", [128, 2 * C], F32)
    KCB = placeA("KCB", [128, C], F32)
    EKT = placeA("EKT", [128, 2 * C], F32)
    EKB = placeA("EKB", [128, C], F32)
    ZB = placeA("ZB", [128, 2 * C], F32R)
    APR = placeA("APR", [128, 2 * C], F32)
    EKTV = placeA("EKTV", [128, C], F32)
    DEN = placeA("DEN", [128, C], F32)
    RWT = placeA("RWT", [128, C], F32)
    ovl_end_A = _off[0]

    _off[0] = ovl0
    WSA = place("WSA", [128, 2 * 4096], BF16)
    WSB = place("WSB", [128, 2 * 4 * C], BF16)
    XK2 = place("XK2", [128, CH * SBA], BF16)
    R2TH = place("R2TH", [128, CH * SBA], BF16)
    ITILE = place("ITILE", [128, SBA], I32)
    ITILE2 = place("ITILE2", [128, SBA], I32)
    Z1B = place("Z1B", [128, SBA], F32)
    FLB = place("FLB", [128, SBA], F32)
    GLB = place("GLB", [128, SBA], F32)
    GHB = place("GHB", [128, SBA], F32)
    ovl_end_B = _off[0]
    _off[0] = max(ovl_end_A, ovl_end_B)

    CROW = OUTSB
    total_kb = (_off[0] - 16512) / 1024.0
    assert _off[0] <= 229344, f"SBUF over budget: {total_kb:.1f} KB used, end={_off[0]}"

    QS = [ctx.enter_context(nc.psum_tensor("Q%d" % i, [128, 1024], F32)) for i in range(4)]

    class Ctr:
        def __init__(s, name):
            s.h = ctx.enter_context(nc.semaphore(name))
            s.n = 0
    cLD = Ctr("cLD")
    cWD = Ctr("cWD")
    cV = Ctr("cV")
    cS = Ctr("cS")
    cT = Ctr("cT")
    cG = Ctr("cG")
    cOUT = Ctr("cOUT")

    sched = {e: [] for e in ENGS}

    def em(eng, make, inc=None, k=1):
        if inc is not None:
            inc.n += k
        def f(e, make=make, inc=inc, k=k):
            r = make(e)
            if inc is not None:
                if r is None:
                    r = e.nop()
                r.then_inc(inc.h, k)
        sched[eng].append(f)

    def wait(eng, ctr, n=None):
        v = ctr.n if n is None else n
        if v > 0:
            sched[eng].append(lambda e, h=ctr.h, v=v: e.wait_ge(h, v))

    _selfc = {"vector": None, "scalar": None}

    def bar(eng):
        c = {"vector": cV, "scalar": cS}[eng]
        wait(eng, c, c.n)

    tapv = {}

    def _tap(name, apf):
        if name in taps:
            ap0 = apf()
            sh = list(ap0.shape)
            t = nc.dram_tensor("tap_" + name, sh, ap0.dtype, kind="ExternalOutput").ap()
            wait("sync", cV)
            wait("sync", cS)
            wait("sync", cG)
            wait("sync", cT)
            em("sync", lambda e, t=t, a=ap0: e.dma_start(out=t, in_=a), cOUT, 16)

    em("sync", lambda e: e.dma_start(out=LTC[:], in_=ltc_in), cLD, 16)
    em("sync", lambda e: e.dma_start(out=ONES[:], in_=ones_in), cLD, 16)
    em("sync", lambda e: e.dma_start(out=IDNF[:], in_=idnf_in), cLD, 16)
    em("sync", lambda e: e.dma_start(out=ONESC[:], in_=onesc_in), cLD, 16)
    em("sync", lambda e: e.dma_start(out=ONESR[:], in_=onesr_in), cLD, 16)
    em("sync", lambda e: e.dma_start(out=CBIA[:], in_=cbias_in), cLD, 16)
    em("sync", lambda e: e.dma_start(out=WCL[:], in_=tdb_in), cLD, 16)
    em("sync", lambda e: e.dma_start(out=UB[:], in_=ub_in), cLD, 16)
    for nme in ("ln1", "ln2", "tmk", "tmv", "tmr", "tmk2", "tmr2"):
        em("sync", lambda e, d=PC[nme], s=vecs[nme]: e.dma_start(
            out=d.ap()[:, :].rearrange("p (x c) -> p x c", x=1), in_=s.rearrange("x (c p) -> p x c", p=128),
            allow_slow_non_contiguous=True), cLD, 16)
    em("sync", lambda e: e.dma_start(out=B1S.ap()[:, :].rearrange("p (x c) -> p x c", x=1),
                                     in_=b1_in.rearrange("x (c p) -> p x c", p=128), allow_slow_non_contiguous=True), cLD, 16)
    em("sync", lambda e: e.dma_start(out=B2S.ap()[:, :].rearrange("p (x c) -> p x c", x=1),
                                     in_=vecs["b2"].rearrange("x (c p) -> p x c", p=128), allow_slow_non_contiguous=True), cLD, 16)
    init_ld = cLD.n

    em("gpsimd", lambda e: e.iota(NIOT[:], [[1, 1]], base=0, channel_multiplier=-1,
                                  allow_small_or_imprecise_dtypes=True), cG)
    em("gpsimd", lambda e: e.iota(IOTM1[:], [[1, 1]], base=-1, channel_multiplier=1,
                                  allow_small_or_imprecise_dtypes=True), cG)
    em("gpsimd", lambda e: e.memzero(CARRY[:]), cG)
    g_init = cG.n

    wait("scalar", cLD, init_ld)
    em("scalar", lambda e: e.activation(out=WCL[:], in_=WCL[:], func=AF.Exp), cS)
    wait("vector", cS)
    em("vector", lambda e: e.tensor_scalar(out=WCL[:], in0=WCL[:], scalar1=-1.0, scalar2=20.0,
                                           op0=OP.mult, op1=OP.min), cV)
    em("vector", lambda e: e.tensor_scalar(out=WCL[:], in0=WCL[:], scalar1=-20.0, scalar2=None,
                                           op0=OP.max), cV)
    wait("vector", cG, g_init)
    em("vector", lambda e: e.tensor_scalar(out=EMIN[:], in0=WCL[:], scalar1=NIOT[:, 0:1], scalar2=None,
                                           op0=OP.mult), cV)
    em("vector", lambda e: e.tensor_scalar(out=ESH[:], in0=WCL[:], scalar1=IOTM1[:, 0:1], scalar2=None,
                                           op0=OP.mult), cV)
    em("vector", lambda e: e.tensor_scalar(out=EW2R[:], in0=WCL[0:1, :], scalar1=128.0, scalar2=None,
                                           op0=OP.mult), cV)
    v_pre = cV.n
    wait("scalar", cV, v_pre)
    em("scalar", lambda e: e.activation(out=EMIN[:], in_=EMIN[:], func=AF.Exp), cS)
    em("scalar", lambda e: e.activation(out=ESH[:], in_=ESH[:], func=AF.Exp), cS)
    em("scalar", lambda e: e.activation(out=EW2R[:], in_=EW2R[:], func=AF.Exp), cS)
    s_pre = cS.n
    wait("vector", cS, s_pre)
    em("vector", lambda e: e.tensor_scalar(out=B1S[:], in0=B1S[:], scalar1=256.0, scalar2=None, op0=OP.mult), cV)
    em("vector", lambda e: e.tensor_scalar(out=B2S[:], in0=B2S[:], scalar1=256.0, scalar2=None, op0=OP.mult), cV)

    wkv_view = {"k": wk_in, "v": wv_in, "r": wr_in}
    WOUTV = K2STG

    def dma_w8(dst_ap, win):
        def f(e, d=dst_ap, w=win):
            return e.dma_start(out=d.rearrange("p (a c) -> p a c", a=CH),
                               in_=w.rearrange("(cc p) c -> p cc c", p=128))
        return f

    wait("tensor", cLD, init_ld)
    wait("gpsimd", cLD, init_ld)
    for a in range(NSB):
        tok0 = a * SBA
        g0 = a * NTIL

        ld_tile = []
        for t in range(NTIL):
            gt = tok0 + t * 128
            wait("sync", cV)
            wait("sync", cS)
            em("sync", lambda e, gt=gt: e.dma_start(out=XT[:], in_=x_in[gt:gt + 128, :]), cLD, 16)
            ldn = cLD.n
            g = g0 + t
            wait("vector", cLD, ldn)
            em("vector", lambda e, g=g: e.tensor_reduce(
                out=SUMS[:, g:g + 1], in_=XT[:], axis=mybir.AxisListType.X, op=OP.add), cV)
            wait("scalar", cLD, ldn)
            wait("scalar", cOUT, cOUT.n)
            wait("scalar", cT, cT.n)
            wait("scalar", cV, cV.n)
            em("scalar", lambda e, g=g: e.activation(
                out=OUTSB[:], in_=XT[:], func=AF.Square), cS)
            s_sq = cS.n
            wait("vector", cS, s_sq)
            em("vector", lambda e, g=g: e.tensor_reduce(
                out=SQS[:, g:g + 1], in_=OUTSB[:], axis=mybir.AxisListType.X, op=OP.add), cV)
            wait("vector", cS)
            em("vector", lambda e, g=g: e.tensor_scalar(
                out=MC[:, g:g + 1], in0=SUMS[:, g:g + 1], scalar1=1.0 / C, scalar2=None, op0=OP.mult), cV)
            bar("vector")
            em("vector", lambda e, g=g, t=t: e.tensor_tensor(
                out=TCA[:, t:t + 1], in0=MC[:, g:g + 1], in1=MC[:, g:g + 1], op=OP.mult), cV)
            bar("vector")
            em("vector", lambda e, g=g, t=t: e.scalar_tensor_tensor(
                out=TCB[:, t:t + 1], in0=SQS[:, g:g + 1], scalar=1.0 / C, in1=TCA[:, t:t + 1],
                op0=OP.mult, op1=OP.subtract), cV)
            v_var = cV.n
            wait("scalar", cV, v_var)
            em("scalar", lambda e, t=t: e.activation(out=TCA[:, t:t + 1], in_=TCB[:, t:t + 1], func=AF.Ln, bias=CBIA[:, 0:1]), cS)
            bar("scalar")
            em("scalar", lambda e, g=g, t=t: e.activation(
                out=RSTD[:, g:g + 1], in_=TCA[:, t:t + 1], func=AF.Exp, scale=-0.5), cS)
            s_rstd = cS.n
            wait("vector", cS, s_rstd)
            em("vector", lambda e, g=g, t=t: e.tensor_tensor(
                out=TCA[:, t:t + 1], in0=RSTD[:, g:g + 1], in1=RSTD[:, g:g + 1], op=OP.mult), cV)
            bar("vector")
            em("vector", lambda e, t=t: e.tensor_tensor(
                out=TCA[:, t:t + 1], in0=TCB[:, t:t + 1], in1=TCA[:, t:t + 1], op=OP.mult), cV)
            v_ms1 = cV.n
            wait("scalar", cV, v_ms1)
            em("scalar", lambda e, t=t: e.activation(out=TCB[:, t:t + 1], in_=TCA[:, t:t + 1], func=AF.Ln, bias=CBIA[:, 1:2]), cS)
            bar("scalar")
            em("scalar", lambda e, g=g, t=t: e.activation(
                out=RSTD1[:, g:g + 1], in_=TCB[:, t:t + 1], func=AF.Exp, scale=-0.5), cS)
            wait("vector", cOUT, cOUT.n)
            wait("vector", cT, cT.n)
            em("vector", lambda e, g=g: e.tensor_scalar(
                out=OUTSB[:], in0=XT[:], scalar1=MC[:, g:g + 1], scalar2=RSTD[:, g:g + 1],
                op0=OP.subtract, op1=OP.mult), cV)
            v_xnb = cV.n
            wait("tensor", cV, v_xnb)
            wait("tensor", cS)
            for cb in range(CH):
                em("tensor", lambda e, cb=cb: e.transpose(
                    QS[2][:, cb * 128:(cb + 1) * 128], OUTSB[:, cb * 128:(cb + 1) * 128], IDNF[:]), cT)
            t_x = cT.n
            wait("vector", cT, t_x)
            em("vector", lambda e, t=t: e.tensor_copy(
                out=XNFM.ap()[:, :].rearrange("p (c w) -> p c w", c=CH)[:, :, t * 128:(t + 1) * 128],
                in_=QS[2].ap()[:, :].rearrange("p (c w) -> p c w", c=CH)), cV)

        if a == 0:
            _tap("sums_a0", lambda: SUMS[:])
            _tap("sqs_a0", lambda: SQS[:])
            _tap("mc_a0", lambda: MC[:])
            _tap("rstd_a0", lambda: RSTD[:])
            _tap("idnf_a0", lambda: IDNF[:])
            _tap("outsb_a0", lambda: OUTSB[:])
        s_r1 = cS.n
        wait("tensor", cS, s_r1)
        wait("tensor", cV)
        for t in range(NTIL):
            em("tensor", lambda e, t=t, g0=g0: e.transpose(
                QS[0][0:1, t * 128:(t + 1) * 128], RSTD1[:, g0 + t:g0 + t + 1], IDNF[:]), cT)
        t_tr = cT.n
        wait("vector", cT, t_tr)
        em("vector", lambda e: e.tensor_copy(out=ZB[0:1, 0:SBA], in_=QS[0][0:1, 0:SBA]), cV)
        v_srp = cV.n
        wait("tensor", cV, v_srp)
        for t in range(NTIL):
            em("tensor", lambda e, t=t: e.matmul(QS[1][:, t * 128:(t + 1) * 128], ONES[:],
                                                 ZB[0:1, t * 128:(t + 1) * 128], start=True, stop=True), cT)
        t_bc = cT.n
        wait("vector", cT, t_bc)
        em("vector", lambda e: e.tensor_copy(out=RSTD1B[:], in_=QS[1][:, 0:SBA]), cV)
        v_r1b = cV.n

        _tap("xnfm_a%d" % a, lambda: XNFM[:])

        hv = HFM.ap()[:, :].rearrange("p (c w) -> p c w", w=SBA + 1)
        if a == 0:
            em("vector", lambda e, hv=hv: e.memset(hv[:, :, 0:1], 0.0), cV)
        else:
            em("vector", lambda e, hv=hv: e.tensor_copy(out=hv[:, :, 0:1], in_=hv[:, :, SBA:SBA + 1]), cV)
        wait("vector", cV, v_r1b)
        for c in range(CH):
            hb = lambda c=c: HFM[:, c * (SBA + 1) + 1:(c + 1) * (SBA + 1)]
            em("vector", lambda e, c=c, hb=hb: e.tensor_tensor(
                out=hb(), in0=XNFM[:, c * SBA:(c + 1) * SBA], in1=RSTD1B[:], op=OP.mult), cV)
            em("vector", lambda e, c=c, hb=hb: e.tensor_scalar(
                out=hb(), in0=hb(), scalar1=PC["ln1"][:, c:c + 1], scalar2=None, op0=OP.mult), cV)
            em("vector", lambda e, c=c, hb=hb: e.tensor_tensor(
                out=DBUF[:, c * SBA:(c + 1) * SBA], in0=hb(),
                in1=HFM[:, c * (SBA + 1):(c + 1) * (SBA + 1) - 1], op=OP.subtract), cV)
        _tap("h_a%d" % a, lambda: HFM[:])

        projs = [("k", PC["tmk"], KTM), ("v", PC["tmv"], VTM), ("r", PC["tmr"], ERT)]
        wait("vector", cOUT, cOUT.n)
        for pr in range(2):
            evac_s = {}
            evac_v = {}
            for pi, (pn, tmcol, dstbuf) in enumerate(projs):
                wait("gpsimd", cT, cT.n)
                em("gpsimd", lambda e, pn=pn: e.dma_start(
                    out=K2STG[:, 0:CH * C].rearrange("p (a c) -> p a c", a=CH),
                    in_=wkv_view[pn].rearrange("(cc p) c -> p cc c", p=128)), cWD, 16)
                wd = cWD.n
                qk = [QS[0], QS[1]] if pn in ("k", "r") else [QS[2], QS[3]]
                if pn == "r":
                    wait("tensor", cS)
                for ci in range(CH):
                    ms = (pi * CH + ci) % 2
                    wait("vector", cT, cT.n)
                    em("vector", lambda e, ms=ms, ci=ci, pr=pr, tmcol=tmcol: e.scalar_tensor_tensor(
                        out=MIXT[:, ms * 256:(ms + 1) * 256],
                        in0=DBUF[:, ci * SBA + pr * 256:ci * SBA + pr * 256 + 256],
                        scalar=tmcol[:, ci:ci + 1],
                        in1=HFM[:, ci * (SBA + 1) + pr * 256:ci * (SBA + 1) + pr * 256 + 256],
                        op0=OP.mult, op1=OP.add), cV)
                    v_mix = cV.n
                    wait("tensor", cV, v_mix)
                    if ci == 0:
                        wait("tensor", cWD, wd)
                    for bl in range(2):
                        for nh in range(2):
                            em("tensor", lambda e, ms=ms, ci=ci, bl=bl, nh=nh, qk=qk: e.matmul(
                                qk[bl][:, nh * 512:(nh + 1) * 512],
                                MIXT[:, ms * 256 + bl * 128:ms * 256 + bl * 128 + 128],
                                K2STG[:, ci * C + nh * 512:ci * C + (nh + 1) * 512],
                                start=(ci == 0), stop=(ci == CH - 1)), cT)
                t_mm = cT.n
                for bl in range(2):
                    if pn == "k":
                        wait("scalar", cT, t_mm)
                        em("scalar", lambda e, bl=bl, qk=qk: e.activation(
                            out=KTM[:, bl * C:(bl + 1) * C], in_=qk[bl][:], func=AF.Copy), cS)
                    elif pn == "v":
                        wait("vector", cT, t_mm)
                        em("vector", lambda e, bl=bl, qk=qk: e.tensor_copy(
                            out=VTM[:, bl * C:(bl + 1) * C], in_=qk[bl][:]), cV)
                    else:
                        wait("scalar", cT, t_mm)
                        em("scalar", lambda e, bl=bl, qk=qk: e.activation(
                            out=ERT[:, bl * C:(bl + 1) * C], in_=qk[bl][:], func=AF.Exp, scale=-1.0), cS)
                evac_s[pn] = cS.n
                evac_v[pn] = cV.n

            for bl in range(2):
                blk = 2 * pr + bl
                gq = [QS[2], QS[3]] if bl == 0 else [QS[0], QS[1]]
                wait("vector", cS)
                wait("vector", cV, evac_v["v"])
                em("vector", lambda e, bl=bl: e.tensor_tensor(
                    out=KCB[:], in0=KTM[:, bl * C:(bl + 1) * C], in1=UB[:], op=OP.add), cV)
                em("vector", lambda e: e.tensor_scalar(
                    out=KCB[:], in0=KCB[:], scalar1=20.0, scalar2=-20.0, op0=OP.min, op1=OP.max), cV)
                v_kc = cV.n
                wait("scalar", cV, v_kc)
                em("scalar", lambda e, bl=bl: e.activation(
                    out=EKT[:, bl * C:(bl + 1) * C], in_=KCB[:], func=AF.Exp), cS)
                s_ekt = cS.n
                wait("vector", cS, s_ekt)
                em("vector", lambda e, bl=bl: e.tensor_scalar(
                    out=KCB[:], in0=KTM[:, bl * C:(bl + 1) * C], scalar1=20.0, scalar2=-20.0,
                    op0=OP.min, op1=OP.max), cV)
                v_kc2 = cV.n
                wait("scalar", cV, v_kc2)
                em("scalar", lambda e: e.activation(out=EKB[:], in_=KCB[:], func=AF.Exp), cS)
                s_ek = cS.n
                wait("vector", cS, s_ek)
                em("vector", lambda e: e.tensor_tensor(
                    out=ZB[:, C:2 * C], in0=EKB[:], in1=EMIN[:], op=OP.mult), cV)
                em("vector", lambda e, bl=bl: e.tensor_tensor(
                    out=ZB[:, 0:C], in0=ZB[:, C:2 * C], in1=VTM[:, bl * C:(bl + 1) * C], op=OP.mult), cV)
                v_z = cV.n
                wait("tensor", cV, v_z)
                wait("tensor", cS)
                for seg in range(4):
                    q = gq[seg // 2]
                    qoff = (seg % 2) * 512
                    em("tensor", lambda e, q=q, qoff=qoff, seg=seg: e.matmul(
                        q[:, qoff:qoff + 512], LTC[:],
                        ZB[:, seg * 512:(seg + 1) * 512], start=True, stop=False), cT)
                    em("tensor", lambda e, q=q, qoff=qoff, seg=seg: e.matmul(
                        q[:, qoff:qoff + 512], ONES[:],
                        CARRY[:, seg * 512:(seg + 1) * 512], start=False, stop=True), cT)
                t_g = cT.n
                wait("vector", cT, t_g)
                em("vector", lambda e, gq=gq: e.tensor_tensor(
                    out=APR[:, 0:C], in0=gq[0][:], in1=ESH[:], op=OP.mult), cV)
                em("vector", lambda e, gq=gq: e.tensor_tensor(
                    out=APR[:, C:2 * C], in0=gq[1][:], in1=ESH[:], op=OP.mult), cV)
                sq = [QS[0], QS[1]] if bl == 0 else [QS[2], QS[3]]
                for seg in range(4):
                    em("tensor", lambda e, sq=sq, seg=seg: e.matmul(
                        sq[seg // 2][0:1, (seg % 2) * 512:(seg % 2) * 512 + 512], ONESR[:],
                        ZB[:, seg * 512:(seg + 1) * 512], start=True, stop=True), cT)
                t_s = cT.n
                wait("vector", cT, t_s)
                em("vector", lambda e, sq=sq: e.tensor_tensor(
                    out=CROW[0:1, 0:C], in0=CARRY[:, 0:C], in1=sq[0][0:1, 0:C], op=OP.add), cV)
                em("vector", lambda e, sq=sq: e.tensor_tensor(
                    out=XT[0:1, 0:C], in0=CARRY[:, C:2 * C], in1=sq[1][0:1, 0:C], op=OP.add), cV)
                em("vector", lambda e: e.tensor_tensor(
                    out=CARRY[:, 0:C], in0=CROW[0:1, 0:C], in1=EW2R[:], op=OP.mult), cV)
                em("vector", lambda e: e.tensor_tensor(
                    out=CARRY[:, C:2 * C], in0=XT[0:1, 0:C], in1=EW2R[:], op=OP.mult), cV)
                em("vector", lambda e, bl=bl: e.tensor_tensor(
                    out=EKTV[:], in0=EKT[:, bl * C:(bl + 1) * C],
                    in1=VTM[:, bl * C:(bl + 1) * C], op=OP.mult), cV)
                em("vector", lambda e: e.tensor_tensor(
                    out=EKTV[:], in0=EKTV[:], in1=APR[:, 0:C], op=OP.add), cV)
                em("vector", lambda e, bl=bl: e.scalar_tensor_tensor(
                    out=DEN[:], in0=EKT[:, bl * C:(bl + 1) * C], scalar=1e-8,
                    in1=APR[:, C:2 * C], op0=OP.add, op1=OP.add), cV)
                wait("vector", cS, evac_s.get("r", 0))
                em("vector", lambda e, bl=bl: e.scalar_tensor_tensor(
                    out=DEN[:], in0=ERT[:, bl * C:(bl + 1) * C], scalar=1.0,
                    in1=DEN[:], op0=OP.add, op1=OP.mult), cV)
                v_den = cV.n
                wait("scalar", cV, v_den)
                em("scalar", lambda e: e.activation(out=KCB[:], in_=DEN[:], func=AF.Ln), cS)
                em("scalar", lambda e: e.activation(out=EKB[:], in_=KCB[:], func=AF.Exp, scale=-1.0), cS)
                s_rd = cS.n
                wait("vector", cS, s_rd)
                em("vector", lambda e: e.tensor_tensor(out=RWT[:], in0=EKTV[:], in1=EKB[:], op=OP.mult), cV)
                v_rwt = cV.n
                wait("tensor", cV, v_rwt)
                tq = gq[0]
                for cb in range(CH):
                    em("tensor", lambda e, cb=cb, tq=tq: e.transpose(
                        tq[:, cb * 128:(cb + 1) * 128], RWT[:, cb * 128:(cb + 1) * 128], IDNF[:]), cT)
                t_rt = cT.n
                wait("vector", cT, t_rt)
                em("vector", lambda e, blk=blk, tq=tq: e.tensor_copy(
                    out=RWKVFM.ap()[:, :].rearrange("p (c w) -> p c w", c=CH)[:, :, blk * 128:(blk + 1) * 128],
                    in_=tq.ap()[:, :].rearrange("p (c w) -> p c w", c=CH)), cV)

        v_rwkv = cV.n
        _tap("rwkv_a%d" % a, lambda: RWKVFM[:])

        wait("gpsimd", cT, cT.n)
        em("gpsimd", dma_w8(WOUTV[:, 0:CH * C], wo_in), cWD, 16)
        wd_wo = cWD.n
        wait("tensor", cV, v_rwkv)
        wait("tensor", cS)
        wait("tensor", cWD, wd_wo)
        for co in range(CH):
            q = QS[co % 2]
            for ci in range(CH):
                em("tensor", lambda e, co=co, ci=ci, q=q: e.matmul(
                    q[:, 0:512], WOUTV[:, ci * C + co * 128:ci * C + co * 128 + 128],
                    RWKVFM[:, ci * SBA:(ci + 1) * SBA],
                    start=(ci == 0), stop=(ci == CH - 1)), cT)
            t_wo = cT.n
            wait("scalar", cT, t_wo)
            em("scalar", lambda e, co=co, q=q: e.activation(
                out=FINALW[:, co * SBA:(co + 1) * SBA], in_=q[:, 0:512], func=AF.Copy), cS)
            s_fw = cS.n
            wait("vector", cT, t_wo)
            wait("vector", cS, s_fw)
            em("vector", lambda e, co=co, q=q: e.tensor_tensor(
                out=X1BF[:, co * SBA:(co + 1) * SBA], in0=q[:, 0:512],
                in1=XNFM[:, co * SBA:(co + 1) * SBA], op=OP.add), cV)
            if co >= 1:
                wait("tensor", cS)
                wait("tensor", cV)
        v_x1 = cV.n
        _tap("x1_a%d" % a, lambda: X1BF[:])
        _tap("att_a%d" % a, lambda: FINALW[:])

        wait("vector", cV, v_x1)
        wait("tensor", cS)
        wait("tensor", cV)
        t_mm2 = {}
        for c in range(CH):
            gb = GBUF if c % 2 == 0 else XR2J
            if c >= 2:
                wait("vector", cT, t_mm2[c - 2])
            em("vector", lambda e, c=c, gb=gb: e.tensor_tensor(
                out=gb[:, 0:SBA], in0=X1BF[:, c * SBA:(c + 1) * SBA],
                in1=X1BF[:, c * SBA:(c + 1) * SBA], op=OP.mult), cV)
            v_sq = cV.n
            wait("tensor", cV, v_sq)
            em("tensor", lambda e, c=c, gb=gb: e.matmul(
                QS[2][0:1, 0:512], ONESC[:], gb[:, 0:SBA], start=(c == 0), stop=(c == CH - 1)), cT)
            t_mm2[c] = cT.n
        t_ms2 = cT.n
        wait("scalar", cT, t_ms2)
        em("scalar", lambda e: e.activation(out=OUTSB[0:1, 0:512], in_=QS[2][0:1, 0:512],
                                            func=AF.Ln, scale=1.0 / C, bias=CBIA[0:1, 1:2]), cS)
        bar("scalar")
        em("scalar", lambda e: e.activation(out=ZB[0:1, 0:512], in_=OUTSB[0:1, 0:512],
                                            func=AF.Exp, scale=-0.5), cS)
        s_rs2 = cS.n
        wait("tensor", cS, s_rs2)
        for t in range(NTIL):
            em("tensor", lambda e, t=t: e.matmul(QS[3][:, t * 128:(t + 1) * 128], ONES[:],
                                                 ZB[0:1, t * 128:(t + 1) * 128], start=True, stop=True), cT)
        t_r2b = cT.n
        wait("vector", cT, t_r2b)
        em("vector", lambda e: e.tensor_copy(out=RSTD2B[:], in_=QS[3][:, 0:SBA]), cV)
        v_r2b = cV.n
        _tap("rstd2_a%d" % a, lambda: RSTD2B[:])

        h2v = H2FM.ap()[:, :].rearrange("p (c w) -> p c w", w=SBA + 1)
        if a == 0:
            em("vector", lambda e, h2v=h2v: e.memset(h2v[:, :, 0:1], 0.0), cV)
        else:
            em("vector", lambda e, h2v=h2v: e.tensor_copy(out=h2v[:, :, 0:1], in_=h2v[:, :, SBA:SBA + 1]), cV)
        wait("vector", cV, v_r2b)
        for c in range(CH):
            h2b = lambda c=c: H2FM[:, c * (SBA + 1) + 1:(c + 1) * (SBA + 1)]
            em("vector", lambda e, c=c, h2b=h2b: e.tensor_tensor(
                out=h2b(), in0=X1BF[:, c * SBA:(c + 1) * SBA], in1=RSTD2B[:], op=OP.mult), cV)
            em("vector", lambda e, c=c, h2b=h2b: e.tensor_scalar(
                out=h2b(), in0=h2b(), scalar1=PC["ln2"][:, c:c + 1], scalar2=None, op0=OP.mult), cV)
            em("vector", lambda e, c=c, h2b=h2b: e.tensor_tensor(
                out=DBUF[:, c * SBA:(c + 1) * SBA], in0=h2b(),
                in1=H2FM[:, c * (SBA + 1):(c + 1) * (SBA + 1) - 1], op=OP.subtract), cV)
            em("vector", lambda e, c=c: e.scalar_tensor_tensor(
                out=XK2[:, c * SBA:(c + 1) * SBA], in0=DBUF[:, c * SBA:(c + 1) * SBA],
                scalar=PC["tmk2"][:, c:c + 1],
                in1=H2FM[:, c * (SBA + 1):(c + 1) * (SBA + 1) - 1],
                op0=OP.mult, op1=OP.add), cV)
        v_xk2 = cV.n
        _tap("xk2_a%d" % a, lambda: XK2[:])

        wait("gpsimd", cT, cT.n)
        em("gpsimd", dma_w8(WOUTV[:, 0:CH * C], wr2_in), cWD, 16)
        wd_r2 = cWD.n
        wait("tensor", cWD, wd_r2)
        wait("tensor", cV, v_r2b)
        for half in range(2):
            for ci in range(CH):
                wait("vector", cT, cT.n)
                em("vector", lambda e, ci=ci: e.scalar_tensor_tensor(
                    out=XR2J[:, 0:SBA], in0=DBUF[:, ci * SBA:(ci + 1) * SBA],
                    scalar=PC["tmr2"][:, ci:ci + 1],
                    in1=H2FM[:, ci * (SBA + 1):(ci + 1) * (SBA + 1) - 1],
                    op0=OP.mult, op1=OP.add), cV)
                v_xr = cV.n
                wait("tensor", cV, v_xr)
                for cg in range(4):
                    co = half * 4 + cg
                    em("tensor", lambda e, co=co, ci=ci, cg=cg: e.matmul(
                        QS[cg][:, 0:512], WOUTV[:, ci * C + co * 128:ci * C + co * 128 + 128],
                        XR2J[:, 0:SBA],
                        start=(ci == 0), stop=(ci == CH - 1)), cT)
            t_r2 = cT.n
            wait("scalar", cT, t_r2)
            for cg in range(4):
                co = half * 4 + cg
                em("scalar", lambda e, co=co, cg=cg: e.activation(
                    out=R2TH[:, co * SBA:(co + 1) * SBA], in_=QS[cg][:, 0:512],
                    func=AF.Tanh, scale=0.5), cS)
            wait("tensor", cS)
        s_r2 = cS.n
        _tap("r2th_a%d" % a, lambda: R2TH[:])

        k2base = 0
        wk2v = wk2_in.rearrange("(cc p) q -> p cc q", p=128)
        w1v = w1_in.rearrange("(cc p) q -> p cc q", p=128)
        for i in range(ICH):
            if i % 4 == 0:
                wait("gpsimd", cT, cT.n)
                ip = (i // 4) % 2
                em("gpsimd", lambda e, i=i, ip=ip: e.dma_start(
                    out=WSA[:, ip * 4096:(ip + 1) * 4096].rearrange("p (b q) -> p b q", b=CH),
                    in_=wk2v[:, :, i * 128:(i + 4) * 128]), cWD, 16)
                wd_k2 = cWD.n
                wait("tensor", cWD, wd_k2)
            q = QS[i % 2]
            ip = (i // 4) % 2
            io = i % 4
            if i == 0:
                wait("tensor", cV, v_xk2)
                wait("tensor", cS, s_r2)
            else:
                wait("tensor", cS)
            for ci in range(CH):
                em("tensor", lambda e, i=i, ci=ci, q=q, ip=ip, io=io: e.matmul(
                    q[:, 0:512],
                    WSA[:, ip * 4096 + ci * 512 + io * 128:ip * 4096 + ci * 512 + io * 128 + 128],
                    XK2[:, ci * SBA:(ci + 1) * SBA],
                    start=(ci == 0), stop=(ci == CH - 1)), cT)
            t_k2 = cT.n
            wait("scalar", cT, t_k2)
            if i > 0:
                wait("scalar", cV, v_sq_prev)
            em("scalar", lambda e, q=q: e.activation(out=GBUF[:], in_=q[:, 0:512], func=AF.Relu), cS)
            s_rl = cS.n
            wait("vector", cS, s_rl)
            em("vector", lambda e, i=i: e.tensor_tensor(
                out=K2STG[:, k2base + i * SBA:k2base + (i + 1) * SBA],
                in0=GBUF[:], in1=GBUF[:], op=OP.mult), cV)
            v_sq_prev = cV.n
        v_k2 = cV.n
        _tap("k2_a%d" % a, lambda: K2STG[:])

        for i in range(ICH):
            if i % 4 == 0:
                wait("gpsimd", cT, cT.n)
                ip = (i // 4) % 2
                em("gpsimd", lambda e, i=i, ip=ip: e.dma_start(
                    out=WSB[:, ip * 4 * C:(ip + 1) * 4 * C].rearrange("p (a c) -> p a c", a=4),
                    in_=wv2_in[i * 128:(i + 4) * 128, :].rearrange("(ii p) c -> p ii c", p=128)), cWD, 16)
                wd_v2 = cWD.n
                wait("tensor", cWD, wd_v2)
            ip = (i // 4) % 2
            io = i % 4
            if i == 0:
                wait("tensor", cV, v_k2)
                wait("tensor", cS, cS.n)
            for co in range(CH):
                q = QS[co // 2]
                qo = (co % 2) * 512
                em("tensor", lambda e, i=i, co=co, q=q, qo=qo, ip=ip, io=io: e.matmul(
                    q[:, qo:qo + 512],
                    WSB[:, ip * 4 * C + io * C + co * 128:ip * 4 * C + io * C + co * 128 + 128],
                    K2STG[:, k2base + i * SBA:k2base + (i + 1) * SBA],
                    start=(i == 0), stop=(i == ICH - 1)), cT)
        t_cm = cT.n
        for co in range(CH):
            q = QS[co // 2]
            qo = (co % 2) * 512
            wait("vector", cT, t_cm)
            em("vector", lambda e, co=co, q=q, qo=qo: e.scalar_tensor_tensor(
                out=GLB[:], in0=R2TH[:, co * SBA:(co + 1) * SBA], scalar=1.0,
                in1=q[:, qo:qo + 512], op0=OP.add, op1=OP.mult), cV)
            em("vector", lambda e, co=co: e.scalar_tensor_tensor(
                out=FINALW[:, co * SBA:(co + 1) * SBA], in0=GLB[:], scalar=0.5,
                in1=FINALW[:, co * SBA:(co + 1) * SBA], op0=OP.mult, op1=OP.add), cV)
        v_cm = cV.n
        _tap("cm_a%d" % a, lambda: FINALW[:])

        v_iq = {}
        g_hqc = {}
        for i in range(ICH):
            IT = ITILE if i % 2 == 0 else ITILE2
            if i % 4 == 0:
                wait("gpsimd", cT, cT.n)
                ip = (i // 4) % 2
                em("gpsimd", lambda e, i=i, ip=ip: e.dma_start(
                    out=WSA[:, ip * 4096:(ip + 1) * 4096].rearrange("p (b q) -> p b q", b=CH),
                    in_=w1v[:, :, i * 128:(i + 4) * 128]), cWD, 16)
                wd_w1 = cWD.n
                wait("tensor", cWD, wd_w1)
            q = QS[i % 2]
            ip = (i // 4) % 2
            io = i % 4
            if i == 0:
                wait("tensor", cV, v_cm)
            else:
                wait("tensor", cV, v_iq[i - 1])
            for ci in range(CH):
                em("tensor", lambda e, i=i, ci=ci, q=q, ip=ip, io=io: e.matmul(
                    q[:, 0:512],
                    WSA[:, ip * 4096 + ci * 512 + io * 128:ip * 4096 + ci * 512 + io * 128 + 128],
                    XNFM[:, ci * SBA:(ci + 1) * SBA],
                    start=(ci == 0), stop=(ci == CH - 1)), cT)
            t_w1 = cT.n
            wait("vector", cT, t_w1)
            if i >= 2:
                wait("vector", cG, g_hqc[i - 2])
            em("vector", lambda e, i=i, q=q, IT=IT: e.tensor_scalar(
                out=IT[:], in0=q[:, 0:512], scalar1=256.0, scalar2=B1S[:, i:i + 1],
                op0=OP.mult, op1=OP.add), cV)
            v_iq[i] = cV.n
            if i == 0 and ("iq0_a%d" % a) in taps:
                tq0 = nc.dram_tensor("tap_iq0_a%d" % a, [128, SBA], I32, kind="ExternalOutput").ap()
                wait("sync", cV)
                em("sync", lambda e, tq0=tq0, IT=IT: e.dma_start(out=tq0, in_=IT[:]), cOUT, 16)
                wait("vector", cOUT, cOUT.n)
            em("vector", lambda e, IT=IT: e.tensor_scalar(
                out=Z1B[:], in0=IT[:], scalar1=0.015625, scalar2=15.5078125,
                op0=OP.mult, op1=OP.add), cV)
            em("vector", lambda e, IT=IT: e.tensor_scalar(out=IT[:], in0=Z1B[:], scalar1=1.0, scalar2=None, op0=OP.mult), cV)
            v_fi = cV.n
            wait("gpsimd", cV, v_fi)
            em("gpsimd", lambda e, IT=IT: e.tensor_copy(out=FLB[:], in_=IT[:]), cG)
            g_fl = cG.n
            wait("scalar", cG, g_fl)
            em("scalar", lambda e: e.activation(out=GLB[:], in_=FLB[:], func=AF.Gelu, scale=0.25, bias=CBIA[:, 2:3]), cS)
            em("scalar", lambda e: e.activation(out=GHB[:], in_=FLB[:], func=AF.Gelu, scale=0.25, bias=CBIA[:, 3:4]), cS)
            s_g = cS.n
            wait("vector", cS, s_g)
            em("vector", lambda e: e.scalar_tensor_tensor(
                out=Z1B[:], in0=FLB[:], scalar=-1.0, in1=Z1B[:], op0=OP.mult, op1=OP.add), cV)
            em("vector", lambda e: e.tensor_tensor(out=GHB[:], in0=GHB[:], in1=GLB[:], op=OP.subtract), cV)
            em("vector", lambda e: e.scalar_tensor_tensor(
                out=GHB[:], in0=Z1B[:], scalar=0.4921875, in1=GHB[:], op0=OP.add, op1=OP.mult), cV)
            em("vector", lambda e: e.tensor_tensor(out=GLB[:], in0=GLB[:], in1=GHB[:], op=OP.add), cV)
            em("vector", lambda e, IT=IT: e.tensor_scalar(
                out=IT[:], in0=GLB[:], scalar1=256.0, scalar2=None, op0=OP.mult), cV)
            v_i2 = cV.n
            wait("gpsimd", cV, v_i2)
            em("gpsimd", lambda e, i=i, IT=IT: e.tensor_scalar(
                out=K2STG[:, k2base + i * SBA:k2base + (i + 1) * SBA],
                in0=IT[:], scalar1=1.0 / 256.0, scalar2=None, op0=OP.mult), cG)
            g_hqc[i] = cG.n
        g_hq = cG.n
        _tap("hq_a%d" % a, lambda: K2STG[:])

        for i in range(ICH):
            if i % 4 == 0:
                wait("gpsimd", cT, cT.n)
                ip = (i // 4) % 2
                em("gpsimd", lambda e, i=i, ip=ip: e.dma_start(
                    out=WSB[:, ip * 4 * C:(ip + 1) * 4 * C].rearrange("p (a c) -> p a c", a=4),
                    in_=w2_in[i * 128:(i + 4) * 128, :].rearrange("(ii p) c -> p ii c", p=128)), cWD, 16)
                wd_w2 = cWD.n
                wait("tensor", cWD, wd_w2)
            ip = (i // 4) % 2
            io = i % 4
            if i == 0:
                wait("tensor", cG, g_hq)
                wait("tensor", cV)
            for co in range(CH):
                q = QS[co // 2]
                qo = (co % 2) * 512
                em("tensor", lambda e, i=i, co=co, q=q, qo=qo, ip=ip, io=io: e.matmul(
                    q[:, qo:qo + 512],
                    WSB[:, ip * 4 * C + io * C + co * 128:ip * 4 * C + io * C + co * 128 + 128],
                    K2STG[:, k2base + i * SBA:k2base + (i + 1) * SBA],
                    start=(i == 0), stop=(i == ICH - 1)), cT)
        t_w2 = cT.n
        for co in range(CH):
            q = QS[co // 2]
            qo = (co % 2) * 512
            wait("vector", cT, t_w2)
            em("vector", lambda e, co=co, q=q, qo=qo: e.tensor_scalar(
                out=ITILE[:], in0=q[:, qo:qo + 512], scalar1=256.0, scalar2=B2S[:, co:co + 1],
                op0=OP.mult, op1=OP.add), cV)
            em("vector", lambda e, co=co: e.scalar_tensor_tensor(
                out=FINALW[:, co * SBA:(co + 1) * SBA], in0=ITILE[:], scalar=1.0 / 256.0,
                in1=FINALW[:, co * SBA:(co + 1) * SBA], op0=OP.mult, op1=OP.add), cV)
        v_ffn = cV.n
        _tap("final_a%d" % a, lambda: FINALW[:])

        fv = FINALW.ap()[:, :].rearrange("p (c w) -> p c w", c=CH)
        for t in range(NTIL):
            g = g0 + t
            gt = tok0 + t * 128
            sl = t % 2
            wait("sync", cV)
            em("sync", lambda e, gt=gt: e.dma_start(out=XT[:], in_=x_in[gt:gt + 128, :]), cLD, 16)
            ld_x = cLD.n
            wait("vector", cLD, ld_x)
            wait("vector", cOUT, cOUT.n)
            em("vector", lambda e, g=g: e.tensor_scalar(
                out=OUTSB[:], in0=XT[:],
                scalar1=MC[:, g:g + 1], scalar2=RSTD[:, g:g + 1],
                op0=OP.subtract, op1=OP.mult), cV)
            em("vector", lambda e: e.tensor_tensor(
                out=OUTSB[:], in0=OUTSB[:], in1=XT[:], op=OP.add), cV)
            v_xpn = cV.n
            q = QS[2 + (t % 2)]
            wait("tensor", cV, v_xpn)
            for cb in range(CH):
                em("tensor", lambda e, cb=cb, t=t, q=q, fv=fv: e.transpose(
                    q[:, cb * 128:(cb + 1) * 128], fv[:, cb, t * 128:(t + 1) * 128], IDNF[:]), cT)
            t_f = cT.n
            wait("vector", cT, t_f)
            em("vector", lambda e, q=q: e.tensor_tensor(
                out=OUTSB[:], in0=OUTSB[:], in1=q[:], op=OP.add), cV)
            v_out = cV.n
            wait("sync", cV, v_out)
            em("sync", lambda e, gt=gt: e.dma_start(
                out=out_t[gt:gt + 128, :], in_=OUTSB[:]), cOUT, 16)

    wait("sync", cOUT, cOUT.n)

    with nc.Block() as block:
        @block.sync
        def _(e):
            for f in sched["sync"]:
                f(e)

        @block.gpsimd
        def _(e):
            for f in sched["gpsimd"]:
                f(e)

        @block.vector
        def _(e):
            for f in sched["vector"]:
                f(e)

        @block.scalar
        def _(e):
            for f in sched["scalar"]:
                f(e)

        @block.tensor
        def _(e):
            for f in sched["tensor"]:
                f(e)

    ctx.close()
    return nc


def _host_inputs(inputs):
    x = np.ascontiguousarray(np.asarray(inputs["L0_attn_x"], dtype=np.float32))
    row = lambda k: np.ascontiguousarray(np.asarray(inputs[k], dtype=np.float32).reshape(1, -1))
    base = {
        "wk": np.ascontiguousarray(np.asarray(inputs["L0_attn_W_key"], np.float32)),
        "wv": np.ascontiguousarray(np.asarray(inputs["L0_attn_W_value"], np.float32)),
        "wr": np.ascontiguousarray(np.asarray(inputs["L0_attn_W_recept"], np.float32)),
        "wo": np.ascontiguousarray(np.asarray(inputs["L0_attn_W_out"], np.float32)),
        "wr2": np.ascontiguousarray(np.asarray(inputs["L0_attn_W_recept2"], np.float32)),
        "wk2": np.ascontiguousarray(np.asarray(inputs["L0_attn_W_key2"], np.float32)),
        "wv2": np.ascontiguousarray(np.asarray(inputs["L0_attn_W_value2"], np.float32)),
        "w1": np.ascontiguousarray(np.asarray(inputs["L0_ffn_w1"], np.float32)),
        "w2": np.ascontiguousarray(np.asarray(inputs["L0_ffn_w2"], np.float32)),
        "ln1": row("L0_attn_ln1_w"), "ln2": row("L0_attn_ln2_w"),
        "tmk": row("L0_attn_time_mix_k"), "tmv": row("L0_attn_time_mix_v"),
        "tmr": row("L0_attn_time_mix_r"), "tmk2": row("L0_attn_time_mix_k2"),
        "tmr2": row("L0_attn_time_mix_r2"),
        "b1": row("L0_ffn_b1"), "b2": row("L0_ffn_b2"),
        "tdb": np.tile(row("L0_attn_time_decay"), (128, 1)),
        "ub": np.tile(row("L0_attn_time_first"), (128, 1)),
        "ltc": np.triu(np.ones((128, 128), np.float32), k=1),
        "ones1": np.ones((1, 128), np.float32),
        "idnf": np.eye(128, dtype=np.float32),
        "onesc": np.ones((128, 1)).astype(ml_dtypes.bfloat16),
        "onesr": np.ones((128, 1), np.float32),
        "cbias": np.tile(np.array([[1e-5, 1e-6, -4.0, -3.75, 0, 0, 0, 0]], np.float32), (128, 1)),
    }
    in_maps = []
    for b in range(NCORE):
        m = dict(base)
        m["x"] = np.ascontiguousarray(x[b])
        in_maps.append(m)
    return in_maps


def kernel(**inputs):
    nc = _build(taps=tuple(_TAPS))
    in_maps = _host_inputs(inputs)
    res = run_bass_kernel_spmd(nc, in_maps, core_ids=list(range(NCORE)))
    out = np.stack([np.asarray(r["out"]) for r in res.results], axis=0)
    return out.astype(np.float32)


# version 02
# speedup vs baseline: 1.2983x; 1.2983x over previous
import numpy as np
import ml_dtypes
import concourse.bass as bass
import concourse.mybir as mybir
from concourse.alu_op_type import AluOpType as OP
from concourse.bass_utils import run_bass_kernel_spmd

F32 = mybir.dt.float32
F32R = mybir.dt.float32r
BF16 = mybir.dt.bfloat16
I32 = mybir.dt.int32
AF = mybir.ActivationFunctionType

T, C, I = 4096, 1024, 4096
CH = C // 128
ICH = I // 128
SBA = 512
NSB = T // SBA
NTIL = SBA // 128
NCORE = 8

ENGS = ("sync", "gpsimd", "vector", "scalar", "tensor")

_TAPS = []


def _build(taps=(), detect_races=False):
    nc = bass.Bass("TRN2", target_bir_lowering=False, detect_race_conditions=detect_races)
    dr = lambda n, sh, dt: nc.dram_tensor(n, sh, dt, kind="ExternalInput").ap()

    x_in = dr("x", [T, C], F32)
    wk_in = dr("wk", [C, C], F32)
    wv_in = dr("wv", [C, C], F32)
    wr_in = dr("wr", [C, C], F32)
    wo_in = dr("wo", [C, C], F32)
    wr2_in = dr("wr2", [C, C], F32)
    wk2_in = dr("wk2", [C, I], F32)
    wv2_in = dr("wv2", [I, C], F32)
    w1_in = dr("w1", [C, I], F32)
    w2_in = dr("w2", [I, C], F32)
    vecs = {n: dr(n, [1, C], F32) for n in
            ("ln1", "ln2", "tmk", "tmv", "tmr", "tmk2", "tmr2", "b2")}
    b1_in = dr("b1", [1, I], F32)
    tdb_in = dr("tdb", [128, C], F32)
    ub_in = dr("ub", [128, C], F32)
    ltc_in = dr("ltc", [128, 128], F32R)
    ones_in = dr("ones1", [1, 128], F32R)
    idnf_in = dr("idnf", [128, 128], F32)
    onesc_in = dr("onesc", [128, 1], BF16)
    onesr_in = dr("onesr", [128, 1], F32R)
    cbias_in = dr("cbias", [128, 8], F32)

    out_t = nc.dram_tensor("out", [T, C], F32, kind="ExternalOutput").ap()

    from contextlib import ExitStack
    ctx = ExitStack()

    _off = [16512]
    handles = {}

    def place(name, shape, dtype, at=None):
        nbytes = int(np.prod(shape[1:])) * mybir.dt.size(dtype)
        if at is None:
            at = _off[0]
            _off[0] = (at + nbytes + 31) & ~31
        h = nc.alloc_sbuf_tensor_at(name, shape, dtype, offset=at)
        handles[name] = (h, at, nbytes)
        return h

    LTC = place("LTC", [128, 128], F32R)
    ONES = place("ONES", [1, 128], F32R)
    IDNF = place("IDNF", [128, 128], F32)
    ONESC = place("ONESC", [128, 1], BF16)
    ONESR = place("ONESR", [128, 1], F32R)
    WCL = place("WCL", [128, C], F32)
    UB = place("UB", [128, C], F32)
    EMIN = place("EMIN", [128, C], F32)
    ESH = place("ESH", [128, C], F32)
    NIOT = place("NIOT", [128, 1], F32)
    IOTM1 = place("IOTM1", [128, 1], F32)
    EW2R = place("EW2R", [1, C], F32)
    CARRY = place("CARRY", [1, 2 * C], F32R)
    PC = {}
    for nme in ("ln1", "ln2", "tmk", "tmv", "tmr", "tmk2", "tmr2"):
        PC[nme] = place("PC_" + nme, [128, CH], F32)
    B1S = place("B1S", [128, ICH], F32)
    B2S = place("B2S", [128, CH], F32)
    SUMS = place("SUMS", [128, 32], F32)
    SQS = place("SQS", [128, 32], F32)
    MC = place("MC", [128, 32], F32)
    RSTD = place("RSTD", [128, 32], F32)
    RSTD1 = place("RSTD1", [128, 32], F32)
    TCA = place("TCA", [128, 4], F32)
    TCB = place("TCB", [128, 4], F32)
    RSTD1B = place("RSTD1B", [128, SBA], BF16)
    RSTD2B = place("RSTD2B", [128, SBA], BF16)
    XNFM = place("XNFM", [128, CH * SBA], BF16)
    X1BF = place("X1BF", [128, CH * SBA], BF16)
    FINALW = place("FINALW", [128, CH * SBA], F32)
    HFM = place("HFM", [128, CH * (SBA + 1)], BF16)
    DBUF = place("DBUF", [128, CH * SBA], BF16)
    RWKVFM = place("RWKVFM", [128, CH * SBA], BF16)
    H2FM = place("H2FM", [128, CH * (SBA + 1)], BF16)
    MIXT = place("MIXT", [128, 2 * 256], BF16)
    XR2J = place("XR2J", [128, SBA], BF16)
    GBUF = place("GBUF", [128, SBA], BF16)
    K2STG = place("K2STG", [128, ICH * SBA], BF16)
    CBIA = place("CBIA", [128, 8], F32)
    XT = place("XT", [128, C], F32)
    OUTSB = place("OUTSB", [128, C], F32)

    ovl0 = _off[0]

    def placeA(name, shape, dtype):
        return place(name, shape, dtype)

    KTM = placeA("KTM", [128, 2 * C], F32)
    VTM = placeA("VTM", [128, 2 * C], F32)
    ERT = placeA("ERT", [128, 2 * C], F32)
    KCB = placeA("KCB", [128, C], F32)
    EKT = placeA("EKT", [128, 2 * C], F32)
    EKB = placeA("EKB", [128, C], F32)
    ZB = placeA("ZB", [128, 2 * C], F32R)
    APR = placeA("APR", [128, 2 * C], F32)
    EKTV = placeA("EKTV", [128, C], F32)
    DEN = placeA("DEN", [128, C], F32)
    RWT = placeA("RWT", [128, C], F32)
    ovl_end_A = _off[0]

    _off[0] = ovl0
    WSA = place("WSA", [128, 2 * 4096], BF16)
    WSB = place("WSB", [128, 2 * 4 * C], BF16)
    XK2 = place("XK2", [128, CH * SBA], BF16)
    R2TH = place("R2TH", [128, CH * SBA], BF16)
    ITILE = place("ITILE", [128, SBA], I32)
    ITILE2 = place("ITILE2", [128, SBA], I32)
    Z1B = place("Z1B", [128, SBA], F32)
    IQF2 = place("IQF2", [128, SBA], F32)
    FLB = place("FLB", [128, SBA], F32)
    FLB2 = place("FLB2", [128, SBA], F32)
    GLB = place("GLB", [128, SBA], F32)
    GLB2 = place("GLB2", [128, SBA], F32)
    GHB = place("GHB", [128, SBA], F32)
    GHB2 = place("GHB2", [128, SBA], F32)
    ovl_end_B = _off[0]
    _off[0] = max(ovl_end_A, ovl_end_B)

    CROW = OUTSB
    total_kb = (_off[0] - 16512) / 1024.0
    assert _off[0] <= 229344, f"SBUF over budget: {total_kb:.1f} KB used, end={_off[0]}"

    QS = [ctx.enter_context(nc.psum_tensor("Q%d" % i, [128, 1024], F32)) for i in range(4)]

    class Ctr:
        def __init__(s, name):
            s.h = ctx.enter_context(nc.semaphore(name))
            s.n = 0
    cLD = Ctr("cLD")
    cWD = Ctr("cWD")
    cV = Ctr("cV")
    cS = Ctr("cS")
    cT = Ctr("cT")
    cG = Ctr("cG")
    cOUT = Ctr("cOUT")

    sched = {e: [] for e in ENGS}

    def em(eng, make, inc=None, k=1):
        if inc is not None:
            inc.n += k
        def f(e, make=make, inc=inc, k=k):
            r = make(e)
            if inc is not None:
                if r is None:
                    r = e.nop()
                r.then_inc(inc.h, k)
        sched[eng].append(f)

    def wait(eng, ctr, n=None):
        v = ctr.n if n is None else n
        if v > 0:
            sched[eng].append(lambda e, h=ctr.h, v=v: e.wait_ge(h, v))

    _selfc = {"vector": None, "scalar": None}

    def bar(eng):
        c = {"vector": cV, "scalar": cS}[eng]
        wait(eng, c, c.n)

    tapv = {}

    def _tap(name, apf):
        if name in taps:
            ap0 = apf()
            sh = list(ap0.shape)
            t = nc.dram_tensor("tap_" + name, sh, ap0.dtype, kind="ExternalOutput").ap()
            wait("sync", cV)
            wait("sync", cS)
            wait("sync", cG)
            wait("sync", cT)
            em("sync", lambda e, t=t, a=ap0: e.dma_start(out=t, in_=a), cOUT, 16)

    em("sync", lambda e: e.dma_start(out=LTC[:], in_=ltc_in), cLD, 16)
    em("sync", lambda e: e.dma_start(out=ONES[:], in_=ones_in), cLD, 16)
    em("sync", lambda e: e.dma_start(out=IDNF[:], in_=idnf_in), cLD, 16)
    em("sync", lambda e: e.dma_start(out=ONESC[:], in_=onesc_in), cLD, 16)
    em("sync", lambda e: e.dma_start(out=ONESR[:], in_=onesr_in), cLD, 16)
    em("sync", lambda e: e.dma_start(out=CBIA[:], in_=cbias_in), cLD, 16)
    em("sync", lambda e: e.dma_start(out=WCL[:], in_=tdb_in), cLD, 16)
    em("sync", lambda e: e.dma_start(out=UB[:], in_=ub_in), cLD, 16)
    for nme in ("ln1", "ln2", "tmk", "tmv", "tmr", "tmk2", "tmr2"):
        em("sync", lambda e, d=PC[nme], s=vecs[nme]: e.dma_start(
            out=d.ap()[:, :].rearrange("p (x c) -> p x c", x=1), in_=s.rearrange("x (c p) -> p x c", p=128),
            allow_slow_non_contiguous=True), cLD, 16)
    em("sync", lambda e: e.dma_start(out=B1S.ap()[:, :].rearrange("p (x c) -> p x c", x=1),
                                     in_=b1_in.rearrange("x (c p) -> p x c", p=128), allow_slow_non_contiguous=True), cLD, 16)
    em("sync", lambda e: e.dma_start(out=B2S.ap()[:, :].rearrange("p (x c) -> p x c", x=1),
                                     in_=vecs["b2"].rearrange("x (c p) -> p x c", p=128), allow_slow_non_contiguous=True), cLD, 16)
    init_ld = cLD.n

    em("gpsimd", lambda e: e.iota(NIOT[:], [[1, 1]], base=0, channel_multiplier=-1,
                                  allow_small_or_imprecise_dtypes=True), cG)
    em("gpsimd", lambda e: e.iota(IOTM1[:], [[1, 1]], base=-1, channel_multiplier=1,
                                  allow_small_or_imprecise_dtypes=True), cG)
    em("gpsimd", lambda e: e.memzero(CARRY[:]), cG)
    g_init = cG.n

    wait("scalar", cLD, init_ld)
    em("scalar", lambda e: e.activation(out=WCL[:], in_=WCL[:], func=AF.Exp), cS)
    wait("vector", cS)
    em("vector", lambda e: e.tensor_scalar(out=WCL[:], in0=WCL[:], scalar1=-1.0, scalar2=20.0,
                                           op0=OP.mult, op1=OP.min), cV)
    em("vector", lambda e: e.tensor_scalar(out=WCL[:], in0=WCL[:], scalar1=-20.0, scalar2=None,
                                           op0=OP.max), cV)
    wait("vector", cG, g_init)
    em("vector", lambda e: e.tensor_scalar(out=EMIN[:], in0=WCL[:], scalar1=NIOT[:, 0:1], scalar2=None,
                                           op0=OP.mult), cV)
    em("vector", lambda e: e.tensor_scalar(out=ESH[:], in0=WCL[:], scalar1=IOTM1[:, 0:1], scalar2=None,
                                           op0=OP.mult), cV)
    em("vector", lambda e: e.tensor_scalar(out=EW2R[:], in0=WCL[0:1, :], scalar1=128.0, scalar2=None,
                                           op0=OP.mult), cV)
    v_pre = cV.n
    wait("scalar", cV, v_pre)
    em("scalar", lambda e: e.activation(out=EMIN[:], in_=EMIN[:], func=AF.Exp), cS)
    em("scalar", lambda e: e.activation(out=ESH[:], in_=ESH[:], func=AF.Exp), cS)
    em("scalar", lambda e: e.activation(out=EW2R[:], in_=EW2R[:], func=AF.Exp), cS)
    s_pre = cS.n
    wait("vector", cS, s_pre)
    em("vector", lambda e: e.tensor_scalar(out=B1S[:], in0=B1S[:], scalar1=256.0, scalar2=None, op0=OP.mult), cV)
    em("vector", lambda e: e.tensor_scalar(out=B2S[:], in0=B2S[:], scalar1=256.0, scalar2=None, op0=OP.mult), cV)

    wkv_view = {"k": wk_in, "v": wv_in, "r": wr_in}
    WOUTV = K2STG

    def dma_w8(dst_ap, win):
        def f(e, d=dst_ap, w=win):
            return e.dma_start(out=d.rearrange("p (a c) -> p a c", a=CH),
                               in_=w.rearrange("(cc p) c -> p cc c", p=128))
        return f

    wait("tensor", cLD, init_ld)
    wait("gpsimd", cLD, init_ld)
    for a in range(NSB):
        tok0 = a * SBA
        g0 = a * NTIL

        ld_tile = []
        for t in range(NTIL):
            gt = tok0 + t * 128
            wait("sync", cV)
            wait("sync", cS)
            em("sync", lambda e, gt=gt: e.dma_start(out=XT[:], in_=x_in[gt:gt + 128, :]), cLD, 16)
            ldn = cLD.n
            g = g0 + t
            wait("vector", cLD, ldn)
            em("vector", lambda e, g=g: e.tensor_reduce(
                out=SUMS[:, g:g + 1], in_=XT[:], axis=mybir.AxisListType.X, op=OP.add), cV)
            wait("scalar", cLD, ldn)
            wait("scalar", cOUT, cOUT.n)
            wait("scalar", cT, cT.n)
            wait("scalar", cV, cV.n)
            em("scalar", lambda e, g=g: e.activation(
                out=OUTSB[:], in_=XT[:], func=AF.Square), cS)
            s_sq = cS.n
            wait("vector", cS, s_sq)
            em("vector", lambda e, g=g: e.tensor_reduce(
                out=SQS[:, g:g + 1], in_=OUTSB[:], axis=mybir.AxisListType.X, op=OP.add), cV)
            wait("vector", cS)
            em("vector", lambda e, g=g: e.tensor_scalar(
                out=MC[:, g:g + 1], in0=SUMS[:, g:g + 1], scalar1=1.0 / C, scalar2=None, op0=OP.mult), cV)
            bar("vector")
            em("vector", lambda e, g=g, t=t: e.tensor_tensor(
                out=TCA[:, t:t + 1], in0=MC[:, g:g + 1], in1=MC[:, g:g + 1], op=OP.mult), cV)
            bar("vector")
            em("vector", lambda e, g=g, t=t: e.scalar_tensor_tensor(
                out=TCB[:, t:t + 1], in0=SQS[:, g:g + 1], scalar=1.0 / C, in1=TCA[:, t:t + 1],
                op0=OP.mult, op1=OP.subtract), cV)
            v_var = cV.n
            wait("scalar", cV, v_var)
            em("scalar", lambda e, t=t: e.activation(out=TCA[:, t:t + 1], in_=TCB[:, t:t + 1], func=AF.Ln, bias=CBIA[:, 0:1]), cS)
            bar("scalar")
            em("scalar", lambda e, g=g, t=t: e.activation(
                out=RSTD[:, g:g + 1], in_=TCA[:, t:t + 1], func=AF.Exp, scale=-0.5), cS)
            s_rstd = cS.n
            wait("vector", cS, s_rstd)
            em("vector", lambda e, g=g, t=t: e.tensor_tensor(
                out=TCA[:, t:t + 1], in0=RSTD[:, g:g + 1], in1=RSTD[:, g:g + 1], op=OP.mult), cV)
            bar("vector")
            em("vector", lambda e, t=t: e.tensor_tensor(
                out=TCA[:, t:t + 1], in0=TCB[:, t:t + 1], in1=TCA[:, t:t + 1], op=OP.mult), cV)
            v_ms1 = cV.n
            wait("scalar", cV, v_ms1)
            em("scalar", lambda e, t=t: e.activation(out=TCB[:, t:t + 1], in_=TCA[:, t:t + 1], func=AF.Ln, bias=CBIA[:, 1:2]), cS)
            bar("scalar")
            em("scalar", lambda e, g=g, t=t: e.activation(
                out=RSTD1[:, g:g + 1], in_=TCB[:, t:t + 1], func=AF.Exp, scale=-0.5), cS)
            wait("vector", cOUT, cOUT.n)
            wait("vector", cT, cT.n)
            em("vector", lambda e, g=g: e.tensor_scalar(
                out=OUTSB[:], in0=XT[:], scalar1=MC[:, g:g + 1], scalar2=RSTD[:, g:g + 1],
                op0=OP.subtract, op1=OP.mult), cV)
            v_xnb = cV.n
            wait("tensor", cV, v_xnb)
            wait("tensor", cS)
            for cb in range(CH):
                em("tensor", lambda e, cb=cb: e.transpose(
                    QS[2][:, cb * 128:(cb + 1) * 128], OUTSB[:, cb * 128:(cb + 1) * 128], IDNF[:]), cT)
            t_x = cT.n
            wait("vector", cT, t_x)
            em("vector", lambda e, t=t: e.tensor_copy(
                out=XNFM.ap()[:, :].rearrange("p (c w) -> p c w", c=CH)[:, :, t * 128:(t + 1) * 128],
                in_=QS[2].ap()[:, :].rearrange("p (c w) -> p c w", c=CH)), cV)

        if a == 0:
            _tap("sums_a0", lambda: SUMS[:])
            _tap("sqs_a0", lambda: SQS[:])
            _tap("mc_a0", lambda: MC[:])
            _tap("rstd_a0", lambda: RSTD[:])
            _tap("idnf_a0", lambda: IDNF[:])
            _tap("outsb_a0", lambda: OUTSB[:])
        s_r1 = cS.n
        wait("tensor", cS, s_r1)
        wait("tensor", cV)
        for t in range(NTIL):
            em("tensor", lambda e, t=t, g0=g0: e.transpose(
                QS[0][0:1, t * 128:(t + 1) * 128], RSTD1[:, g0 + t:g0 + t + 1], IDNF[:]), cT)
        t_tr = cT.n
        wait("vector", cT, t_tr)
        em("vector", lambda e: e.tensor_copy(out=ZB[0:1, 0:SBA], in_=QS[0][0:1, 0:SBA]), cV)
        v_srp = cV.n
        wait("tensor", cV, v_srp)
        for t in range(NTIL):
            em("tensor", lambda e, t=t: e.matmul(QS[1][:, t * 128:(t + 1) * 128], ONES[:],
                                                 ZB[0:1, t * 128:(t + 1) * 128], start=True, stop=True), cT)
        t_bc = cT.n
        wait("vector", cT, t_bc)
        em("vector", lambda e: e.tensor_copy(out=RSTD1B[:], in_=QS[1][:, 0:SBA]), cV)
        v_r1b = cV.n

        _tap("xnfm_a%d" % a, lambda: XNFM[:])

        hv = HFM.ap()[:, :].rearrange("p (c w) -> p c w", w=SBA + 1)
        if a == 0:
            em("vector", lambda e, hv=hv: e.memset(hv[:, :, 0:1], 0.0), cV)
        else:
            em("vector", lambda e, hv=hv: e.tensor_copy(out=hv[:, :, 0:1], in_=hv[:, :, SBA:SBA + 1]), cV)
        wait("vector", cV, v_r1b)
        for c in range(CH):
            hb = lambda c=c: HFM[:, c * (SBA + 1) + 1:(c + 1) * (SBA + 1)]
            em("vector", lambda e, c=c, hb=hb: e.tensor_tensor(
                out=hb(), in0=XNFM[:, c * SBA:(c + 1) * SBA], in1=RSTD1B[:], op=OP.mult), cV)
            em("vector", lambda e, c=c, hb=hb: e.tensor_scalar(
                out=hb(), in0=hb(), scalar1=PC["ln1"][:, c:c + 1], scalar2=None, op0=OP.mult), cV)
            em("vector", lambda e, c=c, hb=hb: e.tensor_tensor(
                out=DBUF[:, c * SBA:(c + 1) * SBA], in0=hb(),
                in1=HFM[:, c * (SBA + 1):(c + 1) * (SBA + 1) - 1], op=OP.subtract), cV)
        _tap("h_a%d" % a, lambda: HFM[:])

        projs = [("k", PC["tmk"], KTM), ("v", PC["tmv"], VTM), ("r", PC["tmr"], ERT)]
        wait("vector", cOUT, cOUT.n)
        for pr in range(2):
            evac_s = {}
            evac_v = {}
            for pi, (pn, tmcol, dstbuf) in enumerate(projs):
                wait("gpsimd", cT, cT.n)
                em("gpsimd", lambda e, pn=pn: e.dma_start(
                    out=K2STG[:, 0:CH * C].rearrange("p (a c) -> p a c", a=CH),
                    in_=wkv_view[pn].rearrange("(cc p) c -> p cc c", p=128)), cWD, 16)
                wd = cWD.n
                qk = [QS[0], QS[1]] if pn in ("k", "r") else [QS[2], QS[3]]
                if pn == "r":
                    wait("tensor", cS)
                for ci in range(CH):
                    ms = (pi * CH + ci) % 2
                    wait("vector", cT, cT.n)
                    em("vector", lambda e, ms=ms, ci=ci, pr=pr, tmcol=tmcol: e.scalar_tensor_tensor(
                        out=MIXT[:, ms * 256:(ms + 1) * 256],
                        in0=DBUF[:, ci * SBA + pr * 256:ci * SBA + pr * 256 + 256],
                        scalar=tmcol[:, ci:ci + 1],
                        in1=HFM[:, ci * (SBA + 1) + pr * 256:ci * (SBA + 1) + pr * 256 + 256],
                        op0=OP.mult, op1=OP.add), cV)
                    v_mix = cV.n
                    wait("tensor", cV, v_mix)
                    if ci == 0:
                        wait("tensor", cWD, wd)
                    for bl in range(2):
                        for nh in range(2):
                            em("tensor", lambda e, ms=ms, ci=ci, bl=bl, nh=nh, qk=qk: e.matmul(
                                qk[bl][:, nh * 512:(nh + 1) * 512],
                                MIXT[:, ms * 256 + bl * 128:ms * 256 + bl * 128 + 128],
                                K2STG[:, ci * C + nh * 512:ci * C + (nh + 1) * 512],
                                start=(ci == 0), stop=(ci == CH - 1)), cT)
                t_mm = cT.n
                for bl in range(2):
                    if pn == "k":
                        wait("scalar", cT, t_mm)
                        em("scalar", lambda e, bl=bl, qk=qk: e.activation(
                            out=KTM[:, bl * C:(bl + 1) * C], in_=qk[bl][:], func=AF.Copy), cS)
                    elif pn == "v":
                        wait("vector", cT, t_mm)
                        em("vector", lambda e, bl=bl, qk=qk: e.tensor_copy(
                            out=VTM[:, bl * C:(bl + 1) * C], in_=qk[bl][:]), cV)
                    else:
                        wait("scalar", cT, t_mm)
                        em("scalar", lambda e, bl=bl, qk=qk: e.activation(
                            out=ERT[:, bl * C:(bl + 1) * C], in_=qk[bl][:], func=AF.Exp, scale=-1.0), cS)
                evac_s[pn] = cS.n
                evac_v[pn] = cV.n

            for bl in range(2):
                blk = 2 * pr + bl
                gq = [QS[2], QS[3]] if bl == 0 else [QS[0], QS[1]]
                wait("vector", cS)
                wait("vector", cV, evac_v["v"])
                em("vector", lambda e, bl=bl: e.tensor_tensor(
                    out=KCB[:], in0=KTM[:, bl * C:(bl + 1) * C], in1=UB[:], op=OP.add), cV)
                em("vector", lambda e: e.tensor_scalar(
                    out=KCB[:], in0=KCB[:], scalar1=20.0, scalar2=-20.0, op0=OP.min, op1=OP.max), cV)
                v_kc = cV.n
                wait("scalar", cV, v_kc)
                em("scalar", lambda e, bl=bl: e.activation(
                    out=EKT[:, bl * C:(bl + 1) * C], in_=KCB[:], func=AF.Exp), cS)
                s_ekt = cS.n
                wait("vector", cS, s_ekt)
                em("vector", lambda e, bl=bl: e.tensor_scalar(
                    out=KCB[:], in0=KTM[:, bl * C:(bl + 1) * C], scalar1=20.0, scalar2=-20.0,
                    op0=OP.min, op1=OP.max), cV)
                v_kc2 = cV.n
                wait("scalar", cV, v_kc2)
                em("scalar", lambda e: e.activation(out=EKB[:], in_=KCB[:], func=AF.Exp), cS)
                s_ek = cS.n
                wait("vector", cS, s_ek)
                em("vector", lambda e: e.tensor_tensor(
                    out=ZB[:, C:2 * C], in0=EKB[:], in1=EMIN[:], op=OP.mult), cV)
                em("vector", lambda e, bl=bl: e.tensor_tensor(
                    out=ZB[:, 0:C], in0=ZB[:, C:2 * C], in1=VTM[:, bl * C:(bl + 1) * C], op=OP.mult), cV)
                v_z = cV.n
                wait("tensor", cV, v_z)
                wait("tensor", cS)
                for seg in range(4):
                    q = gq[seg // 2]
                    qoff = (seg % 2) * 512
                    em("tensor", lambda e, q=q, qoff=qoff, seg=seg: e.matmul(
                        q[:, qoff:qoff + 512], LTC[:],
                        ZB[:, seg * 512:(seg + 1) * 512], start=True, stop=False), cT)
                    em("tensor", lambda e, q=q, qoff=qoff, seg=seg: e.matmul(
                        q[:, qoff:qoff + 512], ONES[:],
                        CARRY[:, seg * 512:(seg + 1) * 512], start=False, stop=True), cT)
                t_g = cT.n
                wait("vector", cT, t_g)
                em("vector", lambda e, gq=gq: e.tensor_tensor(
                    out=APR[:, 0:C], in0=gq[0][:], in1=ESH[:], op=OP.mult), cV)
                em("vector", lambda e, gq=gq: e.tensor_tensor(
                    out=APR[:, C:2 * C], in0=gq[1][:], in1=ESH[:], op=OP.mult), cV)
                sq = [QS[0], QS[1]] if bl == 0 else [QS[2], QS[3]]
                for seg in range(4):
                    em("tensor", lambda e, sq=sq, seg=seg: e.matmul(
                        sq[seg // 2][0:1, (seg % 2) * 512:(seg % 2) * 512 + 512], ONESR[:],
                        ZB[:, seg * 512:(seg + 1) * 512], start=True, stop=True), cT)
                t_s = cT.n
                wait("vector", cT, t_s)
                em("vector", lambda e, sq=sq: e.tensor_tensor(
                    out=CROW[0:1, 0:C], in0=CARRY[:, 0:C], in1=sq[0][0:1, 0:C], op=OP.add), cV)
                em("vector", lambda e, sq=sq: e.tensor_tensor(
                    out=XT[0:1, 0:C], in0=CARRY[:, C:2 * C], in1=sq[1][0:1, 0:C], op=OP.add), cV)
                em("vector", lambda e: e.tensor_tensor(
                    out=CARRY[:, 0:C], in0=CROW[0:1, 0:C], in1=EW2R[:], op=OP.mult), cV)
                em("vector", lambda e: e.tensor_tensor(
                    out=CARRY[:, C:2 * C], in0=XT[0:1, 0:C], in1=EW2R[:], op=OP.mult), cV)
                em("vector", lambda e, bl=bl: e.tensor_tensor(
                    out=EKTV[:], in0=EKT[:, bl * C:(bl + 1) * C],
                    in1=VTM[:, bl * C:(bl + 1) * C], op=OP.mult), cV)
                em("vector", lambda e: e.tensor_tensor(
                    out=EKTV[:], in0=EKTV[:], in1=APR[:, 0:C], op=OP.add), cV)
                em("vector", lambda e, bl=bl: e.scalar_tensor_tensor(
                    out=DEN[:], in0=EKT[:, bl * C:(bl + 1) * C], scalar=1e-8,
                    in1=APR[:, C:2 * C], op0=OP.add, op1=OP.add), cV)
                wait("vector", cS, evac_s.get("r", 0))
                em("vector", lambda e, bl=bl: e.scalar_tensor_tensor(
                    out=DEN[:], in0=ERT[:, bl * C:(bl + 1) * C], scalar=1.0,
                    in1=DEN[:], op0=OP.add, op1=OP.mult), cV)
                v_den = cV.n
                wait("scalar", cV, v_den)
                em("scalar", lambda e: e.activation(out=KCB[:], in_=DEN[:], func=AF.Ln), cS)
                em("scalar", lambda e: e.activation(out=EKB[:], in_=KCB[:], func=AF.Exp, scale=-1.0), cS)
                s_rd = cS.n
                wait("vector", cS, s_rd)
                em("vector", lambda e: e.tensor_tensor(out=RWT[:], in0=EKTV[:], in1=EKB[:], op=OP.mult), cV)
                v_rwt = cV.n
                wait("tensor", cV, v_rwt)
                tq = gq[0]
                for cb in range(CH):
                    em("tensor", lambda e, cb=cb, tq=tq: e.transpose(
                        tq[:, cb * 128:(cb + 1) * 128], RWT[:, cb * 128:(cb + 1) * 128], IDNF[:]), cT)
                t_rt = cT.n
                wait("vector", cT, t_rt)
                em("vector", lambda e, blk=blk, tq=tq: e.tensor_copy(
                    out=RWKVFM.ap()[:, :].rearrange("p (c w) -> p c w", c=CH)[:, :, blk * 128:(blk + 1) * 128],
                    in_=tq.ap()[:, :].rearrange("p (c w) -> p c w", c=CH)), cV)

        v_rwkv = cV.n
        _tap("rwkv_a%d" % a, lambda: RWKVFM[:])

        wait("gpsimd", cT, cT.n)
        em("gpsimd", dma_w8(WOUTV[:, 0:CH * C], wo_in), cWD, 16)
        wd_wo = cWD.n
        wait("tensor", cV, v_rwkv)
        wait("tensor", cS)
        wait("tensor", cWD, wd_wo)
        for co in range(CH):
            q = QS[co % 2]
            for ci in range(CH):
                em("tensor", lambda e, co=co, ci=ci, q=q: e.matmul(
                    q[:, 0:512], WOUTV[:, ci * C + co * 128:ci * C + co * 128 + 128],
                    RWKVFM[:, ci * SBA:(ci + 1) * SBA],
                    start=(ci == 0), stop=(ci == CH - 1)), cT)
            t_wo = cT.n
            wait("scalar", cT, t_wo)
            em("scalar", lambda e, co=co, q=q: e.activation(
                out=FINALW[:, co * SBA:(co + 1) * SBA], in_=q[:, 0:512], func=AF.Copy), cS)
            s_fw = cS.n
            wait("vector", cT, t_wo)
            wait("vector", cS, s_fw)
            em("vector", lambda e, co=co, q=q: e.tensor_tensor(
                out=X1BF[:, co * SBA:(co + 1) * SBA], in0=q[:, 0:512],
                in1=XNFM[:, co * SBA:(co + 1) * SBA], op=OP.add), cV)
            if co >= 1:
                wait("tensor", cS)
                wait("tensor", cV)
        v_x1 = cV.n
        _tap("x1_a%d" % a, lambda: X1BF[:])
        _tap("att_a%d" % a, lambda: FINALW[:])

        wait("vector", cV, v_x1)
        wait("tensor", cS)
        wait("tensor", cV)
        t_mm2 = {}
        for c in range(CH):
            gb = GBUF if c % 2 == 0 else XR2J
            if c >= 2:
                wait("vector", cT, t_mm2[c - 2])
            em("vector", lambda e, c=c, gb=gb: e.tensor_tensor(
                out=gb[:, 0:SBA], in0=X1BF[:, c * SBA:(c + 1) * SBA],
                in1=X1BF[:, c * SBA:(c + 1) * SBA], op=OP.mult), cV)
            v_sq = cV.n
            wait("tensor", cV, v_sq)
            em("tensor", lambda e, c=c, gb=gb: e.matmul(
                QS[2][0:1, 0:512], ONESC[:], gb[:, 0:SBA], start=(c == 0), stop=(c == CH - 1)), cT)
            t_mm2[c] = cT.n
        t_ms2 = cT.n
        wait("scalar", cT, t_ms2)
        em("scalar", lambda e: e.activation(out=OUTSB[0:1, 0:512], in_=QS[2][0:1, 0:512],
                                            func=AF.Ln, scale=1.0 / C, bias=CBIA[0:1, 1:2]), cS)
        bar("scalar")
        em("scalar", lambda e: e.activation(out=ZB[0:1, 0:512], in_=OUTSB[0:1, 0:512],
                                            func=AF.Exp, scale=-0.5), cS)
        s_rs2 = cS.n
        wait("tensor", cS, s_rs2)
        for t in range(NTIL):
            em("tensor", lambda e, t=t: e.matmul(QS[3][:, t * 128:(t + 1) * 128], ONES[:],
                                                 ZB[0:1, t * 128:(t + 1) * 128], start=True, stop=True), cT)
        t_r2b = cT.n
        wait("vector", cT, t_r2b)
        em("vector", lambda e: e.tensor_copy(out=RSTD2B[:], in_=QS[3][:, 0:SBA]), cV)
        v_r2b = cV.n
        _tap("rstd2_a%d" % a, lambda: RSTD2B[:])

        h2v = H2FM.ap()[:, :].rearrange("p (c w) -> p c w", w=SBA + 1)
        if a == 0:
            em("vector", lambda e, h2v=h2v: e.memset(h2v[:, :, 0:1], 0.0), cV)
        else:
            em("vector", lambda e, h2v=h2v: e.tensor_copy(out=h2v[:, :, 0:1], in_=h2v[:, :, SBA:SBA + 1]), cV)
        wait("vector", cV, v_r2b)
        for c in range(CH):
            h2b = lambda c=c: H2FM[:, c * (SBA + 1) + 1:(c + 1) * (SBA + 1)]
            em("vector", lambda e, c=c, h2b=h2b: e.tensor_tensor(
                out=h2b(), in0=X1BF[:, c * SBA:(c + 1) * SBA], in1=RSTD2B[:], op=OP.mult), cV)
            em("vector", lambda e, c=c, h2b=h2b: e.tensor_scalar(
                out=h2b(), in0=h2b(), scalar1=PC["ln2"][:, c:c + 1], scalar2=None, op0=OP.mult), cV)
            em("vector", lambda e, c=c, h2b=h2b: e.tensor_tensor(
                out=DBUF[:, c * SBA:(c + 1) * SBA], in0=h2b(),
                in1=H2FM[:, c * (SBA + 1):(c + 1) * (SBA + 1) - 1], op=OP.subtract), cV)
            em("vector", lambda e, c=c: e.scalar_tensor_tensor(
                out=XK2[:, c * SBA:(c + 1) * SBA], in0=DBUF[:, c * SBA:(c + 1) * SBA],
                scalar=PC["tmk2"][:, c:c + 1],
                in1=H2FM[:, c * (SBA + 1):(c + 1) * (SBA + 1) - 1],
                op0=OP.mult, op1=OP.add), cV)
        v_xk2 = cV.n
        _tap("xk2_a%d" % a, lambda: XK2[:])

        wait("gpsimd", cT, cT.n)
        em("gpsimd", dma_w8(WOUTV[:, 0:CH * C], wr2_in), cWD, 16)
        wd_r2 = cWD.n
        wait("tensor", cWD, wd_r2)
        wait("tensor", cV, v_r2b)
        for half in range(2):
            for ci in range(CH):
                wait("vector", cT, cT.n)
                em("vector", lambda e, ci=ci: e.scalar_tensor_tensor(
                    out=XR2J[:, 0:SBA], in0=DBUF[:, ci * SBA:(ci + 1) * SBA],
                    scalar=PC["tmr2"][:, ci:ci + 1],
                    in1=H2FM[:, ci * (SBA + 1):(ci + 1) * (SBA + 1) - 1],
                    op0=OP.mult, op1=OP.add), cV)
                v_xr = cV.n
                wait("tensor", cV, v_xr)
                for cg in range(4):
                    co = half * 4 + cg
                    em("tensor", lambda e, co=co, ci=ci, cg=cg: e.matmul(
                        QS[cg][:, 0:512], WOUTV[:, ci * C + co * 128:ci * C + co * 128 + 128],
                        XR2J[:, 0:SBA],
                        start=(ci == 0), stop=(ci == CH - 1)), cT)
            t_r2 = cT.n
            wait("scalar", cT, t_r2)
            for cg in range(4):
                co = half * 4 + cg
                em("scalar", lambda e, co=co, cg=cg: e.activation(
                    out=R2TH[:, co * SBA:(co + 1) * SBA], in_=QS[cg][:, 0:512],
                    func=AF.Tanh, scale=0.5), cS)
            wait("tensor", cS)
        s_r2 = cS.n
        _tap("r2th_a%d" % a, lambda: R2TH[:])

        k2base = 0
        wk2v = wk2_in.rearrange("(cc p) q -> p cc q", p=128)
        w1v = w1_in.rearrange("(cc p) q -> p cc q", p=128)
        for i in range(ICH):
            if i % 4 == 0:
                wait("gpsimd", cT, cT.n)
                ip = (i // 4) % 2
                em("gpsimd", lambda e, i=i, ip=ip: e.dma_start(
                    out=WSA[:, ip * 4096:(ip + 1) * 4096].rearrange("p (b q) -> p b q", b=CH),
                    in_=wk2v[:, :, i * 128:(i + 4) * 128]), cWD, 16)
                wd_k2 = cWD.n
                wait("tensor", cWD, wd_k2)
            q = QS[i % 2]
            ip = (i // 4) % 2
            io = i % 4
            if i == 0:
                wait("tensor", cV, v_xk2)
                wait("tensor", cS, s_r2)
            else:
                wait("tensor", cS)
            for ci in range(CH):
                em("tensor", lambda e, i=i, ci=ci, q=q, ip=ip, io=io: e.matmul(
                    q[:, 0:512],
                    WSA[:, ip * 4096 + ci * 512 + io * 128:ip * 4096 + ci * 512 + io * 128 + 128],
                    XK2[:, ci * SBA:(ci + 1) * SBA],
                    start=(ci == 0), stop=(ci == CH - 1)), cT)
            t_k2 = cT.n
            wait("scalar", cT, t_k2)
            if i > 0:
                wait("scalar", cV, v_sq_prev)
            em("scalar", lambda e, q=q: e.activation(out=GBUF[:], in_=q[:, 0:512], func=AF.Relu), cS)
            s_rl = cS.n
            wait("vector", cS, s_rl)
            em("vector", lambda e, i=i: e.tensor_tensor(
                out=K2STG[:, k2base + i * SBA:k2base + (i + 1) * SBA],
                in0=GBUF[:], in1=GBUF[:], op=OP.mult), cV)
            v_sq_prev = cV.n
        v_k2 = cV.n
        _tap("k2_a%d" % a, lambda: K2STG[:])

        for i in range(ICH):
            if i % 4 == 0:
                wait("gpsimd", cT, cT.n)
                ip = (i // 4) % 2
                em("gpsimd", lambda e, i=i, ip=ip: e.dma_start(
                    out=WSB[:, ip * 4 * C:(ip + 1) * 4 * C].rearrange("p (a c) -> p a c", a=4),
                    in_=wv2_in[i * 128:(i + 4) * 128, :].rearrange("(ii p) c -> p ii c", p=128)), cWD, 16)
                wd_v2 = cWD.n
                wait("tensor", cWD, wd_v2)
            ip = (i // 4) % 2
            io = i % 4
            if i == 0:
                wait("tensor", cV, v_k2)
                wait("tensor", cS, cS.n)
            for co in range(CH):
                q = QS[co // 2]
                qo = (co % 2) * 512
                em("tensor", lambda e, i=i, co=co, q=q, qo=qo, ip=ip, io=io: e.matmul(
                    q[:, qo:qo + 512],
                    WSB[:, ip * 4 * C + io * C + co * 128:ip * 4 * C + io * C + co * 128 + 128],
                    K2STG[:, k2base + i * SBA:k2base + (i + 1) * SBA],
                    start=(i == 0), stop=(i == ICH - 1)), cT)
        t_cm = cT.n
        for co in range(CH):
            q = QS[co // 2]
            qo = (co % 2) * 512
            wait("vector", cT, t_cm)
            em("vector", lambda e, co=co, q=q, qo=qo: e.scalar_tensor_tensor(
                out=GLB[:], in0=R2TH[:, co * SBA:(co + 1) * SBA], scalar=1.0,
                in1=q[:, qo:qo + 512], op0=OP.add, op1=OP.mult), cV)
            em("vector", lambda e, co=co: e.scalar_tensor_tensor(
                out=FINALW[:, co * SBA:(co + 1) * SBA], in0=GLB[:], scalar=0.5,
                in1=FINALW[:, co * SBA:(co + 1) * SBA], op0=OP.mult, op1=OP.add), cV)
        v_cm = cV.n
        _tap("cm_a%d" % a, lambda: FINALW[:])

        ITp = (ITILE, ITILE2)
        IQFp = (Z1B, IQF2)
        FLBp = (FLB, FLB2)
        GLBp = (GLB, GLB2)
        GHBp = (GHB, GHB2)
        bmap = [(0, 512), (1, 0), (1, 512), (2, 0), (2, 512), (3, 0), (3, 512)]
        v_iq = {}
        v_fi = {}
        v_i2 = {}
        s_iqf = {}
        s_flb = {}
        s_g = {}
        s_hq = {}
        g_dd = {}
        t_w1 = {}
        for i in range(ICH + 2):
            pi = i % 2
            if i < ICH:
                if i % 4 == 0:
                    wait("gpsimd", cT, cT.n)
                    ip = (i // 4) % 2
                    em("gpsimd", lambda e, i=i, ip=ip: e.dma_start(
                        out=WSA[:, ip * 4096:(ip + 1) * 4096].rearrange("p (b q) -> p b q", b=CH),
                        in_=w1v[:, :, i * 128:(i + 4) * 128]), cWD, 16)
                    em("gpsimd", lambda e, i=i, ip=ip: e.dma_start(
                        out=WSB[:, ip * 4 * C:(ip + 1) * 4 * C].rearrange("p (a c) -> p a c", a=4),
                        in_=w2_in[i * 128:(i + 4) * 128, :].rearrange("(ii p) c -> p ii c", p=128)), cWD, 16)
                    wd_w = cWD.n
                    wait("tensor", cWD, wd_w)
                if i == 30:
                    wait("gpsimd", cT, cT.n)
                    em("gpsimd", lambda e: e.dma_start(
                        out=WSB[:, 0:4096].rearrange("p (j c) -> p j c", j=ICH),
                        in_=w2_in[:, 896:1024].rearrange("(j p) c -> p j c", p=128)), cWD, 16)
                    wd_co7 = cWD.n
                ip = (i // 4) % 2
                io = i % 4
                if i == 0:
                    wait("tensor", cV, v_cm)
                else:
                    wait("tensor", cV, v_iq[i - 1])
                for ci in range(CH):
                    em("tensor", lambda e, i=i, ci=ci, ip=ip, io=io: e.matmul(
                        QS[0][:, 0:512],
                        WSA[:, ip * 4096 + ci * 512 + io * 128:ip * 4096 + ci * 512 + io * 128 + 128],
                        XNFM[:, ci * SBA:(ci + 1) * SBA],
                        start=(ci == 0), stop=(ci == CH - 1)), cT)
                t_w1[i] = cT.n
            j = i - 2
            if 0 <= j < ICH:
                jp = (j // 4) % 2
                jo = j % 4
                wait("tensor", cS, s_hq[j])
                for co in range(7):
                    sl, qo = bmap[co]
                    em("tensor", lambda e, j=j, co=co, sl=sl, qo=qo, jp=jp, jo=jo: e.matmul(
                        QS[sl][:, qo:qo + 512],
                        WSB[:, jp * 4 * C + jo * C + co * 128:jp * 4 * C + jo * C + co * 128 + 128],
                        K2STG[:, k2base + j * SBA:k2base + (j + 1) * SBA],
                        start=(j == 0), stop=(j == ICH - 1)), cT)
            h = i - 1
            if i < ICH:
                wait("vector", cT, t_w1[i])
                if i >= 2:
                    wait("vector", cS, s_hq[i - 2])
                em("vector", lambda e, i=i, pi=pi: e.tensor_scalar(
                    out=ITp[pi][:], in0=QS[0][:, 0:512], scalar1=256.0, scalar2=B1S[:, i:i + 1],
                    op0=OP.mult, op1=OP.add), cV)
                v_iq[i] = cV.n
                if i == 0 and ("iq0_a%d" % a) in taps:
                    tq0 = nc.dram_tensor("tap_iq0_a%d" % a, [128, SBA], I32, kind="ExternalOutput").ap()
                    wait("sync", cV)
                    em("sync", lambda e, tq0=tq0: e.dma_start(out=tq0, in_=ITILE[:]), cOUT, 16)
                    wait("vector", cOUT, cOUT.n)
                wait("scalar", cV, v_iq[i])
                em("scalar", lambda e, pi=pi: e.activation(
                    out=IQFp[pi][:], in_=ITp[pi][:], func=AF.Copy, scale=1.0), cS)
                s_iqf[i] = cS.n
            if 0 <= h < ICH:
                ph = h % 2
                wait("vector", cS, s_flb[h])
                em("vector", lambda e, ph=ph: e.scalar_tensor_tensor(
                    out=IQFp[ph][:], in0=IQFp[ph][:], scalar=0.015625, in1=FLBp[ph][:],
                    op0=OP.mult, op1=OP.subtract), cV)
                wait("vector", cG, g_dd[h])
                em("vector", lambda e, ph=ph: e.scalar_tensor_tensor(
                    out=GHBp[ph][:], in0=IQFp[ph][:], scalar=16.0, in1=GHBp[ph][:],
                    op0=OP.add, op1=OP.mult), cV)
                em("vector", lambda e, ph=ph: e.tensor_tensor(
                    out=GLBp[ph][:], in0=GLBp[ph][:], in1=GHBp[ph][:], op=OP.add), cV)
                em("vector", lambda e, ph=ph: e.tensor_scalar(
                    out=ITp[ph][:], in0=GLBp[ph][:], scalar1=256.0, scalar2=None, op0=OP.mult), cV)
                v_i2[h] = cV.n
                wait("scalar", cV, v_i2[h])
                em("scalar", lambda e, h=h, ph=ph: e.activation(
                    out=K2STG[:, k2base + h * SBA:k2base + (h + 1) * SBA],
                    in_=ITp[ph][:], func=AF.Copy, scale=1.0 / 256.0), cS)
                s_hq[h] = cS.n
            if i < ICH:
                wait("vector", cS, s_iqf[i])
                em("vector", lambda e, pi=pi: e.tensor_scalar(
                    out=ITp[pi][:], in0=IQFp[pi][:], scalar1=0.015625, scalar2=15.5078125,
                    op0=OP.mult, op1=OP.add), cV)
                v_fi[i] = cV.n
                wait("scalar", cV, v_fi[i])
                if i >= 2:
                    wait("scalar", cG, g_dd[i - 2])
                em("scalar", lambda e, pi=pi: e.activation(
                    out=FLBp[pi][:], in_=ITp[pi][:], func=AF.Copy, scale=1.0), cS)
                s_flb[i] = cS.n
                em("scalar", lambda e, pi=pi: e.activation(
                    out=GLBp[pi][:], in_=ITp[pi][:], func=AF.Gelu, scale=0.25, bias=CBIA[:, 2:3]), cS)
                em("scalar", lambda e, pi=pi: e.activation(
                    out=GHBp[pi][:], in_=ITp[pi][:], func=AF.Gelu, scale=0.25, bias=CBIA[:, 3:4]), cS)
                s_g[i] = cS.n
                wait("gpsimd", cS, s_g[i])
                em("gpsimd", lambda e, pi=pi: e.tensor_tensor(
                    out=GHBp[pi][:], in0=GHBp[pi][:], in1=GLBp[pi][:], op=OP.subtract), cG)
                g_dd[i] = cG.n
        _tap("hq_a%d" % a, lambda: K2STG[:])

        wait("tensor", cWD, wd_co7)
        wait("tensor", cV, v_iq[ICH - 1])
        for j in range(ICH):
            em("tensor", lambda e, j=j: e.matmul(
                QS[0][:, 0:512], WSB[:, j * 128:(j + 1) * 128],
                K2STG[:, k2base + j * SBA:k2base + (j + 1) * SBA],
                start=(j == 0), stop=(j == ICH - 1)), cT)
        t_w2 = cT.n
        for co in range(CH):
            if co < 7:
                sl, qo = bmap[co]
            else:
                sl, qo = 0, 0
            wait("vector", cT, t_w2)
            em("vector", lambda e, co=co, sl=sl, qo=qo: e.tensor_scalar(
                out=ITILE[:], in0=QS[sl][:, qo:qo + 512], scalar1=256.0, scalar2=B2S[:, co:co + 1],
                op0=OP.mult, op1=OP.add), cV)
            em("vector", lambda e, co=co: e.scalar_tensor_tensor(
                out=FINALW[:, co * SBA:(co + 1) * SBA], in0=ITILE[:], scalar=1.0 / 256.0,
                in1=FINALW[:, co * SBA:(co + 1) * SBA], op0=OP.mult, op1=OP.add), cV)
        v_ffn = cV.n
        _tap("final_a%d" % a, lambda: FINALW[:])

        fv = FINALW.ap()[:, :].rearrange("p (c w) -> p c w", c=CH)
        for t in range(NTIL):
            g = g0 + t
            gt = tok0 + t * 128
            sl = t % 2
            wait("sync", cV)
            em("sync", lambda e, gt=gt: e.dma_start(out=XT[:], in_=x_in[gt:gt + 128, :]), cLD, 16)
            ld_x = cLD.n
            wait("vector", cLD, ld_x)
            wait("vector", cOUT, cOUT.n)
            em("vector", lambda e, g=g: e.tensor_scalar(
                out=OUTSB[:], in0=XT[:],
                scalar1=MC[:, g:g + 1], scalar2=RSTD[:, g:g + 1],
                op0=OP.subtract, op1=OP.mult), cV)
            em("vector", lambda e: e.tensor_tensor(
                out=OUTSB[:], in0=OUTSB[:], in1=XT[:], op=OP.add), cV)
            v_xpn = cV.n
            q = QS[2 + (t % 2)]
            wait("tensor", cV, v_xpn)
            for cb in range(CH):
                em("tensor", lambda e, cb=cb, t=t, q=q, fv=fv: e.transpose(
                    q[:, cb * 128:(cb + 1) * 128], fv[:, cb, t * 128:(t + 1) * 128], IDNF[:]), cT)
            t_f = cT.n
            wait("vector", cT, t_f)
            em("vector", lambda e, q=q: e.tensor_tensor(
                out=OUTSB[:], in0=OUTSB[:], in1=q[:], op=OP.add), cV)
            v_out = cV.n
            wait("sync", cV, v_out)
            em("sync", lambda e, gt=gt: e.dma_start(
                out=out_t[gt:gt + 128, :], in_=OUTSB[:]), cOUT, 16)

    wait("sync", cOUT, cOUT.n)

    with nc.Block() as block:
        @block.sync
        def _(e):
            for f in sched["sync"]:
                f(e)

        @block.gpsimd
        def _(e):
            for f in sched["gpsimd"]:
                f(e)

        @block.vector
        def _(e):
            for f in sched["vector"]:
                f(e)

        @block.scalar
        def _(e):
            for f in sched["scalar"]:
                f(e)

        @block.tensor
        def _(e):
            for f in sched["tensor"]:
                f(e)

    ctx.close()
    return nc


def _host_inputs(inputs):
    x = np.ascontiguousarray(np.asarray(inputs["L0_attn_x"], dtype=np.float32))
    row = lambda k: np.ascontiguousarray(np.asarray(inputs[k], dtype=np.float32).reshape(1, -1))
    base = {
        "wk": np.ascontiguousarray(np.asarray(inputs["L0_attn_W_key"], np.float32)),
        "wv": np.ascontiguousarray(np.asarray(inputs["L0_attn_W_value"], np.float32)),
        "wr": np.ascontiguousarray(np.asarray(inputs["L0_attn_W_recept"], np.float32)),
        "wo": np.ascontiguousarray(np.asarray(inputs["L0_attn_W_out"], np.float32)),
        "wr2": np.ascontiguousarray(np.asarray(inputs["L0_attn_W_recept2"], np.float32)),
        "wk2": np.ascontiguousarray(np.asarray(inputs["L0_attn_W_key2"], np.float32)),
        "wv2": np.ascontiguousarray(np.asarray(inputs["L0_attn_W_value2"], np.float32)),
        "w1": np.ascontiguousarray(np.asarray(inputs["L0_ffn_w1"], np.float32)),
        "w2": np.ascontiguousarray(np.asarray(inputs["L0_ffn_w2"], np.float32)),
        "ln1": row("L0_attn_ln1_w"), "ln2": row("L0_attn_ln2_w"),
        "tmk": row("L0_attn_time_mix_k"), "tmv": row("L0_attn_time_mix_v"),
        "tmr": row("L0_attn_time_mix_r"), "tmk2": row("L0_attn_time_mix_k2"),
        "tmr2": row("L0_attn_time_mix_r2"),
        "b1": row("L0_ffn_b1"), "b2": row("L0_ffn_b2"),
        "tdb": np.tile(row("L0_attn_time_decay"), (128, 1)),
        "ub": np.tile(row("L0_attn_time_first"), (128, 1)),
        "ltc": np.triu(np.ones((128, 128), np.float32), k=1),
        "ones1": np.ones((1, 128), np.float32),
        "idnf": np.eye(128, dtype=np.float32),
        "onesc": np.ones((128, 1)).astype(ml_dtypes.bfloat16),
        "onesr": np.ones((128, 1), np.float32),
        "cbias": np.tile(np.array([[1e-5, 1e-6, -4.0, -3.75, 0, 0, 0, 0]], np.float32), (128, 1)),
    }
    in_maps = []
    for b in range(NCORE):
        m = dict(base)
        m["x"] = np.ascontiguousarray(x[b])
        in_maps.append(m)
    return in_maps


def kernel(**inputs):
    nc = _build(taps=tuple(_TAPS))
    in_maps = _host_inputs(inputs)
    res = run_bass_kernel_spmd(nc, in_maps, core_ids=list(range(NCORE)))
    out = np.stack([np.asarray(r["out"]) for r in res.results], axis=0)
    return out.astype(np.float32)


# version 03
# speedup vs baseline: 1.4379x; 1.1076x over previous
import numpy as np
import ml_dtypes
import concourse.bass as bass
import concourse.mybir as mybir
from concourse.alu_op_type import AluOpType as OP
from concourse.bass_utils import run_bass_kernel_spmd

F32 = mybir.dt.float32
F32R = mybir.dt.float32r
BF16 = mybir.dt.bfloat16
I32 = mybir.dt.int32
AF = mybir.ActivationFunctionType

T, C, I = 4096, 1024, 4096
CH = C // 128
ICH = I // 128
SBA = 512
NSB = T // SBA
NTIL = SBA // 128
NCORE = 8

ENGS = ("sync", "gpsimd", "vector", "scalar", "tensor")

_TAPS = []


def _build(taps=(), detect_races=False):
    nc = bass.Bass("TRN2", target_bir_lowering=False, detect_race_conditions=detect_races)
    dr = lambda n, sh, dt: nc.dram_tensor(n, sh, dt, kind="ExternalInput").ap()

    x_in = dr("x", [T, C], F32)
    wk_in = dr("wk", [C, C], F32)
    wv_in = dr("wv", [C, C], F32)
    wr_in = dr("wr", [C, C], F32)
    wo_in = dr("wo", [C, C], F32)
    wr2_in = dr("wr2", [C, C], F32)
    wk2_in = dr("wk2", [C, I], F32)
    wv2_in = dr("wv2", [I, C], F32)
    w1_in = dr("w1", [C, I], F32)
    w2_in = dr("w2", [I, C], F32)
    vecs = {n: dr(n, [1, C], F32) for n in
            ("ln1", "ln2", "tmk", "tmv", "tmr", "tmk2", "tmr2", "b2")}
    b1_in = dr("b1", [1, I], F32)
    tdb_in = dr("tdb", [128, C], F32)
    ub_in = dr("ub", [128, C], F32)
    ltc_in = dr("ltc", [128, 128], F32R)
    ones_in = dr("ones1", [1, 128], F32R)
    idnf_in = dr("idnf", [128, 128], F32)
    onesc_in = dr("onesc", [128, 1], BF16)
    onesr_in = dr("onesr", [128, 1], F32R)
    cbias_in = dr("cbias", [128, 8], F32)

    out_t = nc.dram_tensor("out", [T, C], F32, kind="ExternalOutput").ap()

    from contextlib import ExitStack
    ctx = ExitStack()

    _off = [16512]
    handles = {}

    def place(name, shape, dtype, at=None):
        nbytes = int(np.prod(shape[1:])) * mybir.dt.size(dtype)
        if at is None:
            at = _off[0]
            _off[0] = (at + nbytes + 31) & ~31
        h = nc.alloc_sbuf_tensor_at(name, shape, dtype, offset=at)
        handles[name] = (h, at, nbytes)
        return h

    LTC = place("LTC", [128, 128], F32R)
    ONES = place("ONES", [1, 128], F32R)
    IDNF = place("IDNF", [128, 128], F32)
    ONESC = place("ONESC", [128, 1], BF16)
    ONESR = place("ONESR", [128, 1], F32R)
    WCL = place("WCL", [128, C], F32)
    UB = place("UB", [128, C], F32)
    EMIN = place("EMIN", [128, C], F32)
    ESH = place("ESH", [128, C], F32)
    NIOT = place("NIOT", [128, 1], F32)
    IOTM1 = place("IOTM1", [128, 1], F32)
    EW2R = place("EW2R", [1, C], F32)
    CARRY = place("CARRY", [1, 2 * C], F32R)
    PC = {}
    for nme in ("ln1", "ln2", "tmk", "tmv", "tmr", "tmk2", "tmr2"):
        PC[nme] = place("PC_" + nme, [128, CH], F32)
    B1S = place("B1S", [128, ICH], F32)
    B2S = place("B2S", [128, CH], F32)
    SUMS = place("SUMS", [128, 32], F32)
    SQS = place("SQS", [128, 32], F32)
    MC = place("MC", [128, 32], F32)
    RSTD = place("RSTD", [128, 32], F32)
    RSTD1 = place("RSTD1", [128, 32], F32)
    TCA = place("TCA", [128, 4], F32)
    TCB = place("TCB", [128, 4], F32)
    RSTD1B = place("RSTD1B", [128, SBA], BF16)
    RSTD2B = place("RSTD2B", [128, SBA], BF16)
    XNFM = place("XNFM", [128, CH * SBA], BF16)
    X1BF = place("X1BF", [128, CH * SBA], BF16)
    FINALW = place("FINALW", [128, CH * SBA], F32)
    HFM = place("HFM", [128, CH * (SBA + 1)], BF16)
    DBUF = place("DBUF", [128, CH * SBA], BF16)
    RWKVFM = place("RWKVFM", [128, CH * SBA], BF16)
    H2FM = place("H2FM", [128, CH * (SBA + 1)], BF16)
    MIXT = place("MIXT", [128, 2 * 256], BF16)
    XR2J = place("XR2J", [128, SBA], BF16)
    GBUF = place("GBUF", [128, SBA], BF16)
    K2STG = place("K2STG", [128, ICH * SBA], BF16)
    CBIA = place("CBIA", [128, 8], F32)
    XT = place("XT", [128, C], F32)
    OUTSB = place("OUTSB", [128, C], F32)

    ovl0 = _off[0]

    def placeA(name, shape, dtype):
        return place(name, shape, dtype)

    KTM = placeA("KTM", [128, 2 * C], F32)
    VTM = placeA("VTM", [128, 2 * C], F32)
    ERT = placeA("ERT", [128, 2 * C], F32)
    KCB = placeA("KCB", [128, C], F32)
    EKT = placeA("EKT", [128, 2 * C], F32)
    EKB = placeA("EKB", [128, C], F32)
    ZB = placeA("ZB", [128, 2 * C], F32R)
    APR = placeA("APR", [128, 2 * C], F32)
    EKTV = placeA("EKTV", [128, C], F32)
    DEN = placeA("DEN", [128, C], F32)
    RWT = placeA("RWT", [128, C], F32)
    ovl_end_A = _off[0]

    _off[0] = ovl0
    WSA = place("WSA", [128, 2 * 4096], BF16)
    WSB = place("WSB", [128, 2 * 4 * C], BF16)
    XK2 = place("XK2", [128, CH * SBA], BF16)
    R2TH = place("R2TH", [128, CH * SBA], BF16)
    ITILE = place("ITILE", [128, SBA], I32)
    ITILE2 = place("ITILE2", [128, SBA], I32)
    Z1B = place("Z1B", [128, SBA], F32)
    IQF2 = place("IQF2", [128, SBA], F32)
    FLB = place("FLB", [128, SBA], F32)
    FLB2 = place("FLB2", [128, SBA], F32)
    GLB = place("GLB", [128, SBA], F32)
    GLB2 = place("GLB2", [128, SBA], F32)
    GHB = place("GHB", [128, SBA], F32)
    GHB2 = place("GHB2", [128, SBA], F32)
    ovl_end_B = _off[0]
    _off[0] = max(ovl_end_A, ovl_end_B)

    CROW = OUTSB
    total_kb = (_off[0] - 16512) / 1024.0
    assert _off[0] <= 229344, f"SBUF over budget: {total_kb:.1f} KB used, end={_off[0]}"

    QS = [ctx.enter_context(nc.psum_tensor("Q%d" % i, [128, 1024], F32)) for i in range(4)]

    class Ctr:
        def __init__(s, name):
            s.h = ctx.enter_context(nc.semaphore(name))
            s.n = 0
    cLD = Ctr("cLD")
    cWD = Ctr("cWD")
    cV = Ctr("cV")
    cS = Ctr("cS")
    cT = Ctr("cT")
    cG = Ctr("cG")
    cOUT = Ctr("cOUT")

    sched = {e: [] for e in ENGS}

    def em(eng, make, inc=None, k=1):
        if inc is not None:
            inc.n += k
        def f(e, make=make, inc=inc, k=k):
            r = make(e)
            if inc is not None:
                if r is None:
                    r = e.nop()
                r.then_inc(inc.h, k)
        sched[eng].append(f)

    def wait(eng, ctr, n=None):
        v = ctr.n if n is None else n
        if v > 0:
            sched[eng].append(lambda e, h=ctr.h, v=v: e.wait_ge(h, v))

    _selfc = {"vector": None, "scalar": None}

    def bar(eng):
        c = {"vector": cV, "scalar": cS}[eng]
        wait(eng, c, c.n)

    tapv = {}

    def _tap(name, apf):
        if name in taps:
            ap0 = apf()
            sh = list(ap0.shape)
            t = nc.dram_tensor("tap_" + name, sh, ap0.dtype, kind="ExternalOutput").ap()
            wait("sync", cV)
            wait("sync", cS)
            wait("sync", cG)
            wait("sync", cT)
            em("sync", lambda e, t=t, a=ap0: e.dma_start(out=t, in_=a), cOUT, 16)

    em("sync", lambda e: e.dma_start(out=LTC[:], in_=ltc_in), cLD, 16)
    em("sync", lambda e: e.dma_start(out=ONES[:], in_=ones_in), cLD, 16)
    em("sync", lambda e: e.dma_start(out=IDNF[:], in_=idnf_in), cLD, 16)
    em("sync", lambda e: e.dma_start(out=ONESC[:], in_=onesc_in), cLD, 16)
    em("sync", lambda e: e.dma_start(out=ONESR[:], in_=onesr_in), cLD, 16)
    em("sync", lambda e: e.dma_start(out=CBIA[:], in_=cbias_in), cLD, 16)
    em("sync", lambda e: e.dma_start(out=WCL[:], in_=tdb_in), cLD, 16)
    em("sync", lambda e: e.dma_start(out=UB[:], in_=ub_in), cLD, 16)
    for nme in ("ln1", "ln2", "tmk", "tmv", "tmr", "tmk2", "tmr2"):
        em("sync", lambda e, d=PC[nme], s=vecs[nme]: e.dma_start(
            out=d.ap()[:, :].rearrange("p (x c) -> p x c", x=1), in_=s.rearrange("x (c p) -> p x c", p=128),
            allow_slow_non_contiguous=True), cLD, 16)
    em("sync", lambda e: e.dma_start(out=B1S.ap()[:, :].rearrange("p (x c) -> p x c", x=1),
                                     in_=b1_in.rearrange("x (c p) -> p x c", p=128), allow_slow_non_contiguous=True), cLD, 16)
    em("sync", lambda e: e.dma_start(out=B2S.ap()[:, :].rearrange("p (x c) -> p x c", x=1),
                                     in_=vecs["b2"].rearrange("x (c p) -> p x c", p=128), allow_slow_non_contiguous=True), cLD, 16)
    init_ld = cLD.n

    em("gpsimd", lambda e: e.iota(NIOT[:], [[1, 1]], base=0, channel_multiplier=-1,
                                  allow_small_or_imprecise_dtypes=True), cG)
    em("gpsimd", lambda e: e.iota(IOTM1[:], [[1, 1]], base=-1, channel_multiplier=1,
                                  allow_small_or_imprecise_dtypes=True), cG)
    em("gpsimd", lambda e: e.memzero(CARRY[:]), cG)
    g_init = cG.n

    wait("scalar", cLD, init_ld)
    em("scalar", lambda e: e.activation(out=WCL[:], in_=WCL[:], func=AF.Exp), cS)
    wait("vector", cS)
    em("vector", lambda e: e.tensor_scalar(out=WCL[:], in0=WCL[:], scalar1=-1.0, scalar2=20.0,
                                           op0=OP.mult, op1=OP.min), cV)
    em("vector", lambda e: e.tensor_scalar(out=WCL[:], in0=WCL[:], scalar1=-20.0, scalar2=None,
                                           op0=OP.max), cV)
    wait("vector", cG, g_init)
    em("vector", lambda e: e.tensor_scalar(out=EMIN[:], in0=WCL[:], scalar1=NIOT[:, 0:1], scalar2=None,
                                           op0=OP.mult), cV)
    em("vector", lambda e: e.tensor_scalar(out=ESH[:], in0=WCL[:], scalar1=IOTM1[:, 0:1], scalar2=None,
                                           op0=OP.mult), cV)
    em("vector", lambda e: e.tensor_scalar(out=EW2R[:], in0=WCL[0:1, :], scalar1=128.0, scalar2=None,
                                           op0=OP.mult), cV)
    v_pre = cV.n
    wait("scalar", cV, v_pre)
    em("scalar", lambda e: e.activation(out=EMIN[:], in_=EMIN[:], func=AF.Exp), cS)
    em("scalar", lambda e: e.activation(out=ESH[:], in_=ESH[:], func=AF.Exp), cS)
    em("scalar", lambda e: e.activation(out=EW2R[:], in_=EW2R[:], func=AF.Exp), cS)
    s_pre = cS.n
    wait("vector", cS, s_pre)
    em("vector", lambda e: e.tensor_scalar(out=B1S[:], in0=B1S[:], scalar1=256.0, scalar2=None, op0=OP.mult), cV)
    em("vector", lambda e: e.tensor_scalar(out=B2S[:], in0=B2S[:], scalar1=256.0, scalar2=None, op0=OP.mult), cV)

    wkv_view = {"k": wk_in, "v": wv_in, "r": wr_in}
    WOUTV = K2STG

    def dma_w8(dst_ap, win):
        def f(e, d=dst_ap, w=win):
            return e.dma_start(out=d.rearrange("p (a c) -> p a c", a=CH),
                               in_=w.rearrange("(cc p) c -> p cc c", p=128))
        return f

    wait("tensor", cLD, init_ld)
    wait("gpsimd", cLD, init_ld)
    for a in range(NSB):
        tok0 = a * SBA
        g0 = a * NTIL

        ld_tile = []
        for t in range(NTIL):
            gt = tok0 + t * 128
            wait("sync", cV)
            wait("sync", cS)
            em("sync", lambda e, gt=gt: e.dma_start(out=XT[:], in_=x_in[gt:gt + 128, :]), cLD, 16)
            ldn = cLD.n
            g = g0 + t
            wait("vector", cLD, ldn)
            em("vector", lambda e, g=g: e.tensor_reduce(
                out=SUMS[:, g:g + 1], in_=XT[:], axis=mybir.AxisListType.X, op=OP.add), cV)
            wait("scalar", cLD, ldn)
            wait("scalar", cOUT, cOUT.n)
            wait("scalar", cT, cT.n)
            wait("scalar", cV, cV.n)
            em("scalar", lambda e, g=g: e.activation(
                out=OUTSB[:], in_=XT[:], func=AF.Square), cS)
            s_sq = cS.n
            wait("vector", cS, s_sq)
            em("vector", lambda e, g=g: e.tensor_reduce(
                out=SQS[:, g:g + 1], in_=OUTSB[:], axis=mybir.AxisListType.X, op=OP.add), cV)
            wait("vector", cS)
            em("vector", lambda e, g=g: e.tensor_scalar(
                out=MC[:, g:g + 1], in0=SUMS[:, g:g + 1], scalar1=1.0 / C, scalar2=None, op0=OP.mult), cV)
            bar("vector")
            em("vector", lambda e, g=g, t=t: e.tensor_tensor(
                out=TCA[:, t:t + 1], in0=MC[:, g:g + 1], in1=MC[:, g:g + 1], op=OP.mult), cV)
            bar("vector")
            em("vector", lambda e, g=g, t=t: e.scalar_tensor_tensor(
                out=TCB[:, t:t + 1], in0=SQS[:, g:g + 1], scalar=1.0 / C, in1=TCA[:, t:t + 1],
                op0=OP.mult, op1=OP.subtract), cV)
            v_var = cV.n
            wait("scalar", cV, v_var)
            em("scalar", lambda e, t=t: e.activation(out=TCA[:, t:t + 1], in_=TCB[:, t:t + 1], func=AF.Ln, bias=CBIA[:, 0:1]), cS)
            bar("scalar")
            em("scalar", lambda e, g=g, t=t: e.activation(
                out=RSTD[:, g:g + 1], in_=TCA[:, t:t + 1], func=AF.Exp, scale=-0.5), cS)
            s_rstd = cS.n
            wait("vector", cS, s_rstd)
            em("vector", lambda e, g=g, t=t: e.tensor_tensor(
                out=TCA[:, t:t + 1], in0=RSTD[:, g:g + 1], in1=RSTD[:, g:g + 1], op=OP.mult), cV)
            bar("vector")
            em("vector", lambda e, t=t: e.tensor_tensor(
                out=TCA[:, t:t + 1], in0=TCB[:, t:t + 1], in1=TCA[:, t:t + 1], op=OP.mult), cV)
            v_ms1 = cV.n
            wait("scalar", cV, v_ms1)
            em("scalar", lambda e, t=t: e.activation(out=TCB[:, t:t + 1], in_=TCA[:, t:t + 1], func=AF.Ln, bias=CBIA[:, 1:2]), cS)
            bar("scalar")
            em("scalar", lambda e, g=g, t=t: e.activation(
                out=RSTD1[:, g:g + 1], in_=TCB[:, t:t + 1], func=AF.Exp, scale=-0.5), cS)
            wait("vector", cOUT, cOUT.n)
            wait("vector", cT, cT.n)
            em("vector", lambda e, g=g: e.tensor_scalar(
                out=OUTSB[:], in0=XT[:], scalar1=MC[:, g:g + 1], scalar2=RSTD[:, g:g + 1],
                op0=OP.subtract, op1=OP.mult), cV)
            v_xnb = cV.n
            wait("tensor", cV, v_xnb)
            wait("tensor", cS)
            for cb in range(CH):
                em("tensor", lambda e, cb=cb: e.transpose(
                    QS[2][:, cb * 128:(cb + 1) * 128], OUTSB[:, cb * 128:(cb + 1) * 128], IDNF[:]), cT)
            t_x = cT.n
            wait("vector", cT, t_x)
            em("vector", lambda e, t=t: e.tensor_copy(
                out=XNFM.ap()[:, :].rearrange("p (c w) -> p c w", c=CH)[:, :, t * 128:(t + 1) * 128],
                in_=QS[2].ap()[:, :].rearrange("p (c w) -> p c w", c=CH)), cV)

        if a == 0:
            _tap("sums_a0", lambda: SUMS[:])
            _tap("sqs_a0", lambda: SQS[:])
            _tap("mc_a0", lambda: MC[:])
            _tap("rstd_a0", lambda: RSTD[:])
            _tap("idnf_a0", lambda: IDNF[:])
            _tap("outsb_a0", lambda: OUTSB[:])
        s_r1 = cS.n
        wait("tensor", cS, s_r1)
        wait("tensor", cV)
        for t in range(NTIL):
            em("tensor", lambda e, t=t, g0=g0: e.transpose(
                QS[0][0:1, t * 128:(t + 1) * 128], RSTD1[:, g0 + t:g0 + t + 1], IDNF[:]), cT)
        t_tr = cT.n
        wait("vector", cT, t_tr)
        em("vector", lambda e: e.tensor_copy(out=ZB[0:1, 0:SBA], in_=QS[0][0:1, 0:SBA]), cV)
        v_srp = cV.n
        wait("tensor", cV, v_srp)
        for t in range(NTIL):
            em("tensor", lambda e, t=t: e.matmul(QS[1][:, t * 128:(t + 1) * 128], ONES[:],
                                                 ZB[0:1, t * 128:(t + 1) * 128], start=True, stop=True), cT)
        t_bc = cT.n
        wait("vector", cT, t_bc)
        em("vector", lambda e: e.tensor_copy(out=RSTD1B[:], in_=QS[1][:, 0:SBA]), cV)
        v_r1b = cV.n

        _tap("xnfm_a%d" % a, lambda: XNFM[:])

        hv = HFM.ap()[:, :].rearrange("p (c w) -> p c w", w=SBA + 1)
        if a == 0:
            em("vector", lambda e, hv=hv: e.memset(hv[:, :, 0:1], 0.0), cV)
        else:
            em("vector", lambda e, hv=hv: e.tensor_copy(out=hv[:, :, 0:1], in_=hv[:, :, SBA:SBA + 1]), cV)
        wait("vector", cV, v_r1b)
        for c in range(CH):
            hb = lambda c=c: HFM[:, c * (SBA + 1) + 1:(c + 1) * (SBA + 1)]
            em("vector", lambda e, c=c, hb=hb: e.tensor_tensor(
                out=hb(), in0=XNFM[:, c * SBA:(c + 1) * SBA], in1=RSTD1B[:], op=OP.mult), cV)
            em("vector", lambda e, c=c, hb=hb: e.tensor_scalar(
                out=hb(), in0=hb(), scalar1=PC["ln1"][:, c:c + 1], scalar2=None, op0=OP.mult), cV)
            em("vector", lambda e, c=c, hb=hb: e.tensor_tensor(
                out=DBUF[:, c * SBA:(c + 1) * SBA], in0=hb(),
                in1=HFM[:, c * (SBA + 1):(c + 1) * (SBA + 1) - 1], op=OP.subtract), cV)
        _tap("h_a%d" % a, lambda: HFM[:])

        projs = [("k", PC["tmk"], KTM), ("v", PC["tmv"], VTM), ("r", PC["tmr"], ERT)]
        wait("vector", cOUT, cOUT.n)
        for pr in range(2):
            evac_s = {}
            evac_v = {}
            for pi, (pn, tmcol, dstbuf) in enumerate(projs):
                wait("gpsimd", cT, cT.n)
                em("gpsimd", lambda e, pn=pn: e.dma_start(
                    out=K2STG[:, 0:CH * C].rearrange("p (a c) -> p a c", a=CH),
                    in_=wkv_view[pn].rearrange("(cc p) c -> p cc c", p=128)), cWD, 16)
                wd = cWD.n
                qk = [QS[0], QS[1]] if pn in ("k", "r") else [QS[2], QS[3]]
                if pn == "r":
                    wait("tensor", cS)
                for ci in range(CH):
                    ms = (pi * CH + ci) % 2
                    wait("vector", cT, cT.n)
                    em("vector", lambda e, ms=ms, ci=ci, pr=pr, tmcol=tmcol: e.scalar_tensor_tensor(
                        out=MIXT[:, ms * 256:(ms + 1) * 256],
                        in0=DBUF[:, ci * SBA + pr * 256:ci * SBA + pr * 256 + 256],
                        scalar=tmcol[:, ci:ci + 1],
                        in1=HFM[:, ci * (SBA + 1) + pr * 256:ci * (SBA + 1) + pr * 256 + 256],
                        op0=OP.mult, op1=OP.add), cV)
                    v_mix = cV.n
                    wait("tensor", cV, v_mix)
                    if ci == 0:
                        wait("tensor", cWD, wd)
                    for bl in range(2):
                        for nh in range(2):
                            em("tensor", lambda e, ms=ms, ci=ci, bl=bl, nh=nh, qk=qk: e.matmul(
                                qk[bl][:, nh * 512:(nh + 1) * 512],
                                MIXT[:, ms * 256 + bl * 128:ms * 256 + bl * 128 + 128],
                                K2STG[:, ci * C + nh * 512:ci * C + (nh + 1) * 512],
                                start=(ci == 0), stop=(ci == CH - 1)), cT)
                t_mm = cT.n
                for bl in range(2):
                    if pn == "k":
                        wait("scalar", cT, t_mm)
                        em("scalar", lambda e, bl=bl, qk=qk: e.activation(
                            out=KTM[:, bl * C:(bl + 1) * C], in_=qk[bl][:], func=AF.Copy), cS)
                    elif pn == "v":
                        wait("vector", cT, t_mm)
                        em("vector", lambda e, bl=bl, qk=qk: e.tensor_copy(
                            out=VTM[:, bl * C:(bl + 1) * C], in_=qk[bl][:]), cV)
                    else:
                        wait("scalar", cT, t_mm)
                        em("scalar", lambda e, bl=bl, qk=qk: e.activation(
                            out=ERT[:, bl * C:(bl + 1) * C], in_=qk[bl][:], func=AF.Exp, scale=-1.0), cS)
                evac_s[pn] = cS.n
                evac_v[pn] = cV.n

            for bl in range(2):
                blk = 2 * pr + bl
                gq = [QS[2], QS[3]] if bl == 0 else [QS[0], QS[1]]
                wait("vector", cS)
                wait("vector", cV, evac_v["v"])
                em("vector", lambda e, bl=bl: e.tensor_tensor(
                    out=KCB[:], in0=KTM[:, bl * C:(bl + 1) * C], in1=UB[:], op=OP.add), cV)
                em("vector", lambda e: e.tensor_scalar(
                    out=KCB[:], in0=KCB[:], scalar1=20.0, scalar2=-20.0, op0=OP.min, op1=OP.max), cV)
                v_kc = cV.n
                wait("scalar", cV, v_kc)
                em("scalar", lambda e, bl=bl: e.activation(
                    out=EKT[:, bl * C:(bl + 1) * C], in_=KCB[:], func=AF.Exp), cS)
                s_ekt = cS.n
                wait("vector", cS, s_ekt)
                em("vector", lambda e, bl=bl: e.tensor_scalar(
                    out=KCB[:], in0=KTM[:, bl * C:(bl + 1) * C], scalar1=20.0, scalar2=-20.0,
                    op0=OP.min, op1=OP.max), cV)
                v_kc2 = cV.n
                wait("scalar", cV, v_kc2)
                em("scalar", lambda e: e.activation(out=EKB[:], in_=KCB[:], func=AF.Exp), cS)
                s_ek = cS.n
                wait("vector", cS, s_ek)
                em("vector", lambda e: e.tensor_tensor(
                    out=ZB[:, C:2 * C], in0=EKB[:], in1=EMIN[:], op=OP.mult), cV)
                em("vector", lambda e, bl=bl: e.tensor_tensor(
                    out=ZB[:, 0:C], in0=ZB[:, C:2 * C], in1=VTM[:, bl * C:(bl + 1) * C], op=OP.mult), cV)
                v_z = cV.n
                wait("tensor", cV, v_z)
                wait("tensor", cS)
                for seg in range(4):
                    q = gq[seg // 2]
                    qoff = (seg % 2) * 512
                    em("tensor", lambda e, q=q, qoff=qoff, seg=seg: e.matmul(
                        q[:, qoff:qoff + 512], LTC[:],
                        ZB[:, seg * 512:(seg + 1) * 512], start=True, stop=False), cT)
                    em("tensor", lambda e, q=q, qoff=qoff, seg=seg: e.matmul(
                        q[:, qoff:qoff + 512], ONES[:],
                        CARRY[:, seg * 512:(seg + 1) * 512], start=False, stop=True), cT)
                t_g = cT.n
                wait("vector", cT, t_g)
                em("vector", lambda e, gq=gq: e.tensor_tensor(
                    out=APR[:, 0:C], in0=gq[0][:], in1=ESH[:], op=OP.mult), cV)
                em("vector", lambda e, gq=gq: e.tensor_tensor(
                    out=APR[:, C:2 * C], in0=gq[1][:], in1=ESH[:], op=OP.mult), cV)
                sq = [QS[0], QS[1]] if bl == 0 else [QS[2], QS[3]]
                for seg in range(4):
                    em("tensor", lambda e, sq=sq, seg=seg: e.matmul(
                        sq[seg // 2][0:1, (seg % 2) * 512:(seg % 2) * 512 + 512], ONESR[:],
                        ZB[:, seg * 512:(seg + 1) * 512], start=True, stop=True), cT)
                t_s = cT.n
                wait("vector", cT, t_s)
                em("vector", lambda e, sq=sq: e.tensor_tensor(
                    out=CROW[0:1, 0:C], in0=CARRY[:, 0:C], in1=sq[0][0:1, 0:C], op=OP.add), cV)
                em("vector", lambda e, sq=sq: e.tensor_tensor(
                    out=XT[0:1, 0:C], in0=CARRY[:, C:2 * C], in1=sq[1][0:1, 0:C], op=OP.add), cV)
                em("vector", lambda e: e.tensor_tensor(
                    out=CARRY[:, 0:C], in0=CROW[0:1, 0:C], in1=EW2R[:], op=OP.mult), cV)
                em("vector", lambda e: e.tensor_tensor(
                    out=CARRY[:, C:2 * C], in0=XT[0:1, 0:C], in1=EW2R[:], op=OP.mult), cV)
                em("vector", lambda e, bl=bl: e.tensor_tensor(
                    out=EKTV[:], in0=EKT[:, bl * C:(bl + 1) * C],
                    in1=VTM[:, bl * C:(bl + 1) * C], op=OP.mult), cV)
                em("vector", lambda e: e.tensor_tensor(
                    out=EKTV[:], in0=EKTV[:], in1=APR[:, 0:C], op=OP.add), cV)
                em("vector", lambda e, bl=bl: e.scalar_tensor_tensor(
                    out=DEN[:], in0=EKT[:, bl * C:(bl + 1) * C], scalar=1e-8,
                    in1=APR[:, C:2 * C], op0=OP.add, op1=OP.add), cV)
                wait("vector", cS, evac_s.get("r", 0))
                em("vector", lambda e, bl=bl: e.scalar_tensor_tensor(
                    out=DEN[:], in0=ERT[:, bl * C:(bl + 1) * C], scalar=1.0,
                    in1=DEN[:], op0=OP.add, op1=OP.mult), cV)
                v_den = cV.n
                wait("scalar", cV, v_den)
                em("scalar", lambda e: e.activation(out=KCB[:], in_=DEN[:], func=AF.Ln), cS)
                em("scalar", lambda e: e.activation(out=EKB[:], in_=KCB[:], func=AF.Exp, scale=-1.0), cS)
                s_rd = cS.n
                wait("vector", cS, s_rd)
                em("vector", lambda e: e.tensor_tensor(out=RWT[:], in0=EKTV[:], in1=EKB[:], op=OP.mult), cV)
                v_rwt = cV.n
                wait("tensor", cV, v_rwt)
                tq = gq[0]
                for cb in range(CH):
                    em("tensor", lambda e, cb=cb, tq=tq: e.transpose(
                        tq[:, cb * 128:(cb + 1) * 128], RWT[:, cb * 128:(cb + 1) * 128], IDNF[:]), cT)
                t_rt = cT.n
                wait("vector", cT, t_rt)
                em("vector", lambda e, blk=blk, tq=tq: e.tensor_copy(
                    out=RWKVFM.ap()[:, :].rearrange("p (c w) -> p c w", c=CH)[:, :, blk * 128:(blk + 1) * 128],
                    in_=tq.ap()[:, :].rearrange("p (c w) -> p c w", c=CH)), cV)

        v_rwkv = cV.n
        _tap("rwkv_a%d" % a, lambda: RWKVFM[:])

        wait("gpsimd", cT, cT.n)
        em("gpsimd", dma_w8(WOUTV[:, 0:CH * C], wo_in), cWD, 16)
        wd_wo = cWD.n
        wait("tensor", cV, v_rwkv)
        wait("tensor", cS)
        wait("tensor", cWD, wd_wo)
        for co in range(CH):
            q = QS[co % 2]
            for ci in range(CH):
                em("tensor", lambda e, co=co, ci=ci, q=q: e.matmul(
                    q[:, 0:512], WOUTV[:, ci * C + co * 128:ci * C + co * 128 + 128],
                    RWKVFM[:, ci * SBA:(ci + 1) * SBA],
                    start=(ci == 0), stop=(ci == CH - 1)), cT)
            t_wo = cT.n
            wait("scalar", cT, t_wo)
            em("scalar", lambda e, co=co, q=q: e.activation(
                out=FINALW[:, co * SBA:(co + 1) * SBA], in_=q[:, 0:512], func=AF.Copy), cS)
            s_fw = cS.n
            wait("vector", cT, t_wo)
            wait("vector", cS, s_fw)
            em("vector", lambda e, co=co, q=q: e.tensor_tensor(
                out=X1BF[:, co * SBA:(co + 1) * SBA], in0=q[:, 0:512],
                in1=XNFM[:, co * SBA:(co + 1) * SBA], op=OP.add), cV)
            if co >= 1:
                wait("tensor", cS)
                wait("tensor", cV)
        v_x1 = cV.n
        _tap("x1_a%d" % a, lambda: X1BF[:])
        _tap("att_a%d" % a, lambda: FINALW[:])

        wait("vector", cV, v_x1)
        wait("tensor", cS)
        wait("tensor", cV)
        t_mm2 = {}
        for c in range(CH):
            gb = GBUF if c % 2 == 0 else XR2J
            if c >= 2:
                wait("vector", cT, t_mm2[c - 2])
            em("vector", lambda e, c=c, gb=gb: e.tensor_tensor(
                out=gb[:, 0:SBA], in0=X1BF[:, c * SBA:(c + 1) * SBA],
                in1=X1BF[:, c * SBA:(c + 1) * SBA], op=OP.mult), cV)
            v_sq = cV.n
            wait("tensor", cV, v_sq)
            em("tensor", lambda e, c=c, gb=gb: e.matmul(
                QS[2][0:1, 0:512], ONESC[:], gb[:, 0:SBA], start=(c == 0), stop=(c == CH - 1)), cT)
            t_mm2[c] = cT.n
        t_ms2 = cT.n
        wait("scalar", cT, t_ms2)
        em("scalar", lambda e: e.activation(out=OUTSB[0:1, 0:512], in_=QS[2][0:1, 0:512],
                                            func=AF.Ln, scale=1.0 / C, bias=CBIA[0:1, 1:2]), cS)
        bar("scalar")
        em("scalar", lambda e: e.activation(out=ZB[0:1, 0:512], in_=OUTSB[0:1, 0:512],
                                            func=AF.Exp, scale=-0.5), cS)
        s_rs2 = cS.n
        wait("tensor", cS, s_rs2)
        for t in range(NTIL):
            em("tensor", lambda e, t=t: e.matmul(QS[3][:, t * 128:(t + 1) * 128], ONES[:],
                                                 ZB[0:1, t * 128:(t + 1) * 128], start=True, stop=True), cT)
        t_r2b = cT.n
        wait("vector", cT, t_r2b)
        em("vector", lambda e: e.tensor_copy(out=RSTD2B[:], in_=QS[3][:, 0:SBA]), cV)
        v_r2b = cV.n
        _tap("rstd2_a%d" % a, lambda: RSTD2B[:])

        h2v = H2FM.ap()[:, :].rearrange("p (c w) -> p c w", w=SBA + 1)
        if a == 0:
            em("vector", lambda e, h2v=h2v: e.memset(h2v[:, :, 0:1], 0.0), cV)
        else:
            em("vector", lambda e, h2v=h2v: e.tensor_copy(out=h2v[:, :, 0:1], in_=h2v[:, :, SBA:SBA + 1]), cV)
        wait("vector", cV, v_r2b)
        for c in range(CH):
            h2b = lambda c=c: H2FM[:, c * (SBA + 1) + 1:(c + 1) * (SBA + 1)]
            em("vector", lambda e, c=c, h2b=h2b: e.tensor_tensor(
                out=h2b(), in0=X1BF[:, c * SBA:(c + 1) * SBA], in1=RSTD2B[:], op=OP.mult), cV)
            em("vector", lambda e, c=c, h2b=h2b: e.tensor_scalar(
                out=h2b(), in0=h2b(), scalar1=PC["ln2"][:, c:c + 1], scalar2=None, op0=OP.mult), cV)
            em("vector", lambda e, c=c, h2b=h2b: e.tensor_tensor(
                out=DBUF[:, c * SBA:(c + 1) * SBA], in0=h2b(),
                in1=H2FM[:, c * (SBA + 1):(c + 1) * (SBA + 1) - 1], op=OP.subtract), cV)
            em("vector", lambda e, c=c: e.scalar_tensor_tensor(
                out=XK2[:, c * SBA:(c + 1) * SBA], in0=DBUF[:, c * SBA:(c + 1) * SBA],
                scalar=PC["tmk2"][:, c:c + 1],
                in1=H2FM[:, c * (SBA + 1):(c + 1) * (SBA + 1) - 1],
                op0=OP.mult, op1=OP.add), cV)
        v_xk2 = cV.n
        _tap("xk2_a%d" % a, lambda: XK2[:])

        wait("gpsimd", cT, cT.n)
        em("gpsimd", dma_w8(WOUTV[:, 0:CH * C], wr2_in), cWD, 16)
        wd_r2 = cWD.n
        wait("tensor", cWD, wd_r2)
        wait("tensor", cV, v_r2b)
        for half in range(2):
            for ci in range(CH):
                wait("vector", cT, cT.n)
                em("vector", lambda e, ci=ci: e.scalar_tensor_tensor(
                    out=XR2J[:, 0:SBA], in0=DBUF[:, ci * SBA:(ci + 1) * SBA],
                    scalar=PC["tmr2"][:, ci:ci + 1],
                    in1=H2FM[:, ci * (SBA + 1):(ci + 1) * (SBA + 1) - 1],
                    op0=OP.mult, op1=OP.add), cV)
                v_xr = cV.n
                wait("tensor", cV, v_xr)
                for cg in range(4):
                    co = half * 4 + cg
                    em("tensor", lambda e, co=co, ci=ci, cg=cg: e.matmul(
                        QS[cg][:, 0:512], WOUTV[:, ci * C + co * 128:ci * C + co * 128 + 128],
                        XR2J[:, 0:SBA],
                        start=(ci == 0), stop=(ci == CH - 1)), cT)
            t_r2 = cT.n
            wait("scalar", cT, t_r2)
            for cg in range(4):
                co = half * 4 + cg
                em("scalar", lambda e, co=co, cg=cg: e.activation(
                    out=R2TH[:, co * SBA:(co + 1) * SBA], in_=QS[cg][:, 0:512],
                    func=AF.Tanh, scale=0.5), cS)
            wait("tensor", cS)
        s_r2 = cS.n
        _tap("r2th_a%d" % a, lambda: R2TH[:])

        k2base = 0
        wk2v = wk2_in.rearrange("(cc p) q -> p cc q", p=128)
        w1v = w1_in.rearrange("(cc p) q -> p cc q", p=128)
        bmap = [(0, 512), (1, 0), (1, 512), (2, 0), (2, 512), (3, 0), (3, 512)]
        GBp = (GBUF, XR2J)
        s_rl = {}
        v_sq = {}
        t_k2 = {}
        for i in range(ICH + 2):
            pi = i % 2
            if i < ICH:
                if i % 4 == 0:
                    wait("gpsimd", cT, cT.n)
                    ip = (i // 4) % 2
                    em("gpsimd", lambda e, i=i, ip=ip: e.dma_start(
                        out=WSA[:, ip * 4096:(ip + 1) * 4096].rearrange("p (b q) -> p b q", b=CH),
                        in_=wk2v[:, :, i * 128:(i + 4) * 128]), cWD, 16)
                    em("gpsimd", lambda e, i=i, ip=ip: e.dma_start(
                        out=WSB[:, ip * 4 * C:(ip + 1) * 4 * C].rearrange("p (a c) -> p a c", a=4),
                        in_=wv2_in[i * 128:(i + 4) * 128, :].rearrange("(ii p) c -> p ii c", p=128)), cWD, 16)
                    wd_k2 = cWD.n
                    wait("tensor", cWD, wd_k2)
                if i == 30:
                    wait("gpsimd", cT, cT.n)
                    em("gpsimd", lambda e: e.dma_start(
                        out=WSB[:, 0:4096].rearrange("p (j c) -> p j c", j=ICH),
                        in_=wv2_in[:, 896:1024].rearrange("(j p) c -> p j c", p=128)), cWD, 16)
                    wd_cm7 = cWD.n
                ip = (i // 4) % 2
                io = i % 4
                if i == 0:
                    wait("tensor", cV, v_xk2)
                    wait("tensor", cS, s_r2)
                else:
                    wait("tensor", cS, s_rl[i - 1])
                for ci in range(CH):
                    em("tensor", lambda e, i=i, ci=ci, ip=ip, io=io: e.matmul(
                        QS[0][:, 0:512],
                        WSA[:, ip * 4096 + ci * 512 + io * 128:ip * 4096 + ci * 512 + io * 128 + 128],
                        XK2[:, ci * SBA:(ci + 1) * SBA],
                        start=(ci == 0), stop=(ci == CH - 1)), cT)
                t_k2[i] = cT.n
            j = i - 2
            if 0 <= j < ICH:
                jp = (j // 4) % 2
                jo = j % 4
                wait("tensor", cV, v_sq[j])
                for co in range(7):
                    sl, qo = bmap[co]
                    em("tensor", lambda e, j=j, co=co, sl=sl, qo=qo, jp=jp, jo=jo: e.matmul(
                        QS[sl][:, qo:qo + 512],
                        WSB[:, jp * 4 * C + jo * C + co * 128:jp * 4 * C + jo * C + co * 128 + 128],
                        K2STG[:, k2base + j * SBA:k2base + (j + 1) * SBA],
                        start=(j == 0), stop=(j == ICH - 1)), cT)
            if i < ICH:
                wait("scalar", cT, t_k2[i])
                if i >= 2:
                    wait("scalar", cV, v_sq[i - 2])
                em("scalar", lambda e, pi=pi: e.activation(
                    out=GBp[pi][:, 0:SBA], in_=QS[0][:, 0:512], func=AF.Relu), cS)
                s_rl[i] = cS.n
                wait("vector", cS, s_rl[i])
                em("vector", lambda e, i=i, pi=pi: e.tensor_tensor(
                    out=K2STG[:, k2base + i * SBA:k2base + (i + 1) * SBA],
                    in0=GBp[pi][:, 0:SBA], in1=GBp[pi][:, 0:SBA], op=OP.mult), cV)
                v_sq[i] = cV.n
        _tap("k2_a%d" % a, lambda: K2STG[:])

        wait("tensor", cWD, wd_cm7)
        wait("tensor", cS, s_rl[ICH - 1])
        for j in range(ICH):
            em("tensor", lambda e, j=j: e.matmul(
                QS[0][:, 0:512], WSB[:, j * 128:(j + 1) * 128],
                K2STG[:, k2base + j * SBA:k2base + (j + 1) * SBA],
                start=(j == 0), stop=(j == ICH - 1)), cT)
        t_cm = cT.n
        v_cm7 = None
        for co in [7, 0, 1, 2, 3, 4, 5, 6]:
            if co < 7:
                sl, qo = bmap[co]
            else:
                sl, qo = 0, 0
            wait("vector", cT, t_cm)
            em("vector", lambda e, co=co, sl=sl, qo=qo: e.scalar_tensor_tensor(
                out=GLB[:], in0=R2TH[:, co * SBA:(co + 1) * SBA], scalar=1.0,
                in1=QS[sl][:, qo:qo + 512], op0=OP.add, op1=OP.mult), cV)
            em("vector", lambda e, co=co: e.scalar_tensor_tensor(
                out=FINALW[:, co * SBA:(co + 1) * SBA], in0=GLB[:], scalar=0.5,
                in1=FINALW[:, co * SBA:(co + 1) * SBA], op0=OP.mult, op1=OP.add), cV)
            if co == 7:
                v_cm7 = cV.n
        v_cm = cV.n
        _tap("cm_a%d" % a, lambda: FINALW[:])

        ITp = (ITILE, ITILE2)
        IQFp = (Z1B, IQF2)
        FLBp = (FLB, FLB2)
        GLBp = (GLB, GLB2)
        GHBp = (GHB, GHB2)
        bmap = [(0, 512), (1, 0), (1, 512), (2, 0), (2, 512), (3, 0), (3, 512)]
        v_iq = {}
        v_fi = {}
        v_i2 = {}
        s_iqf = {}
        s_flb = {}
        s_g = {}
        s_hq = {}
        g_dd = {}
        t_w1 = {}
        for i in range(ICH + 2):
            pi = i % 2
            if i < ICH:
                if i % 4 == 0:
                    wait("gpsimd", cT, cT.n)
                    ip = (i // 4) % 2
                    em("gpsimd", lambda e, i=i, ip=ip: e.dma_start(
                        out=WSA[:, ip * 4096:(ip + 1) * 4096].rearrange("p (b q) -> p b q", b=CH),
                        in_=w1v[:, :, i * 128:(i + 4) * 128]), cWD, 16)
                    em("gpsimd", lambda e, i=i, ip=ip: e.dma_start(
                        out=WSB[:, ip * 4 * C:(ip + 1) * 4 * C].rearrange("p (a c) -> p a c", a=4),
                        in_=w2_in[i * 128:(i + 4) * 128, :].rearrange("(ii p) c -> p ii c", p=128)), cWD, 16)
                    wd_w = cWD.n
                    wait("tensor", cWD, wd_w)
                if i == 30:
                    wait("gpsimd", cT, cT.n)
                    em("gpsimd", lambda e: e.dma_start(
                        out=WSB[:, 0:4096].rearrange("p (j c) -> p j c", j=ICH),
                        in_=w2_in[:, 896:1024].rearrange("(j p) c -> p j c", p=128)), cWD, 16)
                    wd_co7 = cWD.n
                ip = (i // 4) % 2
                io = i % 4
                if i == 0:
                    wait("tensor", cV, v_cm7)
                else:
                    wait("tensor", cV, v_iq[i - 1])
                for ci in range(CH):
                    em("tensor", lambda e, i=i, ci=ci, ip=ip, io=io: e.matmul(
                        QS[0][:, 0:512],
                        WSA[:, ip * 4096 + ci * 512 + io * 128:ip * 4096 + ci * 512 + io * 128 + 128],
                        XNFM[:, ci * SBA:(ci + 1) * SBA],
                        start=(ci == 0), stop=(ci == CH - 1)), cT)
                t_w1[i] = cT.n
            j = i - 2
            if 0 <= j < ICH:
                jp = (j // 4) % 2
                jo = j % 4
                wait("tensor", cS, s_hq[j])
                if j == 0:
                    wait("tensor", cV, v_cm)
                for co in range(7):
                    sl, qo = bmap[co]
                    em("tensor", lambda e, j=j, co=co, sl=sl, qo=qo, jp=jp, jo=jo: e.matmul(
                        QS[sl][:, qo:qo + 512],
                        WSB[:, jp * 4 * C + jo * C + co * 128:jp * 4 * C + jo * C + co * 128 + 128],
                        K2STG[:, k2base + j * SBA:k2base + (j + 1) * SBA],
                        start=(j == 0), stop=(j == ICH - 1)), cT)
            h = i - 1
            if i < ICH:
                wait("vector", cT, t_w1[i])
                if i >= 1:
                    wait("vector", cS, s_g[i - 1])
                em("vector", lambda e, i=i: e.tensor_scalar(
                    out=ITILE[:], in0=QS[0][:, 0:512], scalar1=256.0, scalar2=B1S[:, i:i + 1],
                    op0=OP.mult, op1=OP.add), cV)
                v_iq[i] = cV.n
                if i == 0 and ("iq0_a%d" % a) in taps:
                    tq0 = nc.dram_tensor("tap_iq0_a%d" % a, [128, SBA], I32, kind="ExternalOutput").ap()
                    wait("sync", cV)
                    em("sync", lambda e, tq0=tq0: e.dma_start(out=tq0, in_=ITILE[:]), cOUT, 16)
                    wait("vector", cOUT, cOUT.n)
                wait("scalar", cV, v_iq[i])
                em("scalar", lambda e, pi=pi: e.activation(
                    out=IQFp[pi][:], in_=ITILE[:], func=AF.Copy, scale=1.0), cS)
                s_iqf[i] = cS.n
            if 0 <= h < ICH:
                ph = h % 2
                wait("vector", cS, s_flb[h])
                em("vector", lambda e, ph=ph: e.scalar_tensor_tensor(
                    out=IQFp[ph][:], in0=IQFp[ph][:], scalar=0.015625, in1=FLBp[ph][:],
                    op0=OP.mult, op1=OP.subtract), cV)
            if i < ICH:
                wait("vector", cS, s_iqf[i])
                em("vector", lambda e, pi=pi: e.tensor_scalar(
                    out=ITILE[:], in0=IQFp[pi][:], scalar1=0.015625, scalar2=15.5078125,
                    op0=OP.mult, op1=OP.add), cV)
                v_fi[i] = cV.n
                wait("scalar", cV, v_fi[i])
                em("scalar", lambda e, pi=pi: e.activation(
                    out=FLBp[pi][:], in_=ITILE[:], func=AF.Copy, scale=1.0), cS)
                s_flb[i] = cS.n
                if i >= 2:
                    wait("scalar", cG, g_dd[i - 2])
                em("scalar", lambda e, pi=pi: e.activation(
                    out=GLBp[pi][:], in_=ITILE[:], func=AF.Gelu, scale=0.25, bias=CBIA[:, 2:3]), cS)
                em("scalar", lambda e, pi=pi: e.activation(
                    out=GHBp[pi][:], in_=ITILE[:], func=AF.Gelu, scale=0.25, bias=CBIA[:, 3:4]), cS)
                s_g[i] = cS.n
                wait("gpsimd", cS, s_g[i])
                em("gpsimd", lambda e, pi=pi: e.tensor_tensor(
                    out=GHBp[pi][:], in0=GHBp[pi][:], in1=GLBp[pi][:], op=OP.subtract), cG)
                g_dd[i] = cG.n
            if 0 <= h < ICH:
                ph = h % 2
                wait("vector", cG, g_dd[h])
                em("vector", lambda e, ph=ph: e.scalar_tensor_tensor(
                    out=GHBp[ph][:], in0=IQFp[ph][:], scalar=16.0, in1=GHBp[ph][:],
                    op0=OP.add, op1=OP.mult), cV)
                em("vector", lambda e, ph=ph: e.tensor_tensor(
                    out=GLBp[ph][:], in0=GLBp[ph][:], in1=GHBp[ph][:], op=OP.add), cV)
                if h >= 1:
                    wait("vector", cS, s_hq[h - 1])
                em("vector", lambda e, ph=ph: e.tensor_scalar(
                    out=ITILE2[:], in0=GLBp[ph][:], scalar1=256.0, scalar2=None, op0=OP.mult), cV)
                v_i2[h] = cV.n
                wait("scalar", cV, v_i2[h])
                em("scalar", lambda e, h=h: e.activation(
                    out=K2STG[:, k2base + h * SBA:k2base + (h + 1) * SBA],
                    in_=ITILE2[:], func=AF.Copy, scale=1.0 / 256.0), cS)
                s_hq[h] = cS.n
        _tap("hq_a%d" % a, lambda: K2STG[:])

        wait("tensor", cWD, wd_co7)
        wait("tensor", cV, v_iq[ICH - 1])
        for j in range(ICH):
            em("tensor", lambda e, j=j: e.matmul(
                QS[0][:, 0:512], WSB[:, j * 128:(j + 1) * 128],
                K2STG[:, k2base + j * SBA:k2base + (j + 1) * SBA],
                start=(j == 0), stop=(j == ICH - 1)), cT)
        t_w2 = cT.n
        for co in range(CH):
            if co < 7:
                sl, qo = bmap[co]
            else:
                sl, qo = 0, 0
            wait("vector", cT, t_w2)
            em("vector", lambda e, co=co, sl=sl, qo=qo: e.tensor_scalar(
                out=ITILE[:], in0=QS[sl][:, qo:qo + 512], scalar1=256.0, scalar2=B2S[:, co:co + 1],
                op0=OP.mult, op1=OP.add), cV)
            em("vector", lambda e, co=co: e.scalar_tensor_tensor(
                out=FINALW[:, co * SBA:(co + 1) * SBA], in0=ITILE[:], scalar=1.0 / 256.0,
                in1=FINALW[:, co * SBA:(co + 1) * SBA], op0=OP.mult, op1=OP.add), cV)
        v_ffn = cV.n
        _tap("final_a%d" % a, lambda: FINALW[:])

        fv = FINALW.ap()[:, :].rearrange("p (c w) -> p c w", c=CH)
        for t in range(NTIL):
            g = g0 + t
            gt = tok0 + t * 128
            sl = t % 2
            wait("sync", cV)
            em("sync", lambda e, gt=gt: e.dma_start(out=XT[:], in_=x_in[gt:gt + 128, :]), cLD, 16)
            ld_x = cLD.n
            wait("vector", cLD, ld_x)
            wait("vector", cOUT, cOUT.n)
            em("vector", lambda e, g=g: e.tensor_scalar(
                out=OUTSB[:], in0=XT[:],
                scalar1=MC[:, g:g + 1], scalar2=RSTD[:, g:g + 1],
                op0=OP.subtract, op1=OP.mult), cV)
            em("vector", lambda e: e.tensor_tensor(
                out=OUTSB[:], in0=OUTSB[:], in1=XT[:], op=OP.add), cV)
            v_xpn = cV.n
            q = QS[2 + (t % 2)]
            wait("tensor", cV, v_xpn)
            for cb in range(CH):
                em("tensor", lambda e, cb=cb, t=t, q=q, fv=fv: e.transpose(
                    q[:, cb * 128:(cb + 1) * 128], fv[:, cb, t * 128:(t + 1) * 128], IDNF[:]), cT)
            t_f = cT.n
            wait("vector", cT, t_f)
            em("vector", lambda e, q=q: e.tensor_tensor(
                out=OUTSB[:], in0=OUTSB[:], in1=q[:], op=OP.add), cV)
            v_out = cV.n
            wait("sync", cV, v_out)
            em("sync", lambda e, gt=gt: e.dma_start(
                out=out_t[gt:gt + 128, :], in_=OUTSB[:]), cOUT, 16)

    wait("sync", cOUT, cOUT.n)

    with nc.Block() as block:
        @block.sync
        def _(e):
            for f in sched["sync"]:
                f(e)

        @block.gpsimd
        def _(e):
            for f in sched["gpsimd"]:
                f(e)

        @block.vector
        def _(e):
            for f in sched["vector"]:
                f(e)

        @block.scalar
        def _(e):
            for f in sched["scalar"]:
                f(e)

        @block.tensor
        def _(e):
            for f in sched["tensor"]:
                f(e)

    ctx.close()
    return nc


def _host_inputs(inputs):
    x = np.ascontiguousarray(np.asarray(inputs["L0_attn_x"], dtype=np.float32))
    row = lambda k: np.ascontiguousarray(np.asarray(inputs[k], dtype=np.float32).reshape(1, -1))
    base = {
        "wk": np.ascontiguousarray(np.asarray(inputs["L0_attn_W_key"], np.float32)),
        "wv": np.ascontiguousarray(np.asarray(inputs["L0_attn_W_value"], np.float32)),
        "wr": np.ascontiguousarray(np.asarray(inputs["L0_attn_W_recept"], np.float32)),
        "wo": np.ascontiguousarray(np.asarray(inputs["L0_attn_W_out"], np.float32)),
        "wr2": np.ascontiguousarray(np.asarray(inputs["L0_attn_W_recept2"], np.float32)),
        "wk2": np.ascontiguousarray(np.asarray(inputs["L0_attn_W_key2"], np.float32)),
        "wv2": np.ascontiguousarray(np.asarray(inputs["L0_attn_W_value2"], np.float32)),
        "w1": np.ascontiguousarray(np.asarray(inputs["L0_ffn_w1"], np.float32)),
        "w2": np.ascontiguousarray(np.asarray(inputs["L0_ffn_w2"], np.float32)),
        "ln1": row("L0_attn_ln1_w"), "ln2": row("L0_attn_ln2_w"),
        "tmk": row("L0_attn_time_mix_k"), "tmv": row("L0_attn_time_mix_v"),
        "tmr": row("L0_attn_time_mix_r"), "tmk2": row("L0_attn_time_mix_k2"),
        "tmr2": row("L0_attn_time_mix_r2"),
        "b1": row("L0_ffn_b1"), "b2": row("L0_ffn_b2"),
        "tdb": np.tile(row("L0_attn_time_decay"), (128, 1)),
        "ub": np.tile(row("L0_attn_time_first"), (128, 1)),
        "ltc": np.triu(np.ones((128, 128), np.float32), k=1),
        "ones1": np.ones((1, 128), np.float32),
        "idnf": np.eye(128, dtype=np.float32),
        "onesc": np.ones((128, 1)).astype(ml_dtypes.bfloat16),
        "onesr": np.ones((128, 1), np.float32),
        "cbias": np.tile(np.array([[1e-5, 1e-6, -4.0, -3.75, 0, 0, 0, 0]], np.float32), (128, 1)),
    }
    in_maps = []
    for b in range(NCORE):
        m = dict(base)
        m["x"] = np.ascontiguousarray(x[b])
        in_maps.append(m)
    return in_maps


def kernel(**inputs):
    nc = _build(taps=tuple(_TAPS))
    in_maps = _host_inputs(inputs)
    res = run_bass_kernel_spmd(nc, in_maps, core_ids=list(range(NCORE)))
    out = np.stack([np.asarray(r["out"]) for r in res.results], axis=0)
    return out.astype(np.float32)


# version 05
# speedup vs baseline: 1.7708x; 1.2315x over previous
import numpy as np
import ml_dtypes
import concourse.bass as bass
import concourse.mybir as mybir
from concourse.alu_op_type import AluOpType as OP
from concourse.bass_utils import run_bass_kernel_spmd

F32 = mybir.dt.float32
F32R = mybir.dt.float32r
BF16 = mybir.dt.bfloat16
I32 = mybir.dt.int32
AF = mybir.ActivationFunctionType

T, C, I = 4096, 1024, 4096
CH = C // 128
ICH = I // 128
SBA = 512
NSB = T // SBA
NTIL = SBA // 128
NCORE = 8

ENGS = ("sync", "gpsimd", "vector", "scalar", "tensor")

_TAPS = []


def _build(taps=(), detect_races=False):
    nc = bass.Bass("TRN2", target_bir_lowering=False, detect_race_conditions=detect_races)
    dr = lambda n, sh, dt: nc.dram_tensor(n, sh, dt, kind="ExternalInput").ap()

    x_in = dr("x", [T, C], F32)
    wk_in = dr("wk", [C, C], F32)
    wv_in = dr("wv", [C, C], F32)
    wr_in = dr("wr", [C, C], F32)
    wo_in = dr("wo", [C, C], F32)
    wr2_in = dr("wr2", [C, C], F32)
    wk2_in = dr("wk2", [C, I], F32)
    wv2_in = dr("wv2", [I, C], F32)
    w1_in = dr("w1", [C, I], F32)
    w2_in = dr("w2", [I, C], F32)
    vecs = {n: dr(n, [1, C], F32) for n in
            ("ln1", "ln2", "tmk", "tmv", "tmr", "tmk2", "tmr2", "b2")}
    b1_in = dr("b1", [1, I], F32)
    tdb_in = dr("tdb", [128, C], F32)
    ub_in = dr("ub", [128, C], F32)
    ltc_in = dr("ltc", [128, 128], F32R)
    ones_in = dr("ones1", [1, 128], F32R)
    idnf_in = dr("idnf", [128, 128], F32)
    onesc_in = dr("onesc", [128, 1], BF16)
    onesr_in = dr("onesr", [128, 1], F32R)
    cbias_in = dr("cbias", [128, 8], F32)

    out_t = nc.dram_tensor("out", [T, C], F32, kind="ExternalOutput").ap()

    drb = lambda n, sh: nc.dram_tensor(n, sh, BF16, kind="Internal").ap()
    kb_s = drb("kb_s", [128, CH, C])
    vb_s = drb("vb_s", [128, CH, C])
    rb_s = drb("rb_s", [128, CH, C])
    wob_s = drb("wob_s", [128, CH, C])
    wr2b_s = drb("wr2b_s", [128, CH, C])
    wk2b_s = drb("wk2b_s", [128, CH, I])
    w1b_s = drb("w1b_s", [128, CH, I])
    wv2b_s = drb("wv2b_s", [128, ICH, C])
    w2b_s = drb("w2b_s", [128, ICH, C])

    from contextlib import ExitStack
    ctx = ExitStack()

    _off = [16512]
    handles = {}

    def place(name, shape, dtype, at=None):
        nbytes = int(np.prod(shape[1:])) * mybir.dt.size(dtype)
        if at is None:
            at = _off[0]
            _off[0] = (at + nbytes + 31) & ~31
        h = nc.alloc_sbuf_tensor_at(name, shape, dtype, offset=at)
        handles[name] = (h, at, nbytes)
        return h

    LTC = place("LTC", [128, 128], F32R)
    ONES = place("ONES", [1, 128], F32R)
    IDNF = place("IDNF", [128, 128], F32)
    ONESC = place("ONESC", [128, 1], BF16)
    ONESR = place("ONESR", [128, 1], F32R)
    WCL = place("WCL", [128, C], F32)
    UB = place("UB", [128, C], F32)
    EMIN = place("EMIN", [128, C], F32)
    ESH = place("ESH", [128, C], F32)
    NIOT = place("NIOT", [128, 1], F32)
    IOTM1 = place("IOTM1", [128, 1], F32)
    EW2R = place("EW2R", [1, C], F32)
    CARRY = place("CARRY", [1, 2 * C], F32R)
    PC = {}
    for nme in ("ln1", "ln2", "tmk", "tmv", "tmr", "tmk2", "tmr2"):
        PC[nme] = place("PC_" + nme, [128, CH], F32)
    B1S = place("B1S", [128, ICH], F32)
    B2S = place("B2S", [128, CH], F32)
    SUMS = place("SUMS", [128, 32], F32)
    SQS = place("SQS", [128, 32], F32)
    MC = place("MC", [128, 32], F32)
    RSTD = place("RSTD", [128, 32], F32)
    RSTD1 = place("RSTD1", [128, 32], F32)
    TCA = place("TCA", [128, 4], F32)
    TCB = place("TCB", [128, 4], F32)
    RSTD1B = place("RSTD1B", [128, SBA], BF16)
    RSTD2B = place("RSTD2B", [128, SBA], BF16)
    XNFM = place("XNFM", [128, CH * SBA], BF16)
    X1BF = place("X1BF", [128, CH * SBA], BF16)
    FINALW = place("FINALW", [128, CH * SBA], F32)
    HFM = place("HFM", [128, CH * (SBA + 1)], BF16)
    DBUF = place("DBUF", [128, CH * SBA], BF16)
    RWKVFM = place("RWKVFM", [128, CH * SBA], BF16)
    H2FM = place("H2FM", [128, CH * (SBA + 1)], BF16)
    MIXT = place("MIXT", [128, 2 * 256], BF16)
    XR2J = place("XR2J", [128, SBA], BF16)
    GBUF = place("GBUF", [128, SBA], BF16)
    K2STG = place("K2STG", [128, ICH * SBA], BF16)
    CBIA = place("CBIA", [128, 8], F32)
    XT = place("XT", [128, C], F32)
    OUTSB = place("OUTSB", [128, C], F32)

    ovl0 = _off[0]

    def placeA(name, shape, dtype):
        return place(name, shape, dtype)

    KTM = placeA("KTM", [128, 2 * C], F32)
    VTM = placeA("VTM", [128, 2 * C], F32)
    ERT = placeA("ERT", [128, 2 * C], F32)
    KCB = placeA("KCB", [128, C], F32)
    EKT = placeA("EKT", [128, 2 * C], F32)
    EKB = placeA("EKB", [128, C], F32)
    ZB = placeA("ZB", [128, 2 * C], F32R)
    APR = placeA("APR", [128, 2 * C], F32)
    EKTV = placeA("EKTV", [128, C], F32)
    DEN = placeA("DEN", [128, C], F32)
    RWT = placeA("RWT", [128, C], F32)
    ovl_end_A = _off[0]

    _off[0] = ovl0
    WSA = place("WSA", [128, 2 * 4096], BF16)
    WSB = place("WSB", [128, 2 * 4 * C], BF16)
    XK2 = place("XK2", [128, CH * SBA], BF16)
    R2TH = place("R2TH", [128, CH * SBA], BF16)
    ITILE = place("ITILE", [128, SBA], I32)
    ITILE2 = place("ITILE2", [128, SBA], I32)
    Z1B = place("Z1B", [128, SBA], F32)
    IQF2 = place("IQF2", [128, SBA], F32)
    FLB = place("FLB", [128, SBA], F32)
    FLB2 = place("FLB2", [128, SBA], F32)
    GLB = place("GLB", [128, SBA], F32)
    GLB2 = place("GLB2", [128, SBA], F32)
    GHB = place("GHB", [128, SBA], F32)
    GHB2 = place("GHB2", [128, SBA], F32)
    ovl_end_B = _off[0]
    _off[0] = max(ovl_end_A, ovl_end_B)

    CROW = OUTSB
    total_kb = (_off[0] - 16512) / 1024.0
    assert _off[0] <= 229344, f"SBUF over budget: {total_kb:.1f} KB used, end={_off[0]}"

    QS = [ctx.enter_context(nc.psum_tensor("Q%d" % i, [128, 1024], F32)) for i in range(4)]

    class Ctr:
        def __init__(s, name):
            s.h = ctx.enter_context(nc.semaphore(name))
            s.n = 0
    cLD = Ctr("cLD")
    cWD = Ctr("cWD")
    cV = Ctr("cV")
    cS = Ctr("cS")
    cT = Ctr("cT")
    cG = Ctr("cG")
    cOUT = Ctr("cOUT")

    sched = {e: [] for e in ENGS}

    def em(eng, make, inc=None, k=1):
        if inc is not None:
            inc.n += k
        def f(e, make=make, inc=inc, k=k):
            r = make(e)
            if inc is not None:
                if r is None:
                    r = e.nop()
                r.then_inc(inc.h, k)
        sched[eng].append(f)

    def wait(eng, ctr, n=None):
        v = ctr.n if n is None else n
        if v > 0:
            sched[eng].append(lambda e, h=ctr.h, v=v: e.wait_ge(h, v))

    _selfc = {"vector": None, "scalar": None}

    def bar(eng):
        c = {"vector": cV, "scalar": cS}[eng]
        wait(eng, c, c.n)

    tapv = {}

    def _tap(name, apf):
        if name in taps:
            ap0 = apf()
            sh = list(ap0.shape)
            t = nc.dram_tensor("tap_" + name, sh, ap0.dtype, kind="ExternalOutput").ap()
            wait("sync", cV)
            wait("sync", cS)
            wait("sync", cG)
            wait("sync", cT)
            em("sync", lambda e, t=t, a=ap0: e.dma_start(out=t, in_=a), cOUT, 16)

    em("sync", lambda e: e.dma_start(out=LTC[:], in_=ltc_in), cLD, 16)
    em("sync", lambda e: e.dma_start(out=ONES[:], in_=ones_in), cLD, 16)
    em("sync", lambda e: e.dma_start(out=IDNF[:], in_=idnf_in), cLD, 16)
    em("sync", lambda e: e.dma_start(out=ONESC[:], in_=onesc_in), cLD, 16)
    em("sync", lambda e: e.dma_start(out=ONESR[:], in_=onesr_in), cLD, 16)
    em("sync", lambda e: e.dma_start(out=CBIA[:], in_=cbias_in), cLD, 16)
    em("sync", lambda e: e.dma_start(out=WCL[:], in_=tdb_in), cLD, 16)
    em("sync", lambda e: e.dma_start(out=UB[:], in_=ub_in), cLD, 16)
    for nme in ("ln1", "ln2", "tmk", "tmv", "tmr", "tmk2", "tmr2"):
        em("sync", lambda e, d=PC[nme], s=vecs[nme]: e.dma_start(
            out=d.ap()[:, :].rearrange("p (x c) -> p x c", x=1), in_=s.rearrange("x (c p) -> p x c", p=128),
            allow_slow_non_contiguous=True), cLD, 16)
    em("sync", lambda e: e.dma_start(out=B1S.ap()[:, :].rearrange("p (x c) -> p x c", x=1),
                                     in_=b1_in.rearrange("x (c p) -> p x c", p=128), allow_slow_non_contiguous=True), cLD, 16)
    em("sync", lambda e: e.dma_start(out=B2S.ap()[:, :].rearrange("p (x c) -> p x c", x=1),
                                     in_=vecs["b2"].rearrange("x (c p) -> p x c", p=128), allow_slow_non_contiguous=True), cLD, 16)
    init_ld = cLD.n

    em("gpsimd", lambda e: e.iota(NIOT[:], [[1, 1]], base=0, channel_multiplier=-1,
                                  allow_small_or_imprecise_dtypes=True), cG)
    em("gpsimd", lambda e: e.iota(IOTM1[:], [[1, 1]], base=-1, channel_multiplier=1,
                                  allow_small_or_imprecise_dtypes=True), cG)
    em("gpsimd", lambda e: e.memzero(CARRY[:]), cG)
    g_init = cG.n

    wait("scalar", cLD, init_ld)
    em("scalar", lambda e: e.activation(out=WCL[:], in_=WCL[:], func=AF.Exp), cS)
    wait("vector", cS)
    em("vector", lambda e: e.tensor_scalar(out=WCL[:], in0=WCL[:], scalar1=-1.0, scalar2=20.0,
                                           op0=OP.mult, op1=OP.min), cV)
    em("vector", lambda e: e.tensor_scalar(out=WCL[:], in0=WCL[:], scalar1=-20.0, scalar2=None,
                                           op0=OP.max), cV)
    wait("vector", cG, g_init)
    em("vector", lambda e: e.tensor_scalar(out=EMIN[:], in0=WCL[:], scalar1=NIOT[:, 0:1], scalar2=None,
                                           op0=OP.mult), cV)
    em("vector", lambda e: e.tensor_scalar(out=ESH[:], in0=WCL[:], scalar1=IOTM1[:, 0:1], scalar2=None,
                                           op0=OP.mult), cV)
    em("vector", lambda e: e.tensor_scalar(out=EW2R[:], in0=WCL[0:1, :], scalar1=128.0, scalar2=None,
                                           op0=OP.mult), cV)
    v_pre = cV.n
    wait("scalar", cV, v_pre)
    em("scalar", lambda e: e.activation(out=EMIN[:], in_=EMIN[:], func=AF.Exp), cS)
    em("scalar", lambda e: e.activation(out=ESH[:], in_=ESH[:], func=AF.Exp), cS)
    em("scalar", lambda e: e.activation(out=EW2R[:], in_=EW2R[:], func=AF.Exp), cS)
    s_pre = cS.n
    wait("vector", cS, s_pre)
    em("vector", lambda e: e.tensor_scalar(out=B1S[:], in0=B1S[:], scalar1=256.0, scalar2=None, op0=OP.mult), cV)
    em("vector", lambda e: e.tensor_scalar(out=B2S[:], in0=B2S[:], scalar1=256.0, scalar2=None, op0=OP.mult), cV)

    wkv_b = {"k": kb_s, "v": vb_s, "r": rb_s}
    WOUTV = K2STG

    def dma_w8(dst_ap, wb):
        def f(e, d=dst_ap, w=wb):
            return e.dma_start(out=d.rearrange("p (a c) -> p a c", a=CH),
                               in_=w[:, :, :])
        return f

    pre_chunks = []
    for src, dst in ((wk_in, kb_s), (wv_in, vb_s), (wr_in, rb_s), (wo_in, wob_s), (wr2_in, wr2b_s)):
        pre_chunks.append((src.rearrange("(cc p) c -> p cc c", p=128), dst[:, :, :]))
    wk2v0 = wk2_in.rearrange("(cc p) q -> p cc q", p=128)
    w1v0 = w1_in.rearrange("(cc p) q -> p cc q", p=128)
    for qc in range(4):
        pre_chunks.append((wk2v0[:, :, qc * 1024:(qc + 1) * 1024], wk2b_s[:, :, qc * 1024:(qc + 1) * 1024]))
    for qc in range(4):
        pre_chunks.append((w1v0[:, :, qc * 1024:(qc + 1) * 1024], w1b_s[:, :, qc * 1024:(qc + 1) * 1024]))
    for g in range(4):
        pre_chunks.append((wv2_in[g * 1024:(g + 1) * 1024, :].rearrange("(ii p) c -> p ii c", p=128),
                           wv2b_s[:, g * 8:(g + 1) * 8, :]))
    for g in range(4):
        pre_chunks.append((w2_in[g * 1024:(g + 1) * 1024, :].rearrange("(ii p) c -> p ii c", p=128),
                           w2b_s[:, g * 8:(g + 1) * 8, :]))
    bnc = lambda half: K2STG.ap()[:, half * 8192:(half + 1) * 8192].rearrange("p (a c) -> p a c", a=CH)
    pre_out = {}
    for n, (src, dst) in enumerate(pre_chunks):
        half = n % 2
        if n >= 2:
            wait("gpsimd", cOUT, pre_out[n - 2])
        em("gpsimd", lambda e, src=src, half=half: e.dma_start(out=bnc(half), in_=src), cWD, 16)
        g_in = cWD.n
        wait("sync", cWD, g_in)
        em("sync", lambda e, dst=dst, half=half: e.dma_start(out=dst, in_=bnc(half)), cOUT, 16)
        pre_out[n] = cOUT.n
    pre_done = cOUT.n

    wait("tensor", cLD, init_ld)
    wait("gpsimd", cLD, init_ld)
    for a in range(NSB):
        tok0 = a * SBA
        g0 = a * NTIL

        ld_tile = []
        for t in range(NTIL):
            gt = tok0 + t * 128
            wait("sync", cV)
            wait("sync", cS)
            em("sync", lambda e, gt=gt: e.dma_start(out=XT[:], in_=x_in[gt:gt + 128, :]), cLD, 16)
            ldn = cLD.n
            g = g0 + t
            wait("vector", cLD, ldn)
            em("vector", lambda e, g=g: e.tensor_reduce(
                out=SUMS[:, g:g + 1], in_=XT[:], axis=mybir.AxisListType.X, op=OP.add), cV)
            wait("scalar", cLD, ldn)
            wait("scalar", cOUT, cOUT.n)
            wait("scalar", cT, cT.n)
            wait("scalar", cV, cV.n)
            em("scalar", lambda e, g=g: e.activation(
                out=OUTSB[:], in_=XT[:], func=AF.Square), cS)
            s_sq = cS.n
            wait("vector", cS, s_sq)
            em("vector", lambda e, g=g: e.tensor_reduce(
                out=SQS[:, g:g + 1], in_=OUTSB[:], axis=mybir.AxisListType.X, op=OP.add), cV)
            wait("vector", cS)
            em("vector", lambda e, g=g: e.tensor_scalar(
                out=MC[:, g:g + 1], in0=SUMS[:, g:g + 1], scalar1=1.0 / C, scalar2=None, op0=OP.mult), cV)
            bar("vector")
            em("vector", lambda e, g=g, t=t: e.tensor_tensor(
                out=TCA[:, t:t + 1], in0=MC[:, g:g + 1], in1=MC[:, g:g + 1], op=OP.mult), cV)
            bar("vector")
            em("vector", lambda e, g=g, t=t: e.scalar_tensor_tensor(
                out=TCB[:, t:t + 1], in0=SQS[:, g:g + 1], scalar=1.0 / C, in1=TCA[:, t:t + 1],
                op0=OP.mult, op1=OP.subtract), cV)
            v_var = cV.n
            wait("scalar", cV, v_var)
            em("scalar", lambda e, t=t: e.activation(out=TCA[:, t:t + 1], in_=TCB[:, t:t + 1], func=AF.Ln, bias=CBIA[:, 0:1]), cS)
            bar("scalar")
            em("scalar", lambda e, g=g, t=t: e.activation(
                out=RSTD[:, g:g + 1], in_=TCA[:, t:t + 1], func=AF.Exp, scale=-0.5), cS)
            s_rstd = cS.n
            wait("vector", cS, s_rstd)
            em("vector", lambda e, g=g, t=t: e.tensor_tensor(
                out=TCA[:, t:t + 1], in0=RSTD[:, g:g + 1], in1=RSTD[:, g:g + 1], op=OP.mult), cV)
            bar("vector")
            em("vector", lambda e, t=t: e.tensor_tensor(
                out=TCA[:, t:t + 1], in0=TCB[:, t:t + 1], in1=TCA[:, t:t + 1], op=OP.mult), cV)
            v_ms1 = cV.n
            wait("scalar", cV, v_ms1)
            em("scalar", lambda e, t=t: e.activation(out=TCB[:, t:t + 1], in_=TCA[:, t:t + 1], func=AF.Ln, bias=CBIA[:, 1:2]), cS)
            bar("scalar")
            em("scalar", lambda e, g=g, t=t: e.activation(
                out=RSTD1[:, g:g + 1], in_=TCB[:, t:t + 1], func=AF.Exp, scale=-0.5), cS)
            wait("vector", cOUT, cOUT.n)
            wait("vector", cT, cT.n)
            em("vector", lambda e, g=g: e.tensor_scalar(
                out=OUTSB[:], in0=XT[:], scalar1=MC[:, g:g + 1], scalar2=RSTD[:, g:g + 1],
                op0=OP.subtract, op1=OP.mult), cV)
            v_xnb = cV.n
            wait("tensor", cV, v_xnb)
            wait("tensor", cS)
            for cb in range(CH):
                em("tensor", lambda e, cb=cb: e.transpose(
                    QS[2][:, cb * 128:(cb + 1) * 128], OUTSB[:, cb * 128:(cb + 1) * 128], IDNF[:]), cT)
            t_x = cT.n
            wait("vector", cT, t_x)
            em("vector", lambda e, t=t: e.tensor_copy(
                out=XNFM.ap()[:, :].rearrange("p (c w) -> p c w", c=CH)[:, :, t * 128:(t + 1) * 128],
                in_=QS[2].ap()[:, :].rearrange("p (c w) -> p c w", c=CH)), cV)

        if a == 0:
            _tap("sums_a0", lambda: SUMS[:])
            _tap("sqs_a0", lambda: SQS[:])
            _tap("mc_a0", lambda: MC[:])
            _tap("rstd_a0", lambda: RSTD[:])
            _tap("idnf_a0", lambda: IDNF[:])
            _tap("outsb_a0", lambda: OUTSB[:])
        s_r1 = cS.n
        wait("tensor", cS, s_r1)
        wait("tensor", cV)
        for t in range(NTIL):
            em("tensor", lambda e, t=t, g0=g0: e.transpose(
                QS[0][0:1, t * 128:(t + 1) * 128], RSTD1[:, g0 + t:g0 + t + 1], IDNF[:]), cT)
        t_tr = cT.n
        wait("vector", cT, t_tr)
        em("vector", lambda e: e.tensor_copy(out=ZB[0:1, 0:SBA], in_=QS[0][0:1, 0:SBA]), cV)
        v_srp = cV.n
        wait("tensor", cV, v_srp)
        for t in range(NTIL):
            em("tensor", lambda e, t=t: e.matmul(QS[1][:, t * 128:(t + 1) * 128], ONES[:],
                                                 ZB[0:1, t * 128:(t + 1) * 128], start=True, stop=True), cT)
        t_bc = cT.n
        wait("vector", cT, t_bc)
        em("vector", lambda e: e.tensor_copy(out=RSTD1B[:], in_=QS[1][:, 0:SBA]), cV)
        v_r1b = cV.n

        _tap("xnfm_a%d" % a, lambda: XNFM[:])

        hv = HFM.ap()[:, :].rearrange("p (c w) -> p c w", w=SBA + 1)
        if a == 0:
            em("vector", lambda e, hv=hv: e.memset(hv[:, :, 0:1], 0.0), cV)
        else:
            em("vector", lambda e, hv=hv: e.tensor_copy(out=hv[:, :, 0:1], in_=hv[:, :, SBA:SBA + 1]), cV)
        wait("vector", cV, v_r1b)
        for c in range(CH):
            hb = lambda c=c: HFM[:, c * (SBA + 1) + 1:(c + 1) * (SBA + 1)]
            em("vector", lambda e, c=c, hb=hb: e.tensor_tensor(
                out=hb(), in0=XNFM[:, c * SBA:(c + 1) * SBA], in1=RSTD1B[:], op=OP.mult), cV)
            em("vector", lambda e, c=c, hb=hb: e.tensor_scalar(
                out=hb(), in0=hb(), scalar1=PC["ln1"][:, c:c + 1], scalar2=None, op0=OP.mult), cV)
            em("vector", lambda e, c=c, hb=hb: e.tensor_tensor(
                out=DBUF[:, c * SBA:(c + 1) * SBA], in0=hb(),
                in1=HFM[:, c * (SBA + 1):(c + 1) * (SBA + 1) - 1], op=OP.subtract), cV)
        _tap("h_a%d" % a, lambda: HFM[:])

        projs = [("k", PC["tmk"], KTM), ("v", PC["tmv"], VTM), ("r", PC["tmr"], ERT)]
        wait("vector", cOUT, cOUT.n)
        for pr in range(2):
            evac_s = {}
            evac_v = {}
            for pi, (pn, tmcol, dstbuf) in enumerate(projs):
                wait("sync", cT, cT.n)
                if a == 0 and pr == 0 and pi == 0:
                    wait("sync", cOUT, pre_done)
                em("sync", lambda e, pn=pn: e.dma_start(
                    out=K2STG[:, 0:CH * C].rearrange("p (a c) -> p a c", a=CH),
                    in_=wkv_b[pn][:, :, :]), cWD, 16)
                wd = cWD.n
                qk = [QS[0], QS[1]] if pn in ("k", "r") else [QS[2], QS[3]]
                if pn == "r":
                    wait("tensor", cS)
                for ci in range(CH):
                    ms = (pi * CH + ci) % 2
                    wait("vector", cT, cT.n)
                    em("vector", lambda e, ms=ms, ci=ci, pr=pr, tmcol=tmcol: e.scalar_tensor_tensor(
                        out=MIXT[:, ms * 256:(ms + 1) * 256],
                        in0=DBUF[:, ci * SBA + pr * 256:ci * SBA + pr * 256 + 256],
                        scalar=tmcol[:, ci:ci + 1],
                        in1=HFM[:, ci * (SBA + 1) + pr * 256:ci * (SBA + 1) + pr * 256 + 256],
                        op0=OP.mult, op1=OP.add), cV)
                    v_mix = cV.n
                    wait("tensor", cV, v_mix)
                    if ci == 0:
                        wait("tensor", cWD, wd)
                    for bl in range(2):
                        for nh in range(2):
                            em("tensor", lambda e, ms=ms, ci=ci, bl=bl, nh=nh, qk=qk: e.matmul(
                                qk[bl][:, nh * 512:(nh + 1) * 512],
                                MIXT[:, ms * 256 + bl * 128:ms * 256 + bl * 128 + 128],
                                K2STG[:, ci * C + nh * 512:ci * C + (nh + 1) * 512],
                                start=(ci == 0), stop=(ci == CH - 1)), cT)
                t_mm = cT.n
                for bl in range(2):
                    if pn == "k":
                        wait("scalar", cT, t_mm)
                        em("scalar", lambda e, bl=bl, qk=qk: e.activation(
                            out=KTM[:, bl * C:(bl + 1) * C], in_=qk[bl][:], func=AF.Copy), cS)
                    elif pn == "v":
                        wait("vector", cT, t_mm)
                        em("vector", lambda e, bl=bl, qk=qk: e.tensor_copy(
                            out=VTM[:, bl * C:(bl + 1) * C], in_=qk[bl][:]), cV)
                    else:
                        wait("scalar", cT, t_mm)
                        em("scalar", lambda e, bl=bl, qk=qk: e.activation(
                            out=ERT[:, bl * C:(bl + 1) * C], in_=qk[bl][:], func=AF.Exp, scale=-1.0), cS)
                evac_s[pn] = cS.n
                evac_v[pn] = cV.n

            for bl in range(2):
                blk = 2 * pr + bl
                gq = [QS[2], QS[3]] if bl == 0 else [QS[0], QS[1]]
                wait("vector", cS)
                wait("vector", cV, evac_v["v"])
                em("vector", lambda e, bl=bl: e.tensor_tensor(
                    out=KCB[:], in0=KTM[:, bl * C:(bl + 1) * C], in1=UB[:], op=OP.add), cV)
                em("vector", lambda e: e.tensor_scalar(
                    out=KCB[:], in0=KCB[:], scalar1=20.0, scalar2=-20.0, op0=OP.min, op1=OP.max), cV)
                v_kc = cV.n
                wait("scalar", cV, v_kc)
                em("scalar", lambda e, bl=bl: e.activation(
                    out=EKT[:, bl * C:(bl + 1) * C], in_=KCB[:], func=AF.Exp), cS)
                s_ekt = cS.n
                wait("vector", cS, s_ekt)
                em("vector", lambda e, bl=bl: e.tensor_scalar(
                    out=KCB[:], in0=KTM[:, bl * C:(bl + 1) * C], scalar1=20.0, scalar2=-20.0,
                    op0=OP.min, op1=OP.max), cV)
                v_kc2 = cV.n
                wait("scalar", cV, v_kc2)
                em("scalar", lambda e: e.activation(out=EKB[:], in_=KCB[:], func=AF.Exp), cS)
                s_ek = cS.n
                wait("vector", cS, s_ek)
                em("vector", lambda e: e.tensor_tensor(
                    out=ZB[:, C:2 * C], in0=EKB[:], in1=EMIN[:], op=OP.mult), cV)
                em("vector", lambda e, bl=bl: e.tensor_tensor(
                    out=ZB[:, 0:C], in0=ZB[:, C:2 * C], in1=VTM[:, bl * C:(bl + 1) * C], op=OP.mult), cV)
                v_z = cV.n
                wait("tensor", cV, v_z)
                wait("tensor", cS)
                for seg in range(4):
                    q = gq[seg // 2]
                    qoff = (seg % 2) * 512
                    em("tensor", lambda e, q=q, qoff=qoff, seg=seg: e.matmul(
                        q[:, qoff:qoff + 512], LTC[:],
                        ZB[:, seg * 512:(seg + 1) * 512], start=True, stop=False), cT)
                    em("tensor", lambda e, q=q, qoff=qoff, seg=seg: e.matmul(
                        q[:, qoff:qoff + 512], ONES[:],
                        CARRY[:, seg * 512:(seg + 1) * 512], start=False, stop=True), cT)
                t_g = cT.n
                wait("vector", cT, t_g)
                em("vector", lambda e, gq=gq: e.tensor_tensor(
                    out=APR[:, 0:C], in0=gq[0][:], in1=ESH[:], op=OP.mult), cV)
                em("vector", lambda e, gq=gq: e.tensor_tensor(
                    out=APR[:, C:2 * C], in0=gq[1][:], in1=ESH[:], op=OP.mult), cV)
                sq = [QS[0], QS[1]] if bl == 0 else [QS[2], QS[3]]
                for seg in range(4):
                    em("tensor", lambda e, sq=sq, seg=seg: e.matmul(
                        sq[seg // 2][0:1, (seg % 2) * 512:(seg % 2) * 512 + 512], ONESR[:],
                        ZB[:, seg * 512:(seg + 1) * 512], start=True, stop=True), cT)
                t_s = cT.n
                wait("vector", cT, t_s)
                em("vector", lambda e, sq=sq: e.tensor_tensor(
                    out=CROW[0:1, 0:C], in0=CARRY[:, 0:C], in1=sq[0][0:1, 0:C], op=OP.add), cV)
                em("vector", lambda e, sq=sq: e.tensor_tensor(
                    out=XT[0:1, 0:C], in0=CARRY[:, C:2 * C], in1=sq[1][0:1, 0:C], op=OP.add), cV)
                em("vector", lambda e: e.tensor_tensor(
                    out=CARRY[:, 0:C], in0=CROW[0:1, 0:C], in1=EW2R[:], op=OP.mult), cV)
                em("vector", lambda e: e.tensor_tensor(
                    out=CARRY[:, C:2 * C], in0=XT[0:1, 0:C], in1=EW2R[:], op=OP.mult), cV)
                em("vector", lambda e, bl=bl: e.tensor_tensor(
                    out=EKTV[:], in0=EKT[:, bl * C:(bl + 1) * C],
                    in1=VTM[:, bl * C:(bl + 1) * C], op=OP.mult), cV)
                em("vector", lambda e: e.tensor_tensor(
                    out=EKTV[:], in0=EKTV[:], in1=APR[:, 0:C], op=OP.add), cV)
                em("vector", lambda e, bl=bl: e.scalar_tensor_tensor(
                    out=DEN[:], in0=EKT[:, bl * C:(bl + 1) * C], scalar=1e-8,
                    in1=APR[:, C:2 * C], op0=OP.add, op1=OP.add), cV)
                wait("vector", cS, evac_s.get("r", 0))
                em("vector", lambda e, bl=bl: e.scalar_tensor_tensor(
                    out=DEN[:], in0=ERT[:, bl * C:(bl + 1) * C], scalar=1.0,
                    in1=DEN[:], op0=OP.add, op1=OP.mult), cV)
                v_den = cV.n
                wait("scalar", cV, v_den)
                em("scalar", lambda e: e.activation(out=KCB[:], in_=DEN[:], func=AF.Ln), cS)
                em("scalar", lambda e: e.activation(out=EKB[:], in_=KCB[:], func=AF.Exp, scale=-1.0), cS)
                s_rd = cS.n
                wait("vector", cS, s_rd)
                em("vector", lambda e: e.tensor_tensor(out=RWT[:], in0=EKTV[:], in1=EKB[:], op=OP.mult), cV)
                v_rwt = cV.n
                wait("tensor", cV, v_rwt)
                tq = gq[0]
                for cb in range(CH):
                    em("tensor", lambda e, cb=cb, tq=tq: e.transpose(
                        tq[:, cb * 128:(cb + 1) * 128], RWT[:, cb * 128:(cb + 1) * 128], IDNF[:]), cT)
                t_rt = cT.n
                wait("vector", cT, t_rt)
                em("vector", lambda e, blk=blk, tq=tq: e.tensor_copy(
                    out=RWKVFM.ap()[:, :].rearrange("p (c w) -> p c w", c=CH)[:, :, blk * 128:(blk + 1) * 128],
                    in_=tq.ap()[:, :].rearrange("p (c w) -> p c w", c=CH)), cV)

        v_rwkv = cV.n
        _tap("rwkv_a%d" % a, lambda: RWKVFM[:])

        wait("sync", cT, cT.n)
        em("sync", dma_w8(WOUTV[:, 0:CH * C], wob_s), cWD, 16)
        wd_wo = cWD.n
        wait("tensor", cV, v_rwkv)
        wait("tensor", cS)
        wait("tensor", cWD, wd_wo)
        for co in range(CH):
            q = QS[co % 2]
            for ci in range(CH):
                em("tensor", lambda e, co=co, ci=ci, q=q: e.matmul(
                    q[:, 0:512], WOUTV[:, ci * C + co * 128:ci * C + co * 128 + 128],
                    RWKVFM[:, ci * SBA:(ci + 1) * SBA],
                    start=(ci == 0), stop=(ci == CH - 1)), cT)
            t_wo = cT.n
            wait("scalar", cT, t_wo)
            em("scalar", lambda e, co=co, q=q: e.activation(
                out=FINALW[:, co * SBA:(co + 1) * SBA], in_=q[:, 0:512], func=AF.Copy), cS)
            s_fw = cS.n
            wait("vector", cT, t_wo)
            wait("vector", cS, s_fw)
            em("vector", lambda e, co=co, q=q: e.tensor_tensor(
                out=X1BF[:, co * SBA:(co + 1) * SBA], in0=q[:, 0:512],
                in1=XNFM[:, co * SBA:(co + 1) * SBA], op=OP.add), cV)
            if co >= 1:
                wait("tensor", cS)
                wait("tensor", cV)
        v_x1 = cV.n
        _tap("x1_a%d" % a, lambda: X1BF[:])
        _tap("att_a%d" % a, lambda: FINALW[:])

        wait("vector", cV, v_x1)
        wait("tensor", cS)
        wait("tensor", cV)
        t_mm2 = {}
        for c in range(CH):
            gb = GBUF if c % 2 == 0 else XR2J
            if c >= 2:
                wait("vector", cT, t_mm2[c - 2])
            em("vector", lambda e, c=c, gb=gb: e.tensor_tensor(
                out=gb[:, 0:SBA], in0=X1BF[:, c * SBA:(c + 1) * SBA],
                in1=X1BF[:, c * SBA:(c + 1) * SBA], op=OP.mult), cV)
            v_sq = cV.n
            wait("tensor", cV, v_sq)
            em("tensor", lambda e, c=c, gb=gb: e.matmul(
                QS[2][0:1, 0:512], ONESC[:], gb[:, 0:SBA], start=(c == 0), stop=(c == CH - 1)), cT)
            t_mm2[c] = cT.n
        t_ms2 = cT.n
        wait("scalar", cT, t_ms2)
        em("scalar", lambda e: e.activation(out=OUTSB[0:1, 0:512], in_=QS[2][0:1, 0:512],
                                            func=AF.Ln, scale=1.0 / C, bias=CBIA[0:1, 1:2]), cS)
        bar("scalar")
        em("scalar", lambda e: e.activation(out=ZB[0:1, 0:512], in_=OUTSB[0:1, 0:512],
                                            func=AF.Exp, scale=-0.5), cS)
        s_rs2 = cS.n
        wait("tensor", cS, s_rs2)
        for t in range(NTIL):
            em("tensor", lambda e, t=t: e.matmul(QS[3][:, t * 128:(t + 1) * 128], ONES[:],
                                                 ZB[0:1, t * 128:(t + 1) * 128], start=True, stop=True), cT)
        t_r2b = cT.n
        wait("vector", cT, t_r2b)
        em("vector", lambda e: e.tensor_copy(out=RSTD2B[:], in_=QS[3][:, 0:SBA]), cV)
        v_r2b = cV.n
        _tap("rstd2_a%d" % a, lambda: RSTD2B[:])

        h2v = H2FM.ap()[:, :].rearrange("p (c w) -> p c w", w=SBA + 1)
        if a == 0:
            em("vector", lambda e, h2v=h2v: e.memset(h2v[:, :, 0:1], 0.0), cV)
        else:
            em("vector", lambda e, h2v=h2v: e.tensor_copy(out=h2v[:, :, 0:1], in_=h2v[:, :, SBA:SBA + 1]), cV)
        wait("vector", cV, v_r2b)
        for c in range(CH):
            h2b = lambda c=c: H2FM[:, c * (SBA + 1) + 1:(c + 1) * (SBA + 1)]
            em("vector", lambda e, c=c, h2b=h2b: e.tensor_tensor(
                out=h2b(), in0=X1BF[:, c * SBA:(c + 1) * SBA], in1=RSTD2B[:], op=OP.mult), cV)
            em("vector", lambda e, c=c, h2b=h2b: e.tensor_scalar(
                out=h2b(), in0=h2b(), scalar1=PC["ln2"][:, c:c + 1], scalar2=None, op0=OP.mult), cV)
            em("vector", lambda e, c=c, h2b=h2b: e.tensor_tensor(
                out=DBUF[:, c * SBA:(c + 1) * SBA], in0=h2b(),
                in1=H2FM[:, c * (SBA + 1):(c + 1) * (SBA + 1) - 1], op=OP.subtract), cV)
            em("vector", lambda e, c=c: e.scalar_tensor_tensor(
                out=XK2[:, c * SBA:(c + 1) * SBA], in0=DBUF[:, c * SBA:(c + 1) * SBA],
                scalar=PC["tmk2"][:, c:c + 1],
                in1=H2FM[:, c * (SBA + 1):(c + 1) * (SBA + 1) - 1],
                op0=OP.mult, op1=OP.add), cV)
        v_xk2 = cV.n
        _tap("xk2_a%d" % a, lambda: XK2[:])

        wait("sync", cT, cT.n)
        em("sync", dma_w8(WOUTV[:, 0:CH * C], wr2b_s), cWD, 16)
        wd_r2 = cWD.n
        wait("tensor", cWD, wd_r2)
        wait("tensor", cV, v_r2b)
        for half in range(2):
            for ci in range(CH):
                wait("vector", cT, cT.n)
                em("vector", lambda e, ci=ci: e.scalar_tensor_tensor(
                    out=XR2J[:, 0:SBA], in0=DBUF[:, ci * SBA:(ci + 1) * SBA],
                    scalar=PC["tmr2"][:, ci:ci + 1],
                    in1=H2FM[:, ci * (SBA + 1):(ci + 1) * (SBA + 1) - 1],
                    op0=OP.mult, op1=OP.add), cV)
                v_xr = cV.n
                wait("tensor", cV, v_xr)
                for cg in range(4):
                    co = half * 4 + cg
                    em("tensor", lambda e, co=co, ci=ci, cg=cg: e.matmul(
                        QS[cg][:, 0:512], WOUTV[:, ci * C + co * 128:ci * C + co * 128 + 128],
                        XR2J[:, 0:SBA],
                        start=(ci == 0), stop=(ci == CH - 1)), cT)
            t_r2 = cT.n
            wait("scalar", cT, t_r2)
            for cg in range(4):
                co = half * 4 + cg
                em("scalar", lambda e, co=co, cg=cg: e.activation(
                    out=R2TH[:, co * SBA:(co + 1) * SBA], in_=QS[cg][:, 0:512],
                    func=AF.Tanh, scale=0.5), cS)
            wait("tensor", cS)
        s_r2 = cS.n
        _tap("r2th_a%d" % a, lambda: R2TH[:])

        k2base = 0
        wk2v = wk2_in.rearrange("(cc p) q -> p cc q", p=128)
        w1v = w1_in.rearrange("(cc p) q -> p cc q", p=128)
        bmap = [(0, 512), (1, 0), (1, 512), (2, 0), (2, 512), (3, 0), (3, 512)]
        GBp = (GBUF, XR2J)
        s_rl = {}
        v_sq = {}
        t_k2 = {}
        def cm_slabs(g):
            gp = g % 2
            wait("sync", cT, cT.n)
            em("sync", lambda e, g=g, gp=gp: e.dma_start(
                out=WSA[:, gp * 4096:(gp + 1) * 4096].rearrange("p (b q) -> p b q", b=CH),
                in_=wk2b_s[:, :, g * 512:g * 512 + 512]), cWD, 16)
            em("sync", lambda e, g=g, gp=gp: e.dma_start(
                out=WSB[:, gp * 4 * C:(gp + 1) * 4 * C].rearrange("p (a c) -> p a c", a=4),
                in_=wv2b_s[:, g * 4:g * 4 + 4, :]), cWD, 16)
            wd_grp[g] = cWD.n
        wd_grp = {}
        cm_slabs(0)
        for i in range(ICH + 2):
            pi = i % 2
            if i < ICH:
                if i % 4 == 0:
                    wait("tensor", cWD, wd_grp[i // 4])
                if i == 30:
                    wait("sync", cT, cT.n)
                    em("sync", lambda e: e.dma_start(
                        out=WSB[:, 0:4096].rearrange("p (j c) -> p j c", j=ICH),
                        in_=wv2b_s[:, :, 896:1024]), cWD, 16)
                    wd_cm7 = cWD.n
                ip = (i // 4) % 2
                io = i % 4
                if i == 0:
                    wait("tensor", cV, v_xk2)
                    wait("tensor", cS, s_r2)
                else:
                    wait("tensor", cS, s_rl[i - 1])
                for ci in range(CH):
                    em("tensor", lambda e, i=i, ci=ci, ip=ip, io=io: e.matmul(
                        QS[0][:, 0:512],
                        WSA[:, ip * 4096 + ci * 512 + io * 128:ip * 4096 + ci * 512 + io * 128 + 128],
                        XK2[:, ci * SBA:(ci + 1) * SBA],
                        start=(ci == 0), stop=(ci == CH - 1)), cT)
                t_k2[i] = cT.n
            j = i - 2
            if 0 <= j < ICH:
                jp = (j // 4) % 2
                jo = j % 4
                wait("tensor", cV, v_sq[j])
                for co in range(7):
                    sl, qo = bmap[co]
                    em("tensor", lambda e, j=j, co=co, sl=sl, qo=qo, jp=jp, jo=jo: e.matmul(
                        QS[sl][:, qo:qo + 512],
                        WSB[:, jp * 4 * C + jo * C + co * 128:jp * 4 * C + jo * C + co * 128 + 128],
                        K2STG[:, k2base + j * SBA:k2base + (j + 1) * SBA],
                        start=(j == 0), stop=(j == ICH - 1)), cT)
            if i % 4 == 1 and i // 4 + 1 < ICH // 4:
                cm_slabs(i // 4 + 1)
            if i < ICH:
                wait("scalar", cT, t_k2[i])
                if i >= 2:
                    wait("scalar", cV, v_sq[i - 2])
                em("scalar", lambda e, pi=pi: e.activation(
                    out=GBp[pi][:, 0:SBA], in_=QS[0][:, 0:512], func=AF.Relu), cS)
                s_rl[i] = cS.n
                wait("vector", cS, s_rl[i])
                em("vector", lambda e, i=i, pi=pi: e.tensor_tensor(
                    out=K2STG[:, k2base + i * SBA:k2base + (i + 1) * SBA],
                    in0=GBp[pi][:, 0:SBA], in1=GBp[pi][:, 0:SBA], op=OP.mult), cV)
                v_sq[i] = cV.n
        _tap("k2_a%d" % a, lambda: K2STG[:])

        wait("tensor", cWD, wd_cm7)
        wait("tensor", cS, s_rl[ICH - 1])
        for j in range(ICH):
            em("tensor", lambda e, j=j: e.matmul(
                QS[0][:, 0:512], WSB[:, j * 128:(j + 1) * 128],
                K2STG[:, k2base + j * SBA:k2base + (j + 1) * SBA],
                start=(j == 0), stop=(j == ICH - 1)), cT)
        t_cm = cT.n
        v_cm7 = None
        for co in [7, 0, 1, 2, 3, 4, 5, 6]:
            if co < 7:
                sl, qo = bmap[co]
            else:
                sl, qo = 0, 0
            wait("vector", cT, t_cm)
            em("vector", lambda e, co=co, sl=sl, qo=qo: e.scalar_tensor_tensor(
                out=GLB[:], in0=R2TH[:, co * SBA:(co + 1) * SBA], scalar=1.0,
                in1=QS[sl][:, qo:qo + 512], op0=OP.add, op1=OP.mult), cV)
            em("vector", lambda e, co=co: e.scalar_tensor_tensor(
                out=FINALW[:, co * SBA:(co + 1) * SBA], in0=GLB[:], scalar=0.5,
                in1=FINALW[:, co * SBA:(co + 1) * SBA], op0=OP.mult, op1=OP.add), cV)
            if co == 7:
                v_cm7 = cV.n
        v_cm = cV.n
        _tap("cm_a%d" % a, lambda: FINALW[:])

        ITp = (ITILE, ITILE2)
        IQFp = (Z1B, IQF2)
        FLBp = (FLB, FLB2)
        GLBp = (GLB, GLB2)
        GHBp = (GHB, GHB2)
        bmap = [(0, 512), (1, 0), (1, 512), (2, 0), (2, 512), (3, 0), (3, 512)]
        v_iq = {}
        v_fi = {}
        v_i2 = {}
        s_iqf = {}
        s_flb = {}
        s_g = {}
        s_hq = {}
        g_dd = {}
        t_w1 = {}
        def ffn_slabs(g):
            gp = g % 2
            wait("sync", cT, cT.n)
            em("sync", lambda e, g=g, gp=gp: e.dma_start(
                out=WSA[:, gp * 4096:(gp + 1) * 4096].rearrange("p (b q) -> p b q", b=CH),
                in_=w1b_s[:, :, g * 512:g * 512 + 512]), cWD, 16)
            em("sync", lambda e, g=g, gp=gp: e.dma_start(
                out=WSB[:, gp * 4 * C:(gp + 1) * 4 * C].rearrange("p (a c) -> p a c", a=4),
                in_=w2b_s[:, g * 4:g * 4 + 4, :]), cWD, 16)
            wd_grp[g] = cWD.n
        wd_grp = {}
        ffn_slabs(0)
        for i in range(ICH + 2):
            pi = i % 2
            if i < ICH:
                if i % 4 == 0:
                    wait("tensor", cWD, wd_grp[i // 4])
                if i == 30:
                    wait("sync", cT, cT.n)
                    em("sync", lambda e: e.dma_start(
                        out=WSB[:, 0:4096].rearrange("p (j c) -> p j c", j=ICH),
                        in_=w2b_s[:, :, 896:1024]), cWD, 16)
                    wd_co7 = cWD.n
                ip = (i // 4) % 2
                io = i % 4
                if i == 0:
                    wait("tensor", cV, v_cm7)
                else:
                    wait("tensor", cV, v_iq[i - 1])
                for ci in range(CH):
                    em("tensor", lambda e, i=i, ci=ci, ip=ip, io=io: e.matmul(
                        QS[0][:, 0:512],
                        WSA[:, ip * 4096 + ci * 512 + io * 128:ip * 4096 + ci * 512 + io * 128 + 128],
                        XNFM[:, ci * SBA:(ci + 1) * SBA],
                        start=(ci == 0), stop=(ci == CH - 1)), cT)
                t_w1[i] = cT.n
            j = i - 2
            if 0 <= j < ICH:
                jp = (j // 4) % 2
                jo = j % 4
                wait("tensor", cS, s_hq[j])
                if j == 0:
                    wait("tensor", cV, v_cm)
                for co in range(7):
                    sl, qo = bmap[co]
                    em("tensor", lambda e, j=j, co=co, sl=sl, qo=qo, jp=jp, jo=jo: e.matmul(
                        QS[sl][:, qo:qo + 512],
                        WSB[:, jp * 4 * C + jo * C + co * 128:jp * 4 * C + jo * C + co * 128 + 128],
                        K2STG[:, k2base + j * SBA:k2base + (j + 1) * SBA],
                        start=(j == 0), stop=(j == ICH - 1)), cT)
            if i % 4 == 1 and i // 4 + 1 < ICH // 4:
                ffn_slabs(i // 4 + 1)
            h = i - 1
            if i < ICH:
                wait("vector", cT, t_w1[i])
                if i >= 1:
                    wait("vector", cS, s_g[i - 1])
                em("vector", lambda e, i=i: e.tensor_scalar(
                    out=ITILE[:], in0=QS[0][:, 0:512], scalar1=256.0, scalar2=B1S[:, i:i + 1],
                    op0=OP.mult, op1=OP.add), cV)
                v_iq[i] = cV.n
                if i == 0 and ("iq0_a%d" % a) in taps:
                    tq0 = nc.dram_tensor("tap_iq0_a%d" % a, [128, SBA], I32, kind="ExternalOutput").ap()
                    wait("sync", cV)
                    em("sync", lambda e, tq0=tq0: e.dma_start(out=tq0, in_=ITILE[:]), cOUT, 16)
                    wait("vector", cOUT, cOUT.n)
                wait("scalar", cV, v_iq[i])
                em("scalar", lambda e, pi=pi: e.activation(
                    out=IQFp[pi][:], in_=ITILE[:], func=AF.Copy, scale=1.0), cS)
                s_iqf[i] = cS.n
            if 0 <= h < ICH:
                ph = h % 2
                wait("vector", cS, s_flb[h])
                em("vector", lambda e, ph=ph: e.scalar_tensor_tensor(
                    out=IQFp[ph][:], in0=IQFp[ph][:], scalar=0.015625, in1=FLBp[ph][:],
                    op0=OP.mult, op1=OP.subtract), cV)
            if i < ICH:
                wait("vector", cS, s_iqf[i])
                em("vector", lambda e, pi=pi: e.tensor_scalar(
                    out=ITILE[:], in0=IQFp[pi][:], scalar1=0.015625, scalar2=15.5078125,
                    op0=OP.mult, op1=OP.add), cV)
                v_fi[i] = cV.n
                wait("scalar", cV, v_fi[i])
                em("scalar", lambda e, pi=pi: e.activation(
                    out=FLBp[pi][:], in_=ITILE[:], func=AF.Copy, scale=1.0), cS)
                s_flb[i] = cS.n
                if i >= 2:
                    wait("scalar", cG, g_dd[i - 2])
                em("scalar", lambda e, pi=pi: e.activation(
                    out=GLBp[pi][:], in_=ITILE[:], func=AF.Gelu, scale=0.25, bias=CBIA[:, 2:3]), cS)
                em("scalar", lambda e, pi=pi: e.activation(
                    out=GHBp[pi][:], in_=ITILE[:], func=AF.Gelu, scale=0.25, bias=CBIA[:, 3:4]), cS)
                s_g[i] = cS.n
                wait("gpsimd", cS, s_g[i])
                em("gpsimd", lambda e, pi=pi: e.tensor_tensor(
                    out=GHBp[pi][:], in0=GHBp[pi][:], in1=GLBp[pi][:], op=OP.subtract), cG)
                g_dd[i] = cG.n
            if 0 <= h < ICH:
                ph = h % 2
                wait("vector", cG, g_dd[h])
                em("vector", lambda e, ph=ph: e.scalar_tensor_tensor(
                    out=GHBp[ph][:], in0=IQFp[ph][:], scalar=16.0, in1=GHBp[ph][:],
                    op0=OP.add, op1=OP.mult), cV)
                em("vector", lambda e, ph=ph: e.tensor_tensor(
                    out=GLBp[ph][:], in0=GLBp[ph][:], in1=GHBp[ph][:], op=OP.add), cV)
                if h >= 1:
                    wait("vector", cS, s_hq[h - 1])
                em("vector", lambda e, ph=ph: e.tensor_scalar(
                    out=ITILE2[:], in0=GLBp[ph][:], scalar1=256.0, scalar2=None, op0=OP.mult), cV)
                v_i2[h] = cV.n
                wait("scalar", cV, v_i2[h])
                em("scalar", lambda e, h=h: e.activation(
                    out=K2STG[:, k2base + h * SBA:k2base + (h + 1) * SBA],
                    in_=ITILE2[:], func=AF.Copy, scale=1.0 / 256.0), cS)
                s_hq[h] = cS.n
        _tap("hq_a%d" % a, lambda: K2STG[:])

        wait("tensor", cWD, wd_co7)
        wait("tensor", cV, v_iq[ICH - 1])
        for j in range(ICH):
            em("tensor", lambda e, j=j: e.matmul(
                QS[0][:, 0:512], WSB[:, j * 128:(j + 1) * 128],
                K2STG[:, k2base + j * SBA:k2base + (j + 1) * SBA],
                start=(j == 0), stop=(j == ICH - 1)), cT)
        t_w2 = cT.n
        for co in range(CH):
            if co < 7:
                sl, qo = bmap[co]
            else:
                sl, qo = 0, 0
            wait("vector", cT, t_w2)
            em("vector", lambda e, co=co, sl=sl, qo=qo: e.tensor_scalar(
                out=ITILE[:], in0=QS[sl][:, qo:qo + 512], scalar1=256.0, scalar2=B2S[:, co:co + 1],
                op0=OP.mult, op1=OP.add), cV)
            em("vector", lambda e, co=co: e.scalar_tensor_tensor(
                out=FINALW[:, co * SBA:(co + 1) * SBA], in0=ITILE[:], scalar=1.0 / 256.0,
                in1=FINALW[:, co * SBA:(co + 1) * SBA], op0=OP.mult, op1=OP.add), cV)
        v_ffn = cV.n
        _tap("final_a%d" % a, lambda: FINALW[:])

        fv = FINALW.ap()[:, :].rearrange("p (c w) -> p c w", c=CH)
        for t in range(NTIL):
            g = g0 + t
            gt = tok0 + t * 128
            sl = t % 2
            wait("sync", cV)
            em("sync", lambda e, gt=gt: e.dma_start(out=XT[:], in_=x_in[gt:gt + 128, :]), cLD, 16)
            ld_x = cLD.n
            wait("vector", cLD, ld_x)
            wait("vector", cOUT, cOUT.n)
            em("vector", lambda e, g=g: e.tensor_scalar(
                out=OUTSB[:], in0=XT[:],
                scalar1=MC[:, g:g + 1], scalar2=RSTD[:, g:g + 1],
                op0=OP.subtract, op1=OP.mult), cV)
            em("vector", lambda e: e.tensor_tensor(
                out=OUTSB[:], in0=OUTSB[:], in1=XT[:], op=OP.add), cV)
            v_xpn = cV.n
            q = QS[2 + (t % 2)]
            wait("tensor", cV, v_xpn)
            for cb in range(CH):
                em("tensor", lambda e, cb=cb, t=t, q=q, fv=fv: e.transpose(
                    q[:, cb * 128:(cb + 1) * 128], fv[:, cb, t * 128:(t + 1) * 128], IDNF[:]), cT)
            t_f = cT.n
            wait("vector", cT, t_f)
            em("vector", lambda e, q=q: e.tensor_tensor(
                out=OUTSB[:], in0=OUTSB[:], in1=q[:], op=OP.add), cV)
            v_out = cV.n
            wait("sync", cV, v_out)
            em("sync", lambda e, gt=gt: e.dma_start(
                out=out_t[gt:gt + 128, :], in_=OUTSB[:]), cOUT, 16)

    wait("sync", cOUT, cOUT.n)

    with nc.Block() as block:
        @block.sync
        def _(e):
            for f in sched["sync"]:
                f(e)

        @block.gpsimd
        def _(e):
            for f in sched["gpsimd"]:
                f(e)

        @block.vector
        def _(e):
            for f in sched["vector"]:
                f(e)

        @block.scalar
        def _(e):
            for f in sched["scalar"]:
                f(e)

        @block.tensor
        def _(e):
            for f in sched["tensor"]:
                f(e)

    ctx.close()
    return nc


def _host_inputs(inputs):
    x = np.ascontiguousarray(np.asarray(inputs["L0_attn_x"], dtype=np.float32))
    row = lambda k: np.ascontiguousarray(np.asarray(inputs[k], dtype=np.float32).reshape(1, -1))
    base = {
        "wk": np.ascontiguousarray(np.asarray(inputs["L0_attn_W_key"], np.float32)),
        "wv": np.ascontiguousarray(np.asarray(inputs["L0_attn_W_value"], np.float32)),
        "wr": np.ascontiguousarray(np.asarray(inputs["L0_attn_W_recept"], np.float32)),
        "wo": np.ascontiguousarray(np.asarray(inputs["L0_attn_W_out"], np.float32)),
        "wr2": np.ascontiguousarray(np.asarray(inputs["L0_attn_W_recept2"], np.float32)),
        "wk2": np.ascontiguousarray(np.asarray(inputs["L0_attn_W_key2"], np.float32)),
        "wv2": np.ascontiguousarray(np.asarray(inputs["L0_attn_W_value2"], np.float32)),
        "w1": np.ascontiguousarray(np.asarray(inputs["L0_ffn_w1"], np.float32)),
        "w2": np.ascontiguousarray(np.asarray(inputs["L0_ffn_w2"], np.float32)),
        "ln1": row("L0_attn_ln1_w"), "ln2": row("L0_attn_ln2_w"),
        "tmk": row("L0_attn_time_mix_k"), "tmv": row("L0_attn_time_mix_v"),
        "tmr": row("L0_attn_time_mix_r"), "tmk2": row("L0_attn_time_mix_k2"),
        "tmr2": row("L0_attn_time_mix_r2"),
        "b1": row("L0_ffn_b1"), "b2": row("L0_ffn_b2"),
        "tdb": np.tile(row("L0_attn_time_decay"), (128, 1)),
        "ub": np.tile(row("L0_attn_time_first"), (128, 1)),
        "ltc": np.triu(np.ones((128, 128), np.float32), k=1),
        "ones1": np.ones((1, 128), np.float32),
        "idnf": np.eye(128, dtype=np.float32),
        "onesc": np.ones((128, 1)).astype(ml_dtypes.bfloat16),
        "onesr": np.ones((128, 1), np.float32),
        "cbias": np.tile(np.array([[1e-5, 1e-6, -4.0, -3.75, 0, 0, 0, 0]], np.float32), (128, 1)),
    }
    in_maps = []
    for b in range(NCORE):
        m = dict(base)
        m["x"] = np.ascontiguousarray(x[b])
        in_maps.append(m)
    return in_maps


def kernel(**inputs):
    nc = _build(taps=tuple(_TAPS))
    in_maps = _host_inputs(inputs)
    res = run_bass_kernel_spmd(nc, in_maps, core_ids=list(range(NCORE)))
    out = np.stack([np.asarray(r["out"]) for r in res.results], axis=0)
    return out.astype(np.float32)


# version 06
# speedup vs baseline: 2.0153x; 1.1381x over previous
import numpy as np
import ml_dtypes
import concourse.bass as bass
import concourse.mybir as mybir
from concourse.alu_op_type import AluOpType as OP
from concourse.bass_utils import run_bass_kernel_spmd

F32 = mybir.dt.float32
F32R = mybir.dt.float32r
BF16 = mybir.dt.bfloat16
I32 = mybir.dt.int32
AF = mybir.ActivationFunctionType

T, C, I = 4096, 1024, 4096
CH = C // 128
ICH = I // 128
SBA = 512
NSB = T // SBA
NTIL = SBA // 128
NCORE = 8

ENGS = ("sync", "gpsimd", "vector", "scalar", "tensor")

_TAPS = []


def _build(taps=(), detect_races=False):
    nc = bass.Bass("TRN2", target_bir_lowering=False, detect_race_conditions=detect_races)
    dr = lambda n, sh, dt: nc.dram_tensor(n, sh, dt, kind="ExternalInput").ap()

    x_in = dr("x", [T, C], F32)
    wk_in = dr("wk", [C, C], F32)
    wv_in = dr("wv", [C, C], F32)
    wr_in = dr("wr", [C, C], F32)
    wo_in = dr("wo", [C, C], F32)
    wr2_in = dr("wr2", [C, C], F32)
    wk2_in = dr("wk2", [C, I], F32)
    wv2_in = dr("wv2", [I, C], F32)
    w1_in = dr("w1", [C, I], F32)
    w2_in = dr("w2", [I, C], F32)
    vecs = {n: dr(n, [1, C], F32) for n in
            ("ln1", "ln2", "tmk", "tmv", "tmr", "tmk2", "tmr2", "b2")}
    b1_in = dr("b1", [1, I], F32)
    tdb_in = dr("tdb", [128, C], F32)
    ub_in = dr("ub", [128, C], F32)
    ltc_in = dr("ltc", [128, 128], F32R)
    ones_in = dr("ones1", [1, 128], F32R)
    idnf_in = dr("idnf", [128, 128], F32)
    onesc_in = dr("onesc", [128, 1], BF16)
    onesr_in = dr("onesr", [128, 1], F32R)
    cbias_in = dr("cbias", [128, 8], F32)

    out_t = nc.dram_tensor("out", [T, C], F32, kind="ExternalOutput").ap()

    drb = lambda n, sh: nc.dram_tensor(n, sh, BF16, kind="Internal").ap()
    kb_s = drb("kb_s", [128, CH, C])
    vb_s = drb("vb_s", [128, CH, C])
    rb_s = drb("rb_s", [128, CH, C])
    wob_s = drb("wob_s", [128, CH, C])
    wr2b_s = drb("wr2b_s", [128, CH, C])
    wk2b_s = drb("wk2b_s", [128, CH, I])
    w1b_s = drb("w1b_s", [128, CH, I])
    wv2b_s = drb("wv2b_s", [128, ICH, C])
    w2b_s = drb("w2b_s", [128, ICH, C])

    from contextlib import ExitStack
    ctx = ExitStack()

    _off = [16512]
    handles = {}

    def place(name, shape, dtype, at=None):
        nbytes = int(np.prod(shape[1:])) * mybir.dt.size(dtype)
        if at is None:
            at = _off[0]
            _off[0] = (at + nbytes + 31) & ~31
        h = nc.alloc_sbuf_tensor_at(name, shape, dtype, offset=at)
        handles[name] = (h, at, nbytes)
        return h

    LTC = place("LTC", [128, 128], F32R)
    ONES = place("ONES", [1, 128], F32R)
    IDNF = place("IDNF", [128, 128], F32)
    ONESC = place("ONESC", [128, 1], BF16)
    ONESR = place("ONESR", [128, 1], F32R)
    WCL = place("WCL", [128, C], F32)
    UB = place("UB", [128, C], F32)
    EMIN = place("EMIN", [128, C], F32)
    ESH = place("ESH", [128, C], F32)
    NIOT = place("NIOT", [128, 1], F32)
    IOTM1 = place("IOTM1", [128, 1], F32)
    EW2R = place("EW2R", [1, C], F32)
    CARRY = place("CARRY", [1, 2 * C], F32R)
    PC = {}
    for nme in ("ln1", "ln2", "tmk", "tmv", "tmr", "tmk2", "tmr2"):
        PC[nme] = place("PC_" + nme, [128, CH], F32)
    B1S = place("B1S", [128, ICH], F32)
    B2S = place("B2S", [128, CH], F32)
    SUMS = place("SUMS", [128, 32], F32)
    SQS = place("SQS", [128, 32], F32)
    MC = place("MC", [128, 32], F32)
    RSTD = place("RSTD", [128, 32], F32)
    RSTD1 = place("RSTD1", [128, 32], F32)
    TCA = place("TCA", [128, 4], F32)
    TCB = place("TCB", [128, 4], F32)
    RSTD1B = place("RSTD1B", [128, SBA], BF16)
    RSTD2B = place("RSTD2B", [128, SBA], BF16)
    XNFM = place("XNFM", [128, CH * SBA], BF16)
    X1BF = place("X1BF", [128, CH * SBA], BF16)
    FINALW = place("FINALW", [128, CH * SBA], F32)
    HFM = place("HFM", [128, CH * (SBA + 1)], BF16)
    DBUF = place("DBUF", [128, CH * SBA], BF16)
    RWKVFM = place("RWKVFM", [128, CH * SBA], BF16)
    H2FM = place("H2FM", [128, CH * (SBA + 1)], BF16)
    MIXT = place("MIXT", [128, 2 * 256], BF16)
    XR2J = place("XR2J", [128, SBA], BF16)
    GBUF = place("GBUF", [128, SBA], BF16)
    K2STG = place("K2STG", [128, ICH * SBA], BF16)
    CBIA = place("CBIA", [128, 8], F32)
    XT = place("XT", [128, C], F32)
    OUTSB = place("OUTSB", [128, C], F32)

    ovl0 = _off[0]

    def placeA(name, shape, dtype):
        return place(name, shape, dtype)

    KTM = placeA("KTM", [128, 2 * C], F32)
    VTM = placeA("VTM", [128, 2 * C], F32)
    ERT = placeA("ERT", [128, 2 * C], F32)
    KCB = placeA("KCB", [128, C], F32)
    EKT = placeA("EKT", [128, 2 * C], F32)
    EKB = placeA("EKB", [128, C], F32)
    ZB = placeA("ZB", [128, 2 * C], F32R)
    APR = placeA("APR", [128, 2 * C], F32)
    EKTV = placeA("EKTV", [128, C], F32)
    DEN = placeA("DEN", [128, C], F32)
    RWT = placeA("RWT", [128, C], F32)
    ovl_end_A = _off[0]

    _off[0] = ovl0
    WSA = place("WSA", [128, 2 * 4096], BF16)
    WSB = place("WSB", [128, 2 * 4 * C], BF16)
    XK2 = place("XK2", [128, CH * SBA], BF16)
    R2TH = place("R2TH", [128, CH * SBA], BF16)
    ITILE = place("ITILE", [128, SBA], I32)
    ITILE2 = place("ITILE2", [128, SBA], I32)
    Z1B = place("Z1B", [128, SBA], F32)
    IQF2 = place("IQF2", [128, SBA], F32)
    FLB = place("FLB", [128, SBA], F32)
    FLB2 = place("FLB2", [128, SBA], F32)
    GLB = place("GLB", [128, SBA], F32)
    GLB2 = place("GLB2", [128, SBA], F32)
    GHB = place("GHB", [128, SBA], F32)
    GHB2 = place("GHB2", [128, SBA], F32)
    ovl_end_B = _off[0]
    _off[0] = max(ovl_end_A, ovl_end_B)

    CROW = OUTSB
    total_kb = (_off[0] - 16512) / 1024.0
    assert _off[0] <= 229344, f"SBUF over budget: {total_kb:.1f} KB used, end={_off[0]}"

    QS = [ctx.enter_context(nc.psum_tensor("Q%d" % i, [128, 1024], F32)) for i in range(4)]

    class Ctr:
        def __init__(s, name):
            s.h = ctx.enter_context(nc.semaphore(name))
            s.n = 0
    cLD = Ctr("cLD")
    cWD = Ctr("cWD")
    cV = Ctr("cV")
    cS = Ctr("cS")
    cT = Ctr("cT")
    cG = Ctr("cG")
    cOUT = Ctr("cOUT")

    sched = {e: [] for e in ENGS}

    def em(eng, make, inc=None, k=1):
        if inc is not None:
            inc.n += k
        def f(e, make=make, inc=inc, k=k):
            r = make(e)
            if inc is not None:
                if r is None:
                    r = e.nop()
                r.then_inc(inc.h, k)
        sched[eng].append(f)

    def wait(eng, ctr, n=None):
        v = ctr.n if n is None else n
        if v > 0:
            sched[eng].append(lambda e, h=ctr.h, v=v: e.wait_ge(h, v))

    _selfc = {"vector": None, "scalar": None}

    def bar(eng):
        c = {"vector": cV, "scalar": cS}[eng]
        wait(eng, c, c.n)

    tapv = {}

    def _tap(name, apf):
        if name in taps:
            ap0 = apf()
            sh = list(ap0.shape)
            t = nc.dram_tensor("tap_" + name, sh, ap0.dtype, kind="ExternalOutput").ap()
            wait("sync", cV)
            wait("sync", cS)
            wait("sync", cG)
            wait("sync", cT)
            em("sync", lambda e, t=t, a=ap0: e.dma_start(out=t, in_=a), cOUT, 16)

    em("sync", lambda e: e.dma_start(out=LTC[:], in_=ltc_in), cLD, 16)
    em("sync", lambda e: e.dma_start(out=ONES[:], in_=ones_in), cLD, 16)
    em("sync", lambda e: e.dma_start(out=IDNF[:], in_=idnf_in), cLD, 16)
    em("sync", lambda e: e.dma_start(out=ONESC[:], in_=onesc_in), cLD, 16)
    em("sync", lambda e: e.dma_start(out=ONESR[:], in_=onesr_in), cLD, 16)
    em("sync", lambda e: e.dma_start(out=CBIA[:], in_=cbias_in), cLD, 16)
    em("sync", lambda e: e.dma_start(out=WCL[:], in_=tdb_in), cLD, 16)
    em("sync", lambda e: e.dma_start(out=UB[:], in_=ub_in), cLD, 16)
    for nme in ("ln1", "ln2", "tmk", "tmv", "tmr", "tmk2", "tmr2"):
        em("sync", lambda e, d=PC[nme], s=vecs[nme]: e.dma_start(
            out=d.ap()[:, :].rearrange("p (x c) -> p x c", x=1), in_=s.rearrange("x (c p) -> p x c", p=128),
            allow_slow_non_contiguous=True), cLD, 16)
    em("sync", lambda e: e.dma_start(out=B1S.ap()[:, :].rearrange("p (x c) -> p x c", x=1),
                                     in_=b1_in.rearrange("x (c p) -> p x c", p=128), allow_slow_non_contiguous=True), cLD, 16)
    em("sync", lambda e: e.dma_start(out=B2S.ap()[:, :].rearrange("p (x c) -> p x c", x=1),
                                     in_=vecs["b2"].rearrange("x (c p) -> p x c", p=128), allow_slow_non_contiguous=True), cLD, 16)
    init_ld = cLD.n

    em("gpsimd", lambda e: e.iota(NIOT[:], [[1, 1]], base=0, channel_multiplier=-1,
                                  allow_small_or_imprecise_dtypes=True), cG)
    em("gpsimd", lambda e: e.iota(IOTM1[:], [[1, 1]], base=-1, channel_multiplier=1,
                                  allow_small_or_imprecise_dtypes=True), cG)
    em("gpsimd", lambda e: e.memzero(CARRY[:]), cG)
    g_init = cG.n

    wait("scalar", cLD, init_ld)
    em("scalar", lambda e: e.activation(out=WCL[:], in_=WCL[:], func=AF.Exp), cS)
    wait("vector", cS)
    em("vector", lambda e: e.tensor_scalar(out=WCL[:], in0=WCL[:], scalar1=-1.0, scalar2=20.0,
                                           op0=OP.mult, op1=OP.min), cV)
    em("vector", lambda e: e.tensor_scalar(out=WCL[:], in0=WCL[:], scalar1=-20.0, scalar2=None,
                                           op0=OP.max), cV)
    wait("vector", cG, g_init)
    em("vector", lambda e: e.tensor_scalar(out=EMIN[:], in0=WCL[:], scalar1=NIOT[:, 0:1], scalar2=None,
                                           op0=OP.mult), cV)
    em("vector", lambda e: e.tensor_scalar(out=ESH[:], in0=WCL[:], scalar1=IOTM1[:, 0:1], scalar2=None,
                                           op0=OP.mult), cV)
    em("vector", lambda e: e.tensor_scalar(out=EW2R[:], in0=WCL[0:1, :], scalar1=128.0, scalar2=None,
                                           op0=OP.mult), cV)
    v_pre = cV.n
    wait("scalar", cV, v_pre)
    em("scalar", lambda e: e.activation(out=EMIN[:], in_=EMIN[:], func=AF.Exp), cS)
    em("scalar", lambda e: e.activation(out=ESH[:], in_=ESH[:], func=AF.Exp), cS)
    em("scalar", lambda e: e.activation(out=EW2R[:], in_=EW2R[:], func=AF.Exp), cS)
    s_pre = cS.n
    wait("vector", cS, s_pre)
    em("vector", lambda e: e.tensor_scalar(out=B1S[:], in0=B1S[:], scalar1=256.0, scalar2=None, op0=OP.mult), cV)
    em("vector", lambda e: e.tensor_scalar(out=B2S[:], in0=B2S[:], scalar1=256.0, scalar2=None, op0=OP.mult), cV)

    wkv_b = {"k": kb_s, "v": vb_s, "r": rb_s}
    WOUTV = K2STG

    def dma_w8(dst_ap, wb):
        def f(e, d=dst_ap, w=wb):
            return e.dma_start(out=d.rearrange("p (a c) -> p a c", a=CH),
                               in_=w[:, :, :])
        return f

    pre_chunks = []
    for src, dst in ((wk_in, kb_s), (wv_in, vb_s), (wr_in, rb_s), (wo_in, wob_s), (wr2_in, wr2b_s)):
        pre_chunks.append((src.rearrange("(cc p) c -> p cc c", p=128), dst[:, :, :]))
    wk2v0 = wk2_in.rearrange("(cc p) q -> p cc q", p=128)
    w1v0 = w1_in.rearrange("(cc p) q -> p cc q", p=128)
    for qc in range(4):
        pre_chunks.append((wk2v0[:, :, qc * 1024:(qc + 1) * 1024], wk2b_s[:, :, qc * 1024:(qc + 1) * 1024]))
    for qc in range(4):
        pre_chunks.append((w1v0[:, :, qc * 1024:(qc + 1) * 1024], w1b_s[:, :, qc * 1024:(qc + 1) * 1024]))
    for g in range(4):
        pre_chunks.append((wv2_in[g * 1024:(g + 1) * 1024, :].rearrange("(ii p) c -> p ii c", p=128),
                           wv2b_s[:, g * 8:(g + 1) * 8, :]))
    for g in range(4):
        pre_chunks.append((w2_in[g * 1024:(g + 1) * 1024, :].rearrange("(ii p) c -> p ii c", p=128),
                           w2b_s[:, g * 8:(g + 1) * 8, :]))
    bnc = lambda half: K2STG.ap()[:, half * 8192:(half + 1) * 8192].rearrange("p (a c) -> p a c", a=CH)
    pre_out = {}
    for n, (src, dst) in enumerate(pre_chunks):
        half = n % 2
        if n >= 2:
            wait("gpsimd", cOUT, pre_out[n - 2])
        em("gpsimd", lambda e, src=src, half=half: e.dma_start(out=bnc(half), in_=src), cWD, 16)
        g_in = cWD.n
        wait("sync", cWD, g_in)
        em("sync", lambda e, dst=dst, half=half: e.dma_start(out=dst, in_=bnc(half)), cOUT, 16)
        pre_out[n] = cOUT.n
    pre_done = cOUT.n

    wait("tensor", cLD, init_ld)
    wait("gpsimd", cLD, init_ld)
    for a in range(NSB):
        tok0 = a * SBA
        g0 = a * NTIL

        ld_tile = []
        for t in range(NTIL):
            gt = tok0 + t * 128
            wait("sync", cV)
            wait("sync", cS)
            em("sync", lambda e, gt=gt: e.dma_start(out=XT[:], in_=x_in[gt:gt + 128, :]), cLD, 16)
            ldn = cLD.n
            g = g0 + t
            wait("vector", cLD, ldn)
            em("vector", lambda e, g=g: e.tensor_reduce(
                out=SUMS[:, g:g + 1], in_=XT[:], axis=mybir.AxisListType.X, op=OP.add), cV)
            wait("scalar", cLD, ldn)
            wait("scalar", cOUT, cOUT.n)
            wait("scalar", cT, cT.n)
            wait("scalar", cV, cV.n)
            em("scalar", lambda e, g=g: e.activation(
                out=OUTSB[:], in_=XT[:], func=AF.Square), cS)
            s_sq = cS.n
            wait("vector", cS, s_sq)
            em("vector", lambda e, g=g: e.tensor_reduce(
                out=SQS[:, g:g + 1], in_=OUTSB[:], axis=mybir.AxisListType.X, op=OP.add), cV)
            wait("vector", cS)
            em("vector", lambda e, g=g: e.tensor_scalar(
                out=MC[:, g:g + 1], in0=SUMS[:, g:g + 1], scalar1=1.0 / C, scalar2=None, op0=OP.mult), cV)
            bar("vector")
            em("vector", lambda e, g=g, t=t: e.tensor_tensor(
                out=TCA[:, t:t + 1], in0=MC[:, g:g + 1], in1=MC[:, g:g + 1], op=OP.mult), cV)
            bar("vector")
            em("vector", lambda e, g=g, t=t: e.scalar_tensor_tensor(
                out=TCB[:, t:t + 1], in0=SQS[:, g:g + 1], scalar=1.0 / C, in1=TCA[:, t:t + 1],
                op0=OP.mult, op1=OP.subtract), cV)
            v_var = cV.n
            wait("scalar", cV, v_var)
            em("scalar", lambda e, t=t: e.activation(out=TCA[:, t:t + 1], in_=TCB[:, t:t + 1], func=AF.Ln, bias=CBIA[:, 0:1]), cS)
            bar("scalar")
            em("scalar", lambda e, g=g, t=t: e.activation(
                out=RSTD[:, g:g + 1], in_=TCA[:, t:t + 1], func=AF.Exp, scale=-0.5), cS)
            s_rstd = cS.n
            wait("vector", cS, s_rstd)
            em("vector", lambda e, g=g, t=t: e.tensor_tensor(
                out=TCA[:, t:t + 1], in0=RSTD[:, g:g + 1], in1=RSTD[:, g:g + 1], op=OP.mult), cV)
            bar("vector")
            em("vector", lambda e, t=t: e.tensor_tensor(
                out=TCA[:, t:t + 1], in0=TCB[:, t:t + 1], in1=TCA[:, t:t + 1], op=OP.mult), cV)
            v_ms1 = cV.n
            wait("scalar", cV, v_ms1)
            em("scalar", lambda e, t=t: e.activation(out=TCB[:, t:t + 1], in_=TCA[:, t:t + 1], func=AF.Ln, bias=CBIA[:, 1:2]), cS)
            bar("scalar")
            em("scalar", lambda e, g=g, t=t: e.activation(
                out=RSTD1[:, g:g + 1], in_=TCB[:, t:t + 1], func=AF.Exp, scale=-0.5), cS)
            wait("vector", cOUT, cOUT.n)
            wait("vector", cT, cT.n)
            em("vector", lambda e, g=g: e.tensor_scalar(
                out=OUTSB[:], in0=XT[:], scalar1=MC[:, g:g + 1], scalar2=RSTD[:, g:g + 1],
                op0=OP.subtract, op1=OP.mult), cV)
            v_xnb = cV.n
            wait("tensor", cV, v_xnb)
            wait("tensor", cS)
            for cb in range(CH):
                em("tensor", lambda e, cb=cb: e.transpose(
                    QS[2][:, cb * 128:(cb + 1) * 128], OUTSB[:, cb * 128:(cb + 1) * 128], IDNF[:]), cT)
            t_x = cT.n
            wait("vector", cT, t_x)
            em("vector", lambda e, t=t: e.tensor_copy(
                out=XNFM.ap()[:, :].rearrange("p (c w) -> p c w", c=CH)[:, :, t * 128:(t + 1) * 128],
                in_=QS[2].ap()[:, :].rearrange("p (c w) -> p c w", c=CH)), cV)

        if a == 0:
            _tap("sums_a0", lambda: SUMS[:])
            _tap("sqs_a0", lambda: SQS[:])
            _tap("mc_a0", lambda: MC[:])
            _tap("rstd_a0", lambda: RSTD[:])
            _tap("idnf_a0", lambda: IDNF[:])
            _tap("outsb_a0", lambda: OUTSB[:])
        s_r1 = cS.n
        wait("tensor", cS, s_r1)
        wait("tensor", cV)
        for t in range(NTIL):
            em("tensor", lambda e, t=t, g0=g0: e.transpose(
                QS[0][0:1, t * 128:(t + 1) * 128], RSTD1[:, g0 + t:g0 + t + 1], IDNF[:]), cT)
        t_tr = cT.n
        wait("vector", cT, t_tr)
        em("vector", lambda e: e.tensor_copy(out=ZB[0:1, 0:SBA], in_=QS[0][0:1, 0:SBA]), cV)
        v_srp = cV.n
        wait("tensor", cV, v_srp)
        for t in range(NTIL):
            em("tensor", lambda e, t=t: e.matmul(QS[1][:, t * 128:(t + 1) * 128], ONES[:],
                                                 ZB[0:1, t * 128:(t + 1) * 128], start=True, stop=True), cT)
        t_bc = cT.n
        wait("vector", cT, t_bc)
        em("vector", lambda e: e.tensor_copy(out=RSTD1B[:], in_=QS[1][:, 0:SBA]), cV)
        v_r1b = cV.n

        _tap("xnfm_a%d" % a, lambda: XNFM[:])

        hv = HFM.ap()[:, :].rearrange("p (c w) -> p c w", w=SBA + 1)
        if a == 0:
            em("vector", lambda e, hv=hv: e.memset(hv[:, :, 0:1], 0.0), cV)
        else:
            em("vector", lambda e, hv=hv: e.tensor_copy(out=hv[:, :, 0:1], in_=hv[:, :, SBA:SBA + 1]), cV)
        wait("vector", cV, v_r1b)
        for c in range(CH):
            hb = lambda c=c: HFM[:, c * (SBA + 1) + 1:(c + 1) * (SBA + 1)]
            em("vector", lambda e, c=c, hb=hb: e.tensor_tensor(
                out=hb(), in0=XNFM[:, c * SBA:(c + 1) * SBA], in1=RSTD1B[:], op=OP.mult), cV)
            em("vector", lambda e, c=c, hb=hb: e.tensor_scalar(
                out=hb(), in0=hb(), scalar1=PC["ln1"][:, c:c + 1], scalar2=None, op0=OP.mult), cV)
            em("vector", lambda e, c=c, hb=hb: e.tensor_tensor(
                out=DBUF[:, c * SBA:(c + 1) * SBA], in0=hb(),
                in1=HFM[:, c * (SBA + 1):(c + 1) * (SBA + 1) - 1], op=OP.subtract), cV)
        _tap("h_a%d" % a, lambda: HFM[:])

        projs = [("k", PC["tmk"], KTM), ("v", PC["tmv"], VTM), ("r", PC["tmr"], ERT)]
        wait("vector", cOUT, cOUT.n)
        kvr_seq = [projs[q % 3][0] for q in range(6)]
        wdP = {}
        projT = {}

        def kvr_load(L):
            if a == 0 and L == 0:
                wait("sync", cOUT, pre_done)
            if L <= 1:
                wait("sync", cT, cT.n)
            else:
                wait("sync", cT, projT[L - 2])
            em("sync", lambda e, L=L: e.dma_start(
                out=K2STG[:, (L % 2) * CH * C:(L % 2 + 1) * CH * C].rearrange("p (a c) -> p a c", a=CH),
                in_=wkv_b[kvr_seq[L]][:, :, :]), cWD, 16)
            wdP[L] = cWD.n
        kvr_load(0)
        kvr_load(1)
        mix_t = []
        for pr in range(2):
            evac_s = {}
            evac_v = {}
            for pi, (pn, tmcol, dstbuf) in enumerate(projs):
                L = pr * 3 + pi
                koff = (L % 2) * CH * C
                qk = [QS[0], QS[1]] if pn in ("k", "r") else [QS[2], QS[3]]
                if pn == "r":
                    wait("tensor", cS)
                for ci in range(CH):
                    ms = len(mix_t) % 2
                    if len(mix_t) >= 2:
                        wait("vector", cT, mix_t[-2])
                    em("vector", lambda e, ms=ms, ci=ci, pr=pr, tmcol=tmcol: e.scalar_tensor_tensor(
                        out=MIXT[:, ms * 256:(ms + 1) * 256],
                        in0=DBUF[:, ci * SBA + pr * 256:ci * SBA + pr * 256 + 256],
                        scalar=tmcol[:, ci:ci + 1],
                        in1=HFM[:, ci * (SBA + 1) + pr * 256:ci * (SBA + 1) + pr * 256 + 256],
                        op0=OP.mult, op1=OP.add), cV)
                    v_mix = cV.n
                    wait("tensor", cV, v_mix)
                    if ci == 0:
                        wait("tensor", cWD, wdP[L])
                    for bl in range(2):
                        for nh in range(2):
                            em("tensor", lambda e, ms=ms, ci=ci, bl=bl, nh=nh, qk=qk, koff=koff: e.matmul(
                                qk[bl][:, nh * 512:(nh + 1) * 512],
                                MIXT[:, ms * 256 + bl * 128:ms * 256 + bl * 128 + 128],
                                K2STG[:, koff + ci * C + nh * 512:koff + ci * C + (nh + 1) * 512],
                                start=(ci == 0), stop=(ci == CH - 1)), cT)
                    mix_t.append(cT.n)
                t_mm = cT.n
                projT[L] = t_mm
                if L + 2 < 6:
                    kvr_load(L + 2)
                for bl in range(2):
                    if pn == "k":
                        wait("scalar", cT, t_mm)
                        em("scalar", lambda e, bl=bl, qk=qk: e.activation(
                            out=KTM[:, bl * C:(bl + 1) * C], in_=qk[bl][:], func=AF.Copy), cS)
                    elif pn == "v":
                        wait("vector", cT, t_mm)
                        em("vector", lambda e, bl=bl, qk=qk: e.tensor_copy(
                            out=VTM[:, bl * C:(bl + 1) * C], in_=qk[bl][:]), cV)
                    else:
                        wait("scalar", cT, t_mm)
                        em("scalar", lambda e, bl=bl, qk=qk: e.activation(
                            out=ERT[:, bl * C:(bl + 1) * C], in_=qk[bl][:], func=AF.Exp, scale=-1.0), cS)
                evac_s[pn] = cS.n
                evac_v[pn] = cV.n

            for bl in range(2):
                blk = 2 * pr + bl
                gq = [QS[2], QS[3]] if bl == 0 else [QS[0], QS[1]]
                wait("vector", cS)
                wait("vector", cV, evac_v["v"])
                em("vector", lambda e, bl=bl: e.tensor_tensor(
                    out=KCB[:], in0=KTM[:, bl * C:(bl + 1) * C], in1=UB[:], op=OP.add), cV)
                em("vector", lambda e: e.tensor_scalar(
                    out=KCB[:], in0=KCB[:], scalar1=20.0, scalar2=-20.0, op0=OP.min, op1=OP.max), cV)
                v_kc = cV.n
                wait("scalar", cV, v_kc)
                em("scalar", lambda e, bl=bl: e.activation(
                    out=EKT[:, bl * C:(bl + 1) * C], in_=KCB[:], func=AF.Exp), cS)
                s_ekt = cS.n
                wait("vector", cS, s_ekt)
                em("vector", lambda e, bl=bl: e.tensor_scalar(
                    out=KCB[:], in0=KTM[:, bl * C:(bl + 1) * C], scalar1=20.0, scalar2=-20.0,
                    op0=OP.min, op1=OP.max), cV)
                v_kc2 = cV.n
                wait("scalar", cV, v_kc2)
                em("scalar", lambda e: e.activation(out=EKB[:], in_=KCB[:], func=AF.Exp), cS)
                s_ek = cS.n
                wait("vector", cS, s_ek)
                em("vector", lambda e: e.tensor_tensor(
                    out=ZB[:, C:2 * C], in0=EKB[:], in1=EMIN[:], op=OP.mult), cV)
                em("vector", lambda e, bl=bl: e.tensor_tensor(
                    out=ZB[:, 0:C], in0=ZB[:, C:2 * C], in1=VTM[:, bl * C:(bl + 1) * C], op=OP.mult), cV)
                v_z = cV.n
                wait("tensor", cV, v_z)
                wait("tensor", cS)
                for seg in range(4):
                    q = gq[seg // 2]
                    qoff = (seg % 2) * 512
                    em("tensor", lambda e, q=q, qoff=qoff, seg=seg: e.matmul(
                        q[:, qoff:qoff + 512], LTC[:],
                        ZB[:, seg * 512:(seg + 1) * 512], start=True, stop=False), cT)
                    em("tensor", lambda e, q=q, qoff=qoff, seg=seg: e.matmul(
                        q[:, qoff:qoff + 512], ONES[:],
                        CARRY[:, seg * 512:(seg + 1) * 512], start=False, stop=True), cT)
                t_g = cT.n
                wait("vector", cT, t_g)
                em("vector", lambda e, gq=gq: e.tensor_tensor(
                    out=APR[:, 0:C], in0=gq[0][:], in1=ESH[:], op=OP.mult), cV)
                em("vector", lambda e, gq=gq: e.tensor_tensor(
                    out=APR[:, C:2 * C], in0=gq[1][:], in1=ESH[:], op=OP.mult), cV)
                sq = [QS[0], QS[1]] if bl == 0 else [QS[2], QS[3]]
                for seg in range(4):
                    em("tensor", lambda e, sq=sq, seg=seg: e.matmul(
                        sq[seg // 2][0:1, (seg % 2) * 512:(seg % 2) * 512 + 512], ONESR[:],
                        ZB[:, seg * 512:(seg + 1) * 512], start=True, stop=True), cT)
                t_s = cT.n
                wait("vector", cT, t_s)
                em("vector", lambda e, sq=sq: e.tensor_tensor(
                    out=CROW[0:1, 0:C], in0=CARRY[:, 0:C], in1=sq[0][0:1, 0:C], op=OP.add), cV)
                em("vector", lambda e, sq=sq: e.tensor_tensor(
                    out=XT[0:1, 0:C], in0=CARRY[:, C:2 * C], in1=sq[1][0:1, 0:C], op=OP.add), cV)
                em("vector", lambda e: e.tensor_tensor(
                    out=CARRY[:, 0:C], in0=CROW[0:1, 0:C], in1=EW2R[:], op=OP.mult), cV)
                em("vector", lambda e: e.tensor_tensor(
                    out=CARRY[:, C:2 * C], in0=XT[0:1, 0:C], in1=EW2R[:], op=OP.mult), cV)
                em("vector", lambda e, bl=bl: e.tensor_tensor(
                    out=EKTV[:], in0=EKT[:, bl * C:(bl + 1) * C],
                    in1=VTM[:, bl * C:(bl + 1) * C], op=OP.mult), cV)
                em("vector", lambda e: e.tensor_tensor(
                    out=EKTV[:], in0=EKTV[:], in1=APR[:, 0:C], op=OP.add), cV)
                em("vector", lambda e, bl=bl: e.scalar_tensor_tensor(
                    out=DEN[:], in0=EKT[:, bl * C:(bl + 1) * C], scalar=1e-8,
                    in1=APR[:, C:2 * C], op0=OP.add, op1=OP.add), cV)
                wait("vector", cS, evac_s.get("r", 0))
                em("vector", lambda e, bl=bl: e.scalar_tensor_tensor(
                    out=DEN[:], in0=ERT[:, bl * C:(bl + 1) * C], scalar=1.0,
                    in1=DEN[:], op0=OP.add, op1=OP.mult), cV)
                v_den = cV.n
                wait("scalar", cV, v_den)
                em("scalar", lambda e: e.activation(out=KCB[:], in_=DEN[:], func=AF.Ln), cS)
                em("scalar", lambda e: e.activation(out=EKB[:], in_=KCB[:], func=AF.Exp, scale=-1.0), cS)
                s_rd = cS.n
                wait("vector", cS, s_rd)
                em("vector", lambda e: e.tensor_tensor(out=RWT[:], in0=EKTV[:], in1=EKB[:], op=OP.mult), cV)
                v_rwt = cV.n
                wait("tensor", cV, v_rwt)
                tq = gq[0]
                for cb in range(CH):
                    em("tensor", lambda e, cb=cb, tq=tq: e.transpose(
                        tq[:, cb * 128:(cb + 1) * 128], RWT[:, cb * 128:(cb + 1) * 128], IDNF[:]), cT)
                t_rt = cT.n
                wait("vector", cT, t_rt)
                em("vector", lambda e, blk=blk, tq=tq: e.tensor_copy(
                    out=RWKVFM.ap()[:, :].rearrange("p (c w) -> p c w", c=CH)[:, :, blk * 128:(blk + 1) * 128],
                    in_=tq.ap()[:, :].rearrange("p (c w) -> p c w", c=CH)), cV)

        v_rwkv = cV.n
        _tap("rwkv_a%d" % a, lambda: RWKVFM[:])

        wait("sync", cT, cT.n)
        em("sync", dma_w8(WOUTV[:, 0:CH * C], wob_s), cWD, 16)
        wd_wo = cWD.n
        wait("tensor", cV, v_rwkv)
        wait("tensor", cS)
        wait("tensor", cWD, wd_wo)
        for co in range(CH):
            q = QS[co % 2]
            for ci in range(CH):
                em("tensor", lambda e, co=co, ci=ci, q=q: e.matmul(
                    q[:, 0:512], WOUTV[:, ci * C + co * 128:ci * C + co * 128 + 128],
                    RWKVFM[:, ci * SBA:(ci + 1) * SBA],
                    start=(ci == 0), stop=(ci == CH - 1)), cT)
            t_wo = cT.n
            wait("scalar", cT, t_wo)
            em("scalar", lambda e, co=co, q=q: e.activation(
                out=FINALW[:, co * SBA:(co + 1) * SBA], in_=q[:, 0:512], func=AF.Copy), cS)
            s_fw = cS.n
            wait("vector", cT, t_wo)
            wait("vector", cS, s_fw)
            em("vector", lambda e, co=co, q=q: e.tensor_tensor(
                out=X1BF[:, co * SBA:(co + 1) * SBA], in0=q[:, 0:512],
                in1=XNFM[:, co * SBA:(co + 1) * SBA], op=OP.add), cV)
            if co >= 1:
                wait("tensor", cS)
                wait("tensor", cV)
        v_x1 = cV.n
        _tap("x1_a%d" % a, lambda: X1BF[:])
        _tap("att_a%d" % a, lambda: FINALW[:])

        wait("vector", cV, v_x1)
        wait("tensor", cS)
        wait("tensor", cV)
        t_mm2 = {}
        for c in range(CH):
            gb = GBUF if c % 2 == 0 else XR2J
            if c >= 2:
                wait("vector", cT, t_mm2[c - 2])
            em("vector", lambda e, c=c, gb=gb: e.tensor_tensor(
                out=gb[:, 0:SBA], in0=X1BF[:, c * SBA:(c + 1) * SBA],
                in1=X1BF[:, c * SBA:(c + 1) * SBA], op=OP.mult), cV)
            v_sq = cV.n
            wait("tensor", cV, v_sq)
            em("tensor", lambda e, c=c, gb=gb: e.matmul(
                QS[2][0:1, 0:512], ONESC[:], gb[:, 0:SBA], start=(c == 0), stop=(c == CH - 1)), cT)
            t_mm2[c] = cT.n
        t_ms2 = cT.n
        wait("scalar", cT, t_ms2)
        em("scalar", lambda e: e.activation(out=OUTSB[0:1, 0:512], in_=QS[2][0:1, 0:512],
                                            func=AF.Ln, scale=1.0 / C, bias=CBIA[0:1, 1:2]), cS)
        bar("scalar")
        em("scalar", lambda e: e.activation(out=ZB[0:1, 0:512], in_=OUTSB[0:1, 0:512],
                                            func=AF.Exp, scale=-0.5), cS)
        s_rs2 = cS.n
        wait("tensor", cS, s_rs2)
        for t in range(NTIL):
            em("tensor", lambda e, t=t: e.matmul(QS[3][:, t * 128:(t + 1) * 128], ONES[:],
                                                 ZB[0:1, t * 128:(t + 1) * 128], start=True, stop=True), cT)
        t_r2b = cT.n
        wait("vector", cT, t_r2b)
        em("vector", lambda e: e.tensor_copy(out=RSTD2B[:], in_=QS[3][:, 0:SBA]), cV)
        v_r2b = cV.n
        _tap("rstd2_a%d" % a, lambda: RSTD2B[:])

        h2v = H2FM.ap()[:, :].rearrange("p (c w) -> p c w", w=SBA + 1)
        if a == 0:
            em("vector", lambda e, h2v=h2v: e.memset(h2v[:, :, 0:1], 0.0), cV)
        else:
            em("vector", lambda e, h2v=h2v: e.tensor_copy(out=h2v[:, :, 0:1], in_=h2v[:, :, SBA:SBA + 1]), cV)
        wait("vector", cV, v_r2b)
        for c in range(CH):
            h2b = lambda c=c: H2FM[:, c * (SBA + 1) + 1:(c + 1) * (SBA + 1)]
            em("vector", lambda e, c=c, h2b=h2b: e.tensor_tensor(
                out=h2b(), in0=X1BF[:, c * SBA:(c + 1) * SBA], in1=RSTD2B[:], op=OP.mult), cV)
            em("vector", lambda e, c=c, h2b=h2b: e.tensor_scalar(
                out=h2b(), in0=h2b(), scalar1=PC["ln2"][:, c:c + 1], scalar2=None, op0=OP.mult), cV)
            em("vector", lambda e, c=c, h2b=h2b: e.tensor_tensor(
                out=DBUF[:, c * SBA:(c + 1) * SBA], in0=h2b(),
                in1=H2FM[:, c * (SBA + 1):(c + 1) * (SBA + 1) - 1], op=OP.subtract), cV)
            em("vector", lambda e, c=c: e.scalar_tensor_tensor(
                out=XK2[:, c * SBA:(c + 1) * SBA], in0=DBUF[:, c * SBA:(c + 1) * SBA],
                scalar=PC["tmk2"][:, c:c + 1],
                in1=H2FM[:, c * (SBA + 1):(c + 1) * (SBA + 1) - 1],
                op0=OP.mult, op1=OP.add), cV)
        v_xk2 = cV.n
        _tap("xk2_a%d" % a, lambda: XK2[:])

        wait("sync", cT, cT.n)
        em("sync", dma_w8(WOUTV[:, 0:CH * C], wr2b_s), cWD, 16)
        wd_r2 = cWD.n
        wait("tensor", cWD, wd_r2)
        wait("tensor", cV, v_r2b)
        for half in range(2):
            for ci in range(CH):
                wait("vector", cT, cT.n)
                em("vector", lambda e, ci=ci: e.scalar_tensor_tensor(
                    out=XR2J[:, 0:SBA], in0=DBUF[:, ci * SBA:(ci + 1) * SBA],
                    scalar=PC["tmr2"][:, ci:ci + 1],
                    in1=H2FM[:, ci * (SBA + 1):(ci + 1) * (SBA + 1) - 1],
                    op0=OP.mult, op1=OP.add), cV)
                v_xr = cV.n
                wait("tensor", cV, v_xr)
                for cg in range(4):
                    co = half * 4 + cg
                    em("tensor", lambda e, co=co, ci=ci, cg=cg: e.matmul(
                        QS[cg][:, 0:512], WOUTV[:, ci * C + co * 128:ci * C + co * 128 + 128],
                        XR2J[:, 0:SBA],
                        start=(ci == 0), stop=(ci == CH - 1)), cT)
            t_r2 = cT.n
            wait("scalar", cT, t_r2)
            for cg in range(4):
                co = half * 4 + cg
                em("scalar", lambda e, co=co, cg=cg: e.activation(
                    out=R2TH[:, co * SBA:(co + 1) * SBA], in_=QS[cg][:, 0:512],
                    func=AF.Tanh, scale=0.5), cS)
            wait("tensor", cS)
        s_r2 = cS.n
        _tap("r2th_a%d" % a, lambda: R2TH[:])

        k2base = 0
        wk2v = wk2_in.rearrange("(cc p) q -> p cc q", p=128)
        w1v = w1_in.rearrange("(cc p) q -> p cc q", p=128)
        bmap = [(0, 512), (1, 0), (1, 512), (2, 0), (2, 512), (3, 0), (3, 512)]
        GBp = (GBUF, XR2J)
        s_rl = {}
        v_sq = {}
        t_k2 = {}
        def cm_slabs(g):
            gp = g % 2
            wait("sync", cT, cT.n)
            em("sync", lambda e, g=g, gp=gp: e.dma_start(
                out=WSA[:, gp * 4096:(gp + 1) * 4096].rearrange("p (b q) -> p b q", b=CH),
                in_=wk2b_s[:, :, g * 512:g * 512 + 512]), cWD, 16)
            em("sync", lambda e, g=g, gp=gp: e.dma_start(
                out=WSB[:, gp * 4 * C:(gp + 1) * 4 * C].rearrange("p (a c) -> p a c", a=4),
                in_=wv2b_s[:, g * 4:g * 4 + 4, :]), cWD, 16)
            wd_grp[g] = cWD.n
        wd_grp = {}
        cm_slabs(0)
        for i in range(ICH + 2):
            pi = i % 2
            if i < ICH:
                if i % 4 == 0:
                    wait("tensor", cWD, wd_grp[i // 4])
                if i == 30:
                    wait("sync", cT, cT.n)
                    em("sync", lambda e: e.dma_start(
                        out=WSB[:, 0:4096].rearrange("p (j c) -> p j c", j=ICH),
                        in_=wv2b_s[:, :, 896:1024]), cWD, 16)
                    wd_cm7 = cWD.n
                ip = (i // 4) % 2
                io = i % 4
                if i == 0:
                    wait("tensor", cV, v_xk2)
                    wait("tensor", cS, s_r2)
                else:
                    wait("tensor", cS, s_rl[i - 1])
                for ci in range(CH):
                    em("tensor", lambda e, i=i, ci=ci, ip=ip, io=io: e.matmul(
                        QS[0][:, 0:512],
                        WSA[:, ip * 4096 + ci * 512 + io * 128:ip * 4096 + ci * 512 + io * 128 + 128],
                        XK2[:, ci * SBA:(ci + 1) * SBA],
                        start=(ci == 0), stop=(ci == CH - 1)), cT)
                t_k2[i] = cT.n
            j = i - 2
            if 0 <= j < ICH:
                jp = (j // 4) % 2
                jo = j % 4
                wait("tensor", cV, v_sq[j])
                for co in range(7):
                    sl, qo = bmap[co]
                    em("tensor", lambda e, j=j, co=co, sl=sl, qo=qo, jp=jp, jo=jo: e.matmul(
                        QS[sl][:, qo:qo + 512],
                        WSB[:, jp * 4 * C + jo * C + co * 128:jp * 4 * C + jo * C + co * 128 + 128],
                        K2STG[:, k2base + j * SBA:k2base + (j + 1) * SBA],
                        start=(j == 0), stop=(j == ICH - 1)), cT)
            if i % 4 == 1 and i // 4 + 1 < ICH // 4:
                cm_slabs(i // 4 + 1)
            if i < ICH:
                wait("scalar", cT, t_k2[i])
                if i >= 2:
                    wait("scalar", cV, v_sq[i - 2])
                em("scalar", lambda e, pi=pi: e.activation(
                    out=GBp[pi][:, 0:SBA], in_=QS[0][:, 0:512], func=AF.Relu), cS)
                s_rl[i] = cS.n
                wait("vector", cS, s_rl[i])
                em("vector", lambda e, i=i, pi=pi: e.tensor_tensor(
                    out=K2STG[:, k2base + i * SBA:k2base + (i + 1) * SBA],
                    in0=GBp[pi][:, 0:SBA], in1=GBp[pi][:, 0:SBA], op=OP.mult), cV)
                v_sq[i] = cV.n
        _tap("k2_a%d" % a, lambda: K2STG[:])

        wait("tensor", cWD, wd_cm7)
        wait("tensor", cS, s_rl[ICH - 1])
        for j in range(ICH):
            em("tensor", lambda e, j=j: e.matmul(
                QS[0][:, 0:512], WSB[:, j * 128:(j + 1) * 128],
                K2STG[:, k2base + j * SBA:k2base + (j + 1) * SBA],
                start=(j == 0), stop=(j == ICH - 1)), cT)
        t_cm = cT.n
        v_cm7 = None
        for co in [7, 0, 1, 2, 3, 4, 5, 6]:
            if co < 7:
                sl, qo = bmap[co]
            else:
                sl, qo = 0, 0
            wait("vector", cT, t_cm)
            em("vector", lambda e, co=co, sl=sl, qo=qo: e.scalar_tensor_tensor(
                out=GLB[:], in0=R2TH[:, co * SBA:(co + 1) * SBA], scalar=1.0,
                in1=QS[sl][:, qo:qo + 512], op0=OP.add, op1=OP.mult), cV)
            em("vector", lambda e, co=co: e.scalar_tensor_tensor(
                out=FINALW[:, co * SBA:(co + 1) * SBA], in0=GLB[:], scalar=0.5,
                in1=FINALW[:, co * SBA:(co + 1) * SBA], op0=OP.mult, op1=OP.add), cV)
            if co == 7:
                v_cm7 = cV.n
        v_cm = cV.n
        _tap("cm_a%d" % a, lambda: FINALW[:])

        ITp = (ITILE, ITILE2)
        IQFp = (Z1B, IQF2)
        FLBp = (FLB, FLB2)
        GLBp = (GLB, GLB2)
        GHBp = (GHB, GHB2)
        bmap = [(0, 512), (1, 0), (1, 512), (2, 0), (2, 512), (3, 0), (3, 512)]
        v_iq = {}
        v_fi = {}
        v_i2 = {}
        s_iqf = {}
        s_flb = {}
        s_g = {}
        s_hq = {}
        g_dd = {}
        t_w1 = {}
        def ffn_slabs(g):
            gp = g % 2
            wait("sync", cT, cT.n)
            em("sync", lambda e, g=g, gp=gp: e.dma_start(
                out=WSA[:, gp * 4096:(gp + 1) * 4096].rearrange("p (b q) -> p b q", b=CH),
                in_=w1b_s[:, :, g * 512:g * 512 + 512]), cWD, 16)
            em("sync", lambda e, g=g, gp=gp: e.dma_start(
                out=WSB[:, gp * 4 * C:(gp + 1) * 4 * C].rearrange("p (a c) -> p a c", a=4),
                in_=w2b_s[:, g * 4:g * 4 + 4, :]), cWD, 16)
            wd_grp[g] = cWD.n
        wd_grp = {}
        ffn_slabs(0)
        for i in range(ICH + 2):
            pi = i % 2
            if i < ICH:
                if i % 4 == 0:
                    wait("tensor", cWD, wd_grp[i // 4])
                if i == 30:
                    wait("sync", cT, cT.n)
                    em("sync", lambda e: e.dma_start(
                        out=WSB[:, 0:4096].rearrange("p (j c) -> p j c", j=ICH),
                        in_=w2b_s[:, :, 896:1024]), cWD, 16)
                    wd_co7 = cWD.n
                ip = (i // 4) % 2
                io = i % 4
                if i == 0:
                    wait("tensor", cV, v_cm7)
                else:
                    wait("tensor", cV, v_iq[i - 1])
                for ci in range(CH):
                    em("tensor", lambda e, i=i, ci=ci, ip=ip, io=io: e.matmul(
                        QS[0][:, 0:512],
                        WSA[:, ip * 4096 + ci * 512 + io * 128:ip * 4096 + ci * 512 + io * 128 + 128],
                        XNFM[:, ci * SBA:(ci + 1) * SBA],
                        start=(ci == 0), stop=(ci == CH - 1)), cT)
                t_w1[i] = cT.n
            j = i - 2
            if 0 <= j < ICH:
                jp = (j // 4) % 2
                jo = j % 4
                wait("tensor", cS, s_hq[j])
                if j == 0:
                    wait("tensor", cV, v_cm)
                for co in range(7):
                    sl, qo = bmap[co]
                    em("tensor", lambda e, j=j, co=co, sl=sl, qo=qo, jp=jp, jo=jo: e.matmul(
                        QS[sl][:, qo:qo + 512],
                        WSB[:, jp * 4 * C + jo * C + co * 128:jp * 4 * C + jo * C + co * 128 + 128],
                        K2STG[:, k2base + j * SBA:k2base + (j + 1) * SBA],
                        start=(j == 0), stop=(j == ICH - 1)), cT)
            if i % 4 == 1 and i // 4 + 1 < ICH // 4:
                ffn_slabs(i // 4 + 1)
            h = i - 1
            if i < ICH:
                wait("vector", cT, t_w1[i])
                if i >= 1:
                    wait("vector", cS, s_g[i - 1])
                em("vector", lambda e, i=i: e.tensor_scalar(
                    out=ITILE[:], in0=QS[0][:, 0:512], scalar1=256.0, scalar2=B1S[:, i:i + 1],
                    op0=OP.mult, op1=OP.add), cV)
                v_iq[i] = cV.n
                if i == 0 and ("iq0_a%d" % a) in taps:
                    tq0 = nc.dram_tensor("tap_iq0_a%d" % a, [128, SBA], I32, kind="ExternalOutput").ap()
                    wait("sync", cV)
                    em("sync", lambda e, tq0=tq0: e.dma_start(out=tq0, in_=ITILE[:]), cOUT, 16)
                    wait("vector", cOUT, cOUT.n)
                wait("scalar", cV, v_iq[i])
                em("scalar", lambda e, pi=pi: e.activation(
                    out=IQFp[pi][:], in_=ITILE[:], func=AF.Copy, scale=1.0), cS)
                s_iqf[i] = cS.n
            if 0 <= h < ICH:
                ph = h % 2
                wait("vector", cS, s_flb[h])
                em("vector", lambda e, ph=ph: e.scalar_tensor_tensor(
                    out=IQFp[ph][:], in0=IQFp[ph][:], scalar=0.015625, in1=FLBp[ph][:],
                    op0=OP.mult, op1=OP.subtract), cV)
            if i < ICH:
                wait("vector", cS, s_iqf[i])
                em("vector", lambda e, pi=pi: e.tensor_scalar(
                    out=ITILE[:], in0=IQFp[pi][:], scalar1=0.015625, scalar2=15.5078125,
                    op0=OP.mult, op1=OP.add), cV)
                v_fi[i] = cV.n
                wait("scalar", cV, v_fi[i])
                em("scalar", lambda e, pi=pi: e.activation(
                    out=FLBp[pi][:], in_=ITILE[:], func=AF.Copy, scale=1.0), cS)
                s_flb[i] = cS.n
                if i >= 2:
                    wait("scalar", cG, g_dd[i - 2])
                em("scalar", lambda e, pi=pi: e.activation(
                    out=GLBp[pi][:], in_=ITILE[:], func=AF.Gelu, scale=0.25, bias=CBIA[:, 2:3]), cS)
                em("scalar", lambda e, pi=pi: e.activation(
                    out=GHBp[pi][:], in_=ITILE[:], func=AF.Gelu, scale=0.25, bias=CBIA[:, 3:4]), cS)
                s_g[i] = cS.n
                wait("gpsimd", cS, s_g[i])
                em("gpsimd", lambda e, pi=pi: e.tensor_tensor(
                    out=GHBp[pi][:], in0=GHBp[pi][:], in1=GLBp[pi][:], op=OP.subtract), cG)
                g_dd[i] = cG.n
            if 0 <= h < ICH:
                ph = h % 2
                wait("vector", cG, g_dd[h])
                em("vector", lambda e, ph=ph: e.scalar_tensor_tensor(
                    out=GHBp[ph][:], in0=IQFp[ph][:], scalar=16.0, in1=GHBp[ph][:],
                    op0=OP.add, op1=OP.mult), cV)
                em("vector", lambda e, ph=ph: e.tensor_tensor(
                    out=GLBp[ph][:], in0=GLBp[ph][:], in1=GHBp[ph][:], op=OP.add), cV)
                if h >= 1:
                    wait("vector", cS, s_hq[h - 1])
                em("vector", lambda e, ph=ph: e.tensor_scalar(
                    out=ITILE2[:], in0=GLBp[ph][:], scalar1=256.0, scalar2=None, op0=OP.mult), cV)
                v_i2[h] = cV.n
                wait("scalar", cV, v_i2[h])
                em("scalar", lambda e, h=h: e.activation(
                    out=K2STG[:, k2base + h * SBA:k2base + (h + 1) * SBA],
                    in_=ITILE2[:], func=AF.Copy, scale=1.0 / 256.0), cS)
                s_hq[h] = cS.n
        _tap("hq_a%d" % a, lambda: K2STG[:])

        wait("tensor", cWD, wd_co7)
        wait("tensor", cV, v_iq[ICH - 1])
        for j in range(ICH):
            em("tensor", lambda e, j=j: e.matmul(
                QS[0][:, 0:512], WSB[:, j * 128:(j + 1) * 128],
                K2STG[:, k2base + j * SBA:k2base + (j + 1) * SBA],
                start=(j == 0), stop=(j == ICH - 1)), cT)
        t_w2 = cT.n
        for co in range(CH):
            if co < 7:
                sl, qo = bmap[co]
            else:
                sl, qo = 0, 0
            wait("vector", cT, t_w2)
            em("vector", lambda e, co=co, sl=sl, qo=qo: e.tensor_scalar(
                out=ITILE[:], in0=QS[sl][:, qo:qo + 512], scalar1=256.0, scalar2=B2S[:, co:co + 1],
                op0=OP.mult, op1=OP.add), cV)
            em("vector", lambda e, co=co: e.scalar_tensor_tensor(
                out=FINALW[:, co * SBA:(co + 1) * SBA], in0=ITILE[:], scalar=1.0 / 256.0,
                in1=FINALW[:, co * SBA:(co + 1) * SBA], op0=OP.mult, op1=OP.add), cV)
        v_ffn = cV.n
        _tap("final_a%d" % a, lambda: FINALW[:])

        fv = FINALW.ap()[:, :].rearrange("p (c w) -> p c w", c=CH)
        for t in range(NTIL):
            g = g0 + t
            gt = tok0 + t * 128
            sl = t % 2
            wait("sync", cV)
            em("sync", lambda e, gt=gt: e.dma_start(out=XT[:], in_=x_in[gt:gt + 128, :]), cLD, 16)
            ld_x = cLD.n
            wait("vector", cLD, ld_x)
            wait("vector", cOUT, cOUT.n)
            em("vector", lambda e, g=g: e.tensor_scalar(
                out=OUTSB[:], in0=XT[:],
                scalar1=MC[:, g:g + 1], scalar2=RSTD[:, g:g + 1],
                op0=OP.subtract, op1=OP.mult), cV)
            em("vector", lambda e: e.tensor_tensor(
                out=OUTSB[:], in0=OUTSB[:], in1=XT[:], op=OP.add), cV)
            v_xpn = cV.n
            q = QS[2 + (t % 2)]
            wait("tensor", cV, v_xpn)
            for cb in range(CH):
                em("tensor", lambda e, cb=cb, t=t, q=q, fv=fv: e.transpose(
                    q[:, cb * 128:(cb + 1) * 128], fv[:, cb, t * 128:(t + 1) * 128], IDNF[:]), cT)
            t_f = cT.n
            wait("vector", cT, t_f)
            em("vector", lambda e, q=q: e.tensor_tensor(
                out=OUTSB[:], in0=OUTSB[:], in1=q[:], op=OP.add), cV)
            v_out = cV.n
            wait("sync", cV, v_out)
            em("sync", lambda e, gt=gt: e.dma_start(
                out=out_t[gt:gt + 128, :], in_=OUTSB[:]), cOUT, 16)

    wait("sync", cOUT, cOUT.n)

    with nc.Block() as block:
        @block.sync
        def _(e):
            for f in sched["sync"]:
                f(e)

        @block.gpsimd
        def _(e):
            for f in sched["gpsimd"]:
                f(e)

        @block.vector
        def _(e):
            for f in sched["vector"]:
                f(e)

        @block.scalar
        def _(e):
            for f in sched["scalar"]:
                f(e)

        @block.tensor
        def _(e):
            for f in sched["tensor"]:
                f(e)

    ctx.close()
    return nc


def _host_inputs(inputs):
    x = np.ascontiguousarray(np.asarray(inputs["L0_attn_x"], dtype=np.float32))
    row = lambda k: np.ascontiguousarray(np.asarray(inputs[k], dtype=np.float32).reshape(1, -1))
    base = {
        "wk": np.ascontiguousarray(np.asarray(inputs["L0_attn_W_key"], np.float32)),
        "wv": np.ascontiguousarray(np.asarray(inputs["L0_attn_W_value"], np.float32)),
        "wr": np.ascontiguousarray(np.asarray(inputs["L0_attn_W_recept"], np.float32)),
        "wo": np.ascontiguousarray(np.asarray(inputs["L0_attn_W_out"], np.float32)),
        "wr2": np.ascontiguousarray(np.asarray(inputs["L0_attn_W_recept2"], np.float32)),
        "wk2": np.ascontiguousarray(np.asarray(inputs["L0_attn_W_key2"], np.float32)),
        "wv2": np.ascontiguousarray(np.asarray(inputs["L0_attn_W_value2"], np.float32)),
        "w1": np.ascontiguousarray(np.asarray(inputs["L0_ffn_w1"], np.float32)),
        "w2": np.ascontiguousarray(np.asarray(inputs["L0_ffn_w2"], np.float32)),
        "ln1": row("L0_attn_ln1_w"), "ln2": row("L0_attn_ln2_w"),
        "tmk": row("L0_attn_time_mix_k"), "tmv": row("L0_attn_time_mix_v"),
        "tmr": row("L0_attn_time_mix_r"), "tmk2": row("L0_attn_time_mix_k2"),
        "tmr2": row("L0_attn_time_mix_r2"),
        "b1": row("L0_ffn_b1"), "b2": row("L0_ffn_b2"),
        "tdb": np.tile(row("L0_attn_time_decay"), (128, 1)),
        "ub": np.tile(row("L0_attn_time_first"), (128, 1)),
        "ltc": np.triu(np.ones((128, 128), np.float32), k=1),
        "ones1": np.ones((1, 128), np.float32),
        "idnf": np.eye(128, dtype=np.float32),
        "onesc": np.ones((128, 1)).astype(ml_dtypes.bfloat16),
        "onesr": np.ones((128, 1), np.float32),
        "cbias": np.tile(np.array([[1e-5, 1e-6, -4.0, -3.75, 0, 0, 0, 0]], np.float32), (128, 1)),
    }
    in_maps = []
    for b in range(NCORE):
        m = dict(base)
        m["x"] = np.ascontiguousarray(x[b])
        in_maps.append(m)
    return in_maps


def kernel(**inputs):
    nc = _build(taps=tuple(_TAPS))
    in_maps = _host_inputs(inputs)
    res = run_bass_kernel_spmd(nc, in_maps, core_ids=list(range(NCORE)))
    out = np.stack([np.asarray(r["out"]) for r in res.results], axis=0)
    return out.astype(np.float32)


# version 07
# speedup vs baseline: 2.0477x; 1.0161x over previous
import numpy as np
import ml_dtypes
import concourse.bass as bass
import concourse.mybir as mybir
from concourse.alu_op_type import AluOpType as OP
from concourse.bass_utils import run_bass_kernel_spmd

F32 = mybir.dt.float32
F32R = mybir.dt.float32r
BF16 = mybir.dt.bfloat16
I32 = mybir.dt.int32
AF = mybir.ActivationFunctionType

T, C, I = 4096, 1024, 4096
CH = C // 128
ICH = I // 128
SBA = 512
NSB = T // SBA
NTIL = SBA // 128
NCORE = 8

ENGS = ("sync", "gpsimd", "vector", "scalar", "tensor")

_TAPS = []


def _build(taps=(), detect_races=False):
    nc = bass.Bass("TRN2", target_bir_lowering=False, detect_race_conditions=detect_races)
    dr = lambda n, sh, dt: nc.dram_tensor(n, sh, dt, kind="ExternalInput").ap()

    x_in = dr("x", [T, C], F32)
    wk_in = dr("wk", [C, C], F32)
    wv_in = dr("wv", [C, C], F32)
    wr_in = dr("wr", [C, C], F32)
    wo_in = dr("wo", [C, C], F32)
    wr2_in = dr("wr2", [C, C], F32)
    wk2_in = dr("wk2", [C, I], F32)
    wv2_in = dr("wv2", [I, C], F32)
    w1_in = dr("w1", [C, I], F32)
    w2_in = dr("w2", [I, C], F32)
    vecs = {n: dr(n, [1, C], F32) for n in
            ("ln1", "ln2", "tmk", "tmv", "tmr", "tmk2", "tmr2", "b2")}
    b1_in = dr("b1", [1, I], F32)
    tdb_in = dr("tdb", [128, C], F32)
    ub_in = dr("ub", [128, C], F32)
    ltc_in = dr("ltc", [128, 128], F32R)
    ones_in = dr("ones1", [1, 128], F32R)
    idnf_in = dr("idnf", [128, 128], F32)
    onesc_in = dr("onesc", [128, 1], BF16)
    onesr_in = dr("onesr", [128, 1], F32R)
    cbias_in = dr("cbias", [128, 8], F32)

    out_t = nc.dram_tensor("out", [T, C], F32, kind="ExternalOutput").ap()

    drb = lambda n, sh: nc.dram_tensor(n, sh, BF16, kind="Internal").ap()
    kb_s = drb("kb_s", [128, CH, C])
    vb_s = drb("vb_s", [128, CH, C])
    rb_s = drb("rb_s", [128, CH, C])
    wob_s = drb("wob_s", [128, CH, C])
    wr2b_s = drb("wr2b_s", [128, CH, C])
    wk2b_s = drb("wk2b_s", [128, CH, I])
    w1b_s = drb("w1b_s", [128, CH, I])
    wv2b_s = drb("wv2b_s", [128, ICH, C])
    w2b_s = drb("w2b_s", [128, ICH, C])

    from contextlib import ExitStack
    ctx = ExitStack()

    _off = [16512]
    handles = {}

    def place(name, shape, dtype, at=None):
        nbytes = int(np.prod(shape[1:])) * mybir.dt.size(dtype)
        if at is None:
            at = _off[0]
            _off[0] = (at + nbytes + 31) & ~31
        h = nc.alloc_sbuf_tensor_at(name, shape, dtype, offset=at)
        handles[name] = (h, at, nbytes)
        return h

    LTC = place("LTC", [128, 128], F32R)
    ONES = place("ONES", [1, 128], F32R)
    IDNF = place("IDNF", [128, 128], F32)
    ONESC = place("ONESC", [128, 1], BF16)
    ONESR = place("ONESR", [128, 1], F32R)
    WCL = place("WCL", [128, C], F32)
    UB = place("UB", [128, C], F32)
    EMIN = place("EMIN", [128, C], F32)
    ESH = place("ESH", [128, C], F32)
    NIOT = place("NIOT", [128, 1], F32)
    IOTM1 = place("IOTM1", [128, 1], F32)
    EW2R = place("EW2R", [1, C], F32)
    CARRY = place("CARRY", [1, 2 * C], F32R)
    PC = {}
    for nme in ("ln1", "ln2", "tmk", "tmv", "tmr", "tmk2", "tmr2"):
        PC[nme] = place("PC_" + nme, [128, CH], F32)
    B1S = place("B1S", [128, ICH], F32)
    B2S = place("B2S", [128, CH], F32)
    SUMS = place("SUMS", [128, 32], F32)
    SQS = place("SQS", [128, 32], F32)
    MC = place("MC", [128, 32], F32)
    RSTD = place("RSTD", [128, 32], F32)
    RSTD1 = place("RSTD1", [128, 32], F32)
    TCA = place("TCA", [128, 4], F32)
    TCB = place("TCB", [128, 4], F32)
    RSTD1B = place("RSTD1B", [128, SBA], BF16)
    RSTD2B = place("RSTD2B", [128, SBA], BF16)
    XNFM = place("XNFM", [128, CH * SBA], BF16)
    X1BF = place("X1BF", [128, CH * SBA], BF16)
    FINALW = place("FINALW", [128, CH * SBA], F32)
    HFM = place("HFM", [128, CH * (SBA + 1)], BF16)
    DBUF = place("DBUF", [128, CH * SBA], BF16)
    RWKVFM = place("RWKVFM", [128, CH * SBA], BF16)
    H2FM = place("H2FM", [128, CH * (SBA + 1)], BF16)
    MIXT = place("MIXT", [128, 2 * 256], BF16)
    XR2J = place("XR2J", [128, SBA], BF16)
    GBUF = place("GBUF", [128, SBA], BF16)
    K2STG = place("K2STG", [128, ICH * SBA], BF16)
    CBIA = place("CBIA", [128, 8], F32)
    XT = place("XT", [128, C], F32)
    OUTSB = place("OUTSB", [128, C], F32)

    ovl0 = _off[0]

    def placeA(name, shape, dtype):
        return place(name, shape, dtype)

    KTM = placeA("KTM", [128, 2 * C], F32)
    VTM = placeA("VTM", [128, 2 * C], F32)
    ERT = placeA("ERT", [128, 2 * C], F32)
    KCB = placeA("KCB", [128, C], F32)
    EKT = placeA("EKT", [128, 2 * C], F32)
    EKB = placeA("EKB", [128, C], F32)
    ZB = placeA("ZB", [128, 2 * C], F32R)
    APR = placeA("APR", [128, 2 * C], F32)
    EKTV = placeA("EKTV", [128, C], F32)
    DEN = placeA("DEN", [128, C], F32)
    RWT = placeA("RWT", [128, C], F32)
    ovl_end_A = _off[0]

    _off[0] = ovl0
    WSA = place("WSA", [128, 2 * 4096], BF16)
    WSB = place("WSB", [128, 2 * 4 * C], BF16)
    XK2 = place("XK2", [128, CH * SBA], BF16)
    R2TH = place("R2TH", [128, CH * SBA], BF16)
    ITILE = place("ITILE", [128, SBA], I32)
    ITILE2 = place("ITILE2", [128, SBA], I32)
    Z1B = place("Z1B", [128, SBA], F32)
    IQF2 = place("IQF2", [128, SBA], F32)
    FLB = place("FLB", [128, SBA], F32)
    FLB2 = place("FLB2", [128, SBA], F32)
    GLB = place("GLB", [128, SBA], F32)
    GLB2 = place("GLB2", [128, SBA], F32)
    GHB = place("GHB", [128, SBA], F32)
    GHB2 = place("GHB2", [128, SBA], F32)
    ovl_end_B = _off[0]
    _off[0] = max(ovl_end_A, ovl_end_B)

    CROW = OUTSB
    total_kb = (_off[0] - 16512) / 1024.0
    assert _off[0] <= 229344, f"SBUF over budget: {total_kb:.1f} KB used, end={_off[0]}"

    QS = [ctx.enter_context(nc.psum_tensor("Q%d" % i, [128, 1024], F32)) for i in range(4)]

    class Ctr:
        def __init__(s, name):
            s.h = ctx.enter_context(nc.semaphore(name))
            s.n = 0
    cLD = Ctr("cLD")
    cWD = Ctr("cWD")
    cV = Ctr("cV")
    cS = Ctr("cS")
    cT = Ctr("cT")
    cG = Ctr("cG")
    cOUT = Ctr("cOUT")

    sched = {e: [] for e in ENGS}

    def em(eng, make, inc=None, k=1):
        if inc is not None:
            inc.n += k
        def f(e, make=make, inc=inc, k=k):
            r = make(e)
            if inc is not None:
                if r is None:
                    r = e.nop()
                r.then_inc(inc.h, k)
        sched[eng].append(f)

    def wait(eng, ctr, n=None):
        v = ctr.n if n is None else n
        if v > 0:
            sched[eng].append(lambda e, h=ctr.h, v=v: e.wait_ge(h, v))

    _selfc = {"vector": None, "scalar": None}

    def bar(eng):
        c = {"vector": cV, "scalar": cS}[eng]
        wait(eng, c, c.n)

    tapv = {}

    def _tap(name, apf):
        if name in taps:
            ap0 = apf()
            sh = list(ap0.shape)
            t = nc.dram_tensor("tap_" + name, sh, ap0.dtype, kind="ExternalOutput").ap()
            wait("sync", cV)
            wait("sync", cS)
            wait("sync", cG)
            wait("sync", cT)
            em("sync", lambda e, t=t, a=ap0: e.dma_start(out=t, in_=a), cOUT, 16)

    em("sync", lambda e: e.dma_start(out=LTC[:], in_=ltc_in), cLD, 16)
    em("sync", lambda e: e.dma_start(out=ONES[:], in_=ones_in), cLD, 16)
    em("sync", lambda e: e.dma_start(out=IDNF[:], in_=idnf_in), cLD, 16)
    em("sync", lambda e: e.dma_start(out=ONESC[:], in_=onesc_in), cLD, 16)
    em("sync", lambda e: e.dma_start(out=ONESR[:], in_=onesr_in), cLD, 16)
    em("sync", lambda e: e.dma_start(out=CBIA[:], in_=cbias_in), cLD, 16)
    em("sync", lambda e: e.dma_start(out=WCL[:], in_=tdb_in), cLD, 16)
    em("sync", lambda e: e.dma_start(out=UB[:], in_=ub_in), cLD, 16)
    for nme in ("ln1", "ln2", "tmk", "tmv", "tmr", "tmk2", "tmr2"):
        em("sync", lambda e, d=PC[nme], s=vecs[nme]: e.dma_start(
            out=d.ap()[:, :].rearrange("p (x c) -> p x c", x=1), in_=s.rearrange("x (c p) -> p x c", p=128),
            allow_slow_non_contiguous=True), cLD, 16)
    em("sync", lambda e: e.dma_start(out=B1S.ap()[:, :].rearrange("p (x c) -> p x c", x=1),
                                     in_=b1_in.rearrange("x (c p) -> p x c", p=128), allow_slow_non_contiguous=True), cLD, 16)
    em("sync", lambda e: e.dma_start(out=B2S.ap()[:, :].rearrange("p (x c) -> p x c", x=1),
                                     in_=vecs["b2"].rearrange("x (c p) -> p x c", p=128), allow_slow_non_contiguous=True), cLD, 16)
    init_ld = cLD.n

    em("gpsimd", lambda e: e.iota(NIOT[:], [[1, 1]], base=0, channel_multiplier=-1,
                                  allow_small_or_imprecise_dtypes=True), cG)
    em("gpsimd", lambda e: e.iota(IOTM1[:], [[1, 1]], base=-1, channel_multiplier=1,
                                  allow_small_or_imprecise_dtypes=True), cG)
    em("gpsimd", lambda e: e.memzero(CARRY[:]), cG)
    g_init = cG.n

    wait("scalar", cLD, init_ld)
    em("scalar", lambda e: e.activation(out=WCL[:], in_=WCL[:], func=AF.Exp), cS)
    wait("vector", cS)
    em("vector", lambda e: e.tensor_scalar(out=WCL[:], in0=WCL[:], scalar1=-1.0, scalar2=20.0,
                                           op0=OP.mult, op1=OP.min), cV)
    em("vector", lambda e: e.tensor_scalar(out=WCL[:], in0=WCL[:], scalar1=-20.0, scalar2=None,
                                           op0=OP.max), cV)
    wait("vector", cG, g_init)
    em("vector", lambda e: e.tensor_scalar(out=EMIN[:], in0=WCL[:], scalar1=NIOT[:, 0:1], scalar2=None,
                                           op0=OP.mult), cV)
    em("vector", lambda e: e.tensor_scalar(out=ESH[:], in0=WCL[:], scalar1=IOTM1[:, 0:1], scalar2=None,
                                           op0=OP.mult), cV)
    em("vector", lambda e: e.tensor_scalar(out=EW2R[:], in0=WCL[0:1, :], scalar1=128.0, scalar2=None,
                                           op0=OP.mult), cV)
    v_pre = cV.n
    wait("scalar", cV, v_pre)
    em("scalar", lambda e: e.activation(out=EMIN[:], in_=EMIN[:], func=AF.Exp), cS)
    em("scalar", lambda e: e.activation(out=ESH[:], in_=ESH[:], func=AF.Exp), cS)
    em("scalar", lambda e: e.activation(out=EW2R[:], in_=EW2R[:], func=AF.Exp), cS)
    s_pre = cS.n
    wait("vector", cS, s_pre)
    em("vector", lambda e: e.tensor_scalar(out=B1S[:], in0=B1S[:], scalar1=256.0, scalar2=None, op0=OP.mult), cV)
    em("vector", lambda e: e.tensor_scalar(out=B2S[:], in0=B2S[:], scalar1=256.0, scalar2=None, op0=OP.mult), cV)

    wkv_b = {"k": kb_s, "v": vb_s, "r": rb_s}
    WOUTV = K2STG

    def dma_w8(dst_ap, wb):
        def f(e, d=dst_ap, w=wb):
            return e.dma_start(out=d.rearrange("p (a c) -> p a c", a=CH),
                               in_=w[:, :, :])
        return f

    pre_chunks = []
    for src, dst in ((wk_in, kb_s), (wv_in, vb_s), (wr_in, rb_s), (wo_in, wob_s), (wr2_in, wr2b_s)):
        pre_chunks.append((src.rearrange("(cc p) c -> p cc c", p=128), dst[:, :, :]))
    wk2v0 = wk2_in.rearrange("(cc p) q -> p cc q", p=128)
    w1v0 = w1_in.rearrange("(cc p) q -> p cc q", p=128)
    for qc in range(4):
        pre_chunks.append((wk2v0[:, :, qc * 1024:(qc + 1) * 1024], wk2b_s[:, :, qc * 1024:(qc + 1) * 1024]))
    for qc in range(4):
        pre_chunks.append((w1v0[:, :, qc * 1024:(qc + 1) * 1024], w1b_s[:, :, qc * 1024:(qc + 1) * 1024]))
    for g in range(4):
        pre_chunks.append((wv2_in[g * 1024:(g + 1) * 1024, :].rearrange("(ii p) c -> p ii c", p=128),
                           wv2b_s[:, g * 8:(g + 1) * 8, :]))
    for g in range(4):
        pre_chunks.append((w2_in[g * 1024:(g + 1) * 1024, :].rearrange("(ii p) c -> p ii c", p=128),
                           w2b_s[:, g * 8:(g + 1) * 8, :]))
    bnc = lambda half: K2STG.ap()[:, half * 8192:(half + 1) * 8192].rearrange("p (a c) -> p a c", a=CH)
    pre_out = {}
    for n, (src, dst) in enumerate(pre_chunks):
        half = n % 2
        if n >= 2:
            wait("gpsimd", cOUT, pre_out[n - 2])
        em("gpsimd", lambda e, src=src, half=half: e.dma_start(out=bnc(half), in_=src), cWD, 16)
        g_in = cWD.n
        wait("sync", cWD, g_in)
        em("sync", lambda e, dst=dst, half=half: e.dma_start(out=dst, in_=bnc(half)), cOUT, 16)
        pre_out[n] = cOUT.n
    pre_done = cOUT.n

    wait("tensor", cLD, init_ld)
    wait("gpsimd", cLD, init_ld)
    for a in range(NSB):
        tok0 = a * SBA
        g0 = a * NTIL

        ld_tile = []
        for t in range(NTIL):
            gt = tok0 + t * 128
            wait("sync", cV)
            wait("sync", cS)
            em("sync", lambda e, gt=gt: e.dma_start(out=XT[:], in_=x_in[gt:gt + 128, :]), cLD, 16)
            ldn = cLD.n
            g = g0 + t
            wait("vector", cLD, ldn)
            em("vector", lambda e, g=g: e.tensor_reduce(
                out=SUMS[:, g:g + 1], in_=XT[:], axis=mybir.AxisListType.X, op=OP.add), cV)
            wait("scalar", cLD, ldn)
            wait("scalar", cOUT, cOUT.n)
            wait("scalar", cT, cT.n)
            wait("scalar", cV, cV.n)
            em("scalar", lambda e, g=g: e.activation(
                out=OUTSB[:], in_=XT[:], func=AF.Square), cS)
            s_sq = cS.n
            wait("vector", cS, s_sq)
            em("vector", lambda e, g=g: e.tensor_reduce(
                out=SQS[:, g:g + 1], in_=OUTSB[:], axis=mybir.AxisListType.X, op=OP.add), cV)
            wait("vector", cS)
            em("vector", lambda e, g=g: e.tensor_scalar(
                out=MC[:, g:g + 1], in0=SUMS[:, g:g + 1], scalar1=1.0 / C, scalar2=None, op0=OP.mult), cV)
            bar("vector")
            em("vector", lambda e, g=g, t=t: e.tensor_tensor(
                out=TCA[:, t:t + 1], in0=MC[:, g:g + 1], in1=MC[:, g:g + 1], op=OP.mult), cV)
            bar("vector")
            em("vector", lambda e, g=g, t=t: e.scalar_tensor_tensor(
                out=TCB[:, t:t + 1], in0=SQS[:, g:g + 1], scalar=1.0 / C, in1=TCA[:, t:t + 1],
                op0=OP.mult, op1=OP.subtract), cV)
            v_var = cV.n
            wait("scalar", cV, v_var)
            em("scalar", lambda e, t=t: e.activation(out=TCA[:, t:t + 1], in_=TCB[:, t:t + 1], func=AF.Ln, bias=CBIA[:, 0:1]), cS)
            bar("scalar")
            em("scalar", lambda e, g=g, t=t: e.activation(
                out=RSTD[:, g:g + 1], in_=TCA[:, t:t + 1], func=AF.Exp, scale=-0.5), cS)
            s_rstd = cS.n
            wait("vector", cS, s_rstd)
            em("vector", lambda e, g=g, t=t: e.tensor_tensor(
                out=TCA[:, t:t + 1], in0=RSTD[:, g:g + 1], in1=RSTD[:, g:g + 1], op=OP.mult), cV)
            bar("vector")
            em("vector", lambda e, t=t: e.tensor_tensor(
                out=TCA[:, t:t + 1], in0=TCB[:, t:t + 1], in1=TCA[:, t:t + 1], op=OP.mult), cV)
            v_ms1 = cV.n
            wait("scalar", cV, v_ms1)
            em("scalar", lambda e, t=t: e.activation(out=TCB[:, t:t + 1], in_=TCA[:, t:t + 1], func=AF.Ln, bias=CBIA[:, 1:2]), cS)
            bar("scalar")
            em("scalar", lambda e, g=g, t=t: e.activation(
                out=RSTD1[:, g:g + 1], in_=TCB[:, t:t + 1], func=AF.Exp, scale=-0.5), cS)
            wait("vector", cOUT, cOUT.n)
            wait("vector", cT, cT.n)
            em("vector", lambda e, g=g: e.tensor_scalar(
                out=OUTSB[:], in0=XT[:], scalar1=MC[:, g:g + 1], scalar2=RSTD[:, g:g + 1],
                op0=OP.subtract, op1=OP.mult), cV)
            v_xnb = cV.n
            wait("tensor", cV, v_xnb)
            wait("tensor", cS)
            for cb in range(CH):
                em("tensor", lambda e, cb=cb: e.transpose(
                    QS[2][:, cb * 128:(cb + 1) * 128], OUTSB[:, cb * 128:(cb + 1) * 128], IDNF[:]), cT)
            t_x = cT.n
            wait("vector", cT, t_x)
            em("vector", lambda e, t=t: e.tensor_copy(
                out=XNFM.ap()[:, :].rearrange("p (c w) -> p c w", c=CH)[:, :, t * 128:(t + 1) * 128],
                in_=QS[2].ap()[:, :].rearrange("p (c w) -> p c w", c=CH)), cV)

        if a == 0:
            _tap("sums_a0", lambda: SUMS[:])
            _tap("sqs_a0", lambda: SQS[:])
            _tap("mc_a0", lambda: MC[:])
            _tap("rstd_a0", lambda: RSTD[:])
            _tap("idnf_a0", lambda: IDNF[:])
            _tap("outsb_a0", lambda: OUTSB[:])
        s_r1 = cS.n
        wait("tensor", cS, s_r1)
        wait("tensor", cV)
        for t in range(NTIL):
            em("tensor", lambda e, t=t, g0=g0: e.transpose(
                QS[0][0:1, t * 128:(t + 1) * 128], RSTD1[:, g0 + t:g0 + t + 1], IDNF[:]), cT)
        t_tr = cT.n
        wait("vector", cT, t_tr)
        em("vector", lambda e: e.tensor_copy(out=ZB[0:1, 0:SBA], in_=QS[0][0:1, 0:SBA]), cV)
        v_srp = cV.n
        wait("tensor", cV, v_srp)
        for t in range(NTIL):
            em("tensor", lambda e, t=t: e.matmul(QS[1][:, t * 128:(t + 1) * 128], ONES[:],
                                                 ZB[0:1, t * 128:(t + 1) * 128], start=True, stop=True), cT)
        t_bc = cT.n
        wait("vector", cT, t_bc)
        em("vector", lambda e: e.tensor_copy(out=RSTD1B[:], in_=QS[1][:, 0:SBA]), cV)
        v_r1b = cV.n

        _tap("xnfm_a%d" % a, lambda: XNFM[:])

        hv = HFM.ap()[:, :].rearrange("p (c w) -> p c w", w=SBA + 1)
        if a == 0:
            em("vector", lambda e, hv=hv: e.memset(hv[:, :, 0:1], 0.0), cV)
        else:
            em("vector", lambda e, hv=hv: e.tensor_copy(out=hv[:, :, 0:1], in_=hv[:, :, SBA:SBA + 1]), cV)
        wait("vector", cV, v_r1b)
        for c in range(CH):
            hb = lambda c=c: HFM[:, c * (SBA + 1) + 1:(c + 1) * (SBA + 1)]
            em("vector", lambda e, c=c, hb=hb: e.tensor_tensor(
                out=hb(), in0=XNFM[:, c * SBA:(c + 1) * SBA], in1=RSTD1B[:], op=OP.mult), cV)
            em("vector", lambda e, c=c, hb=hb: e.tensor_scalar(
                out=hb(), in0=hb(), scalar1=PC["ln1"][:, c:c + 1], scalar2=None, op0=OP.mult), cV)
            em("vector", lambda e, c=c, hb=hb: e.tensor_tensor(
                out=DBUF[:, c * SBA:(c + 1) * SBA], in0=hb(),
                in1=HFM[:, c * (SBA + 1):(c + 1) * (SBA + 1) - 1], op=OP.subtract), cV)
        _tap("h_a%d" % a, lambda: HFM[:])

        projs = [("k", PC["tmk"], KTM), ("v", PC["tmv"], VTM), ("r", PC["tmr"], ERT)]
        wait("vector", cOUT, cOUT.n)
        kvr_seq = [projs[q % 3][0] for q in range(6)]
        wdP = {}
        projT = {}

        def kvr_load(L):
            if a == 0 and L == 0:
                wait("sync", cOUT, pre_done)
            if L <= 1:
                wait("sync", cT, cT.n)
            else:
                wait("sync", cT, projT[L - 2])
            em("sync", lambda e, L=L: e.dma_start(
                out=K2STG[:, (L % 2) * CH * C:(L % 2 + 1) * CH * C].rearrange("p (a c) -> p a c", a=CH),
                in_=wkv_b[kvr_seq[L]][:, :, :]), cWD, 16)
            wdP[L] = cWD.n
        kvr_load(0)
        kvr_load(1)
        mix_t = []
        for pr in range(2):
            evac_s = {}
            evac_v = {}
            for pi, (pn, tmcol, dstbuf) in enumerate(projs):
                L = pr * 3 + pi
                koff = (L % 2) * CH * C
                qk = [QS[0], QS[1]] if pn in ("k", "r") else [QS[2], QS[3]]
                if pn == "r":
                    wait("tensor", cS)
                for ci in range(CH):
                    ms = len(mix_t) % 2
                    if len(mix_t) >= 2:
                        wait("vector", cT, mix_t[-2])
                    em("vector", lambda e, ms=ms, ci=ci, pr=pr, tmcol=tmcol: e.scalar_tensor_tensor(
                        out=MIXT[:, ms * 256:(ms + 1) * 256],
                        in0=DBUF[:, ci * SBA + pr * 256:ci * SBA + pr * 256 + 256],
                        scalar=tmcol[:, ci:ci + 1],
                        in1=HFM[:, ci * (SBA + 1) + pr * 256:ci * (SBA + 1) + pr * 256 + 256],
                        op0=OP.mult, op1=OP.add), cV)
                    v_mix = cV.n
                    wait("tensor", cV, v_mix)
                    if ci == 0:
                        wait("tensor", cWD, wdP[L])
                    for bl in range(2):
                        for nh in range(2):
                            em("tensor", lambda e, ms=ms, ci=ci, bl=bl, nh=nh, qk=qk, koff=koff: e.matmul(
                                qk[bl][:, nh * 512:(nh + 1) * 512],
                                MIXT[:, ms * 256 + bl * 128:ms * 256 + bl * 128 + 128],
                                K2STG[:, koff + ci * C + nh * 512:koff + ci * C + (nh + 1) * 512],
                                start=(ci == 0), stop=(ci == CH - 1)), cT)
                    mix_t.append(cT.n)
                t_mm = cT.n
                projT[L] = t_mm
                if L + 2 < 6:
                    kvr_load(L + 2)
                for bl in range(2):
                    if pn == "k":
                        wait("scalar", cT, t_mm)
                        em("scalar", lambda e, bl=bl, qk=qk: e.activation(
                            out=KTM[:, bl * C:(bl + 1) * C], in_=qk[bl][:], func=AF.Copy), cS)
                    elif pn == "v":
                        wait("vector", cT, t_mm)
                        em("vector", lambda e, bl=bl, qk=qk: e.tensor_copy(
                            out=VTM[:, bl * C:(bl + 1) * C], in_=qk[bl][:]), cV)
                    else:
                        wait("scalar", cT, t_mm)
                        em("scalar", lambda e, bl=bl, qk=qk: e.activation(
                            out=ERT[:, bl * C:(bl + 1) * C], in_=qk[bl][:], func=AF.Exp, scale=-1.0), cS)
                evac_s[pn] = cS.n
                evac_v[pn] = cV.n

            for bl in range(2):
                blk = 2 * pr + bl
                gq = [QS[2], QS[3]] if bl == 0 else [QS[0], QS[1]]
                wait("vector", cS)
                wait("vector", cV, evac_v["v"])
                em("vector", lambda e, bl=bl: e.tensor_tensor(
                    out=KCB[:], in0=KTM[:, bl * C:(bl + 1) * C], in1=UB[:], op=OP.add), cV)
                em("vector", lambda e: e.tensor_scalar(
                    out=KCB[:], in0=KCB[:], scalar1=20.0, scalar2=-20.0, op0=OP.min, op1=OP.max), cV)
                v_kc = cV.n
                wait("scalar", cV, v_kc)
                em("scalar", lambda e, bl=bl: e.activation(
                    out=EKT[:, bl * C:(bl + 1) * C], in_=KCB[:], func=AF.Exp), cS)
                s_ekt = cS.n
                wait("vector", cS, s_ekt)
                em("vector", lambda e, bl=bl: e.tensor_scalar(
                    out=KCB[:], in0=KTM[:, bl * C:(bl + 1) * C], scalar1=20.0, scalar2=-20.0,
                    op0=OP.min, op1=OP.max), cV)
                v_kc2 = cV.n
                wait("scalar", cV, v_kc2)
                em("scalar", lambda e: e.activation(out=EKB[:], in_=KCB[:], func=AF.Exp), cS)
                s_ek = cS.n
                wait("vector", cS, s_ek)
                em("vector", lambda e: e.tensor_tensor(
                    out=ZB[:, C:2 * C], in0=EKB[:], in1=EMIN[:], op=OP.mult), cV)
                em("vector", lambda e, bl=bl: e.tensor_tensor(
                    out=ZB[:, 0:C], in0=ZB[:, C:2 * C], in1=VTM[:, bl * C:(bl + 1) * C], op=OP.mult), cV)
                v_z = cV.n
                wait("tensor", cV, v_z)
                wait("tensor", cS)
                for seg in range(4):
                    q = gq[seg // 2]
                    qoff = (seg % 2) * 512
                    em("tensor", lambda e, q=q, qoff=qoff, seg=seg: e.matmul(
                        q[:, qoff:qoff + 512], LTC[:],
                        ZB[:, seg * 512:(seg + 1) * 512], start=True, stop=False), cT)
                    em("tensor", lambda e, q=q, qoff=qoff, seg=seg: e.matmul(
                        q[:, qoff:qoff + 512], ONES[:],
                        CARRY[:, seg * 512:(seg + 1) * 512], start=False, stop=True), cT)
                t_g = cT.n
                wait("vector", cT, t_g)
                em("vector", lambda e, gq=gq: e.tensor_tensor(
                    out=APR[:, 0:C], in0=gq[0][:], in1=ESH[:], op=OP.mult), cV)
                em("vector", lambda e, gq=gq: e.tensor_tensor(
                    out=APR[:, C:2 * C], in0=gq[1][:], in1=ESH[:], op=OP.mult), cV)
                sq = [QS[0], QS[1]] if bl == 0 else [QS[2], QS[3]]
                for seg in range(4):
                    em("tensor", lambda e, sq=sq, seg=seg: e.matmul(
                        sq[seg // 2][0:1, (seg % 2) * 512:(seg % 2) * 512 + 512], ONESR[:],
                        ZB[:, seg * 512:(seg + 1) * 512], start=True, stop=True), cT)
                t_s = cT.n
                wait("vector", cT, t_s)
                em("vector", lambda e, sq=sq: e.tensor_tensor(
                    out=CROW[0:1, 0:C], in0=CARRY[:, 0:C], in1=sq[0][0:1, 0:C], op=OP.add), cV)
                em("vector", lambda e, sq=sq: e.tensor_tensor(
                    out=XT[0:1, 0:C], in0=CARRY[:, C:2 * C], in1=sq[1][0:1, 0:C], op=OP.add), cV)
                em("vector", lambda e: e.tensor_tensor(
                    out=CARRY[:, 0:C], in0=CROW[0:1, 0:C], in1=EW2R[:], op=OP.mult), cV)
                em("vector", lambda e: e.tensor_tensor(
                    out=CARRY[:, C:2 * C], in0=XT[0:1, 0:C], in1=EW2R[:], op=OP.mult), cV)
                em("vector", lambda e, bl=bl: e.tensor_tensor(
                    out=EKTV[:], in0=EKT[:, bl * C:(bl + 1) * C],
                    in1=VTM[:, bl * C:(bl + 1) * C], op=OP.mult), cV)
                em("vector", lambda e: e.tensor_tensor(
                    out=EKTV[:], in0=EKTV[:], in1=APR[:, 0:C], op=OP.add), cV)
                em("vector", lambda e, bl=bl: e.scalar_tensor_tensor(
                    out=DEN[:], in0=EKT[:, bl * C:(bl + 1) * C], scalar=1e-8,
                    in1=APR[:, C:2 * C], op0=OP.add, op1=OP.add), cV)
                wait("vector", cS, evac_s.get("r", 0))
                em("vector", lambda e, bl=bl: e.scalar_tensor_tensor(
                    out=DEN[:], in0=ERT[:, bl * C:(bl + 1) * C], scalar=1.0,
                    in1=DEN[:], op0=OP.add, op1=OP.mult), cV)
                v_den = cV.n
                wait("scalar", cV, v_den)
                em("scalar", lambda e: e.activation(out=KCB[:], in_=DEN[:], func=AF.Ln), cS)
                em("scalar", lambda e: e.activation(out=EKB[:], in_=KCB[:], func=AF.Exp, scale=-1.0), cS)
                s_rd = cS.n
                wait("vector", cS, s_rd)
                em("vector", lambda e: e.tensor_tensor(out=RWT[:], in0=EKTV[:], in1=EKB[:], op=OP.mult), cV)
                v_rwt = cV.n
                wait("tensor", cV, v_rwt)
                tq = gq[0]
                for cb in range(CH):
                    em("tensor", lambda e, cb=cb, tq=tq: e.transpose(
                        tq[:, cb * 128:(cb + 1) * 128], RWT[:, cb * 128:(cb + 1) * 128], IDNF[:]), cT)
                t_rt = cT.n
                wait("vector", cT, t_rt)
                em("vector", lambda e, blk=blk, tq=tq: e.tensor_copy(
                    out=RWKVFM.ap()[:, :].rearrange("p (c w) -> p c w", c=CH)[:, :, blk * 128:(blk + 1) * 128],
                    in_=tq.ap()[:, :].rearrange("p (c w) -> p c w", c=CH)), cV)

        v_rwkv = cV.n
        _tap("rwkv_a%d" % a, lambda: RWKVFM[:])

        wait("sync", cT, cT.n)
        em("sync", dma_w8(WOUTV[:, 0:CH * C], wob_s), cWD, 16)
        wd_wo = cWD.n
        wait("tensor", cV, v_rwkv)
        wait("tensor", cS)
        wait("tensor", cWD, wd_wo)
        for co in range(CH):
            q = QS[co % 2]
            for ci in range(CH):
                em("tensor", lambda e, co=co, ci=ci, q=q: e.matmul(
                    q[:, 0:512], WOUTV[:, ci * C + co * 128:ci * C + co * 128 + 128],
                    RWKVFM[:, ci * SBA:(ci + 1) * SBA],
                    start=(ci == 0), stop=(ci == CH - 1)), cT)
            t_wo = cT.n
            wait("scalar", cT, t_wo)
            em("scalar", lambda e, co=co, q=q: e.activation(
                out=FINALW[:, co * SBA:(co + 1) * SBA], in_=q[:, 0:512], func=AF.Copy), cS)
            s_fw = cS.n
            wait("vector", cT, t_wo)
            wait("vector", cS, s_fw)
            em("vector", lambda e, co=co, q=q: e.tensor_tensor(
                out=X1BF[:, co * SBA:(co + 1) * SBA], in0=q[:, 0:512],
                in1=XNFM[:, co * SBA:(co + 1) * SBA], op=OP.add), cV)
            if co >= 1:
                wait("tensor", cS)
                wait("tensor", cV)
        v_x1 = cV.n
        _tap("x1_a%d" % a, lambda: X1BF[:])
        _tap("att_a%d" % a, lambda: FINALW[:])

        wait("vector", cV, v_x1)
        wait("tensor", cS)
        wait("tensor", cV)
        t_mm2 = {}
        for c in range(CH):
            gb = GBUF if c % 2 == 0 else XR2J
            if c >= 2:
                wait("vector", cT, t_mm2[c - 2])
            em("vector", lambda e, c=c, gb=gb: e.tensor_tensor(
                out=gb[:, 0:SBA], in0=X1BF[:, c * SBA:(c + 1) * SBA],
                in1=X1BF[:, c * SBA:(c + 1) * SBA], op=OP.mult), cV)
            v_sq = cV.n
            wait("tensor", cV, v_sq)
            em("tensor", lambda e, c=c, gb=gb: e.matmul(
                QS[2][0:1, 0:512], ONESC[:], gb[:, 0:SBA], start=(c == 0), stop=(c == CH - 1)), cT)
            t_mm2[c] = cT.n
        t_ms2 = cT.n
        wait("scalar", cT, t_ms2)
        em("scalar", lambda e: e.activation(out=OUTSB[0:1, 0:512], in_=QS[2][0:1, 0:512],
                                            func=AF.Ln, scale=1.0 / C, bias=CBIA[0:1, 1:2]), cS)
        bar("scalar")
        em("scalar", lambda e: e.activation(out=ZB[0:1, 0:512], in_=OUTSB[0:1, 0:512],
                                            func=AF.Exp, scale=-0.5), cS)
        s_rs2 = cS.n
        wait("tensor", cS, s_rs2)
        for t in range(NTIL):
            em("tensor", lambda e, t=t: e.matmul(QS[3][:, t * 128:(t + 1) * 128], ONES[:],
                                                 ZB[0:1, t * 128:(t + 1) * 128], start=True, stop=True), cT)
        t_r2b = cT.n
        wait("vector", cT, t_r2b)
        em("vector", lambda e: e.tensor_copy(out=RSTD2B[:], in_=QS[3][:, 0:SBA]), cV)
        v_r2b = cV.n
        _tap("rstd2_a%d" % a, lambda: RSTD2B[:])

        h2v = H2FM.ap()[:, :].rearrange("p (c w) -> p c w", w=SBA + 1)
        if a == 0:
            em("vector", lambda e, h2v=h2v: e.memset(h2v[:, :, 0:1], 0.0), cV)
        else:
            em("vector", lambda e, h2v=h2v: e.tensor_copy(out=h2v[:, :, 0:1], in_=h2v[:, :, SBA:SBA + 1]), cV)
        wait("vector", cV, v_r2b)
        for c in range(CH):
            h2b = lambda c=c: H2FM[:, c * (SBA + 1) + 1:(c + 1) * (SBA + 1)]
            em("vector", lambda e, c=c, h2b=h2b: e.tensor_tensor(
                out=h2b(), in0=X1BF[:, c * SBA:(c + 1) * SBA], in1=RSTD2B[:], op=OP.mult), cV)
            em("vector", lambda e, c=c, h2b=h2b: e.tensor_scalar(
                out=h2b(), in0=h2b(), scalar1=PC["ln2"][:, c:c + 1], scalar2=None, op0=OP.mult), cV)
            em("vector", lambda e, c=c, h2b=h2b: e.tensor_tensor(
                out=DBUF[:, c * SBA:(c + 1) * SBA], in0=h2b(),
                in1=H2FM[:, c * (SBA + 1):(c + 1) * (SBA + 1) - 1], op=OP.subtract), cV)
            em("vector", lambda e, c=c: e.scalar_tensor_tensor(
                out=XK2[:, c * SBA:(c + 1) * SBA], in0=DBUF[:, c * SBA:(c + 1) * SBA],
                scalar=PC["tmk2"][:, c:c + 1],
                in1=H2FM[:, c * (SBA + 1):(c + 1) * (SBA + 1) - 1],
                op0=OP.mult, op1=OP.add), cV)
        v_xk2 = cV.n
        _tap("xk2_a%d" % a, lambda: XK2[:])

        wait("sync", cT, cT.n)
        em("sync", dma_w8(WOUTV[:, 0:CH * C], wr2b_s), cWD, 16)
        wd_r2 = cWD.n
        wait("tensor", cWD, wd_r2)
        wait("tensor", cV, v_r2b)
        r2_t = []
        XRp = (XR2J, GBUF)
        for half in range(2):
            for ci in range(CH):
                rr = half * CH + ci
                if rr >= 2:
                    wait("vector", cT, r2_t[-2])
                else:
                    wait("vector", cT, cT.n)
                em("vector", lambda e, ci=ci, rr=rr: e.scalar_tensor_tensor(
                    out=XRp[rr % 2][:, 0:SBA], in0=DBUF[:, ci * SBA:(ci + 1) * SBA],
                    scalar=PC["tmr2"][:, ci:ci + 1],
                    in1=H2FM[:, ci * (SBA + 1):(ci + 1) * (SBA + 1) - 1],
                    op0=OP.mult, op1=OP.add), cV)
                v_xr = cV.n
                wait("tensor", cV, v_xr)
                for cg in range(4):
                    co = half * 4 + cg
                    em("tensor", lambda e, co=co, ci=ci, cg=cg, rr=rr: e.matmul(
                        QS[cg][:, 0:512], WOUTV[:, ci * C + co * 128:ci * C + co * 128 + 128],
                        XRp[rr % 2][:, 0:SBA],
                        start=(ci == 0), stop=(ci == CH - 1)), cT)
                r2_t.append(cT.n)
            t_r2 = cT.n
            wait("scalar", cT, t_r2)
            for cg in range(4):
                co = half * 4 + cg
                em("scalar", lambda e, co=co, cg=cg: e.activation(
                    out=R2TH[:, co * SBA:(co + 1) * SBA], in_=QS[cg][:, 0:512],
                    func=AF.Tanh, scale=0.5), cS)
            wait("tensor", cS)
        s_r2 = cS.n
        _tap("r2th_a%d" % a, lambda: R2TH[:])

        k2base = 0
        wk2v = wk2_in.rearrange("(cc p) q -> p cc q", p=128)
        w1v = w1_in.rearrange("(cc p) q -> p cc q", p=128)
        bmap = [(0, 512), (1, 0), (1, 512), (2, 0), (2, 512), (3, 0), (3, 512)]
        GBp = (GBUF, XR2J)
        s_rl = {}
        v_sq = {}
        t_k2 = {}
        def cm_slabs(g):
            gp = g % 2
            wait("sync", cT, cT.n)
            wait("gpsimd", cT, cT.n)
            em("sync", lambda e, g=g, gp=gp: e.dma_start(
                out=WSA[:, gp * 4096:(gp + 1) * 4096].rearrange("p (b q) -> p b q", b=CH),
                in_=wk2b_s[:, :, g * 512:g * 512 + 512]), cWD, 16)
            em("gpsimd", lambda e, g=g, gp=gp: e.dma_start(
                out=WSB[:, gp * 4 * C:(gp + 1) * 4 * C].rearrange("p (a c) -> p a c", a=4),
                in_=wv2b_s[:, g * 4:g * 4 + 4, :]), cWD, 16)
            wd_grp[g] = cWD.n
        wd_grp = {}
        cm_slabs(0)
        for i in range(ICH + 2):
            pi = i % 2
            if i < ICH:
                if i % 4 == 0:
                    wait("tensor", cWD, wd_grp[i // 4])
                if i == 30:
                    wait("sync", cT, cT.n)
                    em("sync", lambda e: e.dma_start(
                        out=WSB[:, 0:4096].rearrange("p (j c) -> p j c", j=ICH),
                        in_=wv2b_s[:, :, 896:1024]), cWD, 16)
                    wd_cm7 = cWD.n
                ip = (i // 4) % 2
                io = i % 4
                if i == 0:
                    wait("tensor", cV, v_xk2)
                    wait("tensor", cS, s_r2)
                else:
                    wait("tensor", cS, s_rl[i - 1])
                for ci in range(CH):
                    em("tensor", lambda e, i=i, ci=ci, ip=ip, io=io: e.matmul(
                        QS[0][:, 0:512],
                        WSA[:, ip * 4096 + ci * 512 + io * 128:ip * 4096 + ci * 512 + io * 128 + 128],
                        XK2[:, ci * SBA:(ci + 1) * SBA],
                        start=(ci == 0), stop=(ci == CH - 1)), cT)
                t_k2[i] = cT.n
            j = i - 2
            if 0 <= j < ICH:
                jp = (j // 4) % 2
                jo = j % 4
                wait("tensor", cV, v_sq[j])
                for co in range(7):
                    sl, qo = bmap[co]
                    em("tensor", lambda e, j=j, co=co, sl=sl, qo=qo, jp=jp, jo=jo: e.matmul(
                        QS[sl][:, qo:qo + 512],
                        WSB[:, jp * 4 * C + jo * C + co * 128:jp * 4 * C + jo * C + co * 128 + 128],
                        K2STG[:, k2base + j * SBA:k2base + (j + 1) * SBA],
                        start=(j == 0), stop=(j == ICH - 1)), cT)
            if i % 4 == 1 and i // 4 + 1 < ICH // 4:
                cm_slabs(i // 4 + 1)
            if i < ICH:
                wait("scalar", cT, t_k2[i])
                if i >= 2:
                    wait("scalar", cV, v_sq[i - 2])
                em("scalar", lambda e, pi=pi: e.activation(
                    out=GBp[pi][:, 0:SBA], in_=QS[0][:, 0:512], func=AF.Relu), cS)
                s_rl[i] = cS.n
                wait("vector", cS, s_rl[i])
                em("vector", lambda e, i=i, pi=pi: e.tensor_tensor(
                    out=K2STG[:, k2base + i * SBA:k2base + (i + 1) * SBA],
                    in0=GBp[pi][:, 0:SBA], in1=GBp[pi][:, 0:SBA], op=OP.mult), cV)
                v_sq[i] = cV.n
        _tap("k2_a%d" % a, lambda: K2STG[:])

        wait("tensor", cWD, wd_cm7)
        wait("tensor", cS, s_rl[ICH - 1])
        for j in range(ICH):
            em("tensor", lambda e, j=j: e.matmul(
                QS[0][:, 0:512], WSB[:, j * 128:(j + 1) * 128],
                K2STG[:, k2base + j * SBA:k2base + (j + 1) * SBA],
                start=(j == 0), stop=(j == ICH - 1)), cT)
        t_cm = cT.n
        v_cm7 = None
        for co in [7, 0, 1, 2, 3, 4, 5, 6]:
            if co < 7:
                sl, qo = bmap[co]
            else:
                sl, qo = 0, 0
            wait("vector", cT, t_cm)
            em("vector", lambda e, co=co, sl=sl, qo=qo: e.scalar_tensor_tensor(
                out=GLB[:], in0=R2TH[:, co * SBA:(co + 1) * SBA], scalar=1.0,
                in1=QS[sl][:, qo:qo + 512], op0=OP.add, op1=OP.mult), cV)
            em("vector", lambda e, co=co: e.scalar_tensor_tensor(
                out=FINALW[:, co * SBA:(co + 1) * SBA], in0=GLB[:], scalar=0.5,
                in1=FINALW[:, co * SBA:(co + 1) * SBA], op0=OP.mult, op1=OP.add), cV)
            if co == 7:
                v_cm7 = cV.n
        v_cm = cV.n
        _tap("cm_a%d" % a, lambda: FINALW[:])

        ITp = (ITILE, ITILE2)
        IQFp = (Z1B, IQF2)
        FLBp = (FLB, FLB2)
        GLBp = (GLB, GLB2)
        GHBp = (GHB, GHB2)
        bmap = [(0, 512), (1, 0), (1, 512), (2, 0), (2, 512), (3, 0), (3, 512)]
        v_iq = {}
        v_fi = {}
        v_i2 = {}
        s_iqf = {}
        s_flb = {}
        s_g = {}
        s_hq = {}
        g_dd = {}
        t_w1 = {}
        def ffn_slabs(g):
            gp = g % 2
            wait("sync", cT, cT.n)
            wait("gpsimd", cT, cT.n)
            em("sync", lambda e, g=g, gp=gp: e.dma_start(
                out=WSA[:, gp * 4096:(gp + 1) * 4096].rearrange("p (b q) -> p b q", b=CH),
                in_=w1b_s[:, :, g * 512:g * 512 + 512]), cWD, 16)
            em("gpsimd", lambda e, g=g, gp=gp: e.dma_start(
                out=WSB[:, gp * 4 * C:(gp + 1) * 4 * C].rearrange("p (a c) -> p a c", a=4),
                in_=w2b_s[:, g * 4:g * 4 + 4, :]), cWD, 16)
            wd_grp[g] = cWD.n
        wd_grp = {}
        ffn_slabs(0)
        for i in range(ICH + 2):
            pi = i % 2
            if i < ICH:
                if i % 4 == 0:
                    wait("tensor", cWD, wd_grp[i // 4])
                if i == 30:
                    wait("sync", cT, cT.n)
                    em("sync", lambda e: e.dma_start(
                        out=WSB[:, 0:4096].rearrange("p (j c) -> p j c", j=ICH),
                        in_=w2b_s[:, :, 896:1024]), cWD, 16)
                    wd_co7 = cWD.n
                ip = (i // 4) % 2
                io = i % 4
                if i == 0:
                    wait("tensor", cV, v_cm7)
                else:
                    wait("tensor", cV, v_iq[i - 1])
                for ci in range(CH):
                    em("tensor", lambda e, i=i, ci=ci, ip=ip, io=io: e.matmul(
                        QS[0][:, 0:512],
                        WSA[:, ip * 4096 + ci * 512 + io * 128:ip * 4096 + ci * 512 + io * 128 + 128],
                        XNFM[:, ci * SBA:(ci + 1) * SBA],
                        start=(ci == 0), stop=(ci == CH - 1)), cT)
                t_w1[i] = cT.n
            j = i - 2
            if 0 <= j < ICH:
                jp = (j // 4) % 2
                jo = j % 4
                wait("tensor", cS, s_hq[j])
                if j == 0:
                    wait("tensor", cV, v_cm)
                for co in range(7):
                    sl, qo = bmap[co]
                    em("tensor", lambda e, j=j, co=co, sl=sl, qo=qo, jp=jp, jo=jo: e.matmul(
                        QS[sl][:, qo:qo + 512],
                        WSB[:, jp * 4 * C + jo * C + co * 128:jp * 4 * C + jo * C + co * 128 + 128],
                        K2STG[:, k2base + j * SBA:k2base + (j + 1) * SBA],
                        start=(j == 0), stop=(j == ICH - 1)), cT)
            if i % 4 == 1 and i // 4 + 1 < ICH // 4:
                ffn_slabs(i // 4 + 1)
            h = i - 1
            if i < ICH:
                wait("vector", cT, t_w1[i])
                if i >= 1:
                    wait("vector", cS, s_g[i - 1])
                em("vector", lambda e, i=i: e.tensor_scalar(
                    out=ITILE[:], in0=QS[0][:, 0:512], scalar1=256.0, scalar2=B1S[:, i:i + 1],
                    op0=OP.mult, op1=OP.add), cV)
                v_iq[i] = cV.n
                if i == 0 and ("iq0_a%d" % a) in taps:
                    tq0 = nc.dram_tensor("tap_iq0_a%d" % a, [128, SBA], I32, kind="ExternalOutput").ap()
                    wait("sync", cV)
                    em("sync", lambda e, tq0=tq0: e.dma_start(out=tq0, in_=ITILE[:]), cOUT, 16)
                    wait("vector", cOUT, cOUT.n)
                wait("scalar", cV, v_iq[i])
                em("scalar", lambda e, pi=pi: e.activation(
                    out=IQFp[pi][:], in_=ITILE[:], func=AF.Copy, scale=1.0), cS)
                s_iqf[i] = cS.n
            if 0 <= h < ICH:
                ph = h % 2
                wait("vector", cS, s_flb[h])
                em("vector", lambda e, ph=ph: e.scalar_tensor_tensor(
                    out=IQFp[ph][:], in0=IQFp[ph][:], scalar=0.015625, in1=FLBp[ph][:],
                    op0=OP.mult, op1=OP.subtract), cV)
            if i < ICH:
                wait("vector", cS, s_iqf[i])
                em("vector", lambda e, pi=pi: e.tensor_scalar(
                    out=ITILE[:], in0=IQFp[pi][:], scalar1=0.015625, scalar2=15.5078125,
                    op0=OP.mult, op1=OP.add), cV)
                v_fi[i] = cV.n
                wait("scalar", cV, v_fi[i])
                em("scalar", lambda e, pi=pi: e.activation(
                    out=FLBp[pi][:], in_=ITILE[:], func=AF.Copy, scale=1.0), cS)
                s_flb[i] = cS.n
                if i >= 2:
                    wait("scalar", cG, g_dd[i - 2])
                em("scalar", lambda e, pi=pi: e.activation(
                    out=GLBp[pi][:], in_=ITILE[:], func=AF.Gelu, scale=0.25, bias=CBIA[:, 2:3]), cS)
                em("scalar", lambda e, pi=pi: e.activation(
                    out=GHBp[pi][:], in_=ITILE[:], func=AF.Gelu, scale=0.25, bias=CBIA[:, 3:4]), cS)
                s_g[i] = cS.n
                wait("gpsimd", cS, s_g[i])
                em("gpsimd", lambda e, pi=pi: e.tensor_tensor(
                    out=GHBp[pi][:], in0=GHBp[pi][:], in1=GLBp[pi][:], op=OP.subtract), cG)
                g_dd[i] = cG.n
            if 0 <= h < ICH:
                ph = h % 2
                wait("vector", cG, g_dd[h])
                em("vector", lambda e, ph=ph: e.scalar_tensor_tensor(
                    out=GHBp[ph][:], in0=IQFp[ph][:], scalar=16.0, in1=GHBp[ph][:],
                    op0=OP.add, op1=OP.mult), cV)
                em("vector", lambda e, ph=ph: e.tensor_tensor(
                    out=GLBp[ph][:], in0=GLBp[ph][:], in1=GHBp[ph][:], op=OP.add), cV)
                if h >= 1:
                    wait("vector", cS, s_hq[h - 1])
                em("vector", lambda e, ph=ph: e.tensor_scalar(
                    out=ITILE2[:], in0=GLBp[ph][:], scalar1=256.0, scalar2=None, op0=OP.mult), cV)
                v_i2[h] = cV.n
                wait("scalar", cV, v_i2[h])
                em("scalar", lambda e, h=h: e.activation(
                    out=K2STG[:, k2base + h * SBA:k2base + (h + 1) * SBA],
                    in_=ITILE2[:], func=AF.Copy, scale=1.0 / 256.0), cS)
                s_hq[h] = cS.n
        _tap("hq_a%d" % a, lambda: K2STG[:])

        wait("tensor", cWD, wd_co7)
        wait("tensor", cV, v_iq[ICH - 1])
        for j in range(ICH):
            em("tensor", lambda e, j=j: e.matmul(
                QS[0][:, 0:512], WSB[:, j * 128:(j + 1) * 128],
                K2STG[:, k2base + j * SBA:k2base + (j + 1) * SBA],
                start=(j == 0), stop=(j == ICH - 1)), cT)
        t_w2 = cT.n
        for co in range(CH):
            if co < 7:
                sl, qo = bmap[co]
            else:
                sl, qo = 0, 0
            wait("vector", cT, t_w2)
            em("vector", lambda e, co=co, sl=sl, qo=qo: e.tensor_scalar(
                out=ITILE[:], in0=QS[sl][:, qo:qo + 512], scalar1=256.0, scalar2=B2S[:, co:co + 1],
                op0=OP.mult, op1=OP.add), cV)
            em("vector", lambda e, co=co: e.scalar_tensor_tensor(
                out=FINALW[:, co * SBA:(co + 1) * SBA], in0=ITILE[:], scalar=1.0 / 256.0,
                in1=FINALW[:, co * SBA:(co + 1) * SBA], op0=OP.mult, op1=OP.add), cV)
        v_ffn = cV.n
        _tap("final_a%d" % a, lambda: FINALW[:])

        fv = FINALW.ap()[:, :].rearrange("p (c w) -> p c w", c=CH)
        for t in range(NTIL):
            g = g0 + t
            gt = tok0 + t * 128
            sl = t % 2
            wait("sync", cV)
            em("sync", lambda e, gt=gt: e.dma_start(out=XT[:], in_=x_in[gt:gt + 128, :]), cLD, 16)
            ld_x = cLD.n
            wait("vector", cLD, ld_x)
            wait("vector", cOUT, cOUT.n)
            em("vector", lambda e, g=g: e.tensor_scalar(
                out=OUTSB[:], in0=XT[:],
                scalar1=MC[:, g:g + 1], scalar2=RSTD[:, g:g + 1],
                op0=OP.subtract, op1=OP.mult), cV)
            em("vector", lambda e: e.tensor_tensor(
                out=OUTSB[:], in0=OUTSB[:], in1=XT[:], op=OP.add), cV)
            v_xpn = cV.n
            q = QS[2 + (t % 2)]
            wait("tensor", cV, v_xpn)
            for cb in range(CH):
                em("tensor", lambda e, cb=cb, t=t, q=q, fv=fv: e.transpose(
                    q[:, cb * 128:(cb + 1) * 128], fv[:, cb, t * 128:(t + 1) * 128], IDNF[:]), cT)
            t_f = cT.n
            wait("vector", cT, t_f)
            em("vector", lambda e, q=q: e.tensor_tensor(
                out=OUTSB[:], in0=OUTSB[:], in1=q[:], op=OP.add), cV)
            v_out = cV.n
            wait("sync", cV, v_out)
            em("sync", lambda e, gt=gt: e.dma_start(
                out=out_t[gt:gt + 128, :], in_=OUTSB[:]), cOUT, 16)

    wait("sync", cOUT, cOUT.n)

    with nc.Block() as block:
        @block.sync
        def _(e):
            for f in sched["sync"]:
                f(e)

        @block.gpsimd
        def _(e):
            for f in sched["gpsimd"]:
                f(e)

        @block.vector
        def _(e):
            for f in sched["vector"]:
                f(e)

        @block.scalar
        def _(e):
            for f in sched["scalar"]:
                f(e)

        @block.tensor
        def _(e):
            for f in sched["tensor"]:
                f(e)

    ctx.close()
    return nc


def _host_inputs(inputs):
    x = np.ascontiguousarray(np.asarray(inputs["L0_attn_x"], dtype=np.float32))
    row = lambda k: np.ascontiguousarray(np.asarray(inputs[k], dtype=np.float32).reshape(1, -1))
    base = {
        "wk": np.ascontiguousarray(np.asarray(inputs["L0_attn_W_key"], np.float32)),
        "wv": np.ascontiguousarray(np.asarray(inputs["L0_attn_W_value"], np.float32)),
        "wr": np.ascontiguousarray(np.asarray(inputs["L0_attn_W_recept"], np.float32)),
        "wo": np.ascontiguousarray(np.asarray(inputs["L0_attn_W_out"], np.float32)),
        "wr2": np.ascontiguousarray(np.asarray(inputs["L0_attn_W_recept2"], np.float32)),
        "wk2": np.ascontiguousarray(np.asarray(inputs["L0_attn_W_key2"], np.float32)),
        "wv2": np.ascontiguousarray(np.asarray(inputs["L0_attn_W_value2"], np.float32)),
        "w1": np.ascontiguousarray(np.asarray(inputs["L0_ffn_w1"], np.float32)),
        "w2": np.ascontiguousarray(np.asarray(inputs["L0_ffn_w2"], np.float32)),
        "ln1": row("L0_attn_ln1_w"), "ln2": row("L0_attn_ln2_w"),
        "tmk": row("L0_attn_time_mix_k"), "tmv": row("L0_attn_time_mix_v"),
        "tmr": row("L0_attn_time_mix_r"), "tmk2": row("L0_attn_time_mix_k2"),
        "tmr2": row("L0_attn_time_mix_r2"),
        "b1": row("L0_ffn_b1"), "b2": row("L0_ffn_b2"),
        "tdb": np.tile(row("L0_attn_time_decay"), (128, 1)),
        "ub": np.tile(row("L0_attn_time_first"), (128, 1)),
        "ltc": np.triu(np.ones((128, 128), np.float32), k=1),
        "ones1": np.ones((1, 128), np.float32),
        "idnf": np.eye(128, dtype=np.float32),
        "onesc": np.ones((128, 1)).astype(ml_dtypes.bfloat16),
        "onesr": np.ones((128, 1), np.float32),
        "cbias": np.tile(np.array([[1e-5, 1e-6, -4.0, -3.75, 0, 0, 0, 0]], np.float32), (128, 1)),
    }
    in_maps = []
    for b in range(NCORE):
        m = dict(base)
        m["x"] = np.ascontiguousarray(x[b])
        in_maps.append(m)
    return in_maps


def kernel(**inputs):
    nc = _build(taps=tuple(_TAPS))
    in_maps = _host_inputs(inputs)
    res = run_bass_kernel_spmd(nc, in_maps, core_ids=list(range(NCORE)))
    out = np.stack([np.asarray(r["out"]) for r in res.results], axis=0)
    return out.astype(np.float32)


# version 08
# speedup vs baseline: 2.0679x; 1.0099x over previous
import numpy as np
import ml_dtypes
import concourse.bass as bass
import concourse.mybir as mybir
from concourse.alu_op_type import AluOpType as OP
from concourse.bass_utils import run_bass_kernel_spmd

F32 = mybir.dt.float32
F32R = mybir.dt.float32r
BF16 = mybir.dt.bfloat16
I32 = mybir.dt.int32
AF = mybir.ActivationFunctionType

T, C, I = 4096, 1024, 4096
CH = C // 128
ICH = I // 128
SBA = 512
NSB = T // SBA
NTIL = SBA // 128
NCORE = 8

ENGS = ("sync", "gpsimd", "vector", "scalar", "tensor")

_TAPS = []


def _build(taps=(), detect_races=False):
    nc = bass.Bass("TRN2", target_bir_lowering=False, detect_race_conditions=detect_races)
    dr = lambda n, sh, dt: nc.dram_tensor(n, sh, dt, kind="ExternalInput").ap()

    x_in = dr("x", [T, C], F32)
    wk_in = dr("wk", [C, C], F32)
    wv_in = dr("wv", [C, C], F32)
    wr_in = dr("wr", [C, C], F32)
    wo_in = dr("wo", [C, C], F32)
    wr2_in = dr("wr2", [C, C], F32)
    wk2_in = dr("wk2", [C, I], F32)
    wv2_in = dr("wv2", [I, C], F32)
    w1_in = dr("w1", [C, I], F32)
    w2_in = dr("w2", [I, C], F32)
    vecs = {n: dr(n, [1, C], F32) for n in
            ("ln1", "ln2", "tmk", "tmv", "tmr", "tmk2", "tmr2", "b2")}
    b1_in = dr("b1", [1, I], F32)
    tdb_in = dr("tdb", [128, C], F32)
    ub_in = dr("ub", [128, C], F32)
    ltc_in = dr("ltc", [128, 128], F32R)
    ones_in = dr("ones1", [1, 128], F32R)
    idnf_in = dr("idnf", [128, 128], F32)
    onesc_in = dr("onesc", [128, 1], BF16)
    onesr_in = dr("onesr", [128, 1], F32R)
    cbias_in = dr("cbias", [128, 8], F32)

    out_t = nc.dram_tensor("out", [T, C], F32, kind="ExternalOutput").ap()

    drb = lambda n, sh: nc.dram_tensor(n, sh, BF16, kind="Internal").ap()
    kb_s = drb("kb_s", [128, CH, C])
    vb_s = drb("vb_s", [128, CH, C])
    rb_s = drb("rb_s", [128, CH, C])
    wob_s = drb("wob_s", [128, CH, C])
    wr2b_s = drb("wr2b_s", [128, CH, C])
    wk2b_s = drb("wk2b_s", [128, CH, I])
    w1b_s = drb("w1b_s", [128, CH, I])
    wv2b_s = drb("wv2b_s", [128, ICH, C])
    w2b_s = drb("w2b_s", [128, ICH, C])

    from contextlib import ExitStack
    ctx = ExitStack()

    _off = [16512]
    handles = {}

    def place(name, shape, dtype, at=None):
        nbytes = int(np.prod(shape[1:])) * mybir.dt.size(dtype)
        if at is None:
            at = _off[0]
            _off[0] = (at + nbytes + 31) & ~31
        h = nc.alloc_sbuf_tensor_at(name, shape, dtype, offset=at)
        handles[name] = (h, at, nbytes)
        return h

    LTC = place("LTC", [128, 128], F32R)
    ONES = place("ONES", [1, 128], F32R)
    IDNF = place("IDNF", [128, 128], F32)
    ONESC = place("ONESC", [128, 1], BF16)
    ONESR = place("ONESR", [128, 1], F32R)
    WCL = place("WCL", [128, C], F32)
    UB = place("UB", [128, C], F32)
    EMIN = place("EMIN", [128, C], F32)
    ESH = place("ESH", [128, C], F32)
    NIOT = place("NIOT", [128, 1], F32)
    IOTM1 = place("IOTM1", [128, 1], F32)
    EW2R = place("EW2R", [1, C], F32)
    CARRY = place("CARRY", [1, 2 * C], F32R)
    PC = {}
    for nme in ("ln1", "ln2", "tmk", "tmv", "tmr", "tmk2", "tmr2"):
        PC[nme] = place("PC_" + nme, [128, CH], F32)
    B1S = place("B1S", [128, ICH], F32)
    B2S = place("B2S", [128, CH], F32)
    SUMS = place("SUMS", [128, 32], F32)
    SQS = place("SQS", [128, 32], F32)
    MC = place("MC", [128, 32], F32)
    RSTD = place("RSTD", [128, 32], F32)
    RSTD1 = place("RSTD1", [128, 32], F32)
    TCA = place("TCA", [128, 4], F32)
    TCB = place("TCB", [128, 4], F32)
    RSTD1B = place("RSTD1B", [128, SBA], BF16)
    RSTD2B = place("RSTD2B", [128, SBA], BF16)
    XNFM = place("XNFM", [128, CH * SBA], BF16)
    X1BF = place("X1BF", [128, CH * SBA], BF16)
    FINALW = place("FINALW", [128, CH * SBA], F32)
    HFM = place("HFM", [128, CH * (SBA + 1)], BF16)
    DBUF = place("DBUF", [128, CH * SBA], BF16)
    RWKVFM = place("RWKVFM", [128, CH * SBA], BF16)
    H2FM = place("H2FM", [128, CH * (SBA + 1)], BF16)
    MIXT = place("MIXT", [128, 2 * 256], BF16)
    XR2J = place("XR2J", [128, SBA], BF16)
    GBUF = place("GBUF", [128, SBA], BF16)
    K2STG = place("K2STG", [128, ICH * SBA], BF16)
    CBIA = place("CBIA", [128, 8], F32)
    XT = place("XT", [128, C], F32)
    OUTSB = place("OUTSB", [128, C], F32)

    ovl0 = _off[0]

    def placeA(name, shape, dtype):
        return place(name, shape, dtype)

    KTM = placeA("KTM", [128, 2 * C], F32)
    VTM = placeA("VTM", [128, 2 * C], F32)
    ERT = placeA("ERT", [128, 2 * C], F32)
    KCB = placeA("KCB", [128, C], F32)
    EKT = placeA("EKT", [128, 2 * C], F32)
    EKB = placeA("EKB", [128, C], F32)
    ZB = placeA("ZB", [128, 2 * C], F32R)
    APR = placeA("APR", [128, 2 * C], F32)
    EKTV = placeA("EKTV", [128, C], F32)
    DEN = placeA("DEN", [128, C], F32)
    RWT = placeA("RWT", [128, C], F32)
    ovl_end_A = _off[0]

    _off[0] = ovl0
    WSA = place("WSA", [128, 2 * 4096], BF16)
    WSB = place("WSB", [128, 2 * 4 * C], BF16)
    XK2 = place("XK2", [128, CH * SBA], BF16)
    R2TH = place("R2TH", [128, CH * SBA], BF16)
    ITILE = place("ITILE", [128, SBA], I32)
    ITILE2 = place("ITILE2", [128, SBA], I32)
    Z1B = place("Z1B", [128, SBA], F32)
    IQF2 = place("IQF2", [128, SBA], F32)
    FLB = place("FLB", [128, SBA], F32)
    FLB2 = place("FLB2", [128, SBA], F32)
    GLB = place("GLB", [128, SBA], F32)
    GLB2 = place("GLB2", [128, SBA], F32)
    GHB = place("GHB", [128, SBA], F32)
    GHB2 = place("GHB2", [128, SBA], F32)
    ovl_end_B = _off[0]
    _off[0] = max(ovl_end_A, ovl_end_B)

    CROW = OUTSB
    total_kb = (_off[0] - 16512) / 1024.0
    assert _off[0] <= 229344, f"SBUF over budget: {total_kb:.1f} KB used, end={_off[0]}"

    QS = [ctx.enter_context(nc.psum_tensor("Q%d" % i, [128, 1024], F32)) for i in range(4)]

    class Ctr:
        def __init__(s, name):
            s.h = ctx.enter_context(nc.semaphore(name))
            s.n = 0
    cLD = Ctr("cLD")
    cWD = Ctr("cWD")
    cV = Ctr("cV")
    cS = Ctr("cS")
    cT = Ctr("cT")
    cG = Ctr("cG")
    cOUT = Ctr("cOUT")

    sched = {e: [] for e in ENGS}

    def em(eng, make, inc=None, k=1):
        if inc is not None:
            inc.n += k
        def f(e, make=make, inc=inc, k=k):
            r = make(e)
            if inc is not None:
                if r is None:
                    r = e.nop()
                r.then_inc(inc.h, k)
        sched[eng].append(f)

    def wait(eng, ctr, n=None):
        v = ctr.n if n is None else n
        if v > 0:
            sched[eng].append(lambda e, h=ctr.h, v=v: e.wait_ge(h, v))

    _selfc = {"vector": None, "scalar": None}

    def bar(eng):
        c = {"vector": cV, "scalar": cS}[eng]
        wait(eng, c, c.n)

    tapv = {}

    def _tap(name, apf):
        if name in taps:
            ap0 = apf()
            sh = list(ap0.shape)
            t = nc.dram_tensor("tap_" + name, sh, ap0.dtype, kind="ExternalOutput").ap()
            wait("sync", cV)
            wait("sync", cS)
            wait("sync", cG)
            wait("sync", cT)
            em("sync", lambda e, t=t, a=ap0: e.dma_start(out=t, in_=a), cOUT, 16)

    em("sync", lambda e: e.dma_start(out=LTC[:], in_=ltc_in), cLD, 16)
    em("sync", lambda e: e.dma_start(out=ONES[:], in_=ones_in), cLD, 16)
    em("sync", lambda e: e.dma_start(out=IDNF[:], in_=idnf_in), cLD, 16)
    em("sync", lambda e: e.dma_start(out=ONESC[:], in_=onesc_in), cLD, 16)
    em("sync", lambda e: e.dma_start(out=ONESR[:], in_=onesr_in), cLD, 16)
    em("sync", lambda e: e.dma_start(out=CBIA[:], in_=cbias_in), cLD, 16)
    em("sync", lambda e: e.dma_start(out=WCL[:], in_=tdb_in), cLD, 16)
    em("sync", lambda e: e.dma_start(out=UB[:], in_=ub_in), cLD, 16)
    for nme in ("ln1", "ln2", "tmk", "tmv", "tmr", "tmk2", "tmr2"):
        em("sync", lambda e, d=PC[nme], s=vecs[nme]: e.dma_start(
            out=d.ap()[:, :].rearrange("p (x c) -> p x c", x=1), in_=s.rearrange("x (c p) -> p x c", p=128),
            allow_slow_non_contiguous=True), cLD, 16)
    em("sync", lambda e: e.dma_start(out=B1S.ap()[:, :].rearrange("p (x c) -> p x c", x=1),
                                     in_=b1_in.rearrange("x (c p) -> p x c", p=128), allow_slow_non_contiguous=True), cLD, 16)
    em("sync", lambda e: e.dma_start(out=B2S.ap()[:, :].rearrange("p (x c) -> p x c", x=1),
                                     in_=vecs["b2"].rearrange("x (c p) -> p x c", p=128), allow_slow_non_contiguous=True), cLD, 16)
    init_ld = cLD.n

    em("gpsimd", lambda e: e.iota(NIOT[:], [[1, 1]], base=0, channel_multiplier=-1,
                                  allow_small_or_imprecise_dtypes=True), cG)
    em("gpsimd", lambda e: e.iota(IOTM1[:], [[1, 1]], base=-1, channel_multiplier=1,
                                  allow_small_or_imprecise_dtypes=True), cG)
    em("gpsimd", lambda e: e.memzero(CARRY[:]), cG)
    g_init = cG.n

    wait("scalar", cLD, init_ld)
    em("scalar", lambda e: e.activation(out=WCL[:], in_=WCL[:], func=AF.Exp), cS)
    wait("vector", cS)
    em("vector", lambda e: e.tensor_scalar(out=WCL[:], in0=WCL[:], scalar1=-1.0, scalar2=20.0,
                                           op0=OP.mult, op1=OP.min), cV)
    em("vector", lambda e: e.tensor_scalar(out=WCL[:], in0=WCL[:], scalar1=-20.0, scalar2=None,
                                           op0=OP.max), cV)
    wait("vector", cG, g_init)
    em("vector", lambda e: e.tensor_scalar(out=EMIN[:], in0=WCL[:], scalar1=NIOT[:, 0:1], scalar2=None,
                                           op0=OP.mult), cV)
    em("vector", lambda e: e.tensor_scalar(out=ESH[:], in0=WCL[:], scalar1=IOTM1[:, 0:1], scalar2=None,
                                           op0=OP.mult), cV)
    em("vector", lambda e: e.tensor_scalar(out=EW2R[:], in0=WCL[0:1, :], scalar1=128.0, scalar2=None,
                                           op0=OP.mult), cV)
    v_pre = cV.n
    wait("scalar", cV, v_pre)
    em("scalar", lambda e: e.activation(out=EMIN[:], in_=EMIN[:], func=AF.Exp), cS)
    em("scalar", lambda e: e.activation(out=ESH[:], in_=ESH[:], func=AF.Exp), cS)
    em("scalar", lambda e: e.activation(out=EW2R[:], in_=EW2R[:], func=AF.Exp), cS)
    s_pre = cS.n
    wait("vector", cS, s_pre)
    em("vector", lambda e: e.tensor_scalar(out=B1S[:], in0=B1S[:], scalar1=256.0, scalar2=None, op0=OP.mult), cV)
    em("vector", lambda e: e.tensor_scalar(out=B2S[:], in0=B2S[:], scalar1=256.0, scalar2=None, op0=OP.mult), cV)

    wkv_b = {"k": kb_s, "v": vb_s, "r": rb_s}
    WOUTV = K2STG

    def dma_w8(dst_ap, wb):
        def f(e, d=dst_ap, w=wb):
            return e.dma_start(out=d.rearrange("p (a c) -> p a c", a=CH),
                               in_=w[:, :, :])
        return f

    pre_chunks = []
    for src, dst in ((wk_in, kb_s), (wv_in, vb_s), (wr_in, rb_s), (wo_in, wob_s), (wr2_in, wr2b_s)):
        pre_chunks.append((src.rearrange("(cc p) c -> p cc c", p=128), dst[:, :, :]))
    wk2v0 = wk2_in.rearrange("(cc p) q -> p cc q", p=128)
    w1v0 = w1_in.rearrange("(cc p) q -> p cc q", p=128)
    for qc in range(4):
        pre_chunks.append((wk2v0[:, :, qc * 1024:(qc + 1) * 1024], wk2b_s[:, :, qc * 1024:(qc + 1) * 1024]))
    for qc in range(4):
        pre_chunks.append((w1v0[:, :, qc * 1024:(qc + 1) * 1024], w1b_s[:, :, qc * 1024:(qc + 1) * 1024]))
    for g in range(4):
        pre_chunks.append((wv2_in[g * 1024:(g + 1) * 1024, :].rearrange("(ii p) c -> p ii c", p=128),
                           wv2b_s[:, g * 8:(g + 1) * 8, :]))
    for g in range(4):
        pre_chunks.append((w2_in[g * 1024:(g + 1) * 1024, :].rearrange("(ii p) c -> p ii c", p=128),
                           w2b_s[:, g * 8:(g + 1) * 8, :]))
    bnc = lambda half: K2STG.ap()[:, half * 8192:(half + 1) * 8192].rearrange("p (a c) -> p a c", a=CH)
    pre_out = {}
    for n, (src, dst) in enumerate(pre_chunks):
        half = n % 2
        if n >= 2:
            wait("gpsimd", cOUT, pre_out[n - 2])
        em("gpsimd", lambda e, src=src, half=half: e.dma_start(out=bnc(half), in_=src), cWD, 16)
        g_in = cWD.n
        wait("sync", cWD, g_in)
        em("sync", lambda e, dst=dst, half=half: e.dma_start(out=dst, in_=bnc(half)), cOUT, 16)
        pre_out[n] = cOUT.n
    pre_done = cOUT.n

    pm = {"s_ek": 0, "t_S": 0, "t_G": 0, "carry_v": 0, "rwt_v": 0,
          "t_T3": 0, "rwkv_v": 0}
    wait("tensor", cLD, init_ld)
    wait("gpsimd", cLD, init_ld)
    for a in range(NSB):
        tok0 = a * SBA
        g0 = a * NTIL

        ld_tile = []
        for t in range(NTIL):
            gt = tok0 + t * 128
            wait("sync", cV)
            wait("sync", cS)
            em("sync", lambda e, gt=gt: e.dma_start(out=XT[:], in_=x_in[gt:gt + 128, :]), cLD, 16)
            ldn = cLD.n
            g = g0 + t
            wait("vector", cLD, ldn)
            em("vector", lambda e, g=g: e.tensor_reduce(
                out=SUMS[:, g:g + 1], in_=XT[:], axis=mybir.AxisListType.X, op=OP.add), cV)
            wait("scalar", cLD, ldn)
            wait("scalar", cOUT, cOUT.n)
            wait("scalar", cT, cT.n)
            wait("scalar", cV, cV.n)
            em("scalar", lambda e, g=g: e.activation(
                out=OUTSB[:], in_=XT[:], func=AF.Square), cS)
            s_sq = cS.n
            wait("vector", cS, s_sq)
            em("vector", lambda e, g=g: e.tensor_reduce(
                out=SQS[:, g:g + 1], in_=OUTSB[:], axis=mybir.AxisListType.X, op=OP.add), cV)
            wait("vector", cS)
            em("vector", lambda e, g=g: e.tensor_scalar(
                out=MC[:, g:g + 1], in0=SUMS[:, g:g + 1], scalar1=1.0 / C, scalar2=None, op0=OP.mult), cV)
            bar("vector")
            em("vector", lambda e, g=g, t=t: e.tensor_tensor(
                out=TCA[:, t:t + 1], in0=MC[:, g:g + 1], in1=MC[:, g:g + 1], op=OP.mult), cV)
            bar("vector")
            em("vector", lambda e, g=g, t=t: e.scalar_tensor_tensor(
                out=TCB[:, t:t + 1], in0=SQS[:, g:g + 1], scalar=1.0 / C, in1=TCA[:, t:t + 1],
                op0=OP.mult, op1=OP.subtract), cV)
            v_var = cV.n
            wait("scalar", cV, v_var)
            em("scalar", lambda e, t=t: e.activation(out=TCA[:, t:t + 1], in_=TCB[:, t:t + 1], func=AF.Ln, bias=CBIA[:, 0:1]), cS)
            bar("scalar")
            em("scalar", lambda e, g=g, t=t: e.activation(
                out=RSTD[:, g:g + 1], in_=TCA[:, t:t + 1], func=AF.Exp, scale=-0.5), cS)
            s_rstd = cS.n
            wait("vector", cS, s_rstd)
            em("vector", lambda e, g=g, t=t: e.tensor_tensor(
                out=TCA[:, t:t + 1], in0=RSTD[:, g:g + 1], in1=RSTD[:, g:g + 1], op=OP.mult), cV)
            bar("vector")
            em("vector", lambda e, t=t: e.tensor_tensor(
                out=TCA[:, t:t + 1], in0=TCB[:, t:t + 1], in1=TCA[:, t:t + 1], op=OP.mult), cV)
            v_ms1 = cV.n
            wait("scalar", cV, v_ms1)
            em("scalar", lambda e, t=t: e.activation(out=TCB[:, t:t + 1], in_=TCA[:, t:t + 1], func=AF.Ln, bias=CBIA[:, 1:2]), cS)
            bar("scalar")
            em("scalar", lambda e, g=g, t=t: e.activation(
                out=RSTD1[:, g:g + 1], in_=TCB[:, t:t + 1], func=AF.Exp, scale=-0.5), cS)
            wait("vector", cOUT, cOUT.n)
            wait("vector", cT, cT.n)
            em("vector", lambda e, g=g: e.tensor_scalar(
                out=OUTSB[:], in0=XT[:], scalar1=MC[:, g:g + 1], scalar2=RSTD[:, g:g + 1],
                op0=OP.subtract, op1=OP.mult), cV)
            v_xnb = cV.n
            wait("tensor", cV, v_xnb)
            wait("tensor", cS)
            for cb in range(CH):
                em("tensor", lambda e, cb=cb: e.transpose(
                    QS[2][:, cb * 128:(cb + 1) * 128], OUTSB[:, cb * 128:(cb + 1) * 128], IDNF[:]), cT)
            t_x = cT.n
            wait("vector", cT, t_x)
            em("vector", lambda e, t=t: e.tensor_copy(
                out=XNFM.ap()[:, :].rearrange("p (c w) -> p c w", c=CH)[:, :, t * 128:(t + 1) * 128],
                in_=QS[2].ap()[:, :].rearrange("p (c w) -> p c w", c=CH)), cV)

        if a == 0:
            _tap("sums_a0", lambda: SUMS[:])
            _tap("sqs_a0", lambda: SQS[:])
            _tap("mc_a0", lambda: MC[:])
            _tap("rstd_a0", lambda: RSTD[:])
            _tap("idnf_a0", lambda: IDNF[:])
            _tap("outsb_a0", lambda: OUTSB[:])
        s_r1 = cS.n
        wait("tensor", cS, s_r1)
        wait("tensor", cV)
        for t in range(NTIL):
            em("tensor", lambda e, t=t, g0=g0: e.transpose(
                QS[0][0:1, t * 128:(t + 1) * 128], RSTD1[:, g0 + t:g0 + t + 1], IDNF[:]), cT)
        t_tr = cT.n
        wait("vector", cT, t_tr)
        em("vector", lambda e: e.tensor_copy(out=ZB[0:1, 0:SBA], in_=QS[0][0:1, 0:SBA]), cV)
        v_srp = cV.n
        wait("tensor", cV, v_srp)
        for t in range(NTIL):
            em("tensor", lambda e, t=t: e.matmul(QS[1][:, t * 128:(t + 1) * 128], ONES[:],
                                                 ZB[0:1, t * 128:(t + 1) * 128], start=True, stop=True), cT)
        t_bc = cT.n
        wait("vector", cT, t_bc)
        em("vector", lambda e: e.tensor_copy(out=RSTD1B[:], in_=QS[1][:, 0:SBA]), cV)
        v_r1b = cV.n

        _tap("xnfm_a%d" % a, lambda: XNFM[:])

        hv = HFM.ap()[:, :].rearrange("p (c w) -> p c w", w=SBA + 1)
        if a == 0:
            em("vector", lambda e, hv=hv: e.memset(hv[:, :, 0:1], 0.0), cV)
        else:
            em("vector", lambda e, hv=hv: e.tensor_copy(out=hv[:, :, 0:1], in_=hv[:, :, SBA:SBA + 1]), cV)
        wait("vector", cV, v_r1b)
        for c in range(CH):
            hb = lambda c=c: HFM[:, c * (SBA + 1) + 1:(c + 1) * (SBA + 1)]
            em("vector", lambda e, c=c, hb=hb: e.tensor_tensor(
                out=hb(), in0=XNFM[:, c * SBA:(c + 1) * SBA], in1=RSTD1B[:], op=OP.mult), cV)
            em("vector", lambda e, c=c, hb=hb: e.tensor_scalar(
                out=hb(), in0=hb(), scalar1=PC["ln1"][:, c:c + 1], scalar2=None, op0=OP.mult), cV)
            em("vector", lambda e, c=c, hb=hb: e.tensor_tensor(
                out=DBUF[:, c * SBA:(c + 1) * SBA], in0=hb(),
                in1=HFM[:, c * (SBA + 1):(c + 1) * (SBA + 1) - 1], op=OP.subtract), cV)
        _tap("h_a%d" % a, lambda: HFM[:])

        projs = [("k", PC["tmk"], KTM), ("v", PC["tmv"], VTM), ("r", PC["tmr"], ERT)]
        wait("vector", cOUT, cOUT.n)
        kvr_seq = [projs[q % 3][0] for q in range(6)]
        wdP = {}
        projT = {}

        def kvr_load(L):
            if a == 0 and L == 0:
                wait("sync", cOUT, pre_done)
            if L <= 1:
                wait("sync", cT, cT.n)
            else:
                wait("sync", cT, projT[L - 2])
            em("sync", lambda e, L=L: e.dma_start(
                out=K2STG[:, (L % 2) * CH * C:(L % 2 + 1) * CH * C].rearrange("p (a c) -> p a c", a=CH),
                in_=wkv_b[kvr_seq[L]][:, :, :]), cWD, 16)
            wdP[L] = cWD.n
        kvr_load(0)
        kvr_load(1)
        mix_t = []
        for pr in range(2):
            evac_s = {}
            evac_v = {}
            for pi, (pn, tmcol, dstbuf) in enumerate(projs):
                L = pr * 3 + pi
                koff = (L % 2) * CH * C
                qk = [QS[0], QS[1]] if pn in ("k", "r") else [QS[2], QS[3]]
                if pn == "r":
                    wait("tensor", cS)
                for ci in range(CH):
                    ms = len(mix_t) % 2
                    if len(mix_t) >= 2:
                        wait("vector", cT, mix_t[-2])
                    em("vector", lambda e, ms=ms, ci=ci, pr=pr, tmcol=tmcol: e.scalar_tensor_tensor(
                        out=MIXT[:, ms * 256:(ms + 1) * 256],
                        in0=DBUF[:, ci * SBA + pr * 256:ci * SBA + pr * 256 + 256],
                        scalar=tmcol[:, ci:ci + 1],
                        in1=HFM[:, ci * (SBA + 1) + pr * 256:ci * (SBA + 1) + pr * 256 + 256],
                        op0=OP.mult, op1=OP.add), cV)
                    v_mix = cV.n
                    wait("tensor", cV, v_mix)
                    if ci == 0:
                        wait("tensor", cWD, wdP[L])
                    for bl in range(2):
                        for nh in range(2):
                            em("tensor", lambda e, ms=ms, ci=ci, bl=bl, nh=nh, qk=qk, koff=koff: e.matmul(
                                qk[bl][:, nh * 512:(nh + 1) * 512],
                                MIXT[:, ms * 256 + bl * 128:ms * 256 + bl * 128 + 128],
                                K2STG[:, koff + ci * C + nh * 512:koff + ci * C + (nh + 1) * 512],
                                start=(ci == 0), stop=(ci == CH - 1)), cT)
                    mix_t.append(cT.n)
                t_mm = cT.n
                projT[L] = t_mm
                if L + 2 < 6:
                    kvr_load(L + 2)
                for bl in range(2):
                    if pn == "k":
                        wait("scalar", cT, t_mm)
                        em("scalar", lambda e, bl=bl, qk=qk: e.activation(
                            out=KTM[:, bl * C:(bl + 1) * C], in_=qk[bl][:], func=AF.Copy), cS)
                    elif pn == "v":
                        wait("vector", cT, t_mm)
                        em("vector", lambda e, bl=bl, qk=qk: e.tensor_copy(
                            out=VTM[:, bl * C:(bl + 1) * C], in_=qk[bl][:]), cV)
                    else:
                        wait("scalar", cT, t_mm)
                        em("scalar", lambda e, bl=bl, qk=qk: e.activation(
                            out=ERT[:, bl * C:(bl + 1) * C], in_=qk[bl][:], func=AF.Exp, scale=-1.0), cS)
                evac_s[pn] = cS.n
                evac_v[pn] = cV.n

            def p3_prep(bl):
                wait("gpsimd", cS, pm["s_ek"])
                wait("gpsimd", cV, evac_v["v"])
                wait("gpsimd", cS, evac_s["r"])
                em("gpsimd", lambda e, bl=bl: e.tensor_tensor(
                    out=KCB[:], in0=KTM[:, bl * C:(bl + 1) * C], in1=UB[:], op=OP.add), cG)
                g_kc = cG.n
                wait("vector", cG, g_kc)
                em("vector", lambda e: e.tensor_scalar(
                    out=KCB[:], in0=KCB[:], scalar1=20.0, scalar2=-20.0, op0=OP.min, op1=OP.max), cV)
                v_kc = cV.n
                wait("scalar", cV, v_kc)
                em("scalar", lambda e, bl=bl: e.activation(
                    out=EKT[:, bl * C:(bl + 1) * C], in_=KCB[:], func=AF.Exp), cS)
                s_ekt = cS.n
                wait("vector", cS, s_ekt)
                em("vector", lambda e, bl=bl: e.tensor_scalar(
                    out=KCB[:], in0=KTM[:, bl * C:(bl + 1) * C], scalar1=20.0, scalar2=-20.0,
                    op0=OP.min, op1=OP.max), cV)
                v_kc2 = cV.n
                wait("scalar", cV, v_kc2)
                em("scalar", lambda e: e.activation(out=EKB[:], in_=KCB[:], func=AF.Exp), cS)
                pm["s_ek"] = cS.n
                wait("vector", cS, pm["s_ek"])
                wait("vector", cT, pm["t_S"])
                em("vector", lambda e: e.tensor_tensor(
                    out=ZB[:, C:2 * C], in0=EKB[:], in1=EMIN[:], op=OP.mult), cV)
                em("vector", lambda e, bl=bl: e.tensor_tensor(
                    out=ZB[:, 0:C], in0=ZB[:, C:2 * C], in1=VTM[:, bl * C:(bl + 1) * C], op=OP.mult), cV)
                return s_ekt, cV.n

            def p3_T1(gq, v_z):
                wait("tensor", cV, v_z)
                wait("tensor", cV, pm["carry_v"])
                for seg in range(4):
                    q = gq[seg // 2]
                    qoff = (seg % 2) * 512
                    em("tensor", lambda e, q=q, qoff=qoff, seg=seg: e.matmul(
                        q[:, qoff:qoff + 512], LTC[:],
                        ZB[:, seg * 512:(seg + 1) * 512], start=True, stop=False), cT)
                for seg in range(4):
                    q = gq[seg // 2]
                    qoff = (seg % 2) * 512
                    em("tensor", lambda e, q=q, qoff=qoff, seg=seg: e.matmul(
                        q[:, qoff:qoff + 512], ONES[:],
                        CARRY[:, seg * 512:(seg + 1) * 512], start=False, stop=True), cT)
                pm["t_G"] = cT.n

            def p3_T2(sq):
                for seg in range(4):
                    em("tensor", lambda e, sq=sq, seg=seg: e.matmul(
                        sq[seg // 2][0:1, (seg % 2) * 512:(seg % 2) * 512 + 512], ONESR[:],
                        ZB[:, seg * 512:(seg + 1) * 512], start=True, stop=True), cT)
                pm["t_S"] = cT.n

            def p3_ektv(bl, s_ekt):
                wait("gpsimd", cS, s_ekt)
                wait("gpsimd", cV, pm["rwt_v"])
                em("gpsimd", lambda e, bl=bl: e.tensor_tensor(
                    out=EKTV[:], in0=EKT[:, bl * C:(bl + 1) * C],
                    in1=VTM[:, bl * C:(bl + 1) * C], op=OP.mult), cG)
                return cG.n

            def p3_tail(bl, gq, sq, g_ektv):
                wait("vector", cT, pm["t_G"])
                em("vector", lambda e, gq=gq: e.tensor_tensor(
                    out=APR[:, 0:C], in0=gq[0][:], in1=ESH[:], op=OP.mult), cV)
                em("vector", lambda e, gq=gq: e.tensor_tensor(
                    out=APR[:, C:2 * C], in0=gq[1][:], in1=ESH[:], op=OP.mult), cV)
                wait("vector", cT, pm["t_S"])
                em("vector", lambda e, sq=sq: e.tensor_tensor(
                    out=CROW[0:1, 0:C], in0=CARRY[:, 0:C], in1=sq[0][0:1, 0:C], op=OP.add), cV)
                em("vector", lambda e, sq=sq: e.tensor_tensor(
                    out=XT[0:1, 0:C], in0=CARRY[:, C:2 * C], in1=sq[1][0:1, 0:C], op=OP.add), cV)
                em("vector", lambda e: e.tensor_tensor(
                    out=CARRY[:, 0:C], in0=CROW[0:1, 0:C], in1=EW2R[:], op=OP.mult), cV)
                em("vector", lambda e: e.tensor_tensor(
                    out=CARRY[:, C:2 * C], in0=XT[0:1, 0:C], in1=EW2R[:], op=OP.mult), cV)
                pm["carry_v"] = cV.n
                wait("vector", cG, g_ektv)
                em("vector", lambda e: e.tensor_tensor(
                    out=EKTV[:], in0=EKTV[:], in1=APR[:, 0:C], op=OP.add), cV)
                em("vector", lambda e, bl=bl: e.scalar_tensor_tensor(
                    out=DEN[:], in0=EKT[:, bl * C:(bl + 1) * C], scalar=1e-8,
                    in1=APR[:, C:2 * C], op0=OP.add, op1=OP.add), cV)
                em("vector", lambda e, bl=bl: e.scalar_tensor_tensor(
                    out=DEN[:], in0=ERT[:, bl * C:(bl + 1) * C], scalar=1.0,
                    in1=DEN[:], op0=OP.add, op1=OP.mult), cV)
                v_den = cV.n
                wait("scalar", cV, v_den)
                em("scalar", lambda e: e.activation(out=APR[:, 0:C], in_=DEN[:], func=AF.Ln), cS)
                em("scalar", lambda e: e.activation(out=DEN[:], in_=APR[:, 0:C], func=AF.Exp, scale=-1.0), cS)
                s_rd = cS.n
                wait("vector", cS, s_rd)
                wait("vector", cT, pm["t_T3"])
                em("vector", lambda e: e.tensor_tensor(out=RWT[:], in0=EKTV[:], in1=DEN[:], op=OP.mult), cV)
                pm["rwt_v"] = cV.n

            def p3_T3(blk, gq):
                tq = gq[0]
                wait("tensor", cV, pm["rwt_v"])
                for cb in range(CH):
                    em("tensor", lambda e, cb=cb, tq=tq: e.transpose(
                        tq[:, cb * 128:(cb + 1) * 128], RWT[:, cb * 128:(cb + 1) * 128], IDNF[:]), cT)
                pm["t_T3"] = cT.n
                wait("vector", cT, pm["t_T3"])
                em("vector", lambda e, blk=blk, tq=tq: e.tensor_copy(
                    out=RWKVFM.ap()[:, :].rearrange("p (c w) -> p c w", c=CH)[:, :, blk * 128:(blk + 1) * 128],
                    in_=tq.ap()[:, :].rearrange("p (c w) -> p c w", c=CH)), cV)
                pm["rwkv_v"] = cV.n

            GQ = {0: [QS[2], QS[3]], 1: [QS[0], QS[1]]}
            SQ = {0: [QS[0], QS[1]], 1: [QS[2], QS[3]]}
            wait("tensor", cS, evac_s["r"])
            s_ekt0, v_z0 = p3_prep(0)
            p3_T1(GQ[0], v_z0)
            p3_T2(SQ[0])
            g_ektv0 = p3_ektv(0, s_ekt0)
            s_ekt1, v_z1 = p3_prep(1)
            p3_tail(0, GQ[0], SQ[0], g_ektv0)
            p3_T1(GQ[1], v_z1)
            g_ektv1 = p3_ektv(1, s_ekt1)
            p3_T3(2 * pr, GQ[0])
            wait("tensor", cV, pm["rwkv_v"])
            p3_T2(SQ[1])
            p3_tail(1, GQ[1], SQ[1], g_ektv1)
            p3_T3(2 * pr + 1, GQ[1])

        v_rwkv = cV.n
        _tap("rwkv_a%d" % a, lambda: RWKVFM[:])

        wait("sync", cT, cT.n)
        em("sync", dma_w8(WOUTV[:, 0:CH * C], wob_s), cWD, 16)
        wd_wo = cWD.n
        wait("tensor", cV, v_rwkv)
        wait("tensor", cS)
        wait("tensor", cWD, wd_wo)
        for co in range(CH):
            q = QS[co % 2]
            for ci in range(CH):
                em("tensor", lambda e, co=co, ci=ci, q=q: e.matmul(
                    q[:, 0:512], WOUTV[:, ci * C + co * 128:ci * C + co * 128 + 128],
                    RWKVFM[:, ci * SBA:(ci + 1) * SBA],
                    start=(ci == 0), stop=(ci == CH - 1)), cT)
            t_wo = cT.n
            wait("scalar", cT, t_wo)
            em("scalar", lambda e, co=co, q=q: e.activation(
                out=FINALW[:, co * SBA:(co + 1) * SBA], in_=q[:, 0:512], func=AF.Copy), cS)
            s_fw = cS.n
            wait("vector", cT, t_wo)
            wait("vector", cS, s_fw)
            em("vector", lambda e, co=co, q=q: e.tensor_tensor(
                out=X1BF[:, co * SBA:(co + 1) * SBA], in0=q[:, 0:512],
                in1=XNFM[:, co * SBA:(co + 1) * SBA], op=OP.add), cV)
            if co >= 1:
                wait("tensor", cS)
                wait("tensor", cV)
        v_x1 = cV.n
        _tap("x1_a%d" % a, lambda: X1BF[:])
        _tap("att_a%d" % a, lambda: FINALW[:])

        wait("vector", cV, v_x1)
        wait("tensor", cS)
        wait("tensor", cV)
        t_mm2 = {}
        for c in range(CH):
            gb = GBUF if c % 2 == 0 else XR2J
            if c >= 2:
                wait("vector", cT, t_mm2[c - 2])
            em("vector", lambda e, c=c, gb=gb: e.tensor_tensor(
                out=gb[:, 0:SBA], in0=X1BF[:, c * SBA:(c + 1) * SBA],
                in1=X1BF[:, c * SBA:(c + 1) * SBA], op=OP.mult), cV)
            v_sq = cV.n
            wait("tensor", cV, v_sq)
            em("tensor", lambda e, c=c, gb=gb: e.matmul(
                QS[2][0:1, 0:512], ONESC[:], gb[:, 0:SBA], start=(c == 0), stop=(c == CH - 1)), cT)
            t_mm2[c] = cT.n
        t_ms2 = cT.n
        wait("scalar", cT, t_ms2)
        em("scalar", lambda e: e.activation(out=OUTSB[0:1, 0:512], in_=QS[2][0:1, 0:512],
                                            func=AF.Ln, scale=1.0 / C, bias=CBIA[0:1, 1:2]), cS)
        bar("scalar")
        em("scalar", lambda e: e.activation(out=ZB[0:1, 0:512], in_=OUTSB[0:1, 0:512],
                                            func=AF.Exp, scale=-0.5), cS)
        s_rs2 = cS.n
        wait("tensor", cS, s_rs2)
        for t in range(NTIL):
            em("tensor", lambda e, t=t: e.matmul(QS[3][:, t * 128:(t + 1) * 128], ONES[:],
                                                 ZB[0:1, t * 128:(t + 1) * 128], start=True, stop=True), cT)
        t_r2b = cT.n
        wait("vector", cT, t_r2b)
        em("vector", lambda e: e.tensor_copy(out=RSTD2B[:], in_=QS[3][:, 0:SBA]), cV)
        v_r2b = cV.n
        _tap("rstd2_a%d" % a, lambda: RSTD2B[:])

        h2v = H2FM.ap()[:, :].rearrange("p (c w) -> p c w", w=SBA + 1)
        if a == 0:
            em("vector", lambda e, h2v=h2v: e.memset(h2v[:, :, 0:1], 0.0), cV)
        else:
            em("vector", lambda e, h2v=h2v: e.tensor_copy(out=h2v[:, :, 0:1], in_=h2v[:, :, SBA:SBA + 1]), cV)
        wait("vector", cV, v_r2b)
        for c in range(CH):
            h2b = lambda c=c: H2FM[:, c * (SBA + 1) + 1:(c + 1) * (SBA + 1)]
            em("vector", lambda e, c=c, h2b=h2b: e.tensor_tensor(
                out=h2b(), in0=X1BF[:, c * SBA:(c + 1) * SBA], in1=RSTD2B[:], op=OP.mult), cV)
            em("vector", lambda e, c=c, h2b=h2b: e.tensor_scalar(
                out=h2b(), in0=h2b(), scalar1=PC["ln2"][:, c:c + 1], scalar2=None, op0=OP.mult), cV)
            em("vector", lambda e, c=c, h2b=h2b: e.tensor_tensor(
                out=DBUF[:, c * SBA:(c + 1) * SBA], in0=h2b(),
                in1=H2FM[:, c * (SBA + 1):(c + 1) * (SBA + 1) - 1], op=OP.subtract), cV)
            em("vector", lambda e, c=c: e.scalar_tensor_tensor(
                out=XK2[:, c * SBA:(c + 1) * SBA], in0=DBUF[:, c * SBA:(c + 1) * SBA],
                scalar=PC["tmk2"][:, c:c + 1],
                in1=H2FM[:, c * (SBA + 1):(c + 1) * (SBA + 1) - 1],
                op0=OP.mult, op1=OP.add), cV)
        v_xk2 = cV.n
        _tap("xk2_a%d" % a, lambda: XK2[:])

        wait("sync", cT, cT.n)
        em("sync", dma_w8(WOUTV[:, 0:CH * C], wr2b_s), cWD, 16)
        wd_r2 = cWD.n
        wait("tensor", cWD, wd_r2)
        wait("tensor", cV, v_r2b)
        r2_t = []
        XRp = (XR2J, GBUF)
        for half in range(2):
            for ci in range(CH):
                rr = half * CH + ci
                if rr >= 2:
                    wait("vector", cT, r2_t[-2])
                else:
                    wait("vector", cT, cT.n)
                em("vector", lambda e, ci=ci, rr=rr: e.scalar_tensor_tensor(
                    out=XRp[rr % 2][:, 0:SBA], in0=DBUF[:, ci * SBA:(ci + 1) * SBA],
                    scalar=PC["tmr2"][:, ci:ci + 1],
                    in1=H2FM[:, ci * (SBA + 1):(ci + 1) * (SBA + 1) - 1],
                    op0=OP.mult, op1=OP.add), cV)
                v_xr = cV.n
                wait("tensor", cV, v_xr)
                for cg in range(4):
                    co = half * 4 + cg
                    em("tensor", lambda e, co=co, ci=ci, cg=cg, rr=rr: e.matmul(
                        QS[cg][:, 0:512], WOUTV[:, ci * C + co * 128:ci * C + co * 128 + 128],
                        XRp[rr % 2][:, 0:SBA],
                        start=(ci == 0), stop=(ci == CH - 1)), cT)
                r2_t.append(cT.n)
            t_r2 = cT.n
            wait("scalar", cT, t_r2)
            for cg in range(4):
                co = half * 4 + cg
                em("scalar", lambda e, co=co, cg=cg: e.activation(
                    out=R2TH[:, co * SBA:(co + 1) * SBA], in_=QS[cg][:, 0:512],
                    func=AF.Tanh, scale=0.5), cS)
            wait("tensor", cS)
        s_r2 = cS.n
        _tap("r2th_a%d" % a, lambda: R2TH[:])

        k2base = 0
        wk2v = wk2_in.rearrange("(cc p) q -> p cc q", p=128)
        w1v = w1_in.rearrange("(cc p) q -> p cc q", p=128)
        bmap = [(0, 512), (1, 0), (1, 512), (2, 0), (2, 512), (3, 0), (3, 512)]
        GBp = (GBUF, XR2J)
        s_rl = {}
        v_sq = {}
        t_k2 = {}
        def cm_slabs(g):
            gp = g % 2
            wait("sync", cT, cT.n)
            wait("gpsimd", cT, cT.n)
            em("sync", lambda e, g=g, gp=gp: e.dma_start(
                out=WSA[:, gp * 4096:(gp + 1) * 4096].rearrange("p (b q) -> p b q", b=CH),
                in_=wk2b_s[:, :, g * 512:g * 512 + 512]), cWD, 16)
            em("gpsimd", lambda e, g=g, gp=gp: e.dma_start(
                out=WSB[:, gp * 4 * C:(gp + 1) * 4 * C].rearrange("p (a c) -> p a c", a=4),
                in_=wv2b_s[:, g * 4:g * 4 + 4, :]), cWD, 16)
            wd_grp[g] = cWD.n
        wd_grp = {}
        cm_slabs(0)
        for i in range(ICH + 2):
            pi = i % 2
            if i < ICH:
                if i % 4 == 0:
                    wait("tensor", cWD, wd_grp[i // 4])
                if i == 30:
                    wait("sync", cT, cT.n)
                    em("sync", lambda e: e.dma_start(
                        out=WSB[:, 0:4096].rearrange("p (j c) -> p j c", j=ICH),
                        in_=wv2b_s[:, :, 896:1024]), cWD, 16)
                    wd_cm7 = cWD.n
                ip = (i // 4) % 2
                io = i % 4
                if i == 0:
                    wait("tensor", cV, v_xk2)
                    wait("tensor", cS, s_r2)
                else:
                    wait("tensor", cS, s_rl[i - 1])
                for ci in range(CH):
                    em("tensor", lambda e, i=i, ci=ci, ip=ip, io=io: e.matmul(
                        QS[0][:, 0:512],
                        WSA[:, ip * 4096 + ci * 512 + io * 128:ip * 4096 + ci * 512 + io * 128 + 128],
                        XK2[:, ci * SBA:(ci + 1) * SBA],
                        start=(ci == 0), stop=(ci == CH - 1)), cT)
                t_k2[i] = cT.n
            j = i - 2
            if 0 <= j < ICH:
                jp = (j // 4) % 2
                jo = j % 4
                wait("tensor", cV, v_sq[j])
                for co in range(7):
                    sl, qo = bmap[co]
                    em("tensor", lambda e, j=j, co=co, sl=sl, qo=qo, jp=jp, jo=jo: e.matmul(
                        QS[sl][:, qo:qo + 512],
                        WSB[:, jp * 4 * C + jo * C + co * 128:jp * 4 * C + jo * C + co * 128 + 128],
                        K2STG[:, k2base + j * SBA:k2base + (j + 1) * SBA],
                        start=(j == 0), stop=(j == ICH - 1)), cT)
            if i % 4 == 1 and i // 4 + 1 < ICH // 4:
                cm_slabs(i // 4 + 1)
            if i < ICH:
                wait("scalar", cT, t_k2[i])
                if i >= 2:
                    wait("scalar", cV, v_sq[i - 2])
                em("scalar", lambda e, pi=pi: e.activation(
                    out=GBp[pi][:, 0:SBA], in_=QS[0][:, 0:512], func=AF.Relu), cS)
                s_rl[i] = cS.n
                wait("vector", cS, s_rl[i])
                em("vector", lambda e, i=i, pi=pi: e.tensor_tensor(
                    out=K2STG[:, k2base + i * SBA:k2base + (i + 1) * SBA],
                    in0=GBp[pi][:, 0:SBA], in1=GBp[pi][:, 0:SBA], op=OP.mult), cV)
                v_sq[i] = cV.n
        _tap("k2_a%d" % a, lambda: K2STG[:])

        wait("tensor", cWD, wd_cm7)
        wait("tensor", cS, s_rl[ICH - 1])
        for j in range(ICH):
            em("tensor", lambda e, j=j: e.matmul(
                QS[0][:, 0:512], WSB[:, j * 128:(j + 1) * 128],
                K2STG[:, k2base + j * SBA:k2base + (j + 1) * SBA],
                start=(j == 0), stop=(j == ICH - 1)), cT)
        t_cm = cT.n
        v_cm7 = None
        for co in [7, 0, 1, 2, 3, 4, 5, 6]:
            if co < 7:
                sl, qo = bmap[co]
            else:
                sl, qo = 0, 0
            wait("vector", cT, t_cm)
            em("vector", lambda e, co=co, sl=sl, qo=qo: e.scalar_tensor_tensor(
                out=GLB[:], in0=R2TH[:, co * SBA:(co + 1) * SBA], scalar=1.0,
                in1=QS[sl][:, qo:qo + 512], op0=OP.add, op1=OP.mult), cV)
            em("vector", lambda e, co=co: e.scalar_tensor_tensor(
                out=FINALW[:, co * SBA:(co + 1) * SBA], in0=GLB[:], scalar=0.5,
                in1=FINALW[:, co * SBA:(co + 1) * SBA], op0=OP.mult, op1=OP.add), cV)
            if co == 7:
                v_cm7 = cV.n
        v_cm = cV.n
        _tap("cm_a%d" % a, lambda: FINALW[:])

        ITp = (ITILE, ITILE2)
        IQFp = (Z1B, IQF2)
        FLBp = (FLB, FLB2)
        GLBp = (GLB, GLB2)
        GHBp = (GHB, GHB2)
        bmap = [(0, 512), (1, 0), (1, 512), (2, 0), (2, 512), (3, 0), (3, 512)]
        v_iq = {}
        v_fi = {}
        v_i2 = {}
        s_iqf = {}
        s_flb = {}
        s_g = {}
        s_hq = {}
        g_dd = {}
        t_w1 = {}
        def ffn_slabs(g):
            gp = g % 2
            wait("sync", cT, cT.n)
            wait("gpsimd", cT, cT.n)
            em("sync", lambda e, g=g, gp=gp: e.dma_start(
                out=WSA[:, gp * 4096:(gp + 1) * 4096].rearrange("p (b q) -> p b q", b=CH),
                in_=w1b_s[:, :, g * 512:g * 512 + 512]), cWD, 16)
            em("gpsimd", lambda e, g=g, gp=gp: e.dma_start(
                out=WSB[:, gp * 4 * C:(gp + 1) * 4 * C].rearrange("p (a c) -> p a c", a=4),
                in_=w2b_s[:, g * 4:g * 4 + 4, :]), cWD, 16)
            wd_grp[g] = cWD.n
        wd_grp = {}
        ffn_slabs(0)
        for i in range(ICH + 2):
            pi = i % 2
            if i < ICH:
                if i % 4 == 0:
                    wait("tensor", cWD, wd_grp[i // 4])
                if i == 30:
                    wait("sync", cT, cT.n)
                    em("sync", lambda e: e.dma_start(
                        out=WSB[:, 0:4096].rearrange("p (j c) -> p j c", j=ICH),
                        in_=w2b_s[:, :, 896:1024]), cWD, 16)
                    wd_co7 = cWD.n
                ip = (i // 4) % 2
                io = i % 4
                if i == 0:
                    wait("tensor", cV, v_cm7)
                else:
                    wait("tensor", cV, v_iq[i - 1])
                for ci in range(CH):
                    em("tensor", lambda e, i=i, ci=ci, ip=ip, io=io: e.matmul(
                        QS[0][:, 0:512],
                        WSA[:, ip * 4096 + ci * 512 + io * 128:ip * 4096 + ci * 512 + io * 128 + 128],
                        XNFM[:, ci * SBA:(ci + 1) * SBA],
                        start=(ci == 0), stop=(ci == CH - 1)), cT)
                t_w1[i] = cT.n
            j = i - 2
            if 0 <= j < ICH:
                jp = (j // 4) % 2
                jo = j % 4
                wait("tensor", cS, s_hq[j])
                if j == 0:
                    wait("tensor", cV, v_cm)
                for co in range(7):
                    sl, qo = bmap[co]
                    em("tensor", lambda e, j=j, co=co, sl=sl, qo=qo, jp=jp, jo=jo: e.matmul(
                        QS[sl][:, qo:qo + 512],
                        WSB[:, jp * 4 * C + jo * C + co * 128:jp * 4 * C + jo * C + co * 128 + 128],
                        K2STG[:, k2base + j * SBA:k2base + (j + 1) * SBA],
                        start=(j == 0), stop=(j == ICH - 1)), cT)
            if i % 4 == 1 and i // 4 + 1 < ICH // 4:
                ffn_slabs(i // 4 + 1)
            h = i - 1
            if i < ICH:
                wait("vector", cT, t_w1[i])
                if i >= 1:
                    wait("vector", cS, s_g[i - 1])
                em("vector", lambda e, i=i: e.tensor_scalar(
                    out=ITILE[:], in0=QS[0][:, 0:512], scalar1=256.0, scalar2=B1S[:, i:i + 1],
                    op0=OP.mult, op1=OP.add), cV)
                v_iq[i] = cV.n
                if i == 0 and ("iq0_a%d" % a) in taps:
                    tq0 = nc.dram_tensor("tap_iq0_a%d" % a, [128, SBA], I32, kind="ExternalOutput").ap()
                    wait("sync", cV)
                    em("sync", lambda e, tq0=tq0: e.dma_start(out=tq0, in_=ITILE[:]), cOUT, 16)
                    wait("vector", cOUT, cOUT.n)
                wait("scalar", cV, v_iq[i])
                em("scalar", lambda e, pi=pi: e.activation(
                    out=IQFp[pi][:], in_=ITILE[:], func=AF.Copy, scale=1.0), cS)
                s_iqf[i] = cS.n
            if 0 <= h < ICH:
                ph = h % 2
                wait("vector", cS, s_flb[h])
                em("vector", lambda e, ph=ph: e.scalar_tensor_tensor(
                    out=IQFp[ph][:], in0=IQFp[ph][:], scalar=0.015625, in1=FLBp[ph][:],
                    op0=OP.mult, op1=OP.subtract), cV)
            if i < ICH:
                wait("vector", cS, s_iqf[i])
                em("vector", lambda e, pi=pi: e.tensor_scalar(
                    out=ITILE[:], in0=IQFp[pi][:], scalar1=0.015625, scalar2=15.5078125,
                    op0=OP.mult, op1=OP.add), cV)
                v_fi[i] = cV.n
                wait("scalar", cV, v_fi[i])
                em("scalar", lambda e, pi=pi: e.activation(
                    out=FLBp[pi][:], in_=ITILE[:], func=AF.Copy, scale=1.0), cS)
                s_flb[i] = cS.n
                if i >= 2:
                    wait("scalar", cG, g_dd[i - 2])
                em("scalar", lambda e, pi=pi: e.activation(
                    out=GLBp[pi][:], in_=ITILE[:], func=AF.Gelu, scale=0.25, bias=CBIA[:, 2:3]), cS)
                em("scalar", lambda e, pi=pi: e.activation(
                    out=GHBp[pi][:], in_=ITILE[:], func=AF.Gelu, scale=0.25, bias=CBIA[:, 3:4]), cS)
                s_g[i] = cS.n
                wait("gpsimd", cS, s_g[i])
                em("gpsimd", lambda e, pi=pi: e.tensor_tensor(
                    out=GHBp[pi][:], in0=GHBp[pi][:], in1=GLBp[pi][:], op=OP.subtract), cG)
                g_dd[i] = cG.n
            if 0 <= h < ICH:
                ph = h % 2
                wait("vector", cG, g_dd[h])
                em("vector", lambda e, ph=ph: e.scalar_tensor_tensor(
                    out=GHBp[ph][:], in0=IQFp[ph][:], scalar=16.0, in1=GHBp[ph][:],
                    op0=OP.add, op1=OP.mult), cV)
                em("vector", lambda e, ph=ph: e.tensor_tensor(
                    out=GLBp[ph][:], in0=GLBp[ph][:], in1=GHBp[ph][:], op=OP.add), cV)
                if h >= 1:
                    wait("vector", cS, s_hq[h - 1])
                em("vector", lambda e, ph=ph: e.tensor_scalar(
                    out=ITILE2[:], in0=GLBp[ph][:], scalar1=256.0, scalar2=None, op0=OP.mult), cV)
                v_i2[h] = cV.n
                wait("scalar", cV, v_i2[h])
                em("scalar", lambda e, h=h: e.activation(
                    out=K2STG[:, k2base + h * SBA:k2base + (h + 1) * SBA],
                    in_=ITILE2[:], func=AF.Copy, scale=1.0 / 256.0), cS)
                s_hq[h] = cS.n
        _tap("hq_a%d" % a, lambda: K2STG[:])

        wait("tensor", cWD, wd_co7)
        wait("tensor", cV, v_iq[ICH - 1])
        for j in range(ICH):
            em("tensor", lambda e, j=j: e.matmul(
                QS[0][:, 0:512], WSB[:, j * 128:(j + 1) * 128],
                K2STG[:, k2base + j * SBA:k2base + (j + 1) * SBA],
                start=(j == 0), stop=(j == ICH - 1)), cT)
        t_w2 = cT.n
        for co in range(CH):
            if co < 7:
                sl, qo = bmap[co]
            else:
                sl, qo = 0, 0
            wait("vector", cT, t_w2)
            em("vector", lambda e, co=co, sl=sl, qo=qo: e.tensor_scalar(
                out=ITILE[:], in0=QS[sl][:, qo:qo + 512], scalar1=256.0, scalar2=B2S[:, co:co + 1],
                op0=OP.mult, op1=OP.add), cV)
            em("vector", lambda e, co=co: e.scalar_tensor_tensor(
                out=FINALW[:, co * SBA:(co + 1) * SBA], in0=ITILE[:], scalar=1.0 / 256.0,
                in1=FINALW[:, co * SBA:(co + 1) * SBA], op0=OP.mult, op1=OP.add), cV)
        v_ffn = cV.n
        _tap("final_a%d" % a, lambda: FINALW[:])

        fv = FINALW.ap()[:, :].rearrange("p (c w) -> p c w", c=CH)
        for t in range(NTIL):
            g = g0 + t
            gt = tok0 + t * 128
            sl = t % 2
            wait("sync", cV)
            em("sync", lambda e, gt=gt: e.dma_start(out=XT[:], in_=x_in[gt:gt + 128, :]), cLD, 16)
            ld_x = cLD.n
            wait("vector", cLD, ld_x)
            wait("vector", cOUT, cOUT.n)
            em("vector", lambda e, g=g: e.tensor_scalar(
                out=OUTSB[:], in0=XT[:],
                scalar1=MC[:, g:g + 1], scalar2=RSTD[:, g:g + 1],
                op0=OP.subtract, op1=OP.mult), cV)
            em("vector", lambda e: e.tensor_tensor(
                out=OUTSB[:], in0=OUTSB[:], in1=XT[:], op=OP.add), cV)
            v_xpn = cV.n
            q = QS[2 + (t % 2)]
            wait("tensor", cV, v_xpn)
            for cb in range(CH):
                em("tensor", lambda e, cb=cb, t=t, q=q, fv=fv: e.transpose(
                    q[:, cb * 128:(cb + 1) * 128], fv[:, cb, t * 128:(t + 1) * 128], IDNF[:]), cT)
            t_f = cT.n
            wait("vector", cT, t_f)
            em("vector", lambda e, q=q: e.tensor_tensor(
                out=OUTSB[:], in0=OUTSB[:], in1=q[:], op=OP.add), cV)
            v_out = cV.n
            wait("sync", cV, v_out)
            em("sync", lambda e, gt=gt: e.dma_start(
                out=out_t[gt:gt + 128, :], in_=OUTSB[:]), cOUT, 16)

    wait("sync", cOUT, cOUT.n)

    with nc.Block() as block:
        @block.sync
        def _(e):
            for f in sched["sync"]:
                f(e)

        @block.gpsimd
        def _(e):
            for f in sched["gpsimd"]:
                f(e)

        @block.vector
        def _(e):
            for f in sched["vector"]:
                f(e)

        @block.scalar
        def _(e):
            for f in sched["scalar"]:
                f(e)

        @block.tensor
        def _(e):
            for f in sched["tensor"]:
                f(e)

    ctx.close()
    return nc


def _host_inputs(inputs):
    x = np.ascontiguousarray(np.asarray(inputs["L0_attn_x"], dtype=np.float32))
    row = lambda k: np.ascontiguousarray(np.asarray(inputs[k], dtype=np.float32).reshape(1, -1))
    base = {
        "wk": np.ascontiguousarray(np.asarray(inputs["L0_attn_W_key"], np.float32)),
        "wv": np.ascontiguousarray(np.asarray(inputs["L0_attn_W_value"], np.float32)),
        "wr": np.ascontiguousarray(np.asarray(inputs["L0_attn_W_recept"], np.float32)),
        "wo": np.ascontiguousarray(np.asarray(inputs["L0_attn_W_out"], np.float32)),
        "wr2": np.ascontiguousarray(np.asarray(inputs["L0_attn_W_recept2"], np.float32)),
        "wk2": np.ascontiguousarray(np.asarray(inputs["L0_attn_W_key2"], np.float32)),
        "wv2": np.ascontiguousarray(np.asarray(inputs["L0_attn_W_value2"], np.float32)),
        "w1": np.ascontiguousarray(np.asarray(inputs["L0_ffn_w1"], np.float32)),
        "w2": np.ascontiguousarray(np.asarray(inputs["L0_ffn_w2"], np.float32)),
        "ln1": row("L0_attn_ln1_w"), "ln2": row("L0_attn_ln2_w"),
        "tmk": row("L0_attn_time_mix_k"), "tmv": row("L0_attn_time_mix_v"),
        "tmr": row("L0_attn_time_mix_r"), "tmk2": row("L0_attn_time_mix_k2"),
        "tmr2": row("L0_attn_time_mix_r2"),
        "b1": row("L0_ffn_b1"), "b2": row("L0_ffn_b2"),
        "tdb": np.tile(row("L0_attn_time_decay"), (128, 1)),
        "ub": np.tile(row("L0_attn_time_first"), (128, 1)),
        "ltc": np.triu(np.ones((128, 128), np.float32), k=1),
        "ones1": np.ones((1, 128), np.float32),
        "idnf": np.eye(128, dtype=np.float32),
        "onesc": np.ones((128, 1)).astype(ml_dtypes.bfloat16),
        "onesr": np.ones((128, 1), np.float32),
        "cbias": np.tile(np.array([[1e-5, 1e-6, -4.0, -3.75, 0, 0, 0, 0]], np.float32), (128, 1)),
    }
    in_maps = []
    for b in range(NCORE):
        m = dict(base)
        m["x"] = np.ascontiguousarray(x[b])
        in_maps.append(m)
    return in_maps


def kernel(**inputs):
    nc = _build(taps=tuple(_TAPS))
    in_maps = _host_inputs(inputs)
    res = run_bass_kernel_spmd(nc, in_maps, core_ids=list(range(NCORE)))
    out = np.stack([np.asarray(r["out"]) for r in res.results], axis=0)
    return out.astype(np.float32)


# version 11
# speedup vs baseline: 2.1461x; 1.0378x over previous
import numpy as np
import ml_dtypes
import concourse.bass as bass
import concourse.mybir as mybir
from concourse.alu_op_type import AluOpType as OP
from concourse.bass_utils import run_bass_kernel_spmd

F32 = mybir.dt.float32
F32R = mybir.dt.float32r
BF16 = mybir.dt.bfloat16
I32 = mybir.dt.int32
AF = mybir.ActivationFunctionType

T, C, I = 4096, 1024, 4096
CH = C // 128
ICH = I // 128
SBA = 512
NSB = T // SBA
NTIL = SBA // 128
NCORE = 8

ENGS = ("sync", "gpsimd", "vector", "scalar", "tensor")

_TAPS = []


def _build(taps=(), detect_races=False):
    nc = bass.Bass("TRN2", target_bir_lowering=False, detect_race_conditions=detect_races)
    dr = lambda n, sh, dt: nc.dram_tensor(n, sh, dt, kind="ExternalInput").ap()

    x_in = dr("x", [T, C], F32)
    wk_in = dr("wk", [C, C], F32)
    wv_in = dr("wv", [C, C], F32)
    wr_in = dr("wr", [C, C], F32)
    wo_in = dr("wo", [C, C], F32)
    wr2_in = dr("wr2", [C, C], F32)
    wk2_in = dr("wk2", [C, I], F32)
    wv2_in = dr("wv2", [I, C], F32)
    w1_in = dr("w1", [C, I], F32)
    w2_in = dr("w2", [I, C], F32)
    vecs = {n: dr(n, [1, C], F32) for n in
            ("ln1", "ln2", "tmk", "tmv", "tmr", "tmk2", "tmr2", "b2")}
    b1_in = dr("b1", [1, I], F32)
    tdb_in = dr("tdb", [128, C], F32)
    ub_in = dr("ub", [128, C], F32)
    ltc_in = dr("ltc", [128, 128], F32R)
    ones_in = dr("ones1", [1, 128], F32R)
    idnf_in = dr("idnf", [128, 128], F32)
    onesc_in = dr("onesc", [128, 1], BF16)
    onesr_in = dr("onesr", [128, 1], F32R)
    cbias_in = dr("cbias", [128, 8], F32)

    out_t = nc.dram_tensor("out", [T, C], F32, kind="ExternalOutput").ap()

    drb = lambda n, sh: nc.dram_tensor(n, sh, BF16, kind="Internal").ap()
    kb_s = drb("kb_s", [128, CH, C])
    vb_s = drb("vb_s", [128, CH, C])
    rb_s = drb("rb_s", [128, CH, C])
    wob_s = drb("wob_s", [128, CH, C])
    wr2b_s = drb("wr2b_s", [128, CH, C])
    wk2b_s = drb("wk2b_s", [128, CH, I])
    w1b_s = drb("w1b_s", [128, CH, I])
    wv2b_s = drb("wv2b_s", [128, ICH, C])
    w2b_s = drb("w2b_s", [128, ICH, C])

    from contextlib import ExitStack
    ctx = ExitStack()

    _off = [16512]
    handles = {}

    def place(name, shape, dtype, at=None):
        nbytes = int(np.prod(shape[1:])) * mybir.dt.size(dtype)
        if at is None:
            at = _off[0]
            _off[0] = (at + nbytes + 31) & ~31
        h = nc.alloc_sbuf_tensor_at(name, shape, dtype, offset=at)
        handles[name] = (h, at, nbytes)
        return h

    LTC = place("LTC", [128, 128], F32R)
    ONES = place("ONES", [1, 128], F32R)
    IDNF = place("IDNF", [128, 128], F32)
    ONESC = place("ONESC", [128, 1], BF16)
    ONESR = place("ONESR", [128, 1], F32R)
    WCL = place("WCL", [128, C], F32)
    UB = place("UB", [128, C], F32)
    EMIN = place("EMIN", [128, C], F32)
    ESH = place("ESH", [128, C], F32)
    NIOT = place("NIOT", [128, 1], F32)
    IOTM1 = place("IOTM1", [128, 1], F32)
    EW2R = place("EW2R", [1, C], F32)
    CARRY = place("CARRY", [1, 2 * C], F32R)
    PC = {}
    for nme in ("ln1", "ln2", "tmk", "tmv", "tmr", "tmk2", "tmr2"):
        PC[nme] = place("PC_" + nme, [128, CH], F32)
    B1S = place("B1S", [128, ICH], F32)
    B2S = place("B2S", [128, CH], F32)
    SUMS = place("SUMS", [128, 32], F32)
    SQS = place("SQS", [128, 32], F32)
    MC = place("MC", [128, 32], F32)
    RSTD = place("RSTD", [128, 32], F32)
    RSTD1 = place("RSTD1", [128, 32], F32)
    TCA = place("TCA", [128, 4], F32)
    TCB = place("TCB", [128, 4], F32)
    RSTD1B = place("RSTD1B", [128, SBA], BF16)
    RSTD2B = place("RSTD2B", [128, SBA], BF16)
    XNFM = place("XNFM", [128, CH * SBA], BF16)
    X1BF = place("X1BF", [128, CH * SBA], BF16)
    FINALW = place("FINALW", [128, CH * SBA], F32)
    HFM = place("HFM", [128, CH * (SBA + 1)], BF16)
    DBUF = place("DBUF", [128, CH * SBA], BF16)
    RWKVFM = place("RWKVFM", [128, CH * SBA], BF16)
    H2FM = place("H2FM", [128, CH * (SBA + 1)], BF16)
    MIXT = place("MIXT", [128, 2 * 256], BF16)
    XR2J = place("XR2J", [128, SBA], BF16)
    GBUF = place("GBUF", [128, SBA], BF16)
    K2STG = place("K2STG", [128, ICH * SBA], BF16)
    CBIA = place("CBIA", [128, 8], F32)
    XT = place("XT", [128, C], F32)
    OUTSB = place("OUTSB", [128, C], F32)

    ovl0 = _off[0]

    def placeA(name, shape, dtype):
        return place(name, shape, dtype)

    KTM = placeA("KTM", [128, 2 * C], F32)
    VTM = placeA("VTM", [128, 2 * C], F32)
    ERT = placeA("ERT", [128, 2 * C], F32)
    KCB = placeA("KCB", [128, C], F32)
    EKT = placeA("EKT", [128, 2 * C], F32)
    EKB = placeA("EKB", [128, C], F32)
    ZB = placeA("ZB", [128, 2 * C], F32R)
    APR = placeA("APR", [128, 2 * C], F32)
    EKTV = placeA("EKTV", [128, C], F32)
    DEN = placeA("DEN", [128, C], F32)
    RWT = placeA("RWT", [128, C], F32)
    ovl_end_A = _off[0]

    _off[0] = ovl0
    WSA = place("WSA", [128, 2 * 4096], BF16)
    WSB = place("WSB", [128, 2 * 4 * C], BF16)
    XK2 = place("XK2", [128, CH * SBA], BF16)
    R2TH = place("R2TH", [128, CH * SBA], BF16)
    ITILE = place("ITILE", [128, SBA], I32)
    ITILE2 = place("ITILE2", [128, SBA], I32)
    Z1B = place("Z1B", [128, SBA], F32)
    IQF2 = place("IQF2", [128, SBA], F32)
    FLB = place("FLB", [128, SBA], F32)
    FLB2 = place("FLB2", [128, SBA], F32)
    GLB = place("GLB", [128, SBA], F32)
    GLB2 = place("GLB2", [128, SBA], F32)
    GHB = place("GHB", [128, SBA], F32)
    GHB2 = place("GHB2", [128, SBA], F32)
    ovl_end_B = _off[0]
    _off[0] = max(ovl_end_A, ovl_end_B)

    CROW = OUTSB
    total_kb = (_off[0] - 16512) / 1024.0
    assert _off[0] <= 229344, f"SBUF over budget: {total_kb:.1f} KB used, end={_off[0]}"

    QS = [ctx.enter_context(nc.psum_tensor("Q%d" % i, [128, 1024], F32)) for i in range(4)]

    class Ctr:
        def __init__(s, name):
            s.h = ctx.enter_context(nc.semaphore(name))
            s.n = 0
    cLD = Ctr("cLD")
    cWD = Ctr("cWD")
    cV = Ctr("cV")
    cS = Ctr("cS")
    cT = Ctr("cT")
    cG = Ctr("cG")
    cOUT = Ctr("cOUT")
    cLD2 = Ctr("cLD2")
    cWD2 = Ctr("cWD2")

    sched = {e: [] for e in ENGS}

    def em(eng, make, inc=None, k=1):
        if inc is not None:
            inc.n += k
        def f(e, make=make, inc=inc, k=k):
            r = make(e)
            if inc is not None:
                if r is None:
                    r = e.nop()
                r.then_inc(inc.h, k)
        sched[eng].append(f)

    def wait(eng, ctr, n=None):
        v = ctr.n if n is None else n
        if v > 0:
            sched[eng].append(lambda e, h=ctr.h, v=v: e.wait_ge(h, v))

    _selfc = {"vector": None, "scalar": None}

    def bar(eng):
        c = {"vector": cV, "scalar": cS}[eng]
        wait(eng, c, c.n)

    tapv = {}

    def _tap(name, apf):
        if name in taps:
            ap0 = apf()
            sh = list(ap0.shape)
            t = nc.dram_tensor("tap_" + name, sh, ap0.dtype, kind="ExternalOutput").ap()
            wait("sync", cV)
            wait("sync", cS)
            wait("sync", cG)
            wait("sync", cT)
            em("sync", lambda e, t=t, a=ap0: e.dma_start(out=t, in_=a), cOUT, 16)

    em("sync", lambda e: e.dma_start(out=LTC[:], in_=ltc_in), cLD, 16)
    em("sync", lambda e: e.dma_start(out=ONES[:], in_=ones_in), cLD, 16)
    em("sync", lambda e: e.dma_start(out=IDNF[:], in_=idnf_in), cLD, 16)
    em("sync", lambda e: e.dma_start(out=ONESC[:], in_=onesc_in), cLD, 16)
    em("sync", lambda e: e.dma_start(out=ONESR[:], in_=onesr_in), cLD, 16)
    em("sync", lambda e: e.dma_start(out=CBIA[:], in_=cbias_in), cLD, 16)
    em("sync", lambda e: e.dma_start(out=WCL[:], in_=tdb_in), cLD, 16)
    em("sync", lambda e: e.dma_start(out=UB[:], in_=ub_in), cLD, 16)
    for nme in ("ln1", "ln2", "tmk", "tmv", "tmr", "tmk2", "tmr2"):
        em("sync", lambda e, d=PC[nme], s=vecs[nme]: e.dma_start(
            out=d.ap()[:, :].rearrange("p (x c) -> p x c", x=1), in_=s.rearrange("x (c p) -> p x c", p=128),
            allow_slow_non_contiguous=True), cLD, 16)
    em("sync", lambda e: e.dma_start(out=B1S.ap()[:, :].rearrange("p (x c) -> p x c", x=1),
                                     in_=b1_in.rearrange("x (c p) -> p x c", p=128), allow_slow_non_contiguous=True), cLD, 16)
    em("sync", lambda e: e.dma_start(out=B2S.ap()[:, :].rearrange("p (x c) -> p x c", x=1),
                                     in_=vecs["b2"].rearrange("x (c p) -> p x c", p=128), allow_slow_non_contiguous=True), cLD, 16)
    init_ld = cLD.n

    em("gpsimd", lambda e: e.iota(NIOT[:], [[1, 1]], base=0, channel_multiplier=-1,
                                  allow_small_or_imprecise_dtypes=True), cG)
    em("gpsimd", lambda e: e.iota(IOTM1[:], [[1, 1]], base=-1, channel_multiplier=1,
                                  allow_small_or_imprecise_dtypes=True), cG)
    em("gpsimd", lambda e: e.memzero(CARRY[:]), cG)
    g_init = cG.n

    wait("scalar", cLD, init_ld)
    em("scalar", lambda e: e.activation(out=WCL[:], in_=WCL[:], func=AF.Exp), cS)
    wait("vector", cS)
    em("vector", lambda e: e.tensor_scalar(out=WCL[:], in0=WCL[:], scalar1=-1.0, scalar2=20.0,
                                           op0=OP.mult, op1=OP.min), cV)
    em("vector", lambda e: e.tensor_scalar(out=WCL[:], in0=WCL[:], scalar1=-20.0, scalar2=None,
                                           op0=OP.max), cV)
    wait("vector", cG, g_init)
    em("vector", lambda e: e.tensor_scalar(out=EMIN[:], in0=WCL[:], scalar1=NIOT[:, 0:1], scalar2=None,
                                           op0=OP.mult), cV)
    em("vector", lambda e: e.tensor_scalar(out=ESH[:], in0=WCL[:], scalar1=IOTM1[:, 0:1], scalar2=None,
                                           op0=OP.mult), cV)
    em("vector", lambda e: e.tensor_scalar(out=EW2R[:], in0=WCL[0:1, :], scalar1=128.0, scalar2=None,
                                           op0=OP.mult), cV)
    v_pre = cV.n
    wait("scalar", cV, v_pre)
    em("scalar", lambda e: e.activation(out=EMIN[:], in_=EMIN[:], func=AF.Exp), cS)
    em("scalar", lambda e: e.activation(out=ESH[:], in_=ESH[:], func=AF.Exp), cS)
    em("scalar", lambda e: e.activation(out=EW2R[:], in_=EW2R[:], func=AF.Exp), cS)
    s_pre = cS.n
    wait("vector", cS, s_pre)
    em("vector", lambda e: e.tensor_scalar(out=B1S[:], in0=B1S[:], scalar1=256.0, scalar2=None, op0=OP.mult), cV)
    em("vector", lambda e: e.tensor_scalar(out=B2S[:], in0=B2S[:], scalar1=256.0, scalar2=None, op0=OP.mult), cV)

    wkv_b = {"k": kb_s, "v": vb_s, "r": rb_s}
    WOUTV = K2STG

    def dma_w8(dst_ap, wb):
        def f(e, d=dst_ap, w=wb):
            return e.dma_start(out=d.rearrange("p (a c) -> p a c", a=CH),
                               in_=w[:, :, :])
        return f

    pre_chunks = []
    for src, dst in ((wk_in, kb_s), (wv_in, vb_s), (wr_in, rb_s), (wo_in, wob_s), (wr2_in, wr2b_s)):
        pre_chunks.append((src.rearrange("(cc p) c -> p cc c", p=128), dst[:, :, :]))
    wk2v0 = wk2_in.rearrange("(cc p) q -> p cc q", p=128)
    w1v0 = w1_in.rearrange("(cc p) q -> p cc q", p=128)
    for qc in range(4):
        pre_chunks.append((wk2v0[:, :, qc * 1024:(qc + 1) * 1024], wk2b_s[:, :, qc * 1024:(qc + 1) * 1024]))
    for qc in range(4):
        pre_chunks.append((w1v0[:, :, qc * 1024:(qc + 1) * 1024], w1b_s[:, :, qc * 1024:(qc + 1) * 1024]))
    for g in range(4):
        pre_chunks.append((wv2_in[g * 1024:(g + 1) * 1024, :].rearrange("(ii p) c -> p ii c", p=128),
                           wv2b_s[:, g * 8:(g + 1) * 8, :]))
    for g in range(4):
        pre_chunks.append((w2_in[g * 1024:(g + 1) * 1024, :].rearrange("(ii p) c -> p ii c", p=128),
                           w2b_s[:, g * 8:(g + 1) * 8, :]))
    bnc = lambda half: K2STG.ap()[:, half * 8192:(half + 1) * 8192].rearrange("p (a c) -> p a c", a=CH)
    pre_out = {}
    for n, (src, dst) in enumerate(pre_chunks):
        half = n % 2
        if n >= 2:
            wait("gpsimd", cOUT, pre_out[n - 2])
        em("gpsimd", lambda e, src=src, half=half: e.dma_start(out=bnc(half), in_=src), cWD, 16)
        g_in = cWD.n
        wait("sync", cWD, g_in)
        em("sync", lambda e, dst=dst, half=half: e.dma_start(out=dst, in_=bnc(half)), cOUT, 16)
        pre_out[n] = cOUT.n
    pre_done = cOUT.n

    pm = {"s_ek": 0, "t_S": 0, "t_G": 0, "carry_v": 0, "rwt_v": 0,
          "t_T3": 0, "rwkv_v": 0}
    wait("tensor", cLD, init_ld)
    wait("gpsimd", cLD, init_ld)
    for a in range(NSB):
        tok0 = a * SBA
        g0 = a * NTIL

        XTp1 = (XT, KTM)
        OSBp = (OUTSB, VTM)
        ldm = {}
        vxn_m = {}
        sq_m = {}
        tx_m = {}
        cp_m = {}

        def p1_load(t):
            gt = tok0 + t * 128
            ring = "sync" if t % 2 == 0 else "gpsimd"
            ctr = cLD if t % 2 == 0 else cLD2
            if t >= 2:
                wait(ring, cV, vxn_m[t - 2])
                wait(ring, cS, sq_m[t - 2])
            else:
                wait(ring, cV)
                wait(ring, cS)
            em(ring, lambda e, gt=gt, t=t: e.dma_start(
                out=XTp1[t % 2][:, 0:C], in_=x_in[gt:gt + 128, :]), ctr, 16)
            ldm[t] = (ctr, ctr.n)
        p1_load(0)
        p1_load(1)
        for t in range(NTIL):
            gt = tok0 + t * 128
            xt = XTp1[t % 2]
            osb = OSBp[t % 2]
            g = g0 + t
            ldn = ldm[t]
            wait("vector", ldn[0], ldn[1])
            em("vector", lambda e, g=g, xt=xt: e.tensor_reduce(
                out=SUMS[:, g:g + 1], in_=xt[:, 0:C], axis=mybir.AxisListType.X, op=OP.add), cV)
            wait("scalar", ldn[0], ldn[1])
            if t >= 2:
                wait("scalar", cT, tx_m[t - 2])
            else:
                wait("scalar", cOUT, cOUT.n)
                wait("scalar", cT, cT.n)
                wait("scalar", cV, cV.n)
            em("scalar", lambda e, xt=xt, osb=osb: e.activation(
                out=osb[:, 0:C], in_=xt[:, 0:C], func=AF.Square), cS)
            s_sq = cS.n
            sq_m[t] = s_sq
            wait("vector", cS, s_sq)
            em("vector", lambda e, g=g, osb=osb: e.tensor_reduce(
                out=SQS[:, g:g + 1], in_=osb[:, 0:C], axis=mybir.AxisListType.X, op=OP.add), cV)
            wait("vector", cS)
            em("vector", lambda e, g=g: e.tensor_scalar(
                out=MC[:, g:g + 1], in0=SUMS[:, g:g + 1], scalar1=1.0 / C, scalar2=None, op0=OP.mult), cV)
            bar("vector")
            em("vector", lambda e, g=g, t=t: e.tensor_tensor(
                out=TCA[:, t:t + 1], in0=MC[:, g:g + 1], in1=MC[:, g:g + 1], op=OP.mult), cV)
            bar("vector")
            em("vector", lambda e, g=g, t=t: e.scalar_tensor_tensor(
                out=TCB[:, t:t + 1], in0=SQS[:, g:g + 1], scalar=1.0 / C, in1=TCA[:, t:t + 1],
                op0=OP.mult, op1=OP.subtract), cV)
            v_var = cV.n
            wait("scalar", cV, v_var)
            em("scalar", lambda e, t=t: e.activation(out=TCA[:, t:t + 1], in_=TCB[:, t:t + 1], func=AF.Ln, bias=CBIA[:, 0:1]), cS)
            bar("scalar")
            em("scalar", lambda e, g=g, t=t: e.activation(
                out=RSTD[:, g:g + 1], in_=TCA[:, t:t + 1], func=AF.Exp, scale=-0.5), cS)
            s_rstd = cS.n
            wait("vector", cS, s_rstd)
            em("vector", lambda e, g=g, t=t: e.tensor_tensor(
                out=TCA[:, t:t + 1], in0=RSTD[:, g:g + 1], in1=RSTD[:, g:g + 1], op=OP.mult), cV)
            bar("vector")
            em("vector", lambda e, t=t: e.tensor_tensor(
                out=TCA[:, t:t + 1], in0=TCB[:, t:t + 1], in1=TCA[:, t:t + 1], op=OP.mult), cV)
            v_ms1 = cV.n
            wait("scalar", cV, v_ms1)
            em("scalar", lambda e, t=t: e.activation(out=TCB[:, t:t + 1], in_=TCA[:, t:t + 1], func=AF.Ln, bias=CBIA[:, 1:2]), cS)
            bar("scalar")
            em("scalar", lambda e, g=g, t=t: e.activation(
                out=RSTD1[:, g:g + 1], in_=TCB[:, t:t + 1], func=AF.Exp, scale=-0.5), cS)
            em("vector", lambda e, g=g, xt=xt, osb=osb: e.tensor_scalar(
                out=osb[:, 0:C], in0=xt[:, 0:C], scalar1=MC[:, g:g + 1], scalar2=RSTD[:, g:g + 1],
                op0=OP.subtract, op1=OP.mult), cV)
            v_xnb = cV.n
            vxn_m[t] = v_xnb
            if t + 2 < NTIL:
                p1_load(t + 2)
            wait("tensor", cV, v_xnb)
            if t >= 1:
                wait("tensor", cV, cp_m[t - 1])
            for cb in range(CH):
                em("tensor", lambda e, cb=cb, osb=osb: e.transpose(
                    QS[2][:, cb * 128:(cb + 1) * 128], osb[:, cb * 128:(cb + 1) * 128], IDNF[:]), cT)
            t_x = cT.n
            tx_m[t] = t_x
            wait("vector", cT, t_x)
            em("vector", lambda e, t=t: e.tensor_copy(
                out=XNFM.ap()[:, :].rearrange("p (c w) -> p c w", c=CH)[:, :, t * 128:(t + 1) * 128],
                in_=QS[2].ap()[:, :].rearrange("p (c w) -> p c w", c=CH)), cV)
            cp_m[t] = cV.n

        if a == 0:
            _tap("sums_a0", lambda: SUMS[:])
            _tap("sqs_a0", lambda: SQS[:])
            _tap("mc_a0", lambda: MC[:])
            _tap("rstd_a0", lambda: RSTD[:])
            _tap("idnf_a0", lambda: IDNF[:])
            _tap("outsb_a0", lambda: OUTSB[:])
        s_r1 = cS.n
        wait("tensor", cS, s_r1)
        wait("tensor", cV)
        for t in range(NTIL):
            em("tensor", lambda e, t=t, g0=g0: e.transpose(
                QS[0][0:1, t * 128:(t + 1) * 128], RSTD1[:, g0 + t:g0 + t + 1], IDNF[:]), cT)
        t_tr = cT.n
        wait("vector", cT, t_tr)
        em("vector", lambda e: e.tensor_copy(out=ZB[0:1, 0:SBA], in_=QS[0][0:1, 0:SBA]), cV)
        v_srp = cV.n
        wait("tensor", cV, v_srp)
        for t in range(NTIL):
            em("tensor", lambda e, t=t: e.matmul(QS[1][:, t * 128:(t + 1) * 128], ONES[:],
                                                 ZB[0:1, t * 128:(t + 1) * 128], start=True, stop=True), cT)
        t_bc = cT.n
        wait("vector", cT, t_bc)
        em("vector", lambda e: e.tensor_copy(out=RSTD1B[:], in_=QS[1][:, 0:SBA]), cV)
        v_r1b = cV.n

        _tap("xnfm_a%d" % a, lambda: XNFM[:])

        hv = HFM.ap()[:, :].rearrange("p (c w) -> p c w", w=SBA + 1)
        if a == 0:
            em("vector", lambda e, hv=hv: e.memset(hv[:, :, 0:1], 0.0), cV)
        else:
            em("vector", lambda e, hv=hv: e.tensor_copy(out=hv[:, :, 0:1], in_=hv[:, :, SBA:SBA + 1]), cV)
        wait("vector", cV, v_r1b)
        for c in range(CH):
            hb = lambda c=c: HFM[:, c * (SBA + 1) + 1:(c + 1) * (SBA + 1)]
            em("vector", lambda e, c=c, hb=hb: e.scalar_tensor_tensor(
                out=hb(), in0=XNFM[:, c * SBA:(c + 1) * SBA], scalar=PC["ln1"][:, c:c + 1],
                in1=RSTD1B[:], op0=OP.mult, op1=OP.mult), cV)
            em("vector", lambda e, c=c, hb=hb: e.tensor_tensor(
                out=DBUF[:, c * SBA:(c + 1) * SBA], in0=hb(),
                in1=HFM[:, c * (SBA + 1):(c + 1) * (SBA + 1) - 1], op=OP.subtract), cV)
        _tap("h_a%d" % a, lambda: HFM[:])

        projs = [("k", PC["tmk"], KTM), ("v", PC["tmv"], VTM), ("r", PC["tmr"], ERT)]
        wait("vector", cOUT, cOUT.n)
        kvr_seq = [projs[q % 3][0] for q in range(6)]
        wdP = {}
        projT = {}

        def kvr_load(L):
            if a == 0 and L == 0:
                wait("sync", cOUT, pre_done)
            if L <= 1:
                wait("sync", cT, cT.n)
            else:
                wait("sync", cT, projT[L - 2])
            em("sync", lambda e, L=L: e.dma_start(
                out=K2STG[:, (L % 2) * CH * C:(L % 2 + 1) * CH * C].rearrange("p (a c) -> p a c", a=CH),
                in_=wkv_b[kvr_seq[L]][:, :, :]), cWD, 16)
            wdP[L] = cWD.n
        kvr_load(0)
        kvr_load(1)
        mix_t = []
        for pr in range(2):
            evac_s = {}
            evac_v = {}
            for pi, (pn, tmcol, dstbuf) in enumerate(projs):
                L = pr * 3 + pi
                koff = (L % 2) * CH * C
                qk = [QS[0], QS[1]] if pn in ("k", "r") else [QS[2], QS[3]]
                if pn == "r":
                    wait("tensor", cS)
                for ci in range(CH):
                    ms = len(mix_t) % 2
                    if len(mix_t) >= 2:
                        wait("vector", cT, mix_t[-2])
                    em("vector", lambda e, ms=ms, ci=ci, pr=pr, tmcol=tmcol: e.scalar_tensor_tensor(
                        out=MIXT[:, ms * 256:(ms + 1) * 256],
                        in0=DBUF[:, ci * SBA + pr * 256:ci * SBA + pr * 256 + 256],
                        scalar=tmcol[:, ci:ci + 1],
                        in1=HFM[:, ci * (SBA + 1) + pr * 256:ci * (SBA + 1) + pr * 256 + 256],
                        op0=OP.mult, op1=OP.add), cV)
                    v_mix = cV.n
                    wait("tensor", cV, v_mix)
                    if ci == 0:
                        wait("tensor", cWD, wdP[L])
                    for bl in range(2):
                        for nh in range(2):
                            em("tensor", lambda e, ms=ms, ci=ci, bl=bl, nh=nh, qk=qk, koff=koff: e.matmul(
                                qk[bl][:, nh * 512:(nh + 1) * 512],
                                MIXT[:, ms * 256 + bl * 128:ms * 256 + bl * 128 + 128],
                                K2STG[:, koff + ci * C + nh * 512:koff + ci * C + (nh + 1) * 512],
                                start=(ci == 0), stop=(ci == CH - 1)), cT)
                    mix_t.append(cT.n)
                t_mm = cT.n
                projT[L] = t_mm
                if L + 2 < 6:
                    kvr_load(L + 2)
                for bl in range(2):
                    if pn == "k":
                        wait("scalar", cT, t_mm)
                        em("scalar", lambda e, bl=bl, qk=qk: e.activation(
                            out=KTM[:, bl * C:(bl + 1) * C], in_=qk[bl][:], func=AF.Copy), cS)
                    elif pn == "v":
                        wait("vector", cT, t_mm)
                        em("vector", lambda e, bl=bl, qk=qk: e.tensor_copy(
                            out=VTM[:, bl * C:(bl + 1) * C], in_=qk[bl][:]), cV)
                    else:
                        wait("scalar", cT, t_mm)
                        em("scalar", lambda e, bl=bl, qk=qk: e.activation(
                            out=ERT[:, bl * C:(bl + 1) * C], in_=qk[bl][:], func=AF.Exp, scale=-1.0), cS)
                evac_s[pn] = cS.n
                evac_v[pn] = cV.n

            def p3_prep(bl):
                wait("gpsimd", cS, pm["s_ek"])
                wait("gpsimd", cV, evac_v["v"])
                wait("gpsimd", cS, evac_s["r"])
                em("gpsimd", lambda e, bl=bl: e.tensor_tensor(
                    out=KCB[:], in0=KTM[:, bl * C:(bl + 1) * C], in1=UB[:], op=OP.add), cG)
                g_kc = cG.n
                wait("vector", cG, g_kc)
                em("vector", lambda e: e.tensor_scalar(
                    out=KCB[:], in0=KCB[:], scalar1=20.0, scalar2=-20.0, op0=OP.min, op1=OP.max), cV)
                v_kc = cV.n
                wait("scalar", cV, v_kc)
                em("scalar", lambda e, bl=bl: e.activation(
                    out=EKT[:, bl * C:(bl + 1) * C], in_=KCB[:], func=AF.Exp), cS)
                s_ekt = cS.n
                wait("vector", cS, s_ekt)
                em("vector", lambda e, bl=bl: e.tensor_scalar(
                    out=KCB[:], in0=KTM[:, bl * C:(bl + 1) * C], scalar1=20.0, scalar2=-20.0,
                    op0=OP.min, op1=OP.max), cV)
                v_kc2 = cV.n
                wait("scalar", cV, v_kc2)
                em("scalar", lambda e: e.activation(out=EKB[:], in_=KCB[:], func=AF.Exp), cS)
                pm["s_ek"] = cS.n
                wait("vector", cS, pm["s_ek"])
                wait("vector", cT, pm["t_S"])
                em("vector", lambda e: e.tensor_tensor(
                    out=ZB[:, C:2 * C], in0=EKB[:], in1=EMIN[:], op=OP.mult), cV)
                em("vector", lambda e, bl=bl: e.tensor_tensor(
                    out=ZB[:, 0:C], in0=ZB[:, C:2 * C], in1=VTM[:, bl * C:(bl + 1) * C], op=OP.mult), cV)
                return s_ekt, cV.n

            def p3_T1(gq, v_z):
                wait("tensor", cV, v_z)
                wait("tensor", cV, pm["carry_v"])
                for seg in range(4):
                    q = gq[seg // 2]
                    qoff = (seg % 2) * 512
                    em("tensor", lambda e, q=q, qoff=qoff, seg=seg: e.matmul(
                        q[:, qoff:qoff + 512], LTC[:],
                        ZB[:, seg * 512:(seg + 1) * 512], start=True, stop=False), cT)
                for seg in range(4):
                    q = gq[seg // 2]
                    qoff = (seg % 2) * 512
                    em("tensor", lambda e, q=q, qoff=qoff, seg=seg: e.matmul(
                        q[:, qoff:qoff + 512], ONES[:],
                        CARRY[:, seg * 512:(seg + 1) * 512], start=False, stop=True), cT)
                pm["t_G"] = cT.n

            def p3_T2(sq):
                for seg in range(4):
                    em("tensor", lambda e, sq=sq, seg=seg: e.matmul(
                        sq[seg // 2][0:1, (seg % 2) * 512:(seg % 2) * 512 + 512], ONESR[:],
                        ZB[:, seg * 512:(seg + 1) * 512], start=True, stop=True), cT)
                pm["t_S"] = cT.n

            def p3_ektv(bl, s_ekt):
                wait("gpsimd", cS, s_ekt)
                wait("gpsimd", cV, pm["rwt_v"])
                em("gpsimd", lambda e, bl=bl: e.tensor_tensor(
                    out=EKTV[:], in0=EKT[:, bl * C:(bl + 1) * C],
                    in1=VTM[:, bl * C:(bl + 1) * C], op=OP.mult), cG)
                return cG.n

            def p3_tail(bl, gq, sq, g_ektv):
                wait("vector", cT, pm["t_G"])
                em("vector", lambda e, gq=gq: e.tensor_tensor(
                    out=APR[:, 0:C], in0=gq[0][:], in1=ESH[:], op=OP.mult), cV)
                em("vector", lambda e, gq=gq: e.tensor_tensor(
                    out=APR[:, C:2 * C], in0=gq[1][:], in1=ESH[:], op=OP.mult), cV)
                wait("vector", cT, pm["t_S"])
                em("vector", lambda e, sq=sq: e.tensor_tensor(
                    out=CROW[0:1, 0:C], in0=CARRY[:, 0:C], in1=sq[0][0:1, 0:C], op=OP.add), cV)
                em("vector", lambda e, sq=sq: e.tensor_tensor(
                    out=XT[0:1, 0:C], in0=CARRY[:, C:2 * C], in1=sq[1][0:1, 0:C], op=OP.add), cV)
                em("vector", lambda e: e.tensor_tensor(
                    out=CARRY[:, 0:C], in0=CROW[0:1, 0:C], in1=EW2R[:], op=OP.mult), cV)
                em("vector", lambda e: e.tensor_tensor(
                    out=CARRY[:, C:2 * C], in0=XT[0:1, 0:C], in1=EW2R[:], op=OP.mult), cV)
                pm["carry_v"] = cV.n
                wait("vector", cG, g_ektv)
                em("vector", lambda e: e.tensor_tensor(
                    out=EKTV[:], in0=EKTV[:], in1=APR[:, 0:C], op=OP.add), cV)
                em("vector", lambda e, bl=bl: e.scalar_tensor_tensor(
                    out=DEN[:], in0=EKT[:, bl * C:(bl + 1) * C], scalar=1e-8,
                    in1=APR[:, C:2 * C], op0=OP.add, op1=OP.add), cV)
                em("vector", lambda e, bl=bl: e.scalar_tensor_tensor(
                    out=DEN[:], in0=ERT[:, bl * C:(bl + 1) * C], scalar=1.0,
                    in1=DEN[:], op0=OP.add, op1=OP.mult), cV)
                v_den = cV.n
                wait("scalar", cV, v_den)
                em("scalar", lambda e: e.activation(out=APR[:, 0:C], in_=DEN[:], func=AF.Ln), cS)
                em("scalar", lambda e: e.activation(out=DEN[:], in_=APR[:, 0:C], func=AF.Exp, scale=-1.0), cS)
                s_rd = cS.n
                wait("vector", cS, s_rd)
                wait("vector", cT, pm["t_T3"])
                em("vector", lambda e: e.tensor_tensor(out=RWT[:], in0=EKTV[:], in1=DEN[:], op=OP.mult), cV)
                pm["rwt_v"] = cV.n

            def p3_T3(blk, gq):
                tq = gq[0]
                wait("tensor", cV, pm["rwt_v"])
                for cb in range(CH):
                    em("tensor", lambda e, cb=cb, tq=tq: e.transpose(
                        tq[:, cb * 128:(cb + 1) * 128], RWT[:, cb * 128:(cb + 1) * 128], IDNF[:]), cT)
                pm["t_T3"] = cT.n
                wait("vector", cT, pm["t_T3"])
                em("vector", lambda e, blk=blk, tq=tq: e.tensor_copy(
                    out=RWKVFM.ap()[:, :].rearrange("p (c w) -> p c w", c=CH)[:, :, blk * 128:(blk + 1) * 128],
                    in_=tq.ap()[:, :].rearrange("p (c w) -> p c w", c=CH)), cV)
                pm["rwkv_v"] = cV.n

            GQ = {0: [QS[2], QS[3]], 1: [QS[0], QS[1]]}
            SQ = {0: [QS[0], QS[1]], 1: [QS[2], QS[3]]}
            wait("tensor", cS, evac_s["r"])
            s_ekt0, v_z0 = p3_prep(0)
            p3_T1(GQ[0], v_z0)
            wait("tensor", cV, pm["rwkv_v"])
            p3_T2(SQ[0])
            g_ektv0 = p3_ektv(0, s_ekt0)
            s_ekt1, v_z1 = p3_prep(1)
            p3_tail(0, GQ[0], SQ[0], g_ektv0)
            p3_T1(GQ[1], v_z1)
            g_ektv1 = p3_ektv(1, s_ekt1)
            p3_T3(2 * pr, GQ[0])
            wait("tensor", cV, pm["rwkv_v"])
            p3_T2(SQ[1])
            p3_tail(1, GQ[1], SQ[1], g_ektv1)
            p3_T3(2 * pr + 1, GQ[1])

        v_rwkv = cV.n
        _tap("rwkv_a%d" % a, lambda: RWKVFM[:])

        wait("sync", cT, cT.n)
        em("sync", dma_w8(WOUTV[:, 0:CH * C], wob_s), cWD, 16)
        wd_wo = cWD.n
        wait("tensor", cV, v_rwkv)
        wait("tensor", cS)
        wait("tensor", cWD, wd_wo)
        p4v = {}
        for co in range(CH):
            q = QS[co % 2]
            if co >= 2:
                wait("tensor", cV, p4v[co - 2])
            for ci in range(CH):
                em("tensor", lambda e, co=co, ci=ci, q=q: e.matmul(
                    q[:, 0:512], WOUTV[:, ci * C + co * 128:ci * C + co * 128 + 128],
                    RWKVFM[:, ci * SBA:(ci + 1) * SBA],
                    start=(ci == 0), stop=(ci == CH - 1)), cT)
            t_wo = cT.n
            wait("scalar", cT, t_wo)
            em("scalar", lambda e, co=co, q=q: e.activation(
                out=FINALW[:, co * SBA:(co + 1) * SBA], in_=q[:, 0:512], func=AF.Copy), cS)
            s_fw = cS.n
            wait("vector", cT, t_wo)
            wait("vector", cS, s_fw)
            em("vector", lambda e, co=co, q=q: e.tensor_tensor(
                out=X1BF[:, co * SBA:(co + 1) * SBA], in0=q[:, 0:512],
                in1=XNFM[:, co * SBA:(co + 1) * SBA], op=OP.add), cV)
            p4v[co] = cV.n
        v_x1 = cV.n
        _tap("x1_a%d" % a, lambda: X1BF[:])
        _tap("att_a%d" % a, lambda: FINALW[:])

        wait("vector", cV, v_x1)
        wait("tensor", cS)
        wait("tensor", cV)
        t_mm2 = {}
        for c in range(CH):
            gb = GBUF if c % 2 == 0 else XR2J
            if c >= 2:
                wait("vector", cT, t_mm2[c - 2])
            em("vector", lambda e, c=c, gb=gb: e.tensor_tensor(
                out=gb[:, 0:SBA], in0=X1BF[:, c * SBA:(c + 1) * SBA],
                in1=X1BF[:, c * SBA:(c + 1) * SBA], op=OP.mult), cV)
            v_sq = cV.n
            wait("tensor", cV, v_sq)
            em("tensor", lambda e, c=c, gb=gb: e.matmul(
                QS[2][0:1, 0:512], ONESC[:], gb[:, 0:SBA], start=(c == 0), stop=(c == CH - 1)), cT)
            t_mm2[c] = cT.n
        t_ms2 = cT.n
        wait("scalar", cT, t_ms2)
        em("scalar", lambda e: e.activation(out=OUTSB[0:1, 0:512], in_=QS[2][0:1, 0:512],
                                            func=AF.Ln, scale=1.0 / C, bias=CBIA[0:1, 1:2]), cS)
        bar("scalar")
        em("scalar", lambda e: e.activation(out=ZB[0:1, 0:512], in_=OUTSB[0:1, 0:512],
                                            func=AF.Exp, scale=-0.5), cS)
        s_rs2 = cS.n
        wait("tensor", cS, s_rs2)
        for t in range(NTIL):
            em("tensor", lambda e, t=t: e.matmul(QS[3][:, t * 128:(t + 1) * 128], ONES[:],
                                                 ZB[0:1, t * 128:(t + 1) * 128], start=True, stop=True), cT)
        t_r2b = cT.n
        wait("vector", cT, t_r2b)
        em("vector", lambda e: e.tensor_copy(out=RSTD2B[:], in_=QS[3][:, 0:SBA]), cV)
        v_r2b = cV.n
        _tap("rstd2_a%d" % a, lambda: RSTD2B[:])

        h2v = H2FM.ap()[:, :].rearrange("p (c w) -> p c w", w=SBA + 1)
        if a == 0:
            em("vector", lambda e, h2v=h2v: e.memset(h2v[:, :, 0:1], 0.0), cV)
        else:
            em("vector", lambda e, h2v=h2v: e.tensor_copy(out=h2v[:, :, 0:1], in_=h2v[:, :, SBA:SBA + 1]), cV)
        wait("vector", cV, v_r2b)
        for c in range(CH):
            h2b = lambda c=c: H2FM[:, c * (SBA + 1) + 1:(c + 1) * (SBA + 1)]
            em("vector", lambda e, c=c, h2b=h2b: e.scalar_tensor_tensor(
                out=h2b(), in0=X1BF[:, c * SBA:(c + 1) * SBA], scalar=PC["ln2"][:, c:c + 1],
                in1=RSTD2B[:], op0=OP.mult, op1=OP.mult), cV)
            em("vector", lambda e, c=c, h2b=h2b: e.tensor_tensor(
                out=DBUF[:, c * SBA:(c + 1) * SBA], in0=h2b(),
                in1=H2FM[:, c * (SBA + 1):(c + 1) * (SBA + 1) - 1], op=OP.subtract), cV)
            em("vector", lambda e, c=c: e.scalar_tensor_tensor(
                out=XK2[:, c * SBA:(c + 1) * SBA], in0=DBUF[:, c * SBA:(c + 1) * SBA],
                scalar=PC["tmk2"][:, c:c + 1],
                in1=H2FM[:, c * (SBA + 1):(c + 1) * (SBA + 1) - 1],
                op0=OP.mult, op1=OP.add), cV)
        v_xk2 = cV.n
        _tap("xk2_a%d" % a, lambda: XK2[:])

        wait("sync", cT, cT.n)
        em("sync", dma_w8(WOUTV[:, 0:CH * C], wr2b_s), cWD, 16)
        wd_r2 = cWD.n
        wait("tensor", cWD, wd_r2)
        wait("tensor", cV, v_r2b)
        r2_t = []
        XRp = (XR2J, GBUF)
        for half in range(2):
            for ci in range(CH):
                rr = half * CH + ci
                if rr >= 2:
                    wait("vector", cT, r2_t[-2])
                else:
                    wait("vector", cT, cT.n)
                em("vector", lambda e, ci=ci, rr=rr: e.scalar_tensor_tensor(
                    out=XRp[rr % 2][:, 0:SBA], in0=DBUF[:, ci * SBA:(ci + 1) * SBA],
                    scalar=PC["tmr2"][:, ci:ci + 1],
                    in1=H2FM[:, ci * (SBA + 1):(ci + 1) * (SBA + 1) - 1],
                    op0=OP.mult, op1=OP.add), cV)
                v_xr = cV.n
                wait("tensor", cV, v_xr)
                for cg in range(4):
                    co = half * 4 + cg
                    em("tensor", lambda e, co=co, ci=ci, cg=cg, rr=rr: e.matmul(
                        QS[cg][:, 0:512], WOUTV[:, ci * C + co * 128:ci * C + co * 128 + 128],
                        XRp[rr % 2][:, 0:SBA],
                        start=(ci == 0), stop=(ci == CH - 1)), cT)
                r2_t.append(cT.n)
            t_r2 = cT.n
            wait("scalar", cT, t_r2)
            for cg in range(4):
                co = half * 4 + cg
                em("scalar", lambda e, co=co, cg=cg: e.activation(
                    out=R2TH[:, co * SBA:(co + 1) * SBA], in_=QS[cg][:, 0:512],
                    func=AF.Tanh, scale=0.5), cS)
            wait("tensor", cS)
        s_r2 = cS.n
        _tap("r2th_a%d" % a, lambda: R2TH[:])

        k2base = 0
        wk2v = wk2_in.rearrange("(cc p) q -> p cc q", p=128)
        w1v = w1_in.rearrange("(cc p) q -> p cc q", p=128)
        bmap = [(0, 512), (1, 0), (1, 512), (2, 0), (2, 512), (3, 0), (3, 512)]
        GBp = (GBUF, XR2J)
        s_rl = {}
        v_sq = {}
        t_k2 = {}
        def cm_slabs(g):
            gp = g % 2
            wait("sync", cT, cT.n)
            wait("gpsimd", cT, cT.n)
            em("sync", lambda e, g=g, gp=gp: e.dma_start(
                out=WSA[:, gp * 4096:(gp + 1) * 4096].rearrange("p (b q) -> p b q", b=CH),
                in_=wk2b_s[:, :, g * 512:g * 512 + 512]), cWD, 16)
            em("gpsimd", lambda e, g=g, gp=gp: e.dma_start(
                out=WSB[:, gp * 4 * C:(gp + 1) * 4 * C].rearrange("p (a c) -> p a c", a=4),
                in_=wv2b_s[:, g * 4:g * 4 + 4, :]), cWD2, 16)
            wd_grp[g] = (cWD.n, cWD2.n)
        wd_grp = {}
        cm_slabs(0)
        for i in range(ICH + 2):
            pi = i % 2
            if i < ICH:
                if i % 4 == 0:
                    wait("tensor", cWD, wd_grp[i // 4][0])
                    wait("tensor", cWD2, wd_grp[i // 4][1])
                if i == 30:
                    wait("sync", cT, cT.n)
                    em("sync", lambda e: e.dma_start(
                        out=WSB[:, 0:4096].rearrange("p (j c) -> p j c", j=ICH),
                        in_=wv2b_s[:, :, 896:1024]), cWD, 16)
                    wd_cm7 = cWD.n
                ip = (i // 4) % 2
                io = i % 4
                if i == 0:
                    wait("tensor", cV, v_xk2)
                    wait("tensor", cS, s_r2)
                else:
                    wait("tensor", cS, s_rl[i - 1])
                for ci in range(CH):
                    em("tensor", lambda e, i=i, ci=ci, ip=ip, io=io: e.matmul(
                        QS[0][:, 0:512],
                        WSA[:, ip * 4096 + ci * 512 + io * 128:ip * 4096 + ci * 512 + io * 128 + 128],
                        XK2[:, ci * SBA:(ci + 1) * SBA],
                        start=(ci == 0), stop=(ci == CH - 1)), cT)
                t_k2[i] = cT.n
            j = i - 2
            if 0 <= j < ICH:
                jp = (j // 4) % 2
                jo = j % 4
                wait("tensor", cV, v_sq[j])
                for co in range(7):
                    sl, qo = bmap[co]
                    em("tensor", lambda e, j=j, co=co, sl=sl, qo=qo, jp=jp, jo=jo: e.matmul(
                        QS[sl][:, qo:qo + 512],
                        WSB[:, jp * 4 * C + jo * C + co * 128:jp * 4 * C + jo * C + co * 128 + 128],
                        K2STG[:, k2base + j * SBA:k2base + (j + 1) * SBA],
                        start=(j == 0), stop=(j == ICH - 1)), cT)
            if i % 4 == 1 and i // 4 + 1 < ICH // 4:
                cm_slabs(i // 4 + 1)
            if i < ICH:
                wait("scalar", cT, t_k2[i])
                if i >= 2:
                    wait("scalar", cV, v_sq[i - 2])
                em("scalar", lambda e, pi=pi: e.activation(
                    out=GBp[pi][:, 0:SBA], in_=QS[0][:, 0:512], func=AF.Relu), cS)
                s_rl[i] = cS.n
                wait("vector", cS, s_rl[i])
                em("vector", lambda e, i=i, pi=pi: e.tensor_tensor(
                    out=K2STG[:, k2base + i * SBA:k2base + (i + 1) * SBA],
                    in0=GBp[pi][:, 0:SBA], in1=GBp[pi][:, 0:SBA], op=OP.mult), cV)
                v_sq[i] = cV.n
        _tap("k2_a%d" % a, lambda: K2STG[:])

        wait("tensor", cWD, wd_cm7)
        wait("tensor", cS, s_rl[ICH - 1])
        for j in range(ICH):
            em("tensor", lambda e, j=j: e.matmul(
                QS[0][:, 0:512], WSB[:, j * 128:(j + 1) * 128],
                K2STG[:, k2base + j * SBA:k2base + (j + 1) * SBA],
                start=(j == 0), stop=(j == ICH - 1)), cT)
        t_cm = cT.n
        v_cm7 = None
        for co in [7, 0, 1, 2, 3, 4, 5, 6]:
            if co < 7:
                sl, qo = bmap[co]
            else:
                sl, qo = 0, 0
            wait("vector", cT, t_cm)
            em("vector", lambda e, co=co, sl=sl, qo=qo: e.scalar_tensor_tensor(
                out=GLB[:], in0=R2TH[:, co * SBA:(co + 1) * SBA], scalar=1.0,
                in1=QS[sl][:, qo:qo + 512], op0=OP.add, op1=OP.mult), cV)
            em("vector", lambda e, co=co: e.scalar_tensor_tensor(
                out=FINALW[:, co * SBA:(co + 1) * SBA], in0=GLB[:], scalar=0.5,
                in1=FINALW[:, co * SBA:(co + 1) * SBA], op0=OP.mult, op1=OP.add), cV)
            if co == 7:
                v_cm7 = cV.n
        v_cm = cV.n
        _tap("cm_a%d" % a, lambda: FINALW[:])

        ITp = (ITILE, ITILE2)
        IQFp = (Z1B, IQF2)
        FLBp = (FLB, FLB2)
        GLBp = (GLB, GLB2)
        GHBp = (GHB, GHB2)
        bmap = [(0, 512), (1, 0), (1, 512), (2, 0), (2, 512), (3, 0), (3, 512)]
        v_iq = {}
        v_fi = {}
        v_i2 = {}
        v_td = {}
        g_sum = {}
        s_iqf = {}
        s_flb = {}
        s_g = {}
        s_hq = {}
        g_dd = {}
        t_w1 = {}
        def ffn_slabs(g):
            gp = g % 2
            wait("sync", cT, cT.n)
            wait("gpsimd", cT, cT.n)
            em("sync", lambda e, g=g, gp=gp: e.dma_start(
                out=WSA[:, gp * 4096:(gp + 1) * 4096].rearrange("p (b q) -> p b q", b=CH),
                in_=w1b_s[:, :, g * 512:g * 512 + 512]), cWD, 16)
            em("gpsimd", lambda e, g=g, gp=gp: e.dma_start(
                out=WSB[:, gp * 4 * C:(gp + 1) * 4 * C].rearrange("p (a c) -> p a c", a=4),
                in_=w2b_s[:, g * 4:g * 4 + 4, :]), cWD2, 16)
            wd_grp[g] = (cWD.n, cWD2.n)
        wd_grp = {}
        ffn_slabs(0)
        for i in range(ICH + 2):
            pi = i % 2
            if i < ICH:
                if i % 4 == 0:
                    wait("tensor", cWD, wd_grp[i // 4][0])
                    wait("tensor", cWD2, wd_grp[i // 4][1])
                if i == 30:
                    wait("sync", cT, cT.n)
                    em("sync", lambda e: e.dma_start(
                        out=WSB[:, 0:4096].rearrange("p (j c) -> p j c", j=ICH),
                        in_=w2b_s[:, :, 896:1024]), cWD, 16)
                    wd_co7 = cWD.n
                ip = (i // 4) % 2
                io = i % 4
                if i == 0:
                    wait("tensor", cV, v_cm7)
                else:
                    wait("tensor", cV, v_iq[i - 1])
                for ci in range(CH):
                    em("tensor", lambda e, i=i, ci=ci, ip=ip, io=io: e.matmul(
                        QS[0][:, 0:512],
                        WSA[:, ip * 4096 + ci * 512 + io * 128:ip * 4096 + ci * 512 + io * 128 + 128],
                        XNFM[:, ci * SBA:(ci + 1) * SBA],
                        start=(ci == 0), stop=(ci == CH - 1)), cT)
                t_w1[i] = cT.n
            j = i - 2
            if 0 <= j < ICH:
                jp = (j // 4) % 2
                jo = j % 4
                wait("tensor", cS, s_hq[j])
                if j == 0:
                    wait("tensor", cV, v_cm)
                for co in range(7):
                    sl, qo = bmap[co]
                    em("tensor", lambda e, j=j, co=co, sl=sl, qo=qo, jp=jp, jo=jo: e.matmul(
                        QS[sl][:, qo:qo + 512],
                        WSB[:, jp * 4 * C + jo * C + co * 128:jp * 4 * C + jo * C + co * 128 + 128],
                        K2STG[:, k2base + j * SBA:k2base + (j + 1) * SBA],
                        start=(j == 0), stop=(j == ICH - 1)), cT)
            if i % 4 == 1 and i // 4 + 1 < ICH // 4:
                ffn_slabs(i // 4 + 1)
            h = i - 1
            if i < ICH:
                wait("vector", cT, t_w1[i])
                if i >= 1:
                    wait("vector", cS, s_g[i - 1])
                em("vector", lambda e, i=i: e.tensor_scalar(
                    out=ITILE[:], in0=QS[0][:, 0:512], scalar1=256.0, scalar2=B1S[:, i:i + 1],
                    op0=OP.mult, op1=OP.add), cV)
                v_iq[i] = cV.n
                if i == 0 and ("iq0_a%d" % a) in taps:
                    tq0 = nc.dram_tensor("tap_iq0_a%d" % a, [128, SBA], I32, kind="ExternalOutput").ap()
                    wait("sync", cV)
                    em("sync", lambda e, tq0=tq0: e.dma_start(out=tq0, in_=ITILE[:]), cOUT, 16)
                    wait("vector", cOUT, cOUT.n)
                wait("scalar", cV, v_iq[i])
                em("scalar", lambda e, pi=pi: e.activation(
                    out=IQFp[pi][:], in_=ITILE[:], func=AF.Copy, scale=1.0), cS)
                s_iqf[i] = cS.n
            if 0 <= h < ICH:
                ph = h % 2
                wait("vector", cS, s_flb[h])
                em("vector", lambda e, ph=ph: e.scalar_tensor_tensor(
                    out=IQFp[ph][:], in0=IQFp[ph][:], scalar=0.015625, in1=FLBp[ph][:],
                    op0=OP.mult, op1=OP.subtract), cV)
                wait("vector", cG, g_dd[h])
                em("vector", lambda e, ph=ph: e.scalar_tensor_tensor(
                    out=GHBp[ph][:], in0=IQFp[ph][:], scalar=16.0, in1=GHBp[ph][:],
                    op0=OP.add, op1=OP.mult), cV)
                v_td[h] = cV.n
                wait("gpsimd", cV, v_td[h])
                em("gpsimd", lambda e, ph=ph: e.tensor_tensor(
                    out=GLBp[ph][:], in0=GLBp[ph][:], in1=GHBp[ph][:], op=OP.add), cG)
                g_sum[h] = cG.n
            if i < ICH:
                wait("vector", cS, s_iqf[i])
                em("vector", lambda e, pi=pi: e.tensor_scalar(
                    out=ITILE[:], in0=IQFp[pi][:], scalar1=0.015625, scalar2=15.5078125,
                    op0=OP.mult, op1=OP.add), cV)
                v_fi[i] = cV.n
                wait("scalar", cV, v_fi[i])
                em("scalar", lambda e, pi=pi: e.activation(
                    out=FLBp[pi][:], in_=ITILE[:], func=AF.Copy, scale=1.0), cS)
                s_flb[i] = cS.n
                if i >= 2:
                    wait("scalar", cG, g_sum[i - 2])
                em("scalar", lambda e, pi=pi: e.activation(
                    out=GLBp[pi][:], in_=ITILE[:], func=AF.Gelu, scale=0.25, bias=CBIA[:, 2:3]), cS)
                em("scalar", lambda e, pi=pi: e.activation(
                    out=GHBp[pi][:], in_=ITILE[:], func=AF.Gelu, scale=0.25, bias=CBIA[:, 3:4]), cS)
                s_g[i] = cS.n
                wait("gpsimd", cS, s_g[i])
                em("gpsimd", lambda e, pi=pi: e.tensor_tensor(
                    out=GHBp[pi][:], in0=GHBp[pi][:], in1=GLBp[pi][:], op=OP.subtract), cG)
                g_dd[i] = cG.n
            if 0 <= h < ICH:
                ph = h % 2
                wait("vector", cG, g_sum[h])
                if h >= 1:
                    wait("vector", cS, s_hq[h - 1])
                em("vector", lambda e, ph=ph: e.tensor_scalar(
                    out=ITILE2[:], in0=GLBp[ph][:], scalar1=256.0, scalar2=None, op0=OP.mult), cV)
                v_i2[h] = cV.n
                wait("scalar", cV, v_i2[h])
                em("scalar", lambda e, h=h: e.activation(
                    out=K2STG[:, k2base + h * SBA:k2base + (h + 1) * SBA],
                    in_=ITILE2[:], func=AF.Copy, scale=1.0 / 256.0), cS)
                s_hq[h] = cS.n
        _tap("hq_a%d" % a, lambda: K2STG[:])

        wait("tensor", cWD, wd_co7)
        wait("tensor", cV, v_iq[ICH - 1])
        for j in range(ICH):
            em("tensor", lambda e, j=j: e.matmul(
                QS[0][:, 0:512], WSB[:, j * 128:(j + 1) * 128],
                K2STG[:, k2base + j * SBA:k2base + (j + 1) * SBA],
                start=(j == 0), stop=(j == ICH - 1)), cT)
        t_w2 = cT.n
        for co in range(CH):
            if co < 7:
                sl, qo = bmap[co]
            else:
                sl, qo = 0, 0
            wait("vector", cT, t_w2)
            em("vector", lambda e, co=co, sl=sl, qo=qo: e.tensor_scalar(
                out=ITILE[:], in0=QS[sl][:, qo:qo + 512], scalar1=256.0, scalar2=B2S[:, co:co + 1],
                op0=OP.mult, op1=OP.add), cV)
            em("vector", lambda e, co=co: e.scalar_tensor_tensor(
                out=FINALW[:, co * SBA:(co + 1) * SBA], in0=ITILE[:], scalar=1.0 / 256.0,
                in1=FINALW[:, co * SBA:(co + 1) * SBA], op0=OP.mult, op1=OP.add), cV)
        v_ffn = cV.n
        _tap("final_a%d" % a, lambda: FINALW[:])

        fv = FINALW.ap()[:, :].rearrange("p (c w) -> p c w", c=CH)
        f_ld = {}
        f_vxpn = {}
        f_vout = {}
        f_out = {}

        def fin_load(t):
            gt = tok0 + t * 128
            ring = "sync" if t % 2 == 0 else "gpsimd"
            ctr = cLD if t % 2 == 0 else cLD2
            if t >= 2:
                wait(ring, cV, f_vxpn[t - 2])
            else:
                wait(ring, cV)
                wait(ring, cS)
            em(ring, lambda e, gt=gt, t=t: e.dma_start(
                out=XTp1[t % 2][:, 0:C], in_=x_in[gt:gt + 128, :]), ctr, 16)
            f_ld[t] = (ctr, ctr.n)
        fin_load(0)
        fin_load(1)
        for t in range(NTIL):
            g = g0 + t
            gt = tok0 + t * 128
            xt = XTp1[t % 2]
            osb = OSBp[t % 2]
            wait("vector", f_ld[t][0], f_ld[t][1])
            if t >= 2:
                wait("vector", cOUT, f_out[t - 2])
            else:
                wait("vector", cOUT, cOUT.n)
            em("vector", lambda e, g=g, xt=xt, osb=osb: e.tensor_scalar(
                out=osb[:, 0:C], in0=xt[:, 0:C],
                scalar1=MC[:, g:g + 1], scalar2=RSTD[:, g:g + 1],
                op0=OP.subtract, op1=OP.mult), cV)
            em("vector", lambda e, xt=xt, osb=osb: e.tensor_tensor(
                out=osb[:, 0:C], in0=osb[:, 0:C], in1=xt[:, 0:C], op=OP.add), cV)
            v_xpn = cV.n
            f_vxpn[t] = v_xpn
            if t + 2 < NTIL:
                fin_load(t + 2)
            q = QS[2 + (t % 2)]
            wait("tensor", cV, v_xpn)
            if t >= 2:
                wait("tensor", cV, f_vout[t - 2])
            for cb in range(CH):
                em("tensor", lambda e, cb=cb, t=t, q=q, fv=fv: e.transpose(
                    q[:, cb * 128:(cb + 1) * 128], fv[:, cb, t * 128:(t + 1) * 128], IDNF[:]), cT)
            t_f = cT.n
            wait("vector", cT, t_f)
            em("vector", lambda e, q=q, osb=osb: e.tensor_tensor(
                out=osb[:, 0:C], in0=osb[:, 0:C], in1=q[:], op=OP.add), cV)
            f_vout[t] = cV.n
            wait("sync", cV, f_vout[t])
            em("sync", lambda e, gt=gt, osb=osb: e.dma_start(
                out=out_t[gt:gt + 128, :], in_=osb[:, 0:C]), cOUT, 16)
            f_out[t] = cOUT.n

    wait("sync", cOUT, cOUT.n)

    with nc.Block() as block:
        @block.sync
        def _(e):
            for f in sched["sync"]:
                f(e)

        @block.gpsimd
        def _(e):
            for f in sched["gpsimd"]:
                f(e)

        @block.vector
        def _(e):
            for f in sched["vector"]:
                f(e)

        @block.scalar
        def _(e):
            for f in sched["scalar"]:
                f(e)

        @block.tensor
        def _(e):
            for f in sched["tensor"]:
                f(e)

    ctx.close()
    return nc


def _host_inputs(inputs):
    x = np.ascontiguousarray(np.asarray(inputs["L0_attn_x"], dtype=np.float32))
    row = lambda k: np.ascontiguousarray(np.asarray(inputs[k], dtype=np.float32).reshape(1, -1))
    base = {
        "wk": np.ascontiguousarray(np.asarray(inputs["L0_attn_W_key"], np.float32)),
        "wv": np.ascontiguousarray(np.asarray(inputs["L0_attn_W_value"], np.float32)),
        "wr": np.ascontiguousarray(np.asarray(inputs["L0_attn_W_recept"], np.float32)),
        "wo": np.ascontiguousarray(np.asarray(inputs["L0_attn_W_out"], np.float32)),
        "wr2": np.ascontiguousarray(np.asarray(inputs["L0_attn_W_recept2"], np.float32)),
        "wk2": np.ascontiguousarray(np.asarray(inputs["L0_attn_W_key2"], np.float32)),
        "wv2": np.ascontiguousarray(np.asarray(inputs["L0_attn_W_value2"], np.float32)),
        "w1": np.ascontiguousarray(np.asarray(inputs["L0_ffn_w1"], np.float32)),
        "w2": np.ascontiguousarray(np.asarray(inputs["L0_ffn_w2"], np.float32)),
        "ln1": row("L0_attn_ln1_w"), "ln2": row("L0_attn_ln2_w"),
        "tmk": row("L0_attn_time_mix_k"), "tmv": row("L0_attn_time_mix_v"),
        "tmr": row("L0_attn_time_mix_r"), "tmk2": row("L0_attn_time_mix_k2"),
        "tmr2": row("L0_attn_time_mix_r2"),
        "b1": row("L0_ffn_b1"), "b2": row("L0_ffn_b2"),
        "tdb": np.tile(row("L0_attn_time_decay"), (128, 1)),
        "ub": np.tile(row("L0_attn_time_first"), (128, 1)),
        "ltc": np.triu(np.ones((128, 128), np.float32), k=1),
        "ones1": np.ones((1, 128), np.float32),
        "idnf": np.eye(128, dtype=np.float32),
        "onesc": np.ones((128, 1)).astype(ml_dtypes.bfloat16),
        "onesr": np.ones((128, 1), np.float32),
        "cbias": np.tile(np.array([[1e-5, 1e-6, -4.0, -3.75, 0, 0, 0, 0]], np.float32), (128, 1)),
    }
    in_maps = []
    for b in range(NCORE):
        m = dict(base)
        m["x"] = np.ascontiguousarray(x[b])
        in_maps.append(m)
    return in_maps


def kernel(**inputs):
    nc = _build(taps=tuple(_TAPS))
    in_maps = _host_inputs(inputs)
    res = run_bass_kernel_spmd(nc, in_maps, core_ids=list(range(NCORE)))
    out = np.stack([np.asarray(r["out"]) for r in res.results], axis=0)
    return out.astype(np.float32)
